# version 01
# baseline (speedup 1.0000x reference)
import numpy as np
import ml_dtypes
from contextlib import ExitStack

import concourse.bass as bass
import concourse.mybir as mybir
from concourse.bass_utils import run_bass_kernel_spmd

F32 = mybir.dt.float32
F16 = mybir.dt.float16
BF16 = mybir.dt.bfloat16
I32 = mybir.dt.int32
U16 = mybir.dt.uint16
AF = mybir.ActivationFunctionType
ALU = mybir.AluOpType
AX = mybir.AxisListType

BF = ml_dtypes.bfloat16

C = 2048
TPC = 2048
NCORE = 8
NBLK = 16
TT = 512
NTT = TPC // TT


class EQ:
    SERIALIZED = {"q_dve", "q_act", "q_pool"}

    def __init__(self, prog, name):
        self.prog = prog
        self.name = name
        self.cnt = 0
        self.waited = {}
        self.items = []
        self.serialize = name in self.SERIALIZED

    def _wait(self, tag):
        if tag is None:
            return
        sem_name, val = tag
        if val <= 0:
            return
        if self.waited.get(sem_name, 0) >= val:
            return
        self.waited[sem_name] = val

        def w(e, sn=sem_name, v=val):
            e.wait_ge(self.prog.sems[sn], v)
        self.items.append(w)

    def op(self, fn, deps=()):
        if self.serialize and self.cnt > 0:
            self._wait((self.name, self.cnt))
        for d in deps:
            self._wait(d)
        self.cnt += 1

        def g(e, fn=fn):
            fn(e).then_inc(self.prog.sems[self.name], 1)
        self.items.append(g)
        return (self.name, self.cnt)

    def op_quiet(self, fn, deps=()):
        for d in deps:
            self._wait(d)
        self.items.append(lambda e, fn=fn: fn(e))
        return None

    def tag(self):
        return (self.name, self.cnt)


class Prog:
    def __init__(self, nc, ctx):
        self.nc = nc
        self.ctx = ctx
        self.sems = {}
        self.pe = EQ(self, "q_pe")
        self.act = EQ(self, "q_act")
        self.dve = EQ(self, "q_dve")
        self.pool = EQ(self, "q_pool")
        self.sp = EQ(self, "q_sp")
        self.queues = [self.pe, self.act, self.dve, self.pool, self.sp]
        for q in self.queues:
            self.sems[q.name] = ctx.enter_context(nc.semaphore(q.name))
        self.dma_cnt = {}
        self._consts = {}
        self._const_tile = ctx.enter_context(nc.sbuf_tensor("constf_tile", [128, 16], F32))
        self._const_n = 0

    def constf(self, val):
        if val in self._consts:
            return self._consts[val]
        i = self._const_n
        self._const_n += 1
        assert i < 16
        ap = self._const_tile[:, i:i + 1]
        tag = self.dve.op(lambda v, ap=ap, val=val: v.memset(ap, float(val)))
        self._consts[val] = (ap, tag)
        return ap, tag

    def _dma_chan(self, ch):
        key = "dch_" + ch
        if key not in self.sems:
            self.sems[key] = self.ctx.enter_context(self.nc.semaphore(key))
            self.dma_cnt[key] = 0
        return key

    def dma_hw(self, q, ch, out, in_, deps=(), transpose=False):
        key = self._dma_chan(ch)
        for d in deps:
            q._wait(d)
        self.dma_cnt[key] += 16
        val = self.dma_cnt[key]

        def g(e, out=out, in_=in_, t=transpose, key=key):
            e.dma_start(out=out, in_=in_, transpose=t).then_inc(self.sems[key], 16)
        q.items.append(g)
        return (key, val)

    def dma_sw(self, ch, out, in_, deps=()):
        key = self._dma_chan(ch)
        for d in deps:
            self.pool._wait(d)
        self.dma_cnt[key] += 16
        val = self.dma_cnt[key]

        def g(e, out=out, in_=in_, key=key):
            e.dma_start(out=out, in_=in_).then_inc(self.sems[key], 16)
        self.pool.items.append(g)
        return (key, val)

    def barrier(self):
        tags = [q.tag() for q in self.queues]
        tags += [(k, v) for k, v in self.dma_cnt.items()]
        for q in self.queues:
            for t in tags:
                if t[0] != q.name:
                    q._wait(t)

    def finish(self):
        for k, v in self.dma_cnt.items():
            self.sp._wait((k, v))

    def emit(self):
        block = self.ctx.enter_context(self.nc.Block())

        def runner(lst):
            def f(eng):
                for it in lst:
                    it(eng)
            return f
        block.tensor(runner(self.pe.items))
        block.scalar(runner(self.act.items))
        block.vector(runner(self.dve.items))
        block.gpsimd(runner(self.pool.items))
        block.sync(runner(self.sp.items))


def build_A():
    nc = bass.Bass("TRN2", target_bir_lowering=False)
    x_in = nc.dram_tensor("x_in", [TPC, C], F32, kind="ExternalInput").ap()
    wqT = nc.dram_tensor("wqT", [C, C], F32, kind="ExternalInput").ap()
    wkT = nc.dram_tensor("wkT", [C, C], F32, kind="ExternalInput").ap()
    wvT = nc.dram_tensor("wvT", [C, C], F32, kind="ExternalInput").ap()
    identf = nc.dram_tensor("identf", [128, 128], F32, kind="ExternalInput").ap()
    identb = nc.dram_tensor("identb", [128, 128], BF16, kind="ExternalInput").ap()
    gup = nc.dram_tensor("gup", [1408, 256], I32, kind="ExternalInput").ap()
    gup_sc = nc.dram_tensor("gup_sc", [1408, 16], F16, kind="ExternalInput").ap()
    dwp = nc.dram_tensor("dwp", [256, 704], I32, kind="ExternalInput").ap()
    dwp_sc = nc.dram_tensor("dwp_sc", [256, 44], F16, kind="ExternalInput").ap()

    q_o = nc.dram_tensor("q_o", [C, TPC], BF16, kind="ExternalOutput").ap()
    vk_o = nc.dram_tensor("vk_o", [128, 16 * 33], F32, kind="ExternalOutput").ap()
    wgu_o = nc.dram_tensor("wgu_o", [1408, C], BF16, kind="ExternalOutput").ap()
    wdn_o = nc.dram_tensor("wdn_o", [256, 5632], BF16, kind="ExternalOutput").ap()

    qv = q_o.rearrange("(oc p) t -> oc p t", p=128)

    with ExitStack() as ctx:
        def sb(nm, shp, dt, stack=None):
            return (stack or ctx).enter_context(nc.sbuf_tensor(nm, shp, dt))
        p = Prog(nc, ctx)

        idf = sb("idf", [128, 128], F32)
        idb = sb("idb", [128, 128], BF16)
        t_idf = p.dma_hw(p.sp, "idf", idf[:], identf)
        t_idb = p.dma_hw(p.sp, "idb", idb[:], identb)

        xb = sb("xb", [128, 16 * TPC], BF16)

        def xbs(c, tt, w=TT):
            return xb[:, c * TPC + tt * w: c * TPC + (tt + 1) * w]

        with ExitStack() as sctx, \
             nc.psum_tensor("pst0", [128, 128], F32) as pst0, \
             nc.psum_tensor("pst1", [128, 128], F32) as pst1:
            xtm = [sb(f"xtm{i}", [128, C], F32, sctx) for i in range(2)]
            psts = [pst0, pst1]
            t_blk_done = [None, None]
            t_evac = [None, None]
            for blk in range(NBLK):
                buf = xtm[blk % 2]
                tl = p.dma_hw(p.sp, f"xtm{blk % 2}", buf[:], x_in[blk * 128:(blk + 1) * 128, :],
                              deps=[t_blk_done[blk % 2]])
                last_tp = None
                for c in range(16):
                    ps = psts[c % 2]
                    last_tp = p.pe.op(lambda t, ps=ps, buf=buf, c=c:
                                      t.transpose(ps[:], buf[:, c * 128:(c + 1) * 128], idf[:]),
                                      deps=[tl, t_idf, t_evac[c % 2]])
                    t_evac[c % 2] = p.dve.op(
                        lambda v, ps=ps, c=c, blk=blk:
                        v.tensor_copy(xb[:, c * TPC + blk * 128: c * TPC + (blk + 1) * 128], ps[:]),
                        deps=[last_tp])
                t_blk_done[blk % 2] = last_tp
        p.barrier()

        onesb = sb("onesb", [128, 1], BF16)
        onesr = sb("onesr", [1, 128], BF16)
        t_ob = p.dve.op(lambda v: v.memset(onesb[:], 1.0))
        t_or = p.dve.op(lambda v: v.memset(onesr[:], 1.0))
        s23 = ctx.enter_context(ExitStack())
        sqt = [sb(f"sqt{i}", [128, TT], BF16, s23) for i in range(2)]
        sumr = sb("sumr", [1, TPC], F32, s23)
        sqr = sb("sqr", [1, TPC], F32, s23)
        mu_b = sb("mu_b", [1, TPC], BF16, s23)
        rs_b = sb("rs_b", [1, TPC], BF16, s23)
        rowa = sb("rowa", [1, TPC], F32, s23)
        roww = sb("roww", [1, TPC], F32, s23)

        with nc.psum_tensor("pssum", [1, TT], F32) as pssum, \
             nc.psum_tensor("pssq", [1, TT], F32) as pssq:
            t_sum_evac = None
            t_sq_evac = None
            for tt in range(NTT):
                t_mm = None
                for c in range(16):
                    fn = p.pe.op if c == 15 else p.pe.op_quiet
                    t_mm = fn(lambda t, c=c, tt=tt, s=(c == 0), e=(c == 15):
                              t.matmul(pssum[:], onesb[:], xbs(c, tt), start=s, stop=e),
                              deps=[t_ob, t_sum_evac] if c == 0 else ())
                t_sum_evac = p.dve.op(lambda v, tt=tt:
                                      v.tensor_copy(sumr[0:1, tt * TT:(tt + 1) * TT], pssum[:]),
                                      deps=[t_mm])
                t_sqa = [None, None]
                t_mmq = None
                for c in range(16):
                    t_sqa[c % 2] = p.act.op(lambda s, c=c, tt=tt, b=c % 2:
                                            s.activation(sqt[b][:], xbs(c, tt), AF.Square),
                                            deps=[t_mmq])
                    t_mmq = p.pe.op(lambda t, c=c, s=(c == 0), e=(c == 15), b=c % 2:
                                    t.matmul(pssq[:], onesb[:], sqt[b][:], start=s, stop=e),
                                    deps=[t_sqa[c % 2]] + ([t_sq_evac] if c == 0 else []))
                t_sq_evac = p.dve.op(lambda v, tt=tt:
                                     v.tensor_copy(sqr[0:1, tt * TT:(tt + 1) * TT], pssq[:]),
                                     deps=[t_mmq])
        t1 = p.dve.op(lambda v: v.tensor_scalar_mul(rowa[:], sumr[:], 1.0 / C), deps=[t_sum_evac])
        t_mub = p.dve.op(lambda v: v.tensor_copy(mu_b[:], rowa[:]), deps=[t1])
        t2 = p.dve.op(lambda v: v.tensor_scalar_mul(roww[:], sqr[:], 1.0 / C), deps=[t_sq_evac])
        t3 = p.dve.op(lambda v: v.tensor_tensor(sumr[:], rowa[:], rowa[:], ALU.mult), deps=[t1, t2])
        t4 = p.dve.op(lambda v: v.tensor_sub(roww[:], roww[:], sumr[:]), deps=[t3])
        c_eps, tg_eps = p.constf(1e-5)
        t5 = p.act.op(lambda s: s.activation(rowa[:], roww[:], AF.Sqrt, bias=c_eps[0:1, :]),
                      deps=[t4, t_mub, tg_eps])
        t6 = p.dve.op(lambda v: v.reciprocal(roww[:], rowa[:]), deps=[t5])
        t_rsb = p.dve.op(lambda v: v.tensor_copy(rs_b[:], roww[:]), deps=[t6])
        p.barrier()

        mub_s = sb("mub_s", [128, TT], BF16, s23)
        rsb_s = sb("rsb_s", [128, TT], BF16, s23)
        tmp3 = sb("tmp3", [128, TT], BF16, s23)
        with nc.psum_tensor("psbc0", [128, TT], F32) as psbc0, \
             nc.psum_tensor("psbc1", [128, TT], F32) as psbc1:
            te1 = te2 = None
            tb = None
            for tt in range(NTT):
                tm1 = p.pe.op(lambda t, tt=tt: t.matmul(psbc0[:], onesr[:],
                                                        mu_b[0:1, tt * TT:(tt + 1) * TT],
                                                        start=True, stop=True),
                              deps=[t_mub, t_or, te1])
                tm2 = p.pe.op(lambda t, tt=tt: t.matmul(psbc1[:], onesr[:],
                                                        rs_b[0:1, tt * TT:(tt + 1) * TT],
                                                        start=True, stop=True),
                              deps=[t_rsb, te2])
                te1 = p.dve.op(lambda v: v.tensor_copy(mub_s[:], psbc0[:]), deps=[tm1, tb])
                te2 = p.dve.op(lambda v: v.tensor_copy(rsb_s[:], psbc1[:]), deps=[tm2])
                for c in range(16):
                    ta = p.dve.op(lambda v, c=c, tt=tt: v.tensor_sub(tmp3[:], xbs(c, tt), mub_s[:]),
                                  deps=[te1])
                    tb = p.dve.op(lambda v, c=c, tt=tt:
                                  v.tensor_tensor(xbs(c, tt), tmp3[:], rsb_s[:], ALU.mult),
                                  deps=[ta, te2])
        p.barrier()
        t_xn = p.dve.tag()
        s23.close()

        def w_cols(wT, oc, width=128):
            return wT.rearrange("(kc p) o -> p kc o", p=128)[:, :, oc * width:(oc + 1) * width]

        with ExitStack() as s4, \
             nc.psum_tensor("psq0", [128, TT], F32) as psq0, \
             nc.psum_tensor("psq1", [128, TT], F32) as psq1:
            wqb = [sb(f"wqb{i}", [128, 16 * 128], BF16, s4) for i in range(2)]
            qstg = [sb(f"qstg{i}", [128, TT], BF16, s4) for i in range(2)]
            psqs = [psq0, psq1]
            t_wq_free = [None, None]
            t_qdma = [None, None]
            t_evq = [None, None]
            i = 0
            for oc in range(16):
                tw = p.dma_sw(f"wqb{oc % 2}", wqb[oc % 2][:], w_cols(wqT, oc), deps=[t_wq_free[oc % 2]])
                t_mm = None
                for tt in range(NTT):
                    ps = psqs[i % 2]
                    stg = qstg[i % 2]
                    for kc in range(16):
                        fn = p.pe.op if kc == 15 else p.pe.op_quiet
                        t_mm = fn(lambda t, kc=kc, tt=tt, oc=oc, ps=ps, s=(kc == 0), e=(kc == 15):
                                  t.matmul(ps[:], wqb[oc % 2][:, kc * 128:(kc + 1) * 128],
                                           xbs(kc, tt), start=s, stop=e),
                                  deps=([tw, t_evq[i % 2]] if kc == 0 else ()))
                    t_ev = p.act.op(lambda s, ps=ps, stg=stg: s.activation(stg[:], ps[:], AF.Relu),
                                    deps=[t_mm, t_qdma[i % 2]])
                    t_evq[i % 2] = t_ev
                    t_qdma[i % 2] = p.dma_hw(p.sp, f"qst{i % 2}", qv[oc][:, tt * TT:(tt + 1) * TT],
                                             stg[:], deps=[t_ev])
                    i += 1
                t_wq_free[oc % 2] = t_mm
        p.barrier()

        s5 = ctx.enter_context(ExitStack())
        wkb = [sb(f"wkb{i}", [128, 16 * 128], BF16, s5) for i in range(2)]
        wvb = [sb(f"wvb{i}", [128, 16 * 128], BF16, s5) for i in range(2)]
        ktile = [sb(f"ktile{i}", [128, TT], BF16, s5) for i in range(2)]
        vtile = [sb(f"vtile{i}", [128, TT], BF16, s5) for i in range(2)]
        ktm = [sb(f"ktm{i}", [128, 128], BF16, s5) for i in range(2)]
        vtm = [sb(f"vtm{i}", [128, 132], BF16, s5) for i in range(2)]
        vk_sb = sb("vk_sb", [128, 16 * 33], F32, s5)
        t_vob = [p.dve.op(lambda v, b=b: v.memset(
            vtm[b][:].rearrange("p (h e) -> p h e", e=33)[:, :, 32:33], 1.0)) for b in range(2)]

        with nc.psum_tensor("psk", [128, TT], F32) as psk, \
             nc.psum_tensor("psv", [128, TT], F32) as psv, \
             nc.psum_tensor("pstk", [128, 128], BF16) as pstk, \
             nc.psum_tensor("pstv", [128, 128], BF16) as pstv, \
             nc.psum_tensor("psvk0", [128, 33], F32) as psvk0, \
             nc.psum_tensor("psvk1", [128, 33], F32) as psvk1:
            psvks = [psvk0, psvk1]
            t_wk_free = [None, None]
            t_wv_free = [None, None]
            t_kev = None
            t_vev = None
            t_ktm_ev_prev = None
            t_vtm_ev_prev = None
            t_vk_evac = [None, None]
            t_vk_acc = None
            jj = 0
            for cc in range(16):
                twk = p.dma_sw(f"wkb{cc % 2}", wkb[cc % 2][:], w_cols(wkT, cc), deps=[t_wk_free[cc % 2]])
                twv = p.dma_sw(f"wvb{cc % 2}", wvb[cc % 2][:], w_cols(wvT, cc), deps=[t_wv_free[cc % 2]])
                psvk = psvks[cc % 2]
                t_mmk = t_mmv = None
                first_vk = True
                tvk = None
                for tt in range(NTT):
                    kb = jj % 2
                    for kc in range(16):
                        fn = p.pe.op if kc == 15 else p.pe.op_quiet
                        t_mmk = fn(lambda t, kc=kc, tt=tt, cc=cc, s=(kc == 0), e=(kc == 15):
                                   t.matmul(psk[:], wkb[cc % 2][:, kc * 128:(kc + 1) * 128],
                                            xbs(kc, tt), start=s, stop=e),
                                   deps=([twk, t_kev] if kc == 0 else ()))
                    t_kev = p.act.op(lambda s, b=kb: s.activation(ktile[b][:], psk[:], AF.Relu),
                                     deps=[t_mmk])
                    for kc in range(16):
                        fn = p.pe.op if kc == 15 else p.pe.op_quiet
                        t_mmv = fn(lambda t, kc=kc, tt=tt, cc=cc, s=(kc == 0), e=(kc == 15):
                                   t.matmul(psv[:], wvb[cc % 2][:, kc * 128:(kc + 1) * 128],
                                            xbs(kc, tt), start=s, stop=e),
                                   deps=([twv, t_vev] if kc == 0 else ()))
                    t_vev = p.act.op(lambda s, b=kb: s.activation(vtile[b][:], psv[:], AF.Identity),
                                     deps=[t_mmv])
                    for j in range(4):
                        jb = j % 2
                        t_ktp = p.pe.op(lambda t, j=j, b=kb:
                                        t.transpose(pstk[:], ktile[b][:, j * 128:(j + 1) * 128], idb[:]),
                                        deps=[t_kev, t_idb, t_ktm_ev_prev])
                        t_ktm_ev = p.dve.op(lambda v, jb=jb: v.tensor_copy(ktm[jb][:], pstk[:]),
                                            deps=[t_ktp])
                        t_ktm_ev_prev = t_ktm_ev
                        t_vtp = p.pe.op(lambda t, j=j, b=kb:
                                        t.transpose(pstv[:], vtile[b][:, j * 128:(j + 1) * 128], idb[:]),
                                        deps=[t_vev, t_vtm_ev_prev])
                        t_vtm_ev = p.dve.op(
                            lambda v, jb=jb: v.tensor_copy(
                                vtm[jb][:].rearrange("p (h e) -> p h e", e=33)[:, :, 0:32],
                                pstv[:].rearrange("p (h e) -> p h e", e=32)),
                            deps=[t_vtp, t_vob[jb]])
                        t_vtm_ev_prev = t_vtm_ev
                        for h in range(4):
                            fn = p.pe.op if h == 3 else p.pe.op_quiet
                            tvk = fn(lambda t, h=h, jb=jb, psvk=psvk:
                                     t.matmul(psvk[32 * h:32 * h + 32, :],
                                              ktm[jb][:, 32 * h:32 * h + 32],
                                              vtm[jb][:, 33 * h:33 * h + 33],
                                              start=True, stop=True, tile_position=(0, 32 * h)),
                                     deps=([t_ktm_ev, t_vtm_ev] +
                                           ([t_vk_acc] if h == 0 else [])))
                        if first_vk:
                            t_vk_acc = p.dve.op(
                                lambda v, cc=cc, psvk=psvk:
                                v.tensor_copy(vk_sb[:, cc * 33:(cc + 1) * 33], psvk[:]),
                                deps=[tvk])
                        else:
                            t_vk_acc = p.dve.op(
                                lambda v, cc=cc, psvk=psvk:
                                v.scalar_tensor_tensor(vk_sb[:, cc * 33:(cc + 1) * 33],
                                                       psvk[:], 0.0, vk_sb[:, cc * 33:(cc + 1) * 33],
                                                       ALU.add, ALU.add),
                                deps=[tvk])
                        first_vk = False
                    jj += 1
                t_wk_free[cc % 2] = t_mmk
                t_wv_free[cc % 2] = t_mmv
                t_vk_evac[cc % 2] = t_vk_acc
            p.dma_hw(p.sp, "vko", vk_o, vk_sb[:], deps=[t_vk_evac[0], t_vk_evac[1]])
        p.barrier()
        s5.close()

        s6 = ctx.enter_context(ExitStack())
        wI = sb("wI", [128, 704], I32, s6)
        vj_t = sb("vj_t", [128, 704], I32, s6)
        sg_t = sb("sg_t", [128, 704], I32, s6)
        ub_mag = sb("ub_mag", [128, 5632], U16, s6)
        ub_sgn = sb("ub_sgn", [128, 5632], U16, s6)
        tq1 = sb("tq1", [128, 5632], BF16, s6)
        gq = sb("gq", [128, 5632], BF16, s6)
        fin = sb("fin", [128, 5632], U16, s6)
        sclh = sb("sclh", [128, 44], F16, s6)
        scl32 = sb("scl32", [128, 44], F32, s6)
        t_chunk_done = None
        for packed, scales, out_d, nch, W, G in [
            (gup, gup_sc, wgu_o, 11, 256, 16),
            (dwp, dwp_sc, wdn_o, 2, 704, 44),
        ]:
            E = W * 8
            for ch in range(nch):
                tw = p.dma_hw(p.sp, "wI", wI[:, 0:W], packed[ch * 128:(ch + 1) * 128, :],
                              deps=[t_chunk_done])
                tsc = p.dma_hw(p.sp, "sclh", sclh[:, 0:G], scales[ch * 128:(ch + 1) * 128, :],
                               deps=[t_chunk_done])
                tsc32 = p.dve.op(lambda v, G=G: v.tensor_copy(scl32[:, 0:G], sclh[:, 0:G]),
                                 deps=[tsc, t_chunk_done])
                t_mag = []
                t_sgn = []
                for j in range(8):
                    sh = 4 * j - 6
                    op0 = ALU.logical_shift_right if sh >= 0 else ALU.logical_shift_left
                    tv = p.dve.op(lambda v, W=W, sh=abs(sh), op0=op0:
                                  v.tensor_scalar(vj_t[:, 0:W], wI[:, 0:W], sh, 0x1C0,
                                                  op0, ALU.bitwise_and),
                                  deps=[tw, t_chunk_done])
                    t_mag.append(p.dve.op(
                        lambda v, j=j, W=W, E=E: v.tensor_scalar(
                            ub_mag[:, 0:E].rearrange("p (w j) -> p w j", j=8)[:, :, j],
                            vj_t[:, 0:W], 0x3F00, None, ALU.add),
                        deps=[tv]))
                    sh2 = 4 * j - 12
                    op2 = ALU.logical_shift_right if sh2 >= 0 else ALU.logical_shift_left
                    ts2 = p.dve.op(
                        lambda v, W=W, sh2=abs(sh2), op2=op2: v.tensor_scalar(
                            sg_t[:, 0:W], wI[:, 0:W], sh2, 0x8000, op2, ALU.bitwise_and),
                        deps=[tw])
                    t_sgn.append(p.dve.op(
                        lambda v, j=j, W=W, E=E: v.tensor_scalar(
                            ub_sgn[:, 0:E].rearrange("p (w j) -> p w j", j=8)[:, :, j],
                            sg_t[:, 0:W], 0, None, ALU.add),
                        deps=[ts2]))
                c_m1, tg_m1 = p.constf(-1.0)
                c_2, tg_2 = p.constf(2.0)
                t_t1 = p.act.op(lambda s, E=E, c_m1=c_m1, c_2=c_2:
                                s.activation(tq1[:, 0:E], ub_mag[:, 0:E].bitcast(BF16),
                                             AF.Identity, bias=c_m1, scale=c_2),
                                deps=t_mag + [t_chunk_done, tg_m1, tg_2])
                t_g = p.dve.op(lambda v, E=E: v.scalar_tensor_tensor(
                    gq[:, 0:E], tq1[:, 0:E], 0.0, ub_mag[:, 0:E].bitcast(BF16), ALU.max, ALU.min),
                    deps=[t_t1] + t_mag)
                t_s = p.dve.op(lambda v, E=E, G=G: v.tensor_tensor(
                    tq1[:, 0:E].rearrange("p (g k) -> p g k", g=G),
                    gq[:, 0:E].rearrange("p (g k) -> p g k", g=G),
                    scl32[:, 0:G].unsqueeze(-1).broadcast_to([128, G, 128]), ALU.mult),
                    deps=[t_g, tsc32])
                t_f = p.dve.op(lambda v, E=E: v.tensor_tensor(
                    fin[:, 0:E], tq1[:, 0:E].bitcast(U16), ub_sgn[:, 0:E], ALU.bitwise_or),
                    deps=[t_s] + t_sgn)
                t_dma = p.dma_hw(p.sp, "fino", out_d[ch * 128:(ch + 1) * 128, :],
                                 fin[:, 0:E].bitcast(BF16), deps=[t_f])
                t_chunk_done = t_dma

        p.finish()
        p.emit()
    return nc


def build_B():
    nc = bass.Bass("TRN2", target_bir_lowering=False)
    x_in = nc.dram_tensor("x_in", [TPC, C], F32, kind="ExternalInput").ap()
    q_in = nc.dram_tensor("q_in", [C, TPC], BF16, kind="ExternalInput").ap()
    vks = [nc.dram_tensor(f"vk{i}", [128, 16 * 33], F32, kind="ExternalInput").ap()
           for i in range(4)]
    wpT = nc.dram_tensor("wpT", [C, C], F32, kind="ExternalInput").ap()
    bproj = nc.dram_tensor("bproj", [C], F32, kind="ExternalInput").ap()
    e4 = nc.dram_tensor("e4", [4, 128], BF16, kind="ExternalInput").ap()
    identf = nc.dram_tensor("identf", [128, 128], F32, kind="ExternalInput").ap()

    x1_o = nc.dram_tensor("x1_o", [C, TPC], F32, kind="ExternalOutput").ap()
    xn2_o = nc.dram_tensor("xn2_o", [C, TPC], BF16, kind="ExternalOutput").ap()

    qvv = q_in.rearrange("(c p) t -> c p t", p=128)
    x1v = x1_o.rearrange("(c p) t -> c p t", p=128)
    xn2v = xn2_o.rearrange("(c p) t -> c p t", p=128)

    with ExitStack() as ctx:
        def sb(nm, shp, dt, stack=None):
            return (stack or ctx).enter_context(nc.sbuf_tensor(nm, shp, dt))
        p = Prog(nc, ctx)

        idf = sb("idf", [128, 128], F32)
        t_idf = p.dma_hw(p.sp, "idf", idf[:], identf)
        xf = sb("xf", [128, 16 * TPC], F32)

        def xfs(c, tt, w=TT):
            return xf[:, c * TPC + tt * w: c * TPC + (tt + 1) * w]

        with ExitStack() as sctx, \
             nc.psum_tensor("pst0", [128, 128], F32) as pst0, \
             nc.psum_tensor("pst1", [128, 128], F32) as pst1:
            xtm = [sb(f"xtm{i}", [128, C], F32, sctx) for i in range(2)]
            psts = [pst0, pst1]
            t_blk_done = [None, None]
            t_evac = [None, None]
            for blk in range(NBLK):
                buf = xtm[blk % 2]
                tl = p.dma_hw(p.sp, f"xtm{blk % 2}", buf[:], x_in[blk * 128:(blk + 1) * 128, :],
                              deps=[t_blk_done[blk % 2]])
                last_tp = None
                for c in range(16):
                    ps = psts[c % 2]
                    last_tp = p.pe.op(lambda t, ps=ps, buf=buf, c=c:
                                      t.transpose(ps[:], buf[:, c * 128:(c + 1) * 128], idf[:]),
                                      deps=[tl, t_idf, t_evac[c % 2]])
                    t_evac[c % 2] = p.act.op(
                        lambda s, ps=ps, c=c, blk=blk:
                        s.copy(xf[:, c * TPC + blk * 128: c * TPC + (blk + 1) * 128], ps[:]),
                        deps=[last_tp])
                t_blk_done[blk % 2] = last_tp
        p.barrier()

        s2 = ctx.enter_context(ExitStack())
        vk_sb = sb("vkB", [128, 16 * 33], F32, s2)
        vk_tmp = sb("vkBt", [128, 16 * 33], F32, s2)
        t_v0 = p.dma_hw(p.sp, "vkl0", vk_sb[:], vks[0])
        tsum = t_v0
        for i in range(1, 4):
            tv = p.dma_hw(p.sp, "vkl1", vk_tmp[:], vks[i], deps=[tsum])
            tsum = p.dve.op(lambda v: v.tensor_add(vk_sb[:], vk_sb[:], vk_tmp[:]),
                            deps=[tv, tsum])
        bd = sb("bd", [128, 16 * 128], BF16, s2)
        ksd = sb("ksd", [128, 16 * 4], BF16, s2)
        t_bd0 = p.dve.op(lambda v: v.memset(bd[:], 0.0), deps=[])
        t_ks0 = p.dve.op(lambda v: v.memset(ksd[:], 0.0), deps=[])
        t_bd = []
        for c in range(16):
            for h in range(4):
                t_bd.append(p.dve.op(
                    lambda v, c=c, h=h: v.tensor_copy(
                        bd[32 * h:32 * h + 32, c * 128 + 32 * h: c * 128 + 32 * h + 32],
                        vk_sb[32 * h:32 * h + 32, c * 33: c * 33 + 32]),
                    deps=[tsum, t_bd0]))
                t_bd.append(p.dve.op(
                    lambda v, c=c, h=h: v.tensor_copy(
                        ksd[32 * h:32 * h + 32, c * 4 + h: c * 4 + h + 1],
                        vk_sb[32 * h:32 * h + 32, c * 33 + 32: c * 33 + 33]),
                    deps=[tsum, t_ks0]))
        t_bdall = p.dve.tag()
        e4b = sb("e4b", [4, 128], BF16, s2)
        t_e4 = p.dma_hw(p.sp, "e4", e4b[:], e4)
        bias_sb = sb("bias_sb", [128, 16], F32, s2)
        t_bias = None
        for oc in range(16):
            t_bias = p.dma_hw(p.sp, "bias", bias_sb[:, oc:oc + 1],
                              bproj[oc * 128:(oc + 1) * 128].unsqueeze(-1), deps=[t_bias])
        p.barrier()

        s3 = ctx.enter_context(ExitStack())
        qt = [sb(f"qt{i}", [128, TT], BF16, s3) for i in range(2)]
        att = sb("att", [128, 16 * TT], BF16, s3)
        denr = sb("denr", [4, TT], F32, s3)
        denb = sb("denb", [4, TT], BF16, s3)
        rbb = sb("rbb", [128, TT], BF16, s3)
        wpb = [sb(f"wpb{i}", [128, 16 * 128], BF16, s3) for i in range(2)]

        def w_cols(wT, oc, width=128):
            return wT.rearrange("(kc p) o -> p kc o", p=128)[:, :, oc * width:(oc + 1) * width]

        with nc.psum_tensor("psaf", [128, TT], F32) as psaf, \
             nc.psum_tensor("psd", [4, TT], F32) as psd, \
             nc.psum_tensor("psbc", [128, TT], F32) as psbc, \
             nc.psum_tensor("psp0", [128, TT], F32) as psp0, \
             nc.psum_tensor("psp1", [128, TT], F32) as psp1:
            psps = [psp0, psp1]
            t_qload = [None, None]
            t_att_mult = None
            t_den_ev = None
            t_rb_ev = None
            t_x1w = {}
            t_wfree = [None, None]
            for tt in range(NTT):
                for c in range(16):
                    tq = p.dma_hw(p.sp, f"qt{c % 2}", qt[c % 2][:],
                                  qvv[c][:, tt * TT:(tt + 1) * TT], deps=[t_qload[c % 2]])
                    t_num = p.pe.op(lambda t, c=c: t.matmul(psaf[:], bd[:, c * 128:(c + 1) * 128],
                                                            qt[c % 2][:], start=True, stop=True),
                                    deps=[tq, t_bdall, t_att_mult])
                    t_den = p.pe.op(lambda t, c=c: t.matmul(psd[:], ksd[:, c * 4:(c + 1) * 4],
                                                            qt[c % 2][:], start=True, stop=True),
                                    deps=[tq, t_den_ev])
                    t_qload[c % 2] = t_den
                    td1 = p.dve.op(lambda v: v.tensor_scalar(denr[:], psd[:], 1e-15, None, ALU.add),
                                   deps=[t_den])
                    t_den_ev = td1
                    td2 = p.dve.op(lambda v: v.reciprocal(denr[:], denr[:]), deps=[td1])
                    td3 = p.dve.op(lambda v: v.tensor_copy(denb[:], denr[:]), deps=[td2])
                    t_bc = p.pe.op(lambda t: t.matmul(psbc[:], e4b[:], denb[:], start=True, stop=True),
                                   deps=[td3, t_e4, t_rb_ev])
                    t_rb = p.act.op(lambda s: s.copy(rbb[:], psbc[:]), deps=[t_bc])
                    t_rb_ev = t_rb
                    t_att_mult = p.dve.op(
                        lambda v, c=c: v.tensor_tensor(att[:, c * TT:(c + 1) * TT], psaf[:],
                                                       rbb[:], ALU.mult),
                        deps=[t_rb, t_num])
                t_att_done = t_att_mult
                for oc in range(16):
                    tw = p.dma_sw(f"wpb{oc % 2}", wpb[oc % 2][:], w_cols(wpT, oc),
                                  deps=[t_wfree[oc % 2]])
                    ps = psps[oc % 2]
                    t_mm = None
                    for kc in range(16):
                        fn = p.pe.op if kc == 15 else p.pe.op_quiet
                        t_mm = fn(lambda t, kc=kc, oc=oc, ps=ps, s=(kc == 0), e=(kc == 15):
                                  t.matmul(ps[:], wpb[oc % 2][:, kc * 128:(kc + 1) * 128],
                                           att[:, kc * TT:(kc + 1) * TT], start=s, stop=e),
                                  deps=([tw, t_att_done, t_x1w.get((oc % 2, tt))] if kc == 0 else ()))
                    t_wfree[oc % 2] = t_mm
                    t_x1 = p.dve.op(
                        lambda v, oc=oc, tt=tt, ps=ps: v.scalar_tensor_tensor(
                            xfs(oc, tt), ps[:], bias_sb[:, oc:oc + 1], xfs(oc, tt),
                            ALU.add, ALU.add),
                        deps=[t_mm, t_bias])
                    t_x1w[(oc % 2, tt)] = t_x1
                    p.dma_hw(p.sp, f"x1st{oc % 2}", x1v[oc][:, tt * TT:(tt + 1) * TT],
                             xfs(oc, tt), deps=[t_x1])
        p.barrier()
        s3.close()
        s2.close()

        s4 = ctx.enter_context(ExitStack())
        onesb = sb("onesbB", [128, 1], BF16, s4)
        onesr = sb("onesrB", [1, 128], BF16, s4)
        t_ob = p.dve.op(lambda v: v.memset(onesb[:], 1.0))
        t_or = p.dve.op(lambda v: v.memset(onesr[:], 1.0))
        casts = [sb(f"castB{i}", [128, TT], BF16, s4) for i in range(2)]
        sqt = [sb(f"sqtB{i}", [128, TT], BF16, s4) for i in range(2)]
        sumr = sb("sumrB", [1, TPC], F32, s4)
        sqr = sb("sqrB", [1, TPC], F32, s4)
        mu_b = sb("mu_bB", [1, TPC], BF16, s4)
        rs_b = sb("rs_bB", [1, TPC], BF16, s4)
        rowa = sb("rowaB", [1, TPC], F32, s4)
        roww = sb("rowwB", [1, TPC], F32, s4)
        with nc.psum_tensor("pssumB", [1, TT], F32) as pssum, \
             nc.psum_tensor("pssqB", [1, TT], F32) as pssq:
            t_sum_evac = None
            t_sq_evac = None
            for tt in range(NTT):
                t_mm = None
                t_mmq = None
                for c in range(16):
                    tcast = p.dve.op(lambda v, c=c, tt=tt, b=c % 2:
                                     v.tensor_copy(casts[b][:], xfs(c, tt)),
                                     deps=[t_mm])
                    fn = p.pe.op
                    t_mm = fn(lambda t, c=c, s=(c == 0), e=(c == 15), b=c % 2:
                              t.matmul(pssum[:], onesb[:], casts[b][:], start=s, stop=e),
                              deps=[tcast, t_ob] + ([t_sum_evac] if c == 0 else []))
                    t_sq = p.act.op(lambda s, c=c, b=c % 2:
                                    s.activation(sqt[b][:], casts[b][:], AF.Square),
                                    deps=[tcast, t_mmq])
                    t_mmq = fn(lambda t, c=c, s=(c == 0), e=(c == 15), b=c % 2:
                               t.matmul(pssq[:], onesb[:], sqt[b][:], start=s, stop=e),
                               deps=[t_sq] + ([t_sq_evac] if c == 0 else []))
                t_sum_evac = p.dve.op(lambda v, tt=tt:
                                      v.tensor_copy(sumr[0:1, tt * TT:(tt + 1) * TT], pssum[:]),
                                      deps=[t_mm])
                t_sq_evac = p.dve.op(lambda v, tt=tt:
                                     v.tensor_copy(sqr[0:1, tt * TT:(tt + 1) * TT], pssq[:]),
                                     deps=[t_mmq])
        t1 = p.dve.op(lambda v: v.tensor_scalar_mul(rowa[:], sumr[:], 1.0 / C), deps=[t_sum_evac])
        t_mub = p.dve.op(lambda v: v.tensor_copy(mu_b[:], rowa[:]), deps=[t1])
        t2 = p.dve.op(lambda v: v.tensor_scalar_mul(roww[:], sqr[:], 1.0 / C), deps=[t_sq_evac])
        t3 = p.dve.op(lambda v: v.tensor_tensor(sumr[:], rowa[:], rowa[:], ALU.mult), deps=[t1, t2])
        t4 = p.dve.op(lambda v: v.tensor_sub(roww[:], roww[:], sumr[:]), deps=[t3])
        c_eps, tg_eps = p.constf(1e-5)
        t5 = p.act.op(lambda s: s.activation(rowa[:], roww[:], AF.Sqrt, bias=c_eps[0:1, :]),
                      deps=[t4, t_mub, tg_eps])
        t6 = p.dve.op(lambda v: v.reciprocal(roww[:], rowa[:]), deps=[t5])
        t_rsb = p.dve.op(lambda v: v.tensor_copy(rs_b[:], roww[:]), deps=[t6])
        p.barrier()

        mub_s = sb("mub_sB", [128, TT], BF16, s4)
        rsb_s = sb("rsb_sB", [128, TT], BF16, s4)
        xnst = [sb(f"xnstB{i}", [128, TT], BF16, s4) for i in range(2)]
        with nc.psum_tensor("psbcB0", [128, TT], F32) as psb0, \
             nc.psum_tensor("psbcB1", [128, TT], F32) as psb1:
            te1 = te2 = None
            tb_last = None
            t_dma = [None, None]
            for tt in range(NTT):
                tm1 = p.pe.op(lambda t, tt=tt: t.matmul(psb0[:], onesr[:],
                                                        mu_b[0:1, tt * TT:(tt + 1) * TT],
                                                        start=True, stop=True),
                              deps=[t_mub, t_or, te1])
                tm2 = p.pe.op(lambda t, tt=tt: t.matmul(psb1[:], onesr[:],
                                                        rs_b[0:1, tt * TT:(tt + 1) * TT],
                                                        start=True, stop=True),
                              deps=[t_rsb, te2])
                te1 = p.dve.op(lambda v: v.tensor_copy(mub_s[:], psb0[:]), deps=[tm1, tb_last])
                te2 = p.dve.op(lambda v: v.tensor_copy(rsb_s[:], psb1[:]), deps=[tm2])
                for c in range(16):
                    b = c % 2
                    ta = p.dve.op(lambda v, c=c, tt=tt, b=b:
                                  v.tensor_sub(xnst[b][:], xfs(c, tt), mub_s[:]),
                                  deps=[te1, t_dma[b]])
                    tb_last = p.dve.op(lambda v, c=c, tt=tt, b=b:
                                       v.tensor_tensor(xnst[b][:], xnst[b][:], rsb_s[:], ALU.mult),
                                       deps=[ta, te2])
                    t_dma[b] = p.dma_hw(p.sp, f"xnst{b}", xn2v[c][:, tt * TT:(tt + 1) * TT],
                                        xnst[b][:], deps=[tb_last])
        p.finish()
        p.emit()
    return nc


def make_B_inmaps(inputs, a_outs):
    x = np.asarray(inputs["L0_attn_x"])
    w_proj = np.asarray(inputs["L0_attn_w_proj"])
    b_proj = np.asarray(inputs["L0_attn_b_proj"])
    wpT = np.ascontiguousarray(w_proj.T)
    identf = np.eye(128, dtype=np.float32)
    e4 = np.zeros((4, 128), dtype=np.float32)
    for h in range(4):
        e4[h, 32 * h:32 * h + 32] = 1.0
    e4 = e4.astype(BF)
    xs = shard_x(x)
    maps = []
    for core in range(NCORE):
        b = core // 4
        peers = [4 * b + i for i in range(4)]
        m = {
            "x_in": xs[core],
            "q_in": np.ascontiguousarray(a_outs[core]["q_o"]),
            "wpT": wpT, "bproj": b_proj, "e4": e4, "identf": identf,
        }
        for i, pc in enumerate(peers):
            m[f"vk{i}"] = np.ascontiguousarray(a_outs[pc]["vk_o"])
        maps.append(m)
    return maps


def build_C():
    IC = 11264
    HC = 5632
    NH = 44
    nc = bass.Bass("TRN2", target_bir_lowering=False)
    x1_in = nc.dram_tensor("x1_in", [C, TPC], F32, kind="ExternalInput").ap()
    xn2_in = nc.dram_tensor("xn2_in", [C, TPC], BF16, kind="ExternalInput").ap()
    wguT = nc.dram_tensor("wguT", [C, IC], BF16, kind="ExternalInput").ap()
    wdnT = nc.dram_tensor("wdnT", [HC, C], BF16, kind="ExternalInput").ap()
    x2_o = nc.dram_tensor("x2_o", [C, TPC], F32, kind="ExternalOutput").ap()
    mu3_o = nc.dram_tensor("mu3_o", [1, TPC], F32, kind="ExternalOutput").ap()
    rs3_o = nc.dram_tensor("rs3_o", [1, TPC], F32, kind="ExternalOutput").ap()

    x1v = x1_in.rearrange("(c p) t -> c p t", p=128)
    x2v = x2_o.rearrange("(c p) t -> c p t", p=128)
    xnv = xn2_in.rearrange("(c p) t -> c p t", p=128)

    with ExitStack() as ctx:
        def sb(nm, shp, dt, stack=None):
            return (stack or ctx).enter_context(nc.sbuf_tensor(nm, shp, dt))
        p = Prog(nc, ctx)

        xn = sb("xn", [128, 16 * TPC], BF16)
        t_xn = []
        for c in range(16):
            t_xn.append(p.dma_hw(p.sp, f"xnld{c % 4}", xn[:, c * TPC:(c + 1) * TPC], xnv[c],
                                 deps=[t_xn[c - 4] if c >= 4 else None]))
        t_xn_all = t_xn

        def xns(kc, tt, w=TT):
            return xn[:, kc * TPC + tt * w: kc * TPC + (tt + 1) * w]

        onesb = sb("onesbC", [128, 1], BF16)
        onesr = sb("onesrC", [1, 128], BF16)
        t_ob = p.dve.op(lambda v: v.memset(onesb[:], 1.0))
        t_or = p.dve.op(lambda v: v.memset(onesr[:], 1.0))

        h_sb = sb("h_sb", [128, NH * TT], BF16)
        y_sb = sb("y_sb", [128, 16 * TT], BF16)
        wgu_b = [sb(f"wgub{i}", [128, 16 * 128], BF16) for i in range(2)]
        wdn_b = [sb("wdnb0", [128, NH * 128], BF16)] * 2
        sg_t = [sb(f"sgC{i}", [128, TT], BF16) for i in range(2)]
        sqt = [sb(f"sqtC{i}", [128, TT], BF16) for i in range(2)]
        x1t = [sb(f"x1tC{i}", [128, TT], F32) for i in range(2)]
        x2t = [sb(f"x2tC{i}", [128, TT], F32) for i in range(2)]
        x2b = [sb(f"x2bC{i}", [128, TT], BF16) for i in range(2)]
        yt = sb("ytC", [128, TT], BF16)
        rr_s = sb("rrC", [128, TT], BF16)
        rowq = sb("rowqC", [1, TT], F32)
        rowqb = sb("rowqbC", [1, TT], BF16)
        sum3 = sb("sum3C", [1, TPC], F32)
        sq3 = sb("sq3C", [1, TPC], F32)
        rowa = sb("rowaC", [1, TPC], F32)
        roww = sb("rowwC", [1, TPC], F32)

        def w_cols(wT, oc, nkc, width=128):
            return wT.rearrange("(kc p) o -> p kc o", p=128)[:, :, oc * width:(oc + 1) * width]

        c_eps6, tg_e6 = p.constf(1e-6)
        c_eps5, tg_e5 = p.constf(1e-5)

        with nc.psum_tensor("psg", [128, TT], F32) as psg, \
             nc.psum_tensor("psu", [128, TT], F32) as psu, \
             nc.psum_tensor("psd0", [128, TT], F32) as psd0, \
             nc.psum_tensor("psd1", [128, TT], F32) as psd1, \
             nc.psum_tensor("psr", [1, TT], F32) as psr, \
             nc.psum_tensor("pss", [1, TT], F32) as pss, \
             nc.psum_tensor("psb", [128, TT], F32) as psb:
            psds = [psd0, psd1]
            t_wg_free = [None, None]
            t_wd_free = [None, None]
            t_hw = None
            t_sg = None
            t_yev = [None, None]
            t_sqv = None
            t_rr = None
            t_x2w = [None, None]
            t_x2c = [None, None]
            t_x1l = [None, None]
            t_sum_ev = None
            t_sq_ev = None
            t_mmy = None
            t_x2read = [[], []]
            for tt in range(NTT):
                for i in range(NH):
                    bg = i % 2
                    twg = p.dma_hw(p.sp, f"wg{bg}", wgu_b[bg][:], w_cols(wguT, i, 16),
                                   deps=[t_wg_free[bg]])
                    t_mmg = None
                    for kc in range(16):
                        fn = p.pe.op if kc == 15 else p.pe.op_quiet
                        t_mmg = fn(lambda t, kc=kc, tt=tt, bg=bg, s=(kc == 0), e=(kc == 15):
                                   t.matmul(psg[:], wgu_b[bg][:, kc * 128:(kc + 1) * 128],
                                            xns(kc, tt), start=s, stop=e),
                                   deps=([twg, t_sg] + t_xn_all if kc == 0 else ()))
                    t_sg = p.act.op(lambda s, bg=bg: s.activation(sg_t[bg][:], psg[:], AF.Silu),
                                    deps=[t_mmg])
                    twu = p.dma_sw(f"wu{bg}", wgu_b[bg][:], w_cols(wguT, NH + i, 16),
                                   deps=[t_mmg])
                    t_mmu = None
                    for kc in range(16):
                        fn = p.pe.op if kc == 15 else p.pe.op_quiet
                        t_mmu = fn(lambda t, kc=kc, tt=tt, bg=bg, s=(kc == 0), e=(kc == 15):
                                   t.matmul(psu[:], wgu_b[bg][:, kc * 128:(kc + 1) * 128],
                                            xns(kc, tt), start=s, stop=e),
                                   deps=([twu, t_hw] if kc == 0 else ()))
                    t_hw = p.dve.op(lambda v, i=i, bg=bg:
                                    v.tensor_tensor(h_sb[:, i * TT:(i + 1) * TT], psu[:],
                                                    sg_t[bg][:], ALU.mult),
                                    deps=[t_mmu, t_sg])
                    t_wg_free[bg] = t_mmu
                t_h_done = t_hw
                t_mmy = None
                for oc in range(16):
                    bo = oc % 2
                    twd = p.dma_hw(p.sp, "wd0", wdn_b[0][:], w_cols(wdnT, oc, NH),
                                   deps=[t_wd_free[0], t_wd_free[1]])
                    ps = psds[bo]
                    t_mmd = None
                    for kc in range(NH):
                        fn = p.pe.op if kc == NH - 1 else p.pe.op_quiet
                        t_mmd = fn(lambda t, kc=kc, oc=oc, ps=ps, s=(kc == 0), e=(kc == NH - 1):
                                   t.matmul(ps[:], wdn_b[0][:, kc * 128:(kc + 1) * 128],
                                            h_sb[:, kc * TT:(kc + 1) * TT], start=s, stop=e),
                                   deps=([twd, t_h_done, t_yev[bo]] if kc == 0 else ()))
                    t_wd_free[bo] = t_mmd
                    t_yev[bo] = p.dve.op(lambda v, oc=oc, ps=ps:
                                         v.tensor_copy(y_sb[:, oc * TT:(oc + 1) * TT], ps[:]),
                                         deps=[t_mmd])
                    t_sqv = p.act.op(lambda s, oc=oc: s.activation(yt[:], y_sb[:, oc * TT:(oc + 1) * TT],
                                                                   AF.Square),
                                     deps=[t_yev[oc % 2], t_mmy])
                    t_mmy = p.pe.op(lambda t, oc=oc, s=(oc == 0), e=(oc == 15):
                                    t.matmul(psr[:], onesb[:], yt[:], start=s, stop=e),
                                    deps=[t_sqv, t_ob])
                tr1 = p.dve.op(lambda v: v.tensor_scalar_mul(rowq[:], psr[:], 1.0 / C), deps=[t_mmy])
                tr2 = p.act.op(lambda s: s.activation(rowq[:], rowq[:], AF.Sqrt, bias=c_eps6[0:1, :]),
                               deps=[tr1, tg_e6])
                tr3 = p.dve.op(lambda v: v.reciprocal(rowq[:], rowq[:]), deps=[tr2])
                tr4 = p.dve.op(lambda v: v.tensor_copy(rowqb[:], rowq[:]), deps=[tr3])
                t_bcm = p.pe.op(lambda t: t.matmul(psb[:], onesr[:], rowqb[:], start=True, stop=True),
                                deps=[tr4, t_or, t_rr])
                t_rr = p.act.op(lambda s: s.copy(rr_s[:], psb[:]), deps=[t_bcm])
                t_mm3 = None
                t_mm3q = None
                for oc in range(16):
                    b2 = oc % 2
                    tx1 = p.dma_hw(p.sp, f"x1l{b2}", x1t[b2][:], x1v[oc][:, tt * TT:(tt + 1) * TT],
                                   deps=[t_x1l[b2]])
                    tyr = p.dve.op(lambda v, oc=oc, b2=b2: v.tensor_tensor(
                        x2b[b2][:], y_sb[:, oc * TT:(oc + 1) * TT], rr_s[:], ALU.mult),
                        deps=[t_rr, t_x2c[b2]] + t_x2read[b2])
                    tx2 = p.dve.op(lambda v, oc=oc, b2=b2: v.tensor_tensor(
                        x2t[b2][:], x1t[b2][:], x2b[b2][:], ALU.add),
                        deps=[tyr, tx1, t_x2w[b2]])
                    t_x1l[b2] = tx2
                    t_x2w[b2] = p.dma_hw(p.sp, f"x2st{b2}", x2v[oc][:, tt * TT:(tt + 1) * TT],
                                         x2t[b2][:], deps=[tx2])
                    t_x2c[b2] = p.dve.op(lambda v, b2=b2: v.tensor_copy(x2b[b2][:], x2t[b2][:]),
                                         deps=[tx2])
                    t_mm3 = p.pe.op(lambda t, oc=oc, b2=b2, s=(oc == 0), e=(oc == 15):
                                    t.matmul(pss[:], onesb[:], x2b[b2][:], start=s, stop=e),
                                    deps=[t_x2c[b2]] + ([t_sum_ev] if oc == 0 else []))
                    t_sq3 = p.act.op(lambda s, b2=b2: s.activation(sqt[b2][:], x2b[b2][:], AF.Square),
                                     deps=[t_x2c[b2], t_mm3q])
                    t_mm3q = p.pe.op(lambda t, oc=oc, b2=b2, s=(oc == 0), e=(oc == 15):
                                     t.matmul(psr[:], onesb[:], sqt[b2][:], start=s, stop=e),
                                     deps=[t_sq3] + ([t_sq_ev, t_mmy, tr1] if oc == 0 else []))
                    t_x2read[b2] = [t_mm3, t_sq3]
                t_sum_ev = p.dve.op(lambda v, tt=tt: v.tensor_copy(sum3[0:1, tt * TT:(tt + 1) * TT],
                                                                   pss[:]), deps=[t_mm3])
                t_sq_ev = p.dve.op(lambda v, tt=tt: v.tensor_copy(sq3[0:1, tt * TT:(tt + 1) * TT],
                                                                  psr[:]), deps=[t_mm3q])
        t1 = p.dve.op(lambda v: v.tensor_scalar_mul(rowa[:], sum3[:], 1.0 / C), deps=[t_sum_ev])
        t2 = p.dve.op(lambda v: v.tensor_scalar_mul(roww[:], sq3[:], 1.0 / C), deps=[t_sq_ev])
        t3 = p.dve.op(lambda v: v.tensor_tensor(sum3[:], rowa[:], rowa[:], ALU.mult), deps=[t1, t2])
        t4 = p.dve.op(lambda v: v.tensor_sub(roww[:], roww[:], sum3[:]), deps=[t3])
        t5 = p.act.op(lambda s: s.activation(roww[:], roww[:], AF.Sqrt, bias=c_eps5[0:1, :]),
                      deps=[t4, tg_e5])
        t6 = p.dve.op(lambda v: v.reciprocal(roww[:], roww[:]), deps=[t5])
        p.dma_hw(p.sp, "mu3", mu3_o, rowa[:], deps=[t1])
        p.dma_hw(p.sp, "rs3", rs3_o, roww[:], deps=[t6])
        p.finish()
        p.emit()
    return nc


def make_C_inmaps(inputs, a_outs, b_outs):
    gu = np.concatenate([np.asarray(a_outs[i]["wgu_o"]).view(np.uint16) for i in range(NCORE)], axis=0)
    dn = np.concatenate([np.asarray(a_outs[i]["wdn_o"]).view(np.uint16) for i in range(NCORE)], axis=0)
    wguT = np.ascontiguousarray(gu.T)
    wdnT = np.ascontiguousarray(dn.T)
    maps = []
    for core in range(NCORE):
        maps.append({
            "x1_in": np.ascontiguousarray(b_outs[core]["x1_o"]),
            "xn2_in": np.ascontiguousarray(b_outs[core]["xn2_o"]),
            "wguT": wguT.view(BF),
            "wdnT": wdnT.view(BF),
        })
    return maps


def build_D():
    QKVD = 8192
    ZD = 4096
    nc = bass.Bass("TRN2", target_bir_lowering=False)
    x2_in = nc.dram_tensor("x2_in", [C, TPC], F32, kind="ExternalInput").ap()
    mu3_in = nc.dram_tensor("mu3_in", [1, TPC], F32, kind="ExternalInput").ap()
    rs3_in = nc.dram_tensor("rs3_in", [1, TPC], F32, kind="ExternalInput").ap()
    wqkvT = nc.dram_tensor("wqkvT", [C, QKVD], F32, kind="ExternalInput").ap()
    wzT = nc.dram_tensor("wzT", [C, ZD], F32, kind="ExternalInput").ap()
    wbaT = nc.dram_tensor("wbaT", [C, 64], F32, kind="ExternalInput").ap()
    qkv_o = nc.dram_tensor("qkv_o", [QKVD, TPC], BF16, kind="ExternalOutput").ap()
    z_o = nc.dram_tensor("z_o", [ZD, TPC], BF16, kind="ExternalOutput").ap()
    ba_o = nc.dram_tensor("ba_o", [64, TPC], F32, kind="ExternalOutput").ap()

    x2v = x2_in.rearrange("(c p) t -> c p t", p=128)
    qkvv = qkv_o.rearrange("(c p) t -> c p t", p=128)
    zv = z_o.rearrange("(c p) t -> c p t", p=128)

    with ExitStack() as ctx:
        def sb(nm, shp, dt, stack=None):
            return (stack or ctx).enter_context(nc.sbuf_tensor(nm, shp, dt))
        p = Prog(nc, ctx)
        xn = sb("xnD", [128, 16 * TPC], BF16)

        def xns(kc, tt, w=TT):
            return xn[:, kc * TPC + tt * w: kc * TPC + (tt + 1) * w]

        onesr = sb("onesrD", [1, 128], BF16)
        t_or = p.dve.op(lambda v: v.memset(onesr[:], 1.0))
        murow = sb("murowD", [1, TPC], BF16)
        rsrow = sb("rsrowD", [1, TPC], BF16)
        rowt = sb("rowtD", [1, TPC], F32)
        t_m = p.dma_hw(p.sp, "mu3", rowt[:], mu3_in)
        t_m2 = p.dve.op(lambda v: v.tensor_copy(murow[:], rowt[:]), deps=[t_m])
        t_r = p.dma_hw(p.sp, "rs3", rowt[:], rs3_in, deps=[t_m2])
        t_r2 = p.dve.op(lambda v: v.tensor_copy(rsrow[:], rowt[:]), deps=[t_r])

        s1 = ctx.enter_context(ExitStack())
        mub_s = sb("mubD", [128, TT], BF16, s1)
        rsb_s = sb("rsbD", [128, TT], BF16, s1)
        x2t = [sb(f"x2tD{i}", [128, TT], F32, s1) for i in range(2)]
        with nc.psum_tensor("psb0D", [128, TT], F32) as psb0, \
             nc.psum_tensor("psb1D", [128, TT], F32) as psb1:
            te1 = te2 = None
            tb_last = None
            t_xl = [None, None]
            for tt in range(NTT):
                tm1 = p.pe.op(lambda t, tt=tt: t.matmul(psb0[:], onesr[:],
                                                        murow[0:1, tt * TT:(tt + 1) * TT],
                                                        start=True, stop=True),
                              deps=[t_m2, t_or, te1])
                tm2 = p.pe.op(lambda t, tt=tt: t.matmul(psb1[:], onesr[:],
                                                        rsrow[0:1, tt * TT:(tt + 1) * TT],
                                                        start=True, stop=True),
                              deps=[t_r2, te2])
                te1 = p.dve.op(lambda v: v.tensor_copy(mub_s[:], psb0[:]), deps=[tm1, tb_last])
                te2 = p.dve.op(lambda v: v.tensor_copy(rsb_s[:], psb1[:]), deps=[tm2])
                for c in range(16):
                    b = c % 2
                    tx = p.dma_hw(p.sp, f"x2l{b}", x2t[b][:], x2v[c][:, tt * TT:(tt + 1) * TT],
                                  deps=[t_xl[b]])
                    ta = p.dve.op(lambda v, c=c, tt=tt, b=b:
                                  v.tensor_sub(xns(c, tt), x2t[b][:], mub_s[:]),
                                  deps=[te1, tx])
                    t_xl[b] = ta
                    tb_last = p.dve.op(lambda v, c=c, tt=tt:
                                       v.tensor_tensor(xns(c, tt), xns(c, tt), rsb_s[:], ALU.mult),
                                       deps=[ta, te2])
        p.barrier()
        s1.close()
        t_xn = p.dve.tag()

        def w_cols(wT, oc, width=128):
            return wT.rearrange("(kc p) o -> p kc o", p=128)[:, :, oc * width:(oc + 1) * width]

        s2 = ctx.enter_context(ExitStack())
        wb_ = [sb(f"wbD{i}", [128, 16 * 128], BF16, s2) for i in range(2)]
        stg = [sb(f"stgD{i}", [128, TT], BF16, s2) for i in range(2)]
        stgf = [sb(f"stgfD{i}", [64, TT], F32, s2) for i in range(2)]
        with nc.psum_tensor("psqD0", [128, TT], F32) as psq0, \
             nc.psum_tensor("psqD1", [128, TT], F32) as psq1:
            psqs = [psq0, psq1]
            t_w_free = [None, None]
            t_dma = [None, None]
            t_ev = [None, None]
            i = 0
            for wsrc, nov, outv in [(wqkvT, 64, qkvv), (wzT, 32, zv)]:
                for oc in range(nov):
                    tw = p.dma_sw(f"wD{oc % 2}", wb_[oc % 2][:], w_cols(wsrc, oc),
                                  deps=[t_w_free[oc % 2]])
                    t_mm = None
                    for tt in range(NTT):
                        ps = psqs[i % 2]
                        st_ = stg[i % 2]
                        for kc in range(16):
                            fn = p.pe.op if kc == 15 else p.pe.op_quiet
                            t_mm = fn(lambda t, kc=kc, tt=tt, oc=oc, ps=ps, s=(kc == 0), e=(kc == 15):
                                      t.matmul(ps[:], wb_[oc % 2][:, kc * 128:(kc + 1) * 128],
                                               xns(kc, tt), start=s, stop=e),
                                      deps=([tw, t_ev[i % 2]] if kc == 0 else ()))
                        t_e = p.act.op(lambda s, ps=ps, st_=st_: s.copy(st_[:], ps[:]),
                                       deps=[t_mm, t_dma[i % 2]])
                        t_ev[i % 2] = t_e
                        t_dma[i % 2] = p.dma_hw(p.sp, f"stD{i % 2}",
                                                outv[oc][:, tt * TT:(tt + 1) * TT], st_[:],
                                                deps=[t_e])
                        i += 1
                    t_w_free[oc % 2] = t_mm
            twba = p.dma_sw("wba", wb_[0][:, 0:16 * 64],
                            wbaT.rearrange("(kc p) o -> p kc o", p=128),
                            deps=[t_w_free[0]])
            for tt in range(NTT):
                ps = psqs[tt % 2]
                sf = stgf[tt % 2]
                t_mm = None
                for kc in range(16):
                    fn = p.pe.op if kc == 15 else p.pe.op_quiet
                    t_mm = fn(lambda t, kc=kc, tt=tt, ps=ps, s=(kc == 0), e=(kc == 15):
                              t.matmul(ps[0:64, :], wb_[0][:, kc * 64:(kc + 1) * 64],
                                       xns(kc, tt), start=s, stop=e),
                              deps=([twba, t_ev[tt % 2]] if kc == 0 else ()))
                t_e = p.dve.op(lambda v, ps=ps, sf=sf: v.tensor_copy(sf[:], ps[0:64, :]),
                               deps=[t_mm, t_dma[tt % 2]])
                t_ev[tt % 2] = t_e
                t_dma[tt % 2] = p.dma_hw(p.sp, f"baD{tt % 2}", ba_o[:, tt * TT:(tt + 1) * TT],
                                         sf[:], deps=[t_e])
        p.finish()
        p.emit()
    return nc


def make_D_inmaps(inputs, c_outs):
    wqkvT = np.ascontiguousarray(np.asarray(inputs["L1_attn_W_qkv"]).T)
    wzT = np.ascontiguousarray(np.asarray(inputs["L1_attn_W_z"]).T)
    wb = np.asarray(inputs["L1_attn_W_b"])
    wa = np.asarray(inputs["L1_attn_W_a"])
    wbaT = np.ascontiguousarray(np.concatenate([wb, wa], axis=0).T)
    maps = []
    for core in range(NCORE):
        maps.append({
            "x2_in": np.ascontiguousarray(c_outs[core]["x2_o"]),
            "mu3_in": np.ascontiguousarray(c_outs[core]["mu3_o"]),
            "rs3_in": np.ascontiguousarray(c_outs[core]["rs3_o"]),
            "wqkvT": wqkvT, "wzT": wzT, "wbaT": wbaT,
        })
    return maps


def build_E(T=8192, NGL=None, CHL=None, KHL=4, PPL=2, PARTS='dbl os e'):
    L = 128
    NCH = T // L
    GRP = 4
    NG = NCH // GRP
    nc = bass.Bass("TRN2", target_bir_lowering=False)
    qh_in = nc.dram_tensor("qh_in", [512, T], BF16, kind="ExternalInput").ap()
    kh_in = nc.dram_tensor("kh_in", [512, T], BF16, kind="ExternalInput").ap()
    vh_in = nc.dram_tensor("vh_in", [1024, T], BF16, kind="ExternalInput").ap()
    zh_in = nc.dram_tensor("zh_in", [T, 1024], BF16, kind="ExternalInput").ap()
    ba_in = nc.dram_tensor("ba_in", [T, 16], F32, kind="ExternalInput").ap()
    cw_in = nc.dram_tensor("cw_in", [2048, 4], F32, kind="ExternalInput").ap()
    dtb_in = nc.dram_tensor("dtb_in", [1, 8], F32, kind="ExternalInput").ap()
    alog_in = nc.dram_tensor("alog_in", [1, 8], F32, kind="ExternalInput").ap()
    nw_in = nc.dram_tensor("nw_in", [1, 128], F32, kind="ExternalInput").ap()
    identb = nc.dram_tensor("identb", [128, 128], BF16, kind="ExternalInput").ap()
    identf = nc.dram_tensor("identf", [128, 128], F32, kind="ExternalInput").ap()
    ident8 = nc.dram_tensor("ident8", [8, 8], F32, kind="ExternalInput").ap()
    ones8x = nc.dram_tensor("ones8x", [8, 128], F32, kind="ExternalInput").ap()
    e8_in = nc.dram_tensor("e8_in", [8, 8 * 128], F32, kind="ExternalInput").ap()
    triu_in = nc.dram_tensor("triu_in", [128, 128], F32, kind="ExternalInput").ap()
    mA_in = nc.dram_tensor("mA_in", [128, 128], F32, kind="ExternalInput").ap()
    mAT_in = nc.dram_tensor("mAT_in", [128, 128], F32, kind="ExternalInput").ap()
    mW_in = nc.dram_tensor("mW_in", [128, 128], F32, kind="ExternalInput").ap()
    og_o = nc.dram_tensor("og_o", [T, 1024], BF16, kind="ExternalOutput").ap()

    with ExitStack() as ctx:
        def sb(nm, shp, dt, stack=None):
            return (stack or ctx).enter_context(nc.sbuf_tensor(nm, shp, dt))
        p = Prog(nc, ctx)

        idb = sb("idb", [128, 128], BF16)
        idf = sb("idf", [128, 128], F32)
        id8 = sb("id8", [8, 8], F32)
        on8 = sb("on8", [8, 128], F32)
        e8 = sb("e8", [8, 8 * 128], F32)
        triu = sb("triu", [128, 128], F32)
        mA = sb("mA", [128, 128], F32)
        mAT = sb("mAT", [128, 128], F32)
        mW = sb("mW", [128, 128], F32)
        t_c = [p.dma_hw(p.sp, "c0", idb[:], identb), p.dma_hw(p.sp, "c1", id8[:], ident8),
               p.dma_hw(p.sp, "c2", on8[:], ones8x), p.dma_hw(p.sp, "c3", e8[:], e8_in),
               p.dma_hw(p.sp, "c4", triu[:], triu_in), p.dma_hw(p.sp, "c5", mA[:], mA_in),
               p.dma_hw(p.sp, "c6", mAT[:], mAT_in), p.dma_hw(p.sp, "c7", mW[:], mW_in),
               p.dma_hw(p.sp, "c8", idf[:], identf)]
        onesb = sb("onesbE", [128, 1], BF16)
        onesr = sb("onesrE", [1, 128], BF16)
        onesrf = sb("onesrfE", [1, 128], F32)
        t_ob = p.dve.op(lambda v: v.memset(onesb[:], 1.0))
        t_or = p.dve.op(lambda v: v.memset(onesr[:], 1.0))
        t_orf = p.dve.op(lambda v: v.memset(onesrf[:], 1.0))
        c_m1E, tg_m1E = p.constf(-1.0)
        c1e, tg1e = p.constf(1.0)

        ba = sb("baE", [128, NCH * 16], F32)
        t_ba = p.dma_hw(p.sp, "ba", ba[:], ba_in.rearrange("(n p) c -> p n c", p=128))
        rows_dt = sb("rows_dt", [1, 8], F32)
        rows_al = sb("rows_al", [1, 8], F32)
        t_d1 = p.dma_hw(p.sp, "dtb", rows_dt[:], dtb_in)
        t_d2 = p.dma_hw(p.sp, "alog", rows_al[:], alog_in)
        nwrow = sb("nwrow", [1, 128], F32)
        t_nw = p.dma_hw(p.sp, "nw", nwrow[:], nw_in)
        nw2row = sb("nw2row", [1, 256], BF16)
        t_nw1 = p.dve.op(lambda v: v.tensor_copy(nw2row[0:1, 0:128], nwrow[:]), deps=[t_nw])
        t_nw2 = p.dve.op(lambda v: v.tensor_copy(nw2row[0:1, 128:256], nwrow[:]), deps=[t_nw1])

        dtb8 = sb("dtb8", [128, 8], F32)
        expA8 = sb("expA8", [128, 8], F32)
        nwt2 = sb("nwt2", [128, 256], BF16)
        cwsb = sb("cwsb", [128, 64], F32)
        t_cw = p.dma_hw(p.sp, "cw", cwsb[:], cw_in.rearrange("(rc p) j -> p rc j", p=128))
        diags = sb("diags", [128, 64 * 128], BF16)

        psGN = ctx.enter_context(nc.psum_tensor("psGN", [128, 16], F32))
        psR = ctx.enter_context(nc.psum_tensor("psR", [8, 128], F32))
        psZ = ctx.enter_context(nc.psum_tensor("psZ", [128, 128], F32))
        psKQ = ctx.enter_context(nc.psum_tensor("psKQ", [128, 256], F32))
        psT = ctx.enter_context(nc.psum_tensor("psT", [128, 256], BF16))
        psBU = ctx.enter_context(nc.psum_tensor("psBU", [128, 128], F32))
        psSq = ctx.enter_context(nc.psum_tensor("psSq", [128, 256], F32))
        psOS = ctx.enter_context(nc.psum_tensor("psOS", [128, 384], F32))

        tb1 = p.pe.op(lambda t: t.matmul(psGN[:, 0:8], onesrf[:], rows_dt[:], start=True, stop=True),
                      deps=[t_d1, t_orf])
        te1 = p.dve.op(lambda v: v.tensor_copy(dtb8[:], psGN[:, 0:8]), deps=[tb1])
        tb2 = p.pe.op(lambda t: t.matmul(psGN[:, 8:16], onesrf[:], rows_al[:], start=True, stop=True),
                      deps=[t_d2, te1])
        te2 = p.act.op(lambda s: s.activation(expA8[:], psGN[:, 8:16], AF.Exp), deps=[tb2])
        tb3 = p.pe.op(lambda t: t.matmul(psKQ[:, 0:256], onesr[:], nw2row[:], start=True, stop=True),
                      deps=[t_nw2, t_or])
        t_nwt2 = p.dve.op(lambda v: v.tensor_copy(nwt2[:], psKQ[:, 0:256]), deps=[tb3])
        t_diag = []
        for rc in range(16):
            for j in range(4):
                t_diag.append(p.dve.op(
                    lambda v, rc=rc, j=j: v.tensor_scalar(
                        diags[:, (rc * 4 + j) * 128:(rc * 4 + j + 1) * 128],
                        idb[:], cwsb[:, rc * 4 + j: rc * 4 + j + 1], None, ALU.mult),
                    deps=[t_cw, t_c[0]]))
        p.barrier()

        convb = [sb(f"convb{i}", [128, 16 * 512], BF16) for i in range(2)]
        sqb = [sb(f"sqb{i}", [128, 8 * 512], BF16) for i in range(2)]
        intile = [sb(f"intile{i}", [128, 515], BF16) for i in range(2)]
        ztile = [sb(f"ztile{i}", [128, 1024], BF16) for i in range(2)]
        S = [sb(f"S{h}", [128, 128], F32) for h in range(8)]
        Sb = [sb(f"Sb{h}", [128, 128], BF16) for h in range(8)]
        t_S = [p.dve.op(lambda v, h=h: v.memset(S[h][:], 0.0)) for h in range(8)]
        t_Sb = [p.dve.op(lambda v, h=h: v.memset(Sb[h][:], 0.0)) for h in range(8)]
        G8 = sb("G8", [128, 8], F32)
        Gp8 = sb("Gp8", [128, 8], F32)
        ld8 = sb("ld8", [128, 8], F32)
        beta8 = sb("beta8", [128, 8], F32)
        spnb = sb("spnb", [128, 8], F32)
        RowA = sb("RowA", [128, 8], F32)
        CqA = sb("CqA", [128, 8], F32)
        CW = sb("CW", [128, 8], F32)
        lr8 = sb("lr8", [128, 8], F32)
        GL8 = sb("GL8", [128, 8], F32)
        gam8 = sb("gam8", [128, 8], F32)
        negE8 = sb("negE8", [128, 8], F32)
        osc8 = sb("osc8", [128, 8], F32)
        eco8 = sb("eco8", [128, 8], F32)
        diag8 = sb("diag8", [8, 8], F32)
        tmp8a = sb("tmp8a", [128, 8], F32)
        tmp8b = sb("tmp8b", [128, 8], F32)
        tmp8c = sb("tmp8c", [128, 8], F32)
        rows_c = {cl: sb(f"rows_{cl}", [8, 128], F32) for cl in ["G", "RowA", "CqA", "CW"]}
        KKt = sb("KKt", [128, 128], BF16)
        QKt = sb("QKt", [128, 128], BF16)
        Ktm = sb("Ktm", [128, 128], BF16)
        Vtm = sb("Vtm", [128, 128], BF16)
        zD = sb("zD", [128, 256], F32)
        DA = sb("DA", [128, 256], BF16)
        DW = sb("DW", [128, 256], BF16)
        Bt = sb("Bt", [128, 128], BF16)
        Ut = sb("Ut", [128, 128], BF16)
        Uf = sb("Uf", [128, 128], BF16)
        ABbuf = sb("ABbuf", [128, 512], BF16)
        tmpB = sb("tmpB", [128, 128], BF16)
        Ot = sb("Ot", [128, 256], BF16)
        ogt = sb("ogt", [128, 256], BF16)
        rmsc = sb("rmsc", [128, 2], F32)
        silz = sb("silz", [128, 256], BF16)

        def bas(n, c0, w):
            return ba[:, n * 16 + c0: n * 16 + c0 + w]

        t_intile = [None, None]
        t_zt = [None, None]
        t_group_consumed = [None, None]
        t_chunk = None
        t_coef_users = None

        for g in range(NGL if NGL is not None else NG):
            gb = g % 2
            t0 = g * 512
            t_conv_last = None
            t_sq_last = None
            t_conv_ps = [None, None]
            for rc in range(16):
                src = qh_in if rc < 4 else (kh_in if rc < 8 else vh_in)
                row0 = (rc % 4) * 128 if rc < 8 else (rc - 8) * 128
                ib = intile[rc % 2]
                if g == 0:
                    tl1 = p.dve.op(lambda v, ib=ib: v.memset(ib[:, 0:3], 0.0),
                                   deps=[t_intile[rc % 2]])
                    tld_ = p.dma_hw(p.sp, f"cin{rc % 2}", ib[:, 3:515],
                                    src[row0:row0 + 128, t0:t0 + 512], deps=[tl1])
                    tl = [tld_, tl1]
                else:
                    tld_ = p.dma_hw(p.sp, f"cin{rc % 2}", ib[:, 0:515],
                                    src[row0:row0 + 128, t0 - 3:t0 + 512],
                                    deps=[t_intile[rc % 2]])
                    tl = [tld_]
                for half in range(2):
                    ps = psKQ if half == 0 else psSq
                    t_mm = None
                    for j in range(4):
                        fn = p.pe.op if j == 3 else p.pe.op_quiet
                        t_mm = fn(lambda t, rc=rc, j=j, half=half, ib=ib, ps=ps,
                                  s=(j == 0), e=(j == 3):
                                  t.matmul(ps[:, 0:256],
                                           diags[:, (rc * 4 + j) * 128:(rc * 4 + j + 1) * 128],
                                           ib[:, half * 256 + j: half * 256 + j + 256],
                                           start=s, stop=e),
                                  deps=(tl + [t_diag[-1], t_group_consumed[gb], t_chunk,
                                              t_conv_ps[half]]
                                        if j == 0 else ()))
                    tev = p.dve.op(lambda v, rc=rc, half=half, gb=gb, ps=ps: v.tensor_copy(
                        convb[gb][:, rc * 512 + half * 256: rc * 512 + half * 256 + 256],
                        ps[:, 0:256]), deps=[t_mm])
                    t_conv_ps[half] = tev
                    t_conv_last = tev
                    if rc < 8:
                        t_sq_last = p.dve.op(lambda v, rc=rc, half=half, gb=gb: v.tensor_tensor(
                            sqb[gb][:, rc * 512 + half * 256: rc * 512 + half * 256 + 256],
                            convb[gb][:, rc * 512 + half * 256: rc * 512 + half * 256 + 256],
                            convb[gb][:, rc * 512 + half * 256: rc * 512 + half * 256 + 256],
                            ALU.mult), deps=[tev])
                t_intile[rc % 2] = t_mm
            t_convdone = t_conv_last
            t_sqdone = t_sq_last

            for o in range(GRP if CHL is None else CHL):
                n = g * GRP + o
                ts0 = o * 128

                def cslice(rc, w=128):
                    return convb[gb][:, rc * 512 + ts0: rc * 512 + ts0 + w]

                t_zl = p.dma_hw(p.sp, f"z{n % 2}", ztile[n % 2][:],
                                zh_in[n * L:(n + 1) * L, :], deps=[t_zt[n % 2]])

                bsl = bas(n, 0, 8)
                asl = bas(n, 8, 8)
                dd = [t_ba, t_coef_users]
                tsg = p.act.op(lambda s, bsl=bsl: s.activation(beta8[:], bsl, AF.Sigmoid), deps=dd)
                tu = p.dve.op(lambda v, bsl=bsl: v.scalar_tensor_tensor(
                    tmp8a[:], bsl, -1.0, bsl, ALU.mult, ALU.max), deps=dd)
                tex = p.act.op(lambda s: s.activation(tmp8b[:], tmp8a[:], AF.Exp, scale=c_m1E),
                               deps=[tu, tg_m1E, tsg])
                tln = p.act.op(lambda s: s.activation(tmp8b[:], tmp8b[:], AF.Ln, bias=c1e),
                               deps=[tex, tg1e])
                tmx = p.dve.op(lambda v, bsl=bsl: v.tensor_scalar(tmp8a[:], bsl, -1.0, 0.0,
                                                                  ALU.mult, ALU.max), deps=[tln])
                tsp = p.dve.op(lambda v: v.tensor_add(spnb[:], tmp8a[:], tmp8b[:]), deps=[tmx])
                td1 = p.dve.op(lambda v, asl=asl: v.tensor_add(tmp8a[:], asl, dtb8[:]),
                               deps=[te1, tsp] + dd)
                td2 = p.dve.op(lambda v: v.scalar_tensor_tensor(
                    tmp8c[:], tmp8a[:], -1.0, tmp8a[:], ALU.mult, ALU.max), deps=[td1])
                td3 = p.act.op(lambda s: s.activation(tmp8b[:], tmp8c[:], AF.Exp, scale=c_m1E),
                               deps=[td2, tsp])
                td4 = p.act.op(lambda s: s.activation(tmp8b[:], tmp8b[:], AF.Ln, bias=c1e),
                               deps=[td3])
                td5 = p.dve.op(lambda v: v.tensor_scalar(tmp8c[:], tmp8a[:], 0.0, None, ALU.max),
                               deps=[td4])
                td6 = p.dve.op(lambda v: v.tensor_add(tmp8a[:], tmp8b[:], tmp8c[:]), deps=[td5, td4])
                tld = p.dve.op(lambda v: v.tensor_tensor(ld8[:], tmp8a[:], expA8[:], ALU.mult),
                               deps=[td6, te2])
                tldn = p.dve.op(lambda v: v.tensor_scalar_mul(ld8[:], ld8[:], -1.0), deps=[tld])
                tgm = p.pe.op(lambda t: t.matmul(psGN[:, 0:8], triu[:], ld8[:], start=True, stop=True),
                              deps=[tldn, t_c[4], te1])
                tge = p.dve.op(lambda v: v.tensor_copy(G8[:], psGN[:, 0:8]), deps=[tgm])
                tgp = p.dve.op(lambda v: v.tensor_sub(Gp8[:], G8[:], ld8[:]), deps=[tge])
                t_nmm = None
                for c8 in range(8):
                    fn = p.pe.op if c8 == 7 else p.pe.op_quiet
                    t_nmm = fn(lambda t, c8=c8, gb=gb, ts0=ts0: t.matmul(
                        psGN[:, 8 + c8:9 + c8],
                        sqb[gb][:, c8 * 512 + ts0: c8 * 512 + ts0 + 128], onesb[:],
                        start=True, stop=True),
                        deps=[t_sqdone, t_ob, tgp] if c8 == 0 else ())
                tn1 = p.dve.op(lambda v: v.tensor_scalar(lr8[:], psGN[:, 8:16], 1e-24, None,
                                                         ALU.max), deps=[t_nmm])
                tn2 = p.act.op(lambda s: s.activation(lr8[:], lr8[:], AF.Ln), deps=[tn1])
                tn3 = p.dve.op(lambda v: v.tensor_scalar_mul(lr8[:], lr8[:], -0.5), deps=[tn2])
                lrq_rep = lr8[:, 0:4].unsqueeze(-1).broadcast_to([128, 4, 2])
                lrk_rep = lr8[:, 4:8].unsqueeze(-1).broadcast_to([128, 4, 2])
                def v3(t):
                    return t[:].rearrange("p (a b) -> p a b", b=2)
                ta1 = p.dve.op(lambda v: v.tensor_sub(tmp8a[:], Gp8[:], spnb[:]), deps=[tgp, tn3])
                ta2 = p.dve.op(lambda v, lrk_rep=lrk_rep: v.tensor_tensor(v3(RowA), v3(tmp8a),
                                                                          lrk_rep, ALU.add),
                               deps=[ta1])
                ta3 = p.dve.op(lambda v, lrk_rep=lrk_rep: v.scalar_tensor_tensor(
                    v3(CqA), v3(G8), -1.0, lrk_rep, ALU.mult, ALU.add), deps=[ta2])
                ta4 = p.dve.op(lambda v, lrq_rep=lrq_rep: v.tensor_tensor(v3(CW), v3(G8), lrq_rep,
                                                                          ALU.add), deps=[ta3])
                t_rows = {}
                tprev_r = None
                for cl, tile, tdep in [("G", G8, tge), ("RowA", RowA, ta2),
                                       ("CqA", CqA, ta3), ("CW", CW, ta4)]:
                    ttp = p.pe.op(lambda t, tile=tile: t.transpose(psR[:], tile[:], idf[:]),
                                  deps=[tdep, tprev_r, t_c[8]])
                    tevr = p.dve.op(lambda v, cl=cl: v.tensor_copy(rows_c[cl][:], psR[:]),
                                    deps=[ttp])
                    t_rows[cl] = tevr
                    tprev_r = tevr
                tdg = p.dve.op(lambda v: v.tensor_scalar(diag8[:], id8[:],
                                                         rows_c["G"][:, 127:128], None, ALU.mult),
                               deps=[t_rows["G"], t_c[1]])
                tglm = p.pe.op(lambda t: t.matmul(psGN[:, 0:8], on8[:], diag8[:], start=True,
                                                  stop=True), deps=[tdg, t_c[2], tn1])
                tgl = p.dve.op(lambda v: v.tensor_copy(GL8[:], psGN[:, 0:8]), deps=[tglm])
                tgam = p.act.op(lambda s: s.activation(gam8[:], GL8[:], AF.Exp), deps=[tgl])
                tne1 = p.act.op(lambda s: s.activation(negE8[:], RowA[:], AF.Exp), deps=[ta2, tgam])
                tne2 = p.dve.op(lambda v: v.tensor_scalar_mul(negE8[:], negE8[:], -1.0), deps=[tne1])
                tosc = p.act.op(lambda s: s.activation(osc8[:], CW[:], AF.Exp), deps=[ta4, tne1])
                tec1 = p.dve.op(lambda v: v.tensor_add(tmp8c[:], CqA[:], GL8[:]), deps=[ta3, tgl])
                tec2 = p.act.op(lambda s: s.activation(eco8[:], tmp8c[:], AF.Exp), deps=[tec1, tosc])

                t_kk_ev = None
                t_qk_ev = None
                t_ktm_ev = None
                t_vtm_last = None
                t_psz_free = None
                t_psbu_free = None
                t_psos_free = [None, None, None]
                tz1_last = None
                if n == 0:
                    t_ogdma = None
                for kh in range(KHL):
                    kc_t = cslice(4 + kh)
                    qc_t = cslice(kh)
                    tkk = p.pe.op(lambda t, kc_t=kc_t: t.matmul(psKQ[:, 0:128], kc_t, kc_t,
                                                                start=True, stop=True),
                                  deps=[t_convdone, t_qk_ev, t_chunk, tec2])
                    t_kk_ev = p.dve.op(lambda v: v.tensor_scalar_mul(KKt[:], psKQ[:, 0:128], -1.0),
                                       deps=[tkk])
                    tqk = p.pe.op(lambda t, kc_t=kc_t, qc_t=qc_t: t.matmul(psKQ[:, 128:256],
                                                                           kc_t, qc_t,
                                                                           start=True, stop=True),
                                  deps=[t_kk_ev])
                    t_qk_ev = p.dve.op(lambda v: v.tensor_copy(QKt[:], psKQ[:, 128:256]),
                                       deps=[tqk])
                    tkt = p.pe.op(lambda t, kc_t=kc_t: t.transpose(psT[:, 0:128], kc_t, idb[:]),
                                  deps=[t_ktm_ev, t_c[0], t_qk_ev, t_vtm_last])
                    t_ktm_ev = p.dve.op(lambda v: v.tensor_copy(Ktm[:], psT[:, 0:128]), deps=[tkt])
                    for pp in range(PPL):
                        h = 2 * kh + pp
                        if 'd' not in PARTS:
                            continue
                        tza = p.pe.op(lambda t, h=h: t.matmul(psZ[:], e8[:, h * 128:(h + 1) * 128],
                                                              rows_c["CqA"][:], start=True, stop=True),
                                      deps=[t_rows["CqA"], t_c[3], t_psz_free, tkt])
                        tsa = p.dve.op(lambda v, h=h: v.scalar_tensor_tensor(
                            zD[:, 0:128], psZ[:], RowA[:, h:h + 1], mA[:], ALU.add, ALU.min),
                            deps=[tza, t_c[5]])
                        tzat = p.pe.op(lambda t, h=h: t.matmul(psZ[:], e8[:, h * 128:(h + 1) * 128],
                                                               rows_c["RowA"][:], start=True,
                                                               stop=True),
                                       deps=[t_rows["RowA"], tsa])
                        tsat = p.dve.op(lambda v, h=h: v.scalar_tensor_tensor(
                            zD[:, 128:256], psZ[:], CqA[:, h:h + 1], mAT[:], ALU.add, ALU.min),
                            deps=[tzat, t_c[6]])
                        texp = p.act.op(lambda s: s.activation(DA[:, 0:256], zD[:, 0:256], AF.Exp),
                                        deps=[tsat])
                        tma = p.dve.op(lambda v: v.tensor_tensor(ABbuf[:, 0:128], DA[:, 0:128],
                                                                 KKt[:], ALU.mult),
                                       deps=[texp, t_kk_ev])
                        tmat = p.dve.op(lambda v: v.tensor_tensor(ABbuf[:, 128:256], DA[:, 128:256],
                                                                  KKt[:], ALU.mult), deps=[tma])
                        tzw = p.pe.op(lambda t, h=h: t.matmul(psZ[:], e8[:, h * 128:(h + 1) * 128],
                                                              rows_c["CW"][:], start=True, stop=True),
                                      deps=[t_rows["CW"], tsat])
                        t_psz_free = tzw
                        tsw = p.dve.op(lambda v, h=h: v.scalar_tensor_tensor(
                            zD[:, 0:128], psZ[:], CqA[:, h:h + 1], mW[:], ALU.add, ALU.min),
                            deps=[tzw, t_c[7], texp])
                        t_psz_free = tsw
                        texw = p.act.op(lambda s: s.activation(DW[:, 0:128], zD[:, 0:128], AF.Exp),
                                        deps=[tsw])
                        tmw = p.dve.op(lambda v: v.tensor_tensor(DW[:, 128:256], DW[:, 0:128],
                                                                 QKt[:], ALU.mult),
                                       deps=[texw, t_qk_ev])
                        if 'b' not in PARTS:
                            continue
                        vc_t = cslice(8 + h)
                        tvt = p.pe.op(lambda t, vc_t=vc_t: t.transpose(psT[:, 128:256], vc_t,
                                                                       idb[:]),
                                      deps=[t_convdone, t_c[0], tsw, tzw])
                        tvte = p.dve.op(lambda v: v.tensor_copy(Vtm[:], psT[:, 128:256]),
                                        deps=[tvt])
                        t_vtm_last = tvte
                        tks = p.pe.op(lambda t, kc_t=kc_t, h=h: t.matmul(psBU[:], kc_t, Sb[h][:],
                                                                         start=True, stop=True),
                                      deps=[t_Sb[h], tvt, t_psbu_free])
                        tbb1 = p.dve.op(lambda v, h=h: v.tensor_scalar(tmpB[:], Vtm[:],
                                                                       beta8[:, h:h + 1], None,
                                                                       ALU.mult), deps=[tvte])
                        tbb2 = p.dve.op(lambda v, h=h: v.scalar_tensor_tensor(
                            Bt[:], psBU[:], negE8[:, h:h + 1], tmpB[:], ALU.mult, ALU.add),
                            deps=[tks, tne2, tbb1])
                        tu_ = p.dve.op(lambda v: v.tensor_copy(Ut[:], Bt[:]), deps=[tbb2])
                        if 'l' not in PARTS:
                            continue
                        tac = tmat
                        curq = 0
                        for lev in range(7):
                            A_c = ABbuf[:, curq * 256: curq * 256 + 128]
                            AT_c = ABbuf[:, curq * 256 + 128: curq * 256 + 256]
                            tap = p.pe.op(lambda t, AT_c=AT_c: t.matmul(psBU[:], AT_c, Ut[:],
                                                                        start=True, stop=True),
                                          deps=[tu_, tac, tks])
                            tu_ = p.dve.op(lambda v: v.scalar_tensor_tensor(
                                Ut[:], psBU[:], 0.0, Ut[:], ALU.add, ALU.add), deps=[tap])
                            t_psbu_free = tu_
                            if lev < 6:
                                nq = 1 - curq
                                tq1_ = p.pe.op(lambda t, A_c=A_c, AT_c=AT_c: t.matmul(
                                    psSq[:, 0:128], AT_c, A_c, start=True, stop=True),
                                    deps=[tac, tap])
                                tq2_ = p.pe.op(lambda t, A_c=A_c, AT_c=AT_c: t.matmul(
                                    psSq[:, 128:256], A_c, AT_c, start=True, stop=True),
                                    deps=[tq1_])
                                tac = p.dve.op(lambda v, nq=nq: v.tensor_copy(
                                    ABbuf[:, nq * 256:(nq + 1) * 256], psSq[:, 0:256]),
                                    deps=[tq2_, tu_])
                                curq = nq
                        if 'os' not in PARTS:
                            continue
                        to1 = p.pe.op(lambda t, qc_t=qc_t, h=h: t.matmul(psOS[:, 0:128], qc_t,
                                                                         Sb[h][:], start=True,
                                                                         stop=True),
                                      deps=[t_Sb[h], tu_, t_psos_free[0]])
                        to2 = p.pe.op(lambda t: t.matmul(psOS[:, 128:256], DW[:, 128:256], Ut[:],
                                                         start=True, stop=True),
                                      deps=[tmw, tu_, to1, t_psos_free[1]])
                        to3 = p.dve.op(lambda v, h=h: v.tensor_scalar(tmpB[:], psOS[:, 0:128],
                                                                      osc8[:, h:h + 1], None,
                                                                      ALU.mult), deps=[to2, tosc])
                        t_psos_free[0] = to3
                        to4 = p.dve.op(lambda v, pp=pp: v.tensor_tensor(
                            Ot[:, pp * 128:(pp + 1) * 128], psOS[:, 128:256], tmpB[:], ALU.add),
                            deps=[to2, to3])
                        t_psos_free[1] = to4
                        tus = p.dve.op(lambda v, h=h: v.tensor_scalar(Uf[:], Ut[:],
                                                                      eco8[:, h:h + 1], None,
                                                                      ALU.mult), deps=[tu_, tec2, to4])
                        tsmm = p.pe.op(lambda t, h=h: t.matmul(psOS[:, 256:384], Ktm[:], Uf[:],
                                                               start=True, stop=True),
                                       deps=[t_ktm_ev, tus, t_psos_free[2]])
                        tsu = p.dve.op(lambda v, h=h: v.scalar_tensor_tensor(
                            S[h][:], S[h][:], gam8[:, h:h + 1], psOS[:, 256:384], ALU.mult,
                            ALU.add), deps=[tsmm, tgam, t_S[h]])
                        t_psos_free[2] = tsu
                        t_S[h] = tsu
                        t_Sb[h] = p.dve.op(lambda v, h=h: v.tensor_copy(Sb[h][:], S[h][:]),
                                           deps=[tsu])
                    if 'e' not in PARTS:
                        continue
                    trms1 = p.dve.op(lambda v: v.tensor_tensor(ogt[:], Ot[:, 0:256], Ot[:, 0:256],
                                                               ALU.mult), deps=[to4, t_ogdma])
                    trms2 = p.dve.op(lambda v: v.tensor_reduce(
                        rmsc[:], ogt[:].rearrange("p (a b) -> p a b", a=2), AX.X, ALU.add),
                        deps=[trms1])
                    trms3 = p.dve.op(lambda v: v.tensor_scalar(rmsc[:], rmsc[:], 1.0 / 128, 1e-6,
                                                               ALU.mult, ALU.add), deps=[trms2])
                    trms4 = p.act.op(lambda s: s.activation(rmsc[:], rmsc[:], AF.Sqrt),
                                     deps=[trms3, tec2])
                    trms5 = p.dve.op(lambda v: v.reciprocal(rmsc[:], rmsc[:]), deps=[trms4])
                    tz1 = p.act.op(lambda s, kh=kh, n=n: s.activation(
                        silz[:], ztile[n % 2][:, kh * 256:(kh + 1) * 256], AF.Silu),
                        deps=[t_zl, trms4])
                    tz1_last = tz1
                    tg1_ = p.dve.op(lambda v: v.tensor_scalar(ogt[:, 0:128], Ot[:, 0:128],
                                                              rmsc[:, 0:1], None, ALU.mult),
                                    deps=[trms5])
                    tg2_ = p.dve.op(lambda v: v.tensor_scalar(ogt[:, 128:256], Ot[:, 128:256],
                                                              rmsc[:, 1:2], None, ALU.mult),
                                    deps=[tg1_])
                    tg3_ = p.dve.op(lambda v: v.tensor_tensor(ogt[:, 0:256], ogt[:, 0:256],
                                                              nwt2[:], ALU.mult),
                                    deps=[tg2_, t_nwt2])
                    tg4_ = p.dve.op(lambda v: v.tensor_tensor(ogt[:, 0:256], ogt[:, 0:256],
                                                              silz[:], ALU.mult),
                                    deps=[tg3_, tz1])
                    t_ogdma = p.dma_hw(p.sp, f"og{kh % 2}",
                                       og_o[n * L:(n + 1) * L, kh * 256:(kh + 1) * 256],
                                       ogt[:, 0:256], deps=[tg4_])
                t_zt[n % 2] = tz1_last
                t_chunk = p.dve.tag()
                t_coef_users = t_chunk
            t_group_consumed[gb] = t_chunk
        p.finish()
        p.emit()
    return nc


def make_E_inmaps(inputs, d_outs):
    T = 8192
    maps = []
    e8 = np.zeros((8, 8 * 128), dtype=np.float32)
    for h in range(8):
        e8[h, h * 128:(h + 1) * 128] = 1.0
    sidx = np.arange(128)
    triu = (sidx[:, None] <= sidx[None, :]).astype(np.float32)
    mA = np.where(sidx[None, :] < sidx[:, None], 1e30, -1e30).astype(np.float32)
    mAT = np.where(sidx[None, :] > sidx[:, None], 1e30, -1e30).astype(np.float32)
    mW = np.where(sidx[None, :] >= sidx[:, None], 1e30, -1e30).astype(np.float32)
    cw = np.asarray(inputs["L1_attn_conv_w"])
    dtb = np.asarray(inputs["L1_attn_dt_bias"])
    alog = np.asarray(inputs["L1_attn_A_log"])
    nw = np.asarray(inputs["L1_attn_norm_w"])
    for bb in range(2):
        cores = [4 * bb + i for i in range(4)]
        qkv_full = np.zeros((8192, T), dtype=BF)
        z_full = np.zeros((4096, T), dtype=BF)
        ba_full = np.zeros((64, T), dtype=np.float32)
        for m, corec in enumerate(cores):
            qkv_c = np.asarray(d_outs[corec]["qkv_o"]).view(BF)
            z_c = np.asarray(d_outs[corec]["z_o"]).view(BF)
            ba_c = np.asarray(d_outs[corec]["ba_o"])
            for j in range(16):
                blk = 4 * j + m
                qkv_full[:, blk * 128:(blk + 1) * 128] = qkv_c[:, j * 128:(j + 1) * 128]
                z_full[:, blk * 128:(blk + 1) * 128] = z_c[:, j * 128:(j + 1) * 128]
                ba_full[:, blk * 128:(blk + 1) * 128] = ba_c[:, j * 128:(j + 1) * 128]
        for m in range(4):
            core = 4 * bb + m
            qh = np.ascontiguousarray(qkv_full[512 * m:512 * (m + 1)])
            kh = np.ascontiguousarray(qkv_full[2048 + 512 * m:2048 + 512 * (m + 1)])
            vh = np.ascontiguousarray(qkv_full[4096 + 1024 * m:4096 + 1024 * (m + 1)])
            zh = np.ascontiguousarray(z_full[1024 * m:1024 * (m + 1)].T)
            bah = np.ascontiguousarray(
                np.concatenate([ba_full[8 * m:8 * m + 8], ba_full[32 + 8 * m:32 + 8 * m + 8]],
                               axis=0).T)
            cwrows = np.concatenate([cw[2048 + 512 * m + 0:2048 + 512 * m + 0][0:0]] ) if False else None
            cw_q = cw[512 * m:512 * (m + 1)]
            cw_k = cw[2048 + 512 * m:2048 + 512 * (m + 1)]
            cw_v = cw[4096 + 1024 * m:4096 + 1024 * (m + 1)]
            cwh = np.ascontiguousarray(np.concatenate([cw_q, cw_k, cw_v], axis=0))
            maps.append({
                "qh_in": qh, "kh_in": kh, "vh_in": vh, "zh_in": zh, "ba_in": bah,
                "cw_in": cwh,
                "dtb_in": np.ascontiguousarray(dtb[8 * m:8 * m + 8][None, :]),
                "alog_in": np.ascontiguousarray(alog[8 * m:8 * m + 8][None, :]),
                "nw_in": np.ascontiguousarray(nw[None, :]),
                "identb": np.eye(128, dtype=np.float32).astype(BF),
                "identf": np.eye(128, dtype=np.float32),
                "ident8": np.eye(8, dtype=np.float32),
                "ones8x": np.ones((8, 128), dtype=np.float32),
                "e8_in": e8, "triu_in": triu, "mA_in": mA, "mAT_in": mAT, "mW_in": mW,
            })
    return maps


def build_F():
    VD = 4096
    nc = bass.Bass("TRN2", target_bir_lowering=False)
    x2_in = nc.dram_tensor("x2_in", [C, TPC], F32, kind="ExternalInput").ap()
    og_in = nc.dram_tensor("og_in", [VD, TPC], BF16, kind="ExternalInput").ap()
    woT = nc.dram_tensor("woT", [VD, C], F32, kind="ExternalInput").ap()
    waqT = nc.dram_tensor("waqT", [C, C], F32, kind="ExternalInput").ap()
    wakT = nc.dram_tensor("wakT", [C, C], F32, kind="ExternalInput").ap()
    waqlT = nc.dram_tensor("waqlT", [C, 128], F32, kind="ExternalInput").ap()
    waklT = nc.dram_tensor("waklT", [C, 128], F32, kind="ExternalInput").ap()
    wavT = nc.dram_tensor("wavT", [C, C], F32, kind="ExternalInput").ap()
    wrec = nc.dram_tensor("wrec", [16, 1], F32, kind="ExternalInput").ap()
    wstd = nc.dram_tensor("wstd", [16, 1], F32, kind="ExternalInput").ap()
    patlow = nc.dram_tensor("patlow", [16, 128], F32, kind="ExternalInput").ap()
    pathigh = nc.dram_tensor("pathigh", [16, 128], F32, kind="ExternalInput").ap()
    ident16 = nc.dram_tensor("ident16", [16, 16], F32, kind="ExternalInput").ap()
    x3_o = nc.dram_tensor("x3_o", [C, TPC], F32, kind="ExternalOutput").ap()
    qf_o = nc.dram_tensor("qf_o", [C, TPC], BF16, kind="ExternalOutput").ap()
    kf_o = nc.dram_tensor("kf_o", [C, TPC], BF16, kind="ExternalOutput").ap()
    v_o = nc.dram_tensor("v_o", [TPC, C], BF16, kind="ExternalOutput").ap()

    x2v = x2_in.rearrange("(c p) t -> c p t", p=128)
    ogv = og_in.rearrange("(c p) t -> c p t", p=128)
    x3v = x3_o.rearrange("(c p) t -> c p t", p=128)
    qfv = qf_o.rearrange("(c p) t -> c p t", p=128)
    kfv = kf_o.rearrange("(c p) t -> c p t", p=128)

    with ExitStack() as ctx:
        def sb(nm, shp, dt, stack=None):
            return (stack or ctx).enter_context(nc.sbuf_tensor(nm, shp, dt))
        p = Prog(nc, ctx)
        onesb = sb("onesbF", [128, 1], BF16)
        onesr = sb("onesrF", [1, 128], BF16)
        t_ob = p.dve.op(lambda v: v.memset(onesb[:], 1.0))
        t_or = p.dve.op(lambda v: v.memset(onesr[:], 1.0))
        c_eps5, tg_e5 = p.constf(1e-5)
        c_eps8, tg_e8 = p.constf(1e-8)

        xf = sb("xfF", [128, 16 * TPC], BF16)
        xn = sb("xnF", [128, 16 * TPC], BF16)

        def xfs(c, tt, w=TT):
            return xf[:, c * TPC + tt * w: c * TPC + (tt + 1) * w]

        s1 = ctx.enter_context(ExitStack())
        ogb = sb("ogb", [128, 32 * TT], BF16, s1)
        wob = [sb(f"wob{i}", [128, 32 * 128], BF16, s1) for i in range(2)]
        x2t = [sb(f"x2tF{i}", [128, TT], F32, s1) for i in range(2)]
        x3t = [sb(f"x3tF{i}", [128, TT], F32, s1) for i in range(2)]

        def w_cols(wT, oc, nkc, width=128):
            return wT.rearrange("(kc p) o -> p kc o", p=128)[:, :, oc * width:(oc + 1) * width]

        with nc.psum_tensor("psf0", [128, TT], F32) as psf0, \
             nc.psum_tensor("psf1", [128, TT], F32) as psf1:
            psfs = [psf0, psf1]
            t_w_free = [None, None]
            t_og_used = None
            t_x2l = [None, None]
            t_xw = [None, None]
            t_x3st = [None, None]
            for tt in range(NTT):
                t_ogl = []
                for c in range(32):
                    t_ogl.append(p.dma_hw(p.sp, f"ogl{c % 4}", ogb[:, c * TT:(c + 1) * TT],
                                          ogv[c][:, tt * TT:(tt + 1) * TT],
                                          deps=[t_og_used, t_ogl[c - 4] if c >= 4 else None]))
                for oc in range(16):
                    bo = oc % 2
                    tw = p.dma_sw(f"woF{bo}", wob[bo][:], w_cols(woT, oc, 32), deps=[t_w_free[bo]])
                    ps = psfs[bo]
                    t_mm = None
                    for kc in range(32):
                        fn = p.pe.op if kc == 31 else p.pe.op_quiet
                        t_mm = fn(lambda t, kc=kc, oc=oc, ps=ps, bo=bo, s=(kc == 0), e=(kc == 31):
                                  t.matmul(ps[:], wob[bo][:, kc * 128:(kc + 1) * 128],
                                           ogb[:, kc * TT:(kc + 1) * TT], start=s, stop=e),
                                  deps=([tw, t_xw[bo]] + t_ogl if kc == 0 else ()))
                    t_w_free[bo] = t_mm
                    tx2 = p.dma_hw(p.sp, f"x2F{bo}", x2t[bo][:], x2v[oc][:, tt * TT:(tt + 1) * TT],
                                   deps=[t_x2l[bo]])
                    txa = p.dve.op(lambda v, bo=bo, ps=ps:
                                   v.tensor_tensor(x3t[bo][:], x2t[bo][:], ps[:], ALU.add),
                                   deps=[t_mm, tx2, t_x3st[bo]])
                    t_x2l[bo] = txa
                    t_xw[bo] = txa
                    txc = p.dve.op(lambda v, oc=oc, tt=tt, bo=bo:
                                   v.tensor_copy(xfs(oc, tt), x3t[bo][:]), deps=[txa])
                    t_x3st[bo] = p.dma_hw(p.sp, f"x3st{bo}", x3v[oc][:, tt * TT:(tt + 1) * TT],
                                          x3t[bo][:], deps=[txc])
                t_og_used = t_mm
        p.barrier()
        s1.close()

        s2 = ctx.enter_context(ExitStack())
        casts = [sb(f"castF{i}", [128, TT], BF16, s2) for i in range(2)]
        sqt = [sb(f"sqtF{i}", [128, TT], BF16, s2) for i in range(2)]
        sumr = sb("sumrF", [1, TPC], F32, s2)
        sqr = sb("sqrF", [1, TPC], F32, s2)
        mu_b = sb("mu_bF", [1, TPC], BF16, s2)
        rs_b = sb("rs_bF", [1, TPC], BF16, s2)
        rowa = sb("rowaF", [1, TPC], F32, s2)
        roww = sb("rowwF", [1, TPC], F32, s2)
        with nc.psum_tensor("pssumF", [1, TT], F32) as pssum, \
             nc.psum_tensor("pssqF", [1, TT], F32) as pssq:
            t_sum_evac = None
            t_sq_evac = None
            for tt in range(NTT):
                t_mm = None
                t_mmq = None
                for c in range(16):
                    tcast = p.dve.op(lambda v, c=c, tt=tt, b=c % 2:
                                     v.tensor_copy(casts[b][:], xfs(c, tt)), deps=[t_mm])
                    t_mm = p.pe.op(lambda t, c=c, s=(c == 0), e=(c == 15), b=c % 2:
                                   t.matmul(pssum[:], onesb[:], casts[b][:], start=s, stop=e),
                                   deps=[tcast, t_ob] + ([t_sum_evac] if c == 0 else []))
                    t_sq = p.act.op(lambda s, c=c, b=c % 2:
                                    s.activation(sqt[b][:], casts[b][:], AF.Square),
                                    deps=[tcast, t_mmq])
                    t_mmq = p.pe.op(lambda t, c=c, s=(c == 0), e=(c == 15), b=c % 2:
                                    t.matmul(pssq[:], onesb[:], sqt[b][:], start=s, stop=e),
                                    deps=[t_sq] + ([t_sq_evac] if c == 0 else []))
                t_sum_evac = p.dve.op(lambda v, tt=tt:
                                      v.tensor_copy(sumr[0:1, tt * TT:(tt + 1) * TT], pssum[:]),
                                      deps=[t_mm])
                t_sq_evac = p.dve.op(lambda v, tt=tt:
                                     v.tensor_copy(sqr[0:1, tt * TT:(tt + 1) * TT], pssq[:]),
                                     deps=[t_mmq])
        t1 = p.dve.op(lambda v: v.tensor_scalar_mul(rowa[:], sumr[:], 1.0 / C), deps=[t_sum_evac])
        t_mub = p.dve.op(lambda v: v.tensor_copy(mu_b[:], rowa[:]), deps=[t1])
        t2 = p.dve.op(lambda v: v.tensor_scalar_mul(roww[:], sqr[:], 1.0 / C), deps=[t_sq_evac])
        t3 = p.dve.op(lambda v: v.tensor_tensor(sumr[:], rowa[:], rowa[:], ALU.mult), deps=[t1, t2])
        t4 = p.dve.op(lambda v: v.tensor_sub(roww[:], roww[:], sumr[:]), deps=[t3])
        t5 = p.act.op(lambda s: s.activation(rowa[:], roww[:], AF.Sqrt, bias=c_eps5[0:1, :]),
                      deps=[t4, t_mub, tg_e5])
        t6 = p.dve.op(lambda v: v.reciprocal(roww[:], rowa[:]), deps=[t5])
        t_rsb = p.dve.op(lambda v: v.tensor_copy(rs_b[:], roww[:]), deps=[t6])
        p.barrier()

        mub_s = sb("mub_sF", [128, TT], BF16, s2)
        rsb_s = sb("rsb_sF", [128, TT], BF16, s2)
        with nc.psum_tensor("psbF0", [128, TT], F32) as psb0, \
             nc.psum_tensor("psbF1", [128, TT], F32) as psb1:
            te1 = te2 = None
            tb_last = None
            for tt in range(NTT):
                tm1 = p.pe.op(lambda t, tt=tt: t.matmul(psb0[:], onesr[:],
                                                        mu_b[0:1, tt * TT:(tt + 1) * TT],
                                                        start=True, stop=True),
                              deps=[t_mub, t_or, te1])
                tm2 = p.pe.op(lambda t, tt=tt: t.matmul(psb1[:], onesr[:],
                                                        rs_b[0:1, tt * TT:(tt + 1) * TT],
                                                        start=True, stop=True),
                              deps=[t_rsb, te2])
                te1 = p.dve.op(lambda v: v.tensor_copy(mub_s[:], psb0[:]), deps=[tm1, tb_last])
                te2 = p.dve.op(lambda v: v.tensor_copy(rsb_s[:], psb1[:]), deps=[tm2])
                for c in range(16):
                    ta = p.dve.op(lambda v, c=c, tt=tt:
                                  v.tensor_sub(xn[:, c * TPC + tt * TT: c * TPC + (tt + 1) * TT],
                                               xfs(c, tt), mub_s[:]),
                                  deps=[te1])
                    tb_last = p.dve.op(lambda v, c=c, tt=tt: v.tensor_tensor(
                        xn[:, c * TPC + tt * TT: c * TPC + (tt + 1) * TT],
                        xn[:, c * TPC + tt * TT: c * TPC + (tt + 1) * TT], rsb_s[:], ALU.mult),
                        deps=[ta, te2])
        p.barrier()
        s2.close()
        t_xn = p.dve.tag()

        def xns(kc, tt, w=TT):
            return xn[:, kc * TPC + tt * w: kc * TPC + (tt + 1) * w]

        s4 = ctx.enter_context(ExitStack())
        w16 = sb("w16", [16, 2], F32, s4)
        m16 = sb("m16", [16, 1], F32, s4)
        s16 = sb("s16d", [16, 16], F32, s4)
        pat_l = sb("pat_l", [16, 128], F32, s4)
        pat_h = sb("pat_h", [16, 128], F32, s4)
        id16 = sb("id16", [16, 16], F32, s4)
        csq = sb("csq", [128, 16], F32, s4)
        mrep = sb("mrep", [128, 16], F32, s4)
        tw1 = p.dma_hw(p.sp, "wrec", w16[:, 0:1], wrec)
        tw2 = p.dma_hw(p.sp, "wstd", w16[:, 1:2], wstd)
        tp1 = p.dma_hw(p.sp, "patl", pat_l[:], patlow)
        tp2 = p.dma_hw(p.sp, "path", pat_h[:], pathigh)
        tp3 = p.dma_hw(p.sp, "id16", id16[:], ident16)
        tm_ = p.dve.op(lambda v: v.tensor_single_scalar(m16[:], w16[:, 0:1], 0.1, ALU.is_gt),
                       deps=[tw1])
        ts1 = p.dve.op(lambda v: v.tensor_scalar(w16[:, 0:1], w16[:, 0:1], 1e-8, None, ALU.max),
                       deps=[tm_])
        ts2 = p.dve.op(lambda v: v.tensor_scalar(w16[:, 1:2], w16[:, 1:2], 1e-8, None, ALU.max),
                       deps=[tw2, ts1])
        ts3 = p.act.op(lambda s: s.activation(w16[:, 0:2], w16[:, 0:2], AF.Sqrt), deps=[ts2])
        with nc.psum_tensor("pscs", [128, 16], F32) as pscs, \
             nc.psum_tensor("psmr", [128, 16], F32) as psmr:
            td1_ = p.dve.op(lambda v: v.tensor_scalar(s16[:], id16[:], w16[:, 0:1], None, ALU.mult),
                            deps=[ts3, tp3])
            tcs1 = p.pe.op(lambda t: t.matmul(pscs[:], pat_l[:], s16[:], start=True, stop=False),
                           deps=[td1_, tp1])
            td2_ = p.dve.op(lambda v: v.tensor_scalar(s16[:], id16[:], w16[:, 1:2], None, ALU.mult),
                            deps=[tcs1])
            tcs2 = p.pe.op(lambda t: t.matmul(pscs[:], pat_h[:], s16[:], start=False, stop=True),
                           deps=[td2_, tp2])
            tcse = p.dve.op(lambda v: v.tensor_copy(csq[:], pscs[:]), deps=[tcs2])
            td3_ = p.dve.op(lambda v: v.tensor_scalar(s16[:], id16[:], m16[:], None, ALU.mult),
                            deps=[tcse, tm_])
            tmr1 = p.pe.op(lambda t: t.matmul(psmr[:], pat_l[:], s16[:], start=True, stop=False),
                           deps=[td3_, tcse])
            tmr2 = p.pe.op(lambda t: t.matmul(psmr[:], pat_h[:], s16[:], start=False, stop=True),
                           deps=[tmr1])
            tmre = p.dve.op(lambda v: v.tensor_copy(mrep[:], psmr[:]), deps=[tmr2])
        p.barrier()

        wb_ = [sb(f"wbF{i}", [128, 16 * 128], BF16, s4) for i in range(2)]
        wlo = [sb(f"wlo{i}", [128, 16 * 8], BF16, s4) for i in range(2)]
        stg = [sb(f"stgF{i}", [128, TT], BF16, s4) for i in range(2)]
        with nc.psum_tensor("psqF0", [128, TT], F32) as psq0, \
             nc.psum_tensor("psqF1", [128, TT], F32) as psq1:
            psqs = [psq0, psq1]
            t_w_free = [None, None]
            t_lo_free = [None, None]
            t_dma = [None, None]
            t_ev = [None, None]
            i = 0
            for which, wmain, walt, outv in [("q", waqT, waklT, qfv), ("k", wakT, waqlT, kfv)]:
                for hh in range(16):
                    bo = hh % 2
                    tw = p.dma_sw(f"wF{bo}", wb_[bo][:], w_cols(wmain, hh, 16),
                                  deps=[t_w_free[bo]])
                    twl = p.dma_sw(f"wl{bo}", wlo[bo][:],
                                   walt.rearrange("(kc p) o -> p kc o", p=128)[:, :, hh * 8:(hh + 1) * 8],
                                   deps=[t_lo_free[bo]])
                    t_bl = None
                    for kc in range(16):
                        tdiff = p.dve.op(lambda v, kc=kc, bo=bo: v.tensor_sub(
                            wlo[bo][:, kc * 8:(kc + 1) * 8], wlo[bo][:, kc * 8:(kc + 1) * 8],
                            wb_[bo][:, kc * 128:kc * 128 + 8]), deps=[tw, twl])
                        t_bl = p.dve.op(lambda v, kc=kc, bo=bo, hh=hh: v.scalar_tensor_tensor(
                            wb_[bo][:, kc * 128:kc * 128 + 8], wlo[bo][:, kc * 8:(kc + 1) * 8],
                            mrep[:, hh:hh + 1], wb_[bo][:, kc * 128:kc * 128 + 8],
                            ALU.mult, ALU.add), deps=[tdiff, tmre])
                    t_lo_free[bo] = t_bl
                    t_mm = None
                    for tt in range(NTT):
                        ps = psqs[i % 2]
                        st_ = stg[i % 2]
                        for kc in range(16):
                            fn = p.pe.op if kc == 15 else p.pe.op_quiet
                            t_mm = fn(lambda t, kc=kc, tt=tt, ps=ps, bo=bo, s=(kc == 0), e=(kc == 15):
                                      t.matmul(ps[:], wb_[bo][:, kc * 128:(kc + 1) * 128],
                                               xns(kc, tt), start=s, stop=e),
                                      deps=([tw, t_bl, t_ev[i % 2]] if kc == 0 else ()))
                        t_e = p.dve.op(lambda v, ps=ps, st_=st_, hh=hh: v.tensor_scalar(
                            st_[:], ps[:], csq[:, hh:hh + 1], None, ALU.mult),
                            deps=[t_mm, t_dma[i % 2], tcse])
                        t_ev[i % 2] = t_e
                        t_dma[i % 2] = p.dma_hw(p.sp, f"sF{i % 2}",
                                                outv[hh][:, tt * TT:(tt + 1) * TT], st_[:],
                                                deps=[t_e])
                        i += 1
                    t_w_free[bo] = t_mm
            wv_sb = sb("wv_sb", [128, 16 * TT], BF16, s4)
            t_vw_free = None
            for vg in range(4):
                t_vw = p.dma_sw("wvF", wv_sb[:],
                                wavT.rearrange("(kc p) o -> p kc o", p=128)[:, :, vg * TT:(vg + 1) * TT],
                                deps=[t_vw_free])
                t_mm = None
                for tsb in range(16):
                    ps = psqs[tsb % 2]
                    st_ = stg[tsb % 2]
                    for kc in range(16):
                        fn = p.pe.op if kc == 15 else p.pe.op_quiet
                        t_mm = fn(lambda t, kc=kc, tsb=tsb, vg=vg, ps=ps, s=(kc == 0), e=(kc == 15):
                                  t.matmul(ps[:], xn[:, kc * TPC + tsb * 128:kc * TPC + (tsb + 1) * 128],
                                           wv_sb[:, kc * TT:(kc + 1) * TT], start=s, stop=e),
                                  deps=([t_vw, t_ev[tsb % 2]] if kc == 0 else ()))
                    t_e = p.act.op(lambda s, ps=ps, st_=st_: s.copy(st_[:], ps[:]),
                                   deps=[t_mm, t_dma[tsb % 2]])
                    t_ev[tsb % 2] = t_e
                    t_dma[tsb % 2] = p.dma_hw(p.sp, f"vF{tsb % 2}",
                                              v_o[tsb * 128:(tsb + 1) * 128, vg * TT:(vg + 1) * TT],
                                              st_[:], deps=[t_e])
                t_vw_free = t_mm
        p.finish()
        p.emit()
    return nc


def build_G(m):
    T = 8192
    NH = 16
    nc = bass.Bass("TRN2", target_bir_lowering=False)
    x3_in = nc.dram_tensor("x3_in", [C, TPC], F32, kind="ExternalInput").ap()
    qf_in = nc.dram_tensor("qf_in", [C, TPC], BF16, kind="ExternalInput").ap()
    kfull = nc.dram_tensor("kfull", [C, T], BF16, kind="ExternalInput").ap()
    vfull = nc.dram_tensor("vfull", [T, C], BF16, kind="ExternalInput").ap()
    wp_in = nc.dram_tensor("wp_in", [C, C], F32, kind="ExternalInput").ap()
    identf = nc.dram_tensor("identf", [128, 128], F32, kind="ExternalInput").ap()
    trineg = nc.dram_tensor("trineg", [128, 128], F32, kind="ExternalInput").ap()
    out_o = nc.dram_tensor("out_o", [TPC, C], F32, kind="ExternalOutput").ap()

    x3v = x3_in.rearrange("(c p) t -> c p t", p=128)
    qfv = qf_in.rearrange("(c p) t -> c p t", p=128)
    SCALE = 1.0 / np.sqrt(128.0)

    with ExitStack() as ctx:
        def sb(nm, shp, dt, stack=None):
            return (stack or ctx).enter_context(nc.sbuf_tensor(nm, shp, dt))
        p = Prog(nc, ctx)
        idf = sb("idfG", [128, 128], F32)
        tri = sb("triG", [128, 128], F32)
        onesb = sb("onesbG", [128, 1], BF16)
        onesr = sb("onesrG", [1, 128], BF16)
        t_ob = p.dve.op(lambda v: v.memset(onesb[:], 1.0))
        t_or = p.dve.op(lambda v: v.memset(onesr[:], 1.0))
        t_idf = p.dma_hw(p.sp, "idf", idf[:], identf)
        t_tri = p.dma_hw(p.sp, "tri", tri[:], trineg)

        Kh = sb("Kh", [128, T], BF16)
        Vh = sb("Vh", [128, T], BF16)
        sqh = sb("sqh", [128, T], BF16)
        Qall = sb("Qall", [128, TPC], BF16)
        outF = [sb(f"outF{i}", [128, NH * TT], BF16) for i in range(4)]
        P32 = sb("P32", [128, TT], F32)
        Pb = sb("Pb", [128, TT], BF16)
        lrow = sb("lrowG", [1, TT], F32)
        lrowb = sb("lrowbG", [1, TT], BF16)
        rb = sb("rbG", [128, TT], BF16)
        mrow = sb("mrowG", [1, 4], F32)
        mcol = sb("mcolG", [128, 1], F32)
        negMh = sb("negMhG", [128, 1], F32)

        sfl = ctx.enter_context(ExitStack())
        psS = sfl.enter_context(nc.psum_tensor("psSG", [128, TT], F32))
        psO = sfl.enter_context(nc.psum_tensor("psOG", [128, TT], F32))
        psL = sfl.enter_context(nc.psum_tensor("psLG", [1, TT], F32))
        psM = sfl.enter_context(nc.psum_tensor("psMG", [128, 64], F32))
        psB = sfl.enter_context(nc.psum_tensor("psBG", [128, TT], F32))

        onesrfG = sb("onesrfG", [1, 128], F32)
        t_orf = p.dve.op(lambda v: v.memset(onesrfG[:], 1.0))
        t_head_done = None
        t_outF_used = [None] * 4
        t_l_ev = None
        t_o_ev = None
        t_rb_ev = None
        t_x4u = None
        for h in range(NH):
            tkl = p.dma_hw(p.sp, "khl", Kh[:], kfull[h * 128:(h + 1) * 128, :],
                           deps=[t_head_done])
            tvl = p.dma_hw(p.sp, "vhl", Vh[:],
                           vfull.rearrange("(kb p) d -> p kb d", p=128)[:, :, h * 128:(h + 1) * 128],
                           deps=[t_head_done])
            tql = p.dma_hw(p.sp, "qhl", Qall[:], qfv[h], deps=[t_head_done])
            tsq = p.act.op(lambda s: s.activation(sqh[:], Kh[:], AF.Square),
                           deps=[tkl, t_head_done])
            t_mx = None
            for kb in range(64):
                fn = p.pe.op if kb == 63 else p.pe.op_quiet
                t_mx = fn(lambda t, kb=kb: t.matmul(psM[:, kb:kb + 1],
                                                    sqh[:, kb * 128:(kb + 1) * 128], onesb[:],
                                                    start=True, stop=True),
                          deps=[tsq, t_ob, t_head_done] if kb == 0 else ())
            tmk1 = p.dve.op(lambda v: v.tensor_reduce(mcol[:], psM[:, 0:64], AX.X, ALU.max),
                            deps=[t_mx])
            ttk = p.pe.op(lambda t: t.transpose(psS[0:1, 0:128], mcol[:], idf[:]),
                          deps=[tmk1, t_idf])
            tmk2 = p.dve.op(lambda v: v.tensor_reduce(mrow[0:1, 1:2], psS[0:1, 0:128], AX.X,
                                                      ALU.max), deps=[ttk])
            tsq2 = p.act.op(lambda s: s.activation(sqh[:, 0:TPC], Qall[:], AF.Square),
                            deps=[tql, t_mx])
            t_mq = None
            for qb in range(16):
                fn = p.pe.op if qb == 15 else p.pe.op_quiet
                t_mq = fn(lambda t, qb=qb: t.matmul(psM[:, qb:qb + 1],
                                                    sqh[:, qb * 128:(qb + 1) * 128], onesb[:],
                                                    start=True, stop=True),
                          deps=[tsq2, tmk1] if qb == 0 else ())
            tmq1 = p.dve.op(lambda v: v.tensor_reduce(mcol[:], psM[:, 0:16], AX.X, ALU.max),
                            deps=[t_mq, tmk2])
            ttq = p.pe.op(lambda t: t.transpose(psS[0:1, 0:128], mcol[:], idf[:]),
                          deps=[tmq1, tmk2])
            tmq2 = p.dve.op(lambda v: v.tensor_reduce(mrow[0:1, 0:1], psS[0:1, 0:128], AX.X,
                                                      ALU.max), deps=[ttq])
            tm3 = p.dve.op(lambda v: v.tensor_tensor(mrow[0:1, 2:3], mrow[0:1, 0:1],
                                                     mrow[0:1, 1:2], ALU.mult), deps=[tmq2])
            tm4 = p.act.op(lambda s: s.activation(mrow[0:1, 2:3], mrow[0:1, 2:3], AF.Sqrt),
                           deps=[tm3])
            tm5 = p.dve.op(lambda v: v.tensor_scalar_mul(mrow[0:1, 2:3], mrow[0:1, 2:3],
                                                         -SCALE), deps=[tm4])
            tbc = p.pe.op(lambda t: t.matmul(psS[:, 0:1], onesrfG[:], mrow[0:1, 2:3],
                                             start=True, stop=True), deps=[tm5, t_orf])
            t_nm = p.dve.op(lambda v: v.tensor_copy(negMh[:], psS[:, 0:1]), deps=[tbc])

            for qt in range(NTT):
                kb_max = 16 * qt + 12 + m
                nkb = kb_max + 1
                t_sc_ev = None
                t_pb = None
                for kb in range(nkb):
                    tmm = p.pe.op(lambda t, kb=kb, qt=qt:
                                  t.matmul(psS[:], Kh[:, kb * 128:(kb + 1) * 128],
                                           Qall[:, qt * TT:(qt + 1) * TT], start=True, stop=True),
                                  deps=([tkl, t_nm, t_sc_ev] if kb == 0 else [t_sc_ev]))
                    subops = []
                    for sub in range(4):
                        qbg = 4 * (4 * qt + sub) + m
                        sl = slice(sub * 128, (sub + 1) * 128)
                        if kb < qbg:
                            so = p.dve.op(lambda v, sl=sl: v.tensor_scalar_mul(
                                P32[:, sl], psS[:, sl], SCALE), deps=[tmm, t_pb])
                        elif kb == qbg:
                            so = p.dve.op(lambda v, sl=sl: v.scalar_tensor_tensor(
                                P32[:, sl], psS[:, sl], SCALE, tri[:], ALU.mult, ALU.add),
                                deps=[tmm, t_tri, t_pb])
                        else:
                            so = p.dve.op(lambda v, sl=sl: v.memset(P32[:, sl], -1e30),
                                          deps=[t_pb])
                        subops.append(so)
                    t_sc_ev = subops[-1]
                    texp = p.act.op(lambda s: s.activation(Pb[:], P32[:], AF.Exp,
                                                           bias=negMh[:]),
                                    deps=subops + [t_nm])
                    t_pb = texp
                    tlm = p.pe.op(lambda t, kb=kb, s=(kb == 0), e=(kb == nkb - 1):
                                  t.matmul(psL[:], onesb[:], Pb[:], start=s, stop=e),
                                  deps=[texp] + ([t_l_ev] if kb == 0 else []))
                    tom = p.pe.op(lambda t, kb=kb, qt=qt, s=(kb == 0), e=(kb == nkb - 1):
                                  t.matmul(psO[:], Vh[:, kb * 128:(kb + 1) * 128], Pb[:],
                                           start=s, stop=e),
                                  deps=[tvl, tlm] + ([t_o_ev] if kb == 0 else []))
                    t_pb = tom
                t_l1 = p.dve.op(lambda v: v.tensor_copy(lrow[:], psL[:]), deps=[tom])
                t_l_ev = t_l1
                t_l2 = p.dve.op(lambda v: v.reciprocal(lrow[:], lrow[:]), deps=[t_l1])
                t_l3 = p.dve.op(lambda v: v.tensor_copy(lrowb[:], lrow[:]), deps=[t_l2])
                t_bc2 = p.pe.op(lambda t: t.matmul(psB[:], onesr[:], lrowb[:], start=True,
                                                   stop=True), deps=[t_l3, t_or, t_rb_ev])
                t_rb = p.act.op(lambda s: s.copy(rb[:], psB[:]), deps=[t_bc2])
                t_rb_ev = t_rb
                t_of = p.dve.op(lambda v, h=h, qt=qt: v.tensor_tensor(
                    outF[qt][:, h * TT:(h + 1) * TT], psO[:], rb[:], ALU.mult),
                    deps=[t_rb, tom, t_outF_used[qt]])
                t_o_ev = t_of
            t_head_done = t_of

        p.barrier()
        sfl.close()
        s2 = ctx.enter_context(ExitStack())
        wpb = [sb(f"wpbG{i}", [128, 16 * 128], BF16, s2) for i in range(2)]
        x3t = [sb(f"x3tG{i}", [128, TT], F32, s2) for i in range(2)]
        x4b = sb("x4bG", [128, 16 * TT], F32, s2)
        tmt = sb("tmtG", [128, C], F32, s2)

        def w_cols(wT, oc, width=128):
            return wT.rearrange("(kc p) o -> p kc o", p=128)[:, :, oc * width:(oc + 1) * width]

        with nc.psum_tensor("psp0G", [128, TT], F32) as psp0, \
             nc.psum_tensor("psp1G", [128, TT], F32) as psp1, \
             nc.psum_tensor("pst0G", [128, 128], F32) as pst0, \
             nc.psum_tensor("pst1G", [128, 128], F32) as pst1:
            psps = [psp0, psp1]
            psts = [pst0, pst1]
            t_wfree = [None, None]
            t_x3l = [None, None]
            t_x4w = [None, None]
            t_tp_ev = [None, None]
            t_tmdma = None
            for qt in range(NTT):
                for oc in range(16):
                    bo = oc % 2
                    tw = p.dma_sw(f"wpG{bo}", wpb[bo][:], w_cols(wp_in, oc),
                                  deps=[t_wfree[bo]])
                    ps = psps[bo]
                    t_mm = None
                    for kc in range(16):
                        fn = p.pe.op if kc == 15 else p.pe.op_quiet
                        t_mm = fn(lambda t, kc=kc, oc=oc, qt=qt, ps=ps, bo=bo,
                                  s=(kc == 0), e=(kc == 15):
                                  t.matmul(ps[:], wpb[bo][:, kc * 128:(kc + 1) * 128],
                                           outF[qt][:, kc * TT:(kc + 1) * TT], start=s, stop=e),
                                  deps=([tw, t_x4w[bo]] if kc == 0 else ()))
                    t_wfree[bo] = t_mm
                    t_outF_used[qt] = t_mm
                    tx3 = p.dma_hw(p.sp, f"x3G{bo}", x3t[bo][:],
                                   x3v[oc][:, qt * TT:(qt + 1) * TT], deps=[t_x3l[bo]])
                    txa = p.dve.op(lambda v, oc=oc, bo=bo, ps=ps: v.tensor_tensor(
                        x4b[:, oc * TT:(oc + 1) * TT], x3t[bo][:], ps[:], ALU.add),
                        deps=[t_mm, tx3, t_x4u])
                    t_x3l[bo] = txa
                    t_x4w[bo] = txa
                t_x4done = txa
                for tb in range(4):
                    t_tp = None
                    for oc in range(16):
                        bo2 = oc % 2
                        ttp = p.pe.op(lambda t, oc=oc, tb=tb, bo2=bo2:
                                      t.transpose(psts[bo2][:],
                                                  x4b[:, oc * TT + tb * 128: oc * TT + (tb + 1) * 128],
                                                  idf[:]),
                                      deps=[t_x4done, t_idf, t_tp_ev[bo2]])
                        t_tp_ev[bo2] = p.act.op(lambda s, oc=oc, bo2=bo2:
                                                s.copy(tmt[:, oc * 128:(oc + 1) * 128], psts[bo2][:]),
                                                deps=[ttp, t_tmdma])
                    t_tmdma = p.dma_hw(p.sp, f"outG{tb % 2}",
                                       out_o[qt * TT + tb * 128: qt * TT + (tb + 1) * 128, :],
                                       tmt[:], deps=[t_tp_ev[0], t_tp_ev[1]])
                    t_x4u = t_tmdma
        p.finish()
        p.emit()
    return nc


def blocks_for_core(core):
    b, m = divmod(core, 4)
    return b, [4 * j + m for j in range(16)]


def shard_x(x):
    out = []
    for core in range(NCORE):
        b, blks = blocks_for_core(core)
        rows = np.concatenate([np.asarray(x[b, blk * 128:(blk + 1) * 128, :]) for blk in blks], axis=0)
        out.append(np.ascontiguousarray(rows, dtype=np.float32))
    return out


def make_A_inmaps(inputs):
    x = np.asarray(inputs["L0_attn_x"])
    w_qkv = np.asarray(inputs["L0_attn_w_qkv"])
    wqT = np.ascontiguousarray(w_qkv[0:C].T)
    wkT = np.ascontiguousarray(w_qkv[C:2 * C].T)
    wvT = np.ascontiguousarray(w_qkv[2 * C:].T)
    identf = np.eye(128, dtype=np.float32)
    identb = np.eye(128, dtype=np.float32).astype(BF)
    gup = np.asarray(inputs["L0_ffn_gate_up_packed"])
    gsc = np.asarray(inputs["L0_ffn_gate_up_scales"])
    dwp = np.asarray(inputs["L0_ffn_down_packed"])
    dsc = np.asarray(inputs["L0_ffn_down_scales"])
    xs = shard_x(x)
    maps = []
    for core in range(NCORE):
        r0, r1 = core * 1408, (core + 1) * 1408
        d0, d1 = core * 256, (core + 1) * 256
        maps.append({
            "x_in": xs[core],
            "wqT": wqT, "wkT": wkT, "wvT": wvT,
            "identf": identf, "identb": identb,
            "gup": np.ascontiguousarray(gup[r0:r1]),
            "gup_sc": np.ascontiguousarray(gsc[:, r0:r1].T),
            "dwp": np.ascontiguousarray(dwp[d0:d1]),
            "dwp_sc": np.ascontiguousarray(dsc[:, d0:d1].T),
        })
    return maps


def make_F_inmaps(inputs, c_outs, e_outs):
    Wa = np.asarray(inputs["L1_ffn_Wa"])
    W_out = np.asarray(inputs["L1_attn_W_out"])
    w_std = np.asarray(inputs["L1_ffn_w_std"])
    w_rec = np.asarray(inputs["L1_ffn_w_rec"])
    H, Ds, R = 16, 120, 8
    q_std = Wa[:, :H * Ds]
    k_std = Wa[:, H * Ds:2 * H * Ds]
    v_w = Wa[:, 2 * H * Ds:2 * H * Ds + 2048]
    q_low = Wa[:, 2 * H * Ds + 2048:2 * H * Ds + 2048 + 128]
    k_low = Wa[:, 2 * H * Ds + 2048 + 128:]
    waq = np.concatenate([np.concatenate([q_low[:, h * 8:(h + 1) * 8],
                                          q_std[:, h * Ds:(h + 1) * Ds]], 1) for h in range(H)], 1)
    wak = np.concatenate([np.concatenate([k_low[:, h * 8:(h + 1) * 8],
                                          k_std[:, h * Ds:(h + 1) * Ds]], 1) for h in range(H)], 1)
    patlow = np.broadcast_to((np.arange(128) < 8).astype(np.float32), (16, 128)).copy()
    pathigh = np.broadcast_to((np.arange(128) >= 8).astype(np.float32), (16, 128)).copy()
    maps = []
    og_fm = {}
    for bb in range(2):
        og_full = np.concatenate([np.asarray(e_outs[4 * bb + mm2]["og_o"]).view(BF)
                                  for mm2 in range(4)], axis=1)
        og_fm[bb] = np.ascontiguousarray(og_full.T)
    for core in range(NCORE):
        bb, blks = blocks_for_core(core)
        og_c = np.concatenate([og_fm[bb][:, blk * 128:(blk + 1) * 128] for blk in blks], axis=1)
        maps.append({
            "x2_in": np.ascontiguousarray(c_outs[core]["x2_o"]),
            "og_in": np.ascontiguousarray(og_c),
            "woT": np.ascontiguousarray(W_out.T),
            "waqT": np.ascontiguousarray(waq), "wakT": np.ascontiguousarray(wak),
            "waqlT": np.ascontiguousarray(q_low), "waklT": np.ascontiguousarray(k_low),
            "wavT": np.ascontiguousarray(v_w),
            "wrec": np.ascontiguousarray(w_rec[:, None]),
            "wstd": np.ascontiguousarray(w_std[:, None]),
            "patlow": patlow, "pathigh": pathigh,
            "ident16": np.eye(16, dtype=np.float32),
        })
    return maps


def make_G_inmaps(inputs, f_outs, mslot):
    Wp = np.asarray(inputs["L1_ffn_Wp"])
    sidx = np.arange(128)
    trineg = np.where(sidx[:, None] <= sidx[None, :], 0.0, -1e30).astype(np.float32)
    kfull = {}
    vfull = {}
    for bb in range(2):
        kf = np.zeros((2048, 8192), dtype=BF)
        vf = np.zeros((8192, 2048), dtype=BF)
        for mm2 in range(4):
            core = 4 * bb + mm2
            kc = np.asarray(f_outs[core]["kf_o"]).view(BF)
            vc = np.asarray(f_outs[core]["v_o"]).view(BF)
            for j in range(16):
                blk = 4 * j + mm2
                kf[:, blk * 128:(blk + 1) * 128] = kc[:, j * 128:(j + 1) * 128]
                vf[blk * 128:(blk + 1) * 128, :] = vc[j * 128:(j + 1) * 128, :]
        kfull[bb] = kf
        vfull[bb] = vf
    maps = []
    for bb in range(2):
        core = 4 * bb + mslot
        maps.append({
            "x3_in": np.ascontiguousarray(f_outs[core]["x3_o"]),
            "qf_in": np.ascontiguousarray(f_outs[core]["qf_o"]),
            "kfull": np.ascontiguousarray(kfull[bb]),
            "vfull": np.ascontiguousarray(vfull[bb]),
            "wp_in": Wp,
            "identf": np.eye(128, dtype=np.float32),
            "trineg": trineg,
        })
    return maps


def kernel(**inputs):
    a_maps = make_A_inmaps(inputs)
    ncA = build_A()
    a_outs = run_bass_kernel_spmd(ncA, a_maps, core_ids=list(range(NCORE))).results
    b_maps = make_B_inmaps(inputs, a_outs)
    ncB = build_B()
    b_outs = run_bass_kernel_spmd(ncB, b_maps, core_ids=list(range(NCORE))).results
    c_maps = make_C_inmaps(inputs, a_outs, b_outs)
    ncC = build_C()
    c_outs = run_bass_kernel_spmd(ncC, c_maps, core_ids=list(range(NCORE))).results
    d_maps = make_D_inmaps(inputs, c_outs)
    ncD = build_D()
    d_outs = run_bass_kernel_spmd(ncD, d_maps, core_ids=list(range(NCORE))).results
    e_maps = make_E_inmaps(inputs, d_outs)
    ncE = build_E()
    e_outs = run_bass_kernel_spmd(ncE, e_maps, core_ids=list(range(NCORE))).results
    f_maps = make_F_inmaps(inputs, c_outs, e_outs)
    ncF = build_F()
    f_outs = run_bass_kernel_spmd(ncF, f_maps, core_ids=list(range(NCORE))).results
    out = np.zeros((2, 8192, C), dtype=np.float32)
    for mslot in range(4):
        g_maps = make_G_inmaps(inputs, f_outs, mslot)
        ncG = build_G(mslot)
        g_res = run_bass_kernel_spmd(ncG, g_maps, core_ids=[mslot, 4 + mslot]).results
        for bb in range(2):
            shard = np.asarray(g_res[bb]["out_o"])
            for j in range(16):
                blk = 4 * j + mslot
                out[bb, blk * 128:(blk + 1) * 128, :] = shard[j * 128:(j + 1) * 128, :]
    return out


# version 02
# speedup vs baseline: 2.2120x; 2.2120x over previous
import numpy as np
import ml_dtypes
from contextlib import ExitStack

import concourse.bass as bass
import concourse.mybir as mybir
from concourse.bass_utils import run_bass_kernel_spmd

F32 = mybir.dt.float32
F16 = mybir.dt.float16
BF16 = mybir.dt.bfloat16
I32 = mybir.dt.int32
U16 = mybir.dt.uint16
AF = mybir.ActivationFunctionType
ALU = mybir.AluOpType
AX = mybir.AxisListType

BF = ml_dtypes.bfloat16

C = 2048
TPC = 2048
NCORE = 8
NBLK = 16
TT = 512
NTT = TPC // TT


class EQ:
    SERIALIZED = {"q_dve", "q_act", "q_pool"}

    def __init__(self, prog, name):
        self.prog = prog
        self.name = name
        self.cnt = 0
        self.waited = {}
        self.items = []
        self.serialize = name in self.SERIALIZED

    def _wait(self, tag):
        if tag is None:
            return
        sem_name, val = tag
        if val <= 0:
            return
        if self.waited.get(sem_name, 0) >= val:
            return
        self.waited[sem_name] = val

        def w(e, sn=sem_name, v=val):
            e.wait_ge(self.prog.sems[sn], v)
        self.items.append(w)

    def op(self, fn, deps=()):
        if self.serialize and self.cnt > 0:
            self._wait((self.name, self.cnt))
        for d in deps:
            self._wait(d)
        self.cnt += 1

        def g(e, fn=fn):
            fn(e).then_inc(self.prog.sems[self.name], 1)
        self.items.append(g)
        return (self.name, self.cnt)

    def op_quiet(self, fn, deps=()):
        for d in deps:
            self._wait(d)
        self.items.append(lambda e, fn=fn: fn(e))
        return None

    def tag(self):
        return (self.name, self.cnt)


class Prog:
    def __init__(self, nc, ctx):
        self.nc = nc
        self.ctx = ctx
        self.sems = {}
        self.pe = EQ(self, "q_pe")
        self.act = EQ(self, "q_act")
        self.dve = EQ(self, "q_dve")
        self.pool = EQ(self, "q_pool")
        self.sp = EQ(self, "q_sp")
        self.queues = [self.pe, self.act, self.dve, self.pool, self.sp]
        for q in self.queues:
            self.sems[q.name] = ctx.enter_context(nc.semaphore(q.name))
        self.dma_cnt = {}
        self._consts = {}
        self._const_tile = ctx.enter_context(nc.sbuf_tensor("constf_tile", [128, 16], F32))
        self._const_n = 0

    def constf(self, val):
        if val in self._consts:
            return self._consts[val]
        i = self._const_n
        self._const_n += 1
        assert i < 16
        ap = self._const_tile[:, i:i + 1]
        tag = self.dve.op(lambda v, ap=ap, val=val: v.memset(ap, float(val)))
        self._consts[val] = (ap, tag)
        return ap, tag

    def _dma_chan(self, ch):
        key = "dch_" + ch
        if key not in self.sems:
            self.sems[key] = self.ctx.enter_context(self.nc.semaphore(key))
            self.dma_cnt[key] = 0
        return key

    def dma_hw(self, q, ch, out, in_, deps=(), transpose=False):
        key = self._dma_chan(ch)
        for d in deps:
            q._wait(d)
        self.dma_cnt[key] += 16
        val = self.dma_cnt[key]

        def g(e, out=out, in_=in_, t=transpose, key=key):
            e.dma_start(out=out, in_=in_, transpose=t).then_inc(self.sems[key], 16)
        q.items.append(g)
        return (key, val)

    def dma_sw(self, ch, out, in_, deps=()):
        key = self._dma_chan(ch)
        for d in deps:
            self.pool._wait(d)
        self.dma_cnt[key] += 16
        val = self.dma_cnt[key]

        def g(e, out=out, in_=in_, key=key):
            e.dma_start(out=out, in_=in_).then_inc(self.sems[key], 16)
        self.pool.items.append(g)
        return (key, val)

    def barrier(self):
        tags = [q.tag() for q in self.queues]
        tags += [(k, v) for k, v in self.dma_cnt.items()]
        for q in self.queues:
            for t in tags:
                if t[0] != q.name:
                    q._wait(t)

    def finish(self):
        for k, v in self.dma_cnt.items():
            self.sp._wait((k, v))

    def emit(self):
        block = self.ctx.enter_context(self.nc.Block())

        def runner(lst):
            def f(eng):
                for it in lst:
                    it(eng)
            return f
        block.tensor(runner(self.pe.items))
        block.scalar(runner(self.act.items))
        block.vector(runner(self.dve.items))
        block.gpsimd(runner(self.pool.items))
        block.sync(runner(self.sp.items))


def build_A():
    nc = bass.Bass("TRN2", target_bir_lowering=False)
    x_in = nc.dram_tensor("x_in", [TPC, C], F32, kind="ExternalInput").ap()
    wqT = nc.dram_tensor("wqT", [C, C], F32, kind="ExternalInput").ap()
    wkT = nc.dram_tensor("wkT", [C, C], F32, kind="ExternalInput").ap()
    wvT = nc.dram_tensor("wvT", [C, C], F32, kind="ExternalInput").ap()
    identf = nc.dram_tensor("identf", [128, 128], F32, kind="ExternalInput").ap()
    identb = nc.dram_tensor("identb", [128, 128], BF16, kind="ExternalInput").ap()
    gup = nc.dram_tensor("gup", [1408, 256], I32, kind="ExternalInput").ap()
    gup_sc = nc.dram_tensor("gup_sc", [1408, 16], F16, kind="ExternalInput").ap()
    dwp = nc.dram_tensor("dwp", [256, 704], I32, kind="ExternalInput").ap()
    dwp_sc = nc.dram_tensor("dwp_sc", [256, 44], F16, kind="ExternalInput").ap()

    q_o = nc.dram_tensor("q_o", [C, TPC], BF16, kind="ExternalOutput").ap()
    vk_o = nc.dram_tensor("vk_o", [128, 16 * 33], F32, kind="ExternalOutput").ap()
    wgu_o = nc.dram_tensor("wgu_o", [1408, C], BF16, kind="ExternalOutput").ap()
    wdn_o = nc.dram_tensor("wdn_o", [256, 5632], BF16, kind="ExternalOutput").ap()

    qv = q_o.rearrange("(oc p) t -> oc p t", p=128)

    with ExitStack() as ctx:
        def sb(nm, shp, dt, stack=None):
            return (stack or ctx).enter_context(nc.sbuf_tensor(nm, shp, dt))
        p = Prog(nc, ctx)

        idf = sb("idf", [128, 128], F32)
        idb = sb("idb", [128, 128], BF16)
        t_idf = p.dma_hw(p.sp, "idf", idf[:], identf)
        t_idb = p.dma_hw(p.sp, "idb", idb[:], identb)

        xb = sb("xb", [128, 16 * TPC], BF16)

        def xbs(c, tt, w=TT):
            return xb[:, c * TPC + tt * w: c * TPC + (tt + 1) * w]

        with ExitStack() as sctx, \
             nc.psum_tensor("pst0", [128, 128], F32) as pst0, \
             nc.psum_tensor("pst1", [128, 128], F32) as pst1:
            xtm = [sb(f"xtm{i}", [128, C], F32, sctx) for i in range(2)]
            psts = [pst0, pst1]
            t_blk_done = [None, None]
            t_evac = [None, None]
            for blk in range(NBLK):
                buf = xtm[blk % 2]
                tl = p.dma_hw(p.sp, f"xtm{blk % 2}", buf[:], x_in[blk * 128:(blk + 1) * 128, :],
                              deps=[t_blk_done[blk % 2]])
                last_tp = None
                for c in range(16):
                    ps = psts[c % 2]
                    last_tp = p.pe.op(lambda t, ps=ps, buf=buf, c=c:
                                      t.transpose(ps[:], buf[:, c * 128:(c + 1) * 128], idf[:]),
                                      deps=[tl, t_idf, t_evac[c % 2]])
                    t_evac[c % 2] = p.dve.op(
                        lambda v, ps=ps, c=c, blk=blk:
                        v.tensor_copy(xb[:, c * TPC + blk * 128: c * TPC + (blk + 1) * 128], ps[:]),
                        deps=[last_tp])
                t_blk_done[blk % 2] = last_tp
        p.barrier()

        onesb = sb("onesb", [128, 1], BF16)
        onesr = sb("onesr", [1, 128], BF16)
        t_ob = p.dve.op(lambda v: v.memset(onesb[:], 1.0))
        t_or = p.dve.op(lambda v: v.memset(onesr[:], 1.0))
        s23 = ctx.enter_context(ExitStack())
        sqt = [sb(f"sqt{i}", [128, TT], BF16, s23) for i in range(2)]
        sumr = sb("sumr", [1, TPC], F32, s23)
        sqr = sb("sqr", [1, TPC], F32, s23)
        mu_b = sb("mu_b", [1, TPC], BF16, s23)
        rs_b = sb("rs_b", [1, TPC], BF16, s23)
        rowa = sb("rowa", [1, TPC], F32, s23)
        roww = sb("roww", [1, TPC], F32, s23)

        with nc.psum_tensor("pssum", [1, TT], F32) as pssum, \
             nc.psum_tensor("pssq", [1, TT], F32) as pssq:
            t_sum_evac = None
            t_sq_evac = None
            for tt in range(NTT):
                t_mm = None
                for c in range(16):
                    fn = p.pe.op if c == 15 else p.pe.op_quiet
                    t_mm = fn(lambda t, c=c, tt=tt, s=(c == 0), e=(c == 15):
                              t.matmul(pssum[:], onesb[:], xbs(c, tt), start=s, stop=e),
                              deps=[t_ob, t_sum_evac] if c == 0 else ())
                t_sum_evac = p.dve.op(lambda v, tt=tt:
                                      v.tensor_copy(sumr[0:1, tt * TT:(tt + 1) * TT], pssum[:]),
                                      deps=[t_mm])
                t_sqa = [None, None]
                t_mmq = None
                for c in range(16):
                    t_sqa[c % 2] = p.act.op(lambda s, c=c, tt=tt, b=c % 2:
                                            s.activation(sqt[b][:], xbs(c, tt), AF.Square),
                                            deps=[t_mmq])
                    t_mmq = p.pe.op(lambda t, c=c, s=(c == 0), e=(c == 15), b=c % 2:
                                    t.matmul(pssq[:], onesb[:], sqt[b][:], start=s, stop=e),
                                    deps=[t_sqa[c % 2]] + ([t_sq_evac] if c == 0 else []))
                t_sq_evac = p.dve.op(lambda v, tt=tt:
                                     v.tensor_copy(sqr[0:1, tt * TT:(tt + 1) * TT], pssq[:]),
                                     deps=[t_mmq])
        t1 = p.dve.op(lambda v: v.tensor_scalar_mul(rowa[:], sumr[:], 1.0 / C), deps=[t_sum_evac])
        t_mub = p.dve.op(lambda v: v.tensor_copy(mu_b[:], rowa[:]), deps=[t1])
        t2 = p.dve.op(lambda v: v.tensor_scalar_mul(roww[:], sqr[:], 1.0 / C), deps=[t_sq_evac])
        t3 = p.dve.op(lambda v: v.tensor_tensor(sumr[:], rowa[:], rowa[:], ALU.mult), deps=[t1, t2])
        t4 = p.dve.op(lambda v: v.tensor_sub(roww[:], roww[:], sumr[:]), deps=[t3])
        c_eps, tg_eps = p.constf(1e-5)
        t5 = p.act.op(lambda s: s.activation(rowa[:], roww[:], AF.Sqrt, bias=c_eps[0:1, :]),
                      deps=[t4, t_mub, tg_eps])
        t6 = p.dve.op(lambda v: v.reciprocal(roww[:], rowa[:]), deps=[t5])
        t_rsb = p.dve.op(lambda v: v.tensor_copy(rs_b[:], roww[:]), deps=[t6])
        p.barrier()

        mub_s = sb("mub_s", [128, TT], BF16, s23)
        rsb_s = sb("rsb_s", [128, TT], BF16, s23)
        tmp3 = sb("tmp3", [128, TT], BF16, s23)
        with nc.psum_tensor("psbc0", [128, TT], F32) as psbc0, \
             nc.psum_tensor("psbc1", [128, TT], F32) as psbc1:
            te1 = te2 = None
            tb = None
            for tt in range(NTT):
                tm1 = p.pe.op(lambda t, tt=tt: t.matmul(psbc0[:], onesr[:],
                                                        mu_b[0:1, tt * TT:(tt + 1) * TT],
                                                        start=True, stop=True),
                              deps=[t_mub, t_or, te1])
                tm2 = p.pe.op(lambda t, tt=tt: t.matmul(psbc1[:], onesr[:],
                                                        rs_b[0:1, tt * TT:(tt + 1) * TT],
                                                        start=True, stop=True),
                              deps=[t_rsb, te2])
                te1 = p.dve.op(lambda v: v.tensor_copy(mub_s[:], psbc0[:]), deps=[tm1, tb])
                te2 = p.dve.op(lambda v: v.tensor_copy(rsb_s[:], psbc1[:]), deps=[tm2])
                for c in range(16):
                    ta = p.dve.op(lambda v, c=c, tt=tt: v.tensor_sub(tmp3[:], xbs(c, tt), mub_s[:]),
                                  deps=[te1])
                    tb = p.dve.op(lambda v, c=c, tt=tt:
                                  v.tensor_tensor(xbs(c, tt), tmp3[:], rsb_s[:], ALU.mult),
                                  deps=[ta, te2])
        p.barrier()
        t_xn = p.dve.tag()
        s23.close()

        def w_cols(wT, oc, width=128):
            return wT.rearrange("(kc p) o -> p kc o", p=128)[:, :, oc * width:(oc + 1) * width]

        with ExitStack() as s4, \
             nc.psum_tensor("psq0", [128, TT], F32) as psq0, \
             nc.psum_tensor("psq1", [128, TT], F32) as psq1:
            wqb = [sb(f"wqb{i}", [128, 16 * 128], BF16, s4) for i in range(2)]
            qstg = [sb(f"qstg{i}", [128, TT], BF16, s4) for i in range(2)]
            psqs = [psq0, psq1]
            t_wq_free = [None, None]
            t_qdma = [None, None]
            t_evq = [None, None]
            i = 0
            for oc in range(16):
                tw = p.dma_sw(f"wqb{oc % 2}", wqb[oc % 2][:], w_cols(wqT, oc), deps=[t_wq_free[oc % 2]])
                t_mm = None
                for tt in range(NTT):
                    ps = psqs[i % 2]
                    stg = qstg[i % 2]
                    for kc in range(16):
                        fn = p.pe.op if kc == 15 else p.pe.op_quiet
                        t_mm = fn(lambda t, kc=kc, tt=tt, oc=oc, ps=ps, s=(kc == 0), e=(kc == 15):
                                  t.matmul(ps[:], wqb[oc % 2][:, kc * 128:(kc + 1) * 128],
                                           xbs(kc, tt), start=s, stop=e),
                                  deps=([tw, t_evq[i % 2]] if kc == 0 else ()))
                    t_ev = p.act.op(lambda s, ps=ps, stg=stg: s.activation(stg[:], ps[:], AF.Relu),
                                    deps=[t_mm, t_qdma[i % 2]])
                    t_evq[i % 2] = t_ev
                    t_qdma[i % 2] = p.dma_hw(p.sp, f"qst{i % 2}", qv[oc][:, tt * TT:(tt + 1) * TT],
                                             stg[:], deps=[t_ev])
                    i += 1
                t_wq_free[oc % 2] = t_mm
        p.barrier()

        s5 = ctx.enter_context(ExitStack())
        wkb = [sb(f"wkb{i}", [128, 16 * 128], BF16, s5) for i in range(2)]
        wvb = [sb(f"wvb{i}", [128, 16 * 128], BF16, s5) for i in range(2)]
        ktile = [sb(f"ktile{i}", [128, TT], BF16, s5) for i in range(2)]
        vtile = [sb(f"vtile{i}", [128, TT], BF16, s5) for i in range(2)]
        ktm = [sb(f"ktm{i}", [128, 128], BF16, s5) for i in range(2)]
        vtm = [sb(f"vtm{i}", [128, 132], BF16, s5) for i in range(2)]
        vk_sb = sb("vk_sb", [128, 16 * 33], F32, s5)
        t_vob = [p.dve.op(lambda v, b=b: v.memset(
            vtm[b][:].rearrange("p (h e) -> p h e", e=33)[:, :, 32:33], 1.0)) for b in range(2)]

        with nc.psum_tensor("psk", [128, TT], F32) as psk, \
             nc.psum_tensor("psv", [128, TT], F32) as psv, \
             nc.psum_tensor("pstk", [128, 128], BF16) as pstk, \
             nc.psum_tensor("pstv", [128, 128], BF16) as pstv, \
             nc.psum_tensor("psvk0", [128, 33], F32) as psvk0, \
             nc.psum_tensor("psvk1", [128, 33], F32) as psvk1:
            psvks = [psvk0, psvk1]
            t_wk_free = [None, None]
            t_wv_free = [None, None]
            t_kev = None
            t_vev = None
            t_ktm_ev_prev = None
            t_vtm_ev_prev = None
            t_vk_evac = [None, None]
            t_vk_acc = None
            jj = 0
            for cc in range(16):
                twk = p.dma_sw(f"wkb{cc % 2}", wkb[cc % 2][:], w_cols(wkT, cc), deps=[t_wk_free[cc % 2]])
                twv = p.dma_sw(f"wvb{cc % 2}", wvb[cc % 2][:], w_cols(wvT, cc), deps=[t_wv_free[cc % 2]])
                psvk = psvks[cc % 2]
                t_mmk = t_mmv = None
                first_vk = True
                tvk = None
                for tt in range(NTT):
                    kb = jj % 2
                    for kc in range(16):
                        fn = p.pe.op if kc == 15 else p.pe.op_quiet
                        t_mmk = fn(lambda t, kc=kc, tt=tt, cc=cc, s=(kc == 0), e=(kc == 15):
                                   t.matmul(psk[:], wkb[cc % 2][:, kc * 128:(kc + 1) * 128],
                                            xbs(kc, tt), start=s, stop=e),
                                   deps=([twk, t_kev] if kc == 0 else ()))
                    t_kev = p.act.op(lambda s, b=kb: s.activation(ktile[b][:], psk[:], AF.Relu),
                                     deps=[t_mmk])
                    for kc in range(16):
                        fn = p.pe.op if kc == 15 else p.pe.op_quiet
                        t_mmv = fn(lambda t, kc=kc, tt=tt, cc=cc, s=(kc == 0), e=(kc == 15):
                                   t.matmul(psv[:], wvb[cc % 2][:, kc * 128:(kc + 1) * 128],
                                            xbs(kc, tt), start=s, stop=e),
                                   deps=([twv, t_vev] if kc == 0 else ()))
                    t_vev = p.act.op(lambda s, b=kb: s.activation(vtile[b][:], psv[:], AF.Identity),
                                     deps=[t_mmv])
                    for j in range(4):
                        jb = j % 2
                        t_ktp = p.pe.op(lambda t, j=j, b=kb:
                                        t.transpose(pstk[:], ktile[b][:, j * 128:(j + 1) * 128], idb[:]),
                                        deps=[t_kev, t_idb, t_ktm_ev_prev])
                        t_ktm_ev = p.dve.op(lambda v, jb=jb: v.tensor_copy(ktm[jb][:], pstk[:]),
                                            deps=[t_ktp])
                        t_ktm_ev_prev = t_ktm_ev
                        t_vtp = p.pe.op(lambda t, j=j, b=kb:
                                        t.transpose(pstv[:], vtile[b][:, j * 128:(j + 1) * 128], idb[:]),
                                        deps=[t_vev, t_vtm_ev_prev])
                        t_vtm_ev = p.dve.op(
                            lambda v, jb=jb: v.tensor_copy(
                                vtm[jb][:].rearrange("p (h e) -> p h e", e=33)[:, :, 0:32],
                                pstv[:].rearrange("p (h e) -> p h e", e=32)),
                            deps=[t_vtp, t_vob[jb]])
                        t_vtm_ev_prev = t_vtm_ev
                        for h in range(4):
                            fn = p.pe.op if h == 3 else p.pe.op_quiet
                            tvk = fn(lambda t, h=h, jb=jb, psvk=psvk:
                                     t.matmul(psvk[32 * h:32 * h + 32, :],
                                              ktm[jb][:, 32 * h:32 * h + 32],
                                              vtm[jb][:, 33 * h:33 * h + 33],
                                              start=True, stop=True, tile_position=(0, 32 * h)),
                                     deps=([t_ktm_ev, t_vtm_ev] +
                                           ([t_vk_acc] if h == 0 else [])))
                        if first_vk:
                            t_vk_acc = p.dve.op(
                                lambda v, cc=cc, psvk=psvk:
                                v.tensor_copy(vk_sb[:, cc * 33:(cc + 1) * 33], psvk[:]),
                                deps=[tvk])
                        else:
                            t_vk_acc = p.dve.op(
                                lambda v, cc=cc, psvk=psvk:
                                v.scalar_tensor_tensor(vk_sb[:, cc * 33:(cc + 1) * 33],
                                                       psvk[:], 0.0, vk_sb[:, cc * 33:(cc + 1) * 33],
                                                       ALU.add, ALU.add),
                                deps=[tvk])
                        first_vk = False
                    jj += 1
                t_wk_free[cc % 2] = t_mmk
                t_wv_free[cc % 2] = t_mmv
                t_vk_evac[cc % 2] = t_vk_acc
            p.dma_hw(p.sp, "vko", vk_o, vk_sb[:], deps=[t_vk_evac[0], t_vk_evac[1]])
        p.barrier()
        s5.close()

        s6 = ctx.enter_context(ExitStack())
        wI = sb("wI", [128, 704], I32, s6)
        vj_t = sb("vj_t", [128, 704], I32, s6)
        sg_t = sb("sg_t", [128, 704], I32, s6)
        ub_mag = sb("ub_mag", [128, 5632], U16, s6)
        ub_sgn = sb("ub_sgn", [128, 5632], U16, s6)
        tq1 = sb("tq1", [128, 5632], BF16, s6)
        gq = sb("gq", [128, 5632], BF16, s6)
        fin = sb("fin", [128, 5632], U16, s6)
        sclh = sb("sclh", [128, 44], F16, s6)
        scl32 = sb("scl32", [128, 44], F32, s6)
        t_chunk_done = None
        for packed, scales, out_d, nch, W, G in [
            (gup, gup_sc, wgu_o, 11, 256, 16),
            (dwp, dwp_sc, wdn_o, 2, 704, 44),
        ]:
            E = W * 8
            for ch in range(nch):
                tw = p.dma_hw(p.sp, "wI", wI[:, 0:W], packed[ch * 128:(ch + 1) * 128, :],
                              deps=[t_chunk_done])
                tsc = p.dma_hw(p.sp, "sclh", sclh[:, 0:G], scales[ch * 128:(ch + 1) * 128, :],
                               deps=[t_chunk_done])
                tsc32 = p.dve.op(lambda v, G=G: v.tensor_copy(scl32[:, 0:G], sclh[:, 0:G]),
                                 deps=[tsc, t_chunk_done])
                t_mag = []
                t_sgn = []
                for j in range(8):
                    sh = 4 * j - 6
                    op0 = ALU.logical_shift_right if sh >= 0 else ALU.logical_shift_left
                    tv = p.dve.op(lambda v, W=W, sh=abs(sh), op0=op0:
                                  v.tensor_scalar(vj_t[:, 0:W], wI[:, 0:W], sh, 0x1C0,
                                                  op0, ALU.bitwise_and),
                                  deps=[tw, t_chunk_done])
                    t_mag.append(p.dve.op(
                        lambda v, j=j, W=W, E=E: v.tensor_scalar(
                            ub_mag[:, 0:E].rearrange("p (w j) -> p w j", j=8)[:, :, j],
                            vj_t[:, 0:W], 0x3F00, None, ALU.add),
                        deps=[tv]))
                    sh2 = 4 * j - 12
                    op2 = ALU.logical_shift_right if sh2 >= 0 else ALU.logical_shift_left
                    ts2 = p.dve.op(
                        lambda v, W=W, sh2=abs(sh2), op2=op2: v.tensor_scalar(
                            sg_t[:, 0:W], wI[:, 0:W], sh2, 0x8000, op2, ALU.bitwise_and),
                        deps=[tw])
                    t_sgn.append(p.dve.op(
                        lambda v, j=j, W=W, E=E: v.tensor_scalar(
                            ub_sgn[:, 0:E].rearrange("p (w j) -> p w j", j=8)[:, :, j],
                            sg_t[:, 0:W], 0, None, ALU.add),
                        deps=[ts2]))
                c_m1, tg_m1 = p.constf(-1.0)
                c_2, tg_2 = p.constf(2.0)
                t_t1 = p.act.op(lambda s, E=E, c_m1=c_m1, c_2=c_2:
                                s.activation(tq1[:, 0:E], ub_mag[:, 0:E].bitcast(BF16),
                                             AF.Identity, bias=c_m1, scale=c_2),
                                deps=t_mag + [t_chunk_done, tg_m1, tg_2])
                t_g = p.dve.op(lambda v, E=E: v.scalar_tensor_tensor(
                    gq[:, 0:E], tq1[:, 0:E], 0.0, ub_mag[:, 0:E].bitcast(BF16), ALU.max, ALU.min),
                    deps=[t_t1] + t_mag)
                t_s = p.dve.op(lambda v, E=E, G=G: v.tensor_tensor(
                    tq1[:, 0:E].rearrange("p (g k) -> p g k", g=G),
                    gq[:, 0:E].rearrange("p (g k) -> p g k", g=G),
                    scl32[:, 0:G].unsqueeze(-1).broadcast_to([128, G, 128]), ALU.mult),
                    deps=[t_g, tsc32])
                t_f = p.dve.op(lambda v, E=E: v.tensor_tensor(
                    fin[:, 0:E], tq1[:, 0:E].bitcast(U16), ub_sgn[:, 0:E], ALU.bitwise_or),
                    deps=[t_s] + t_sgn)
                t_dma = p.dma_hw(p.sp, "fino", out_d[ch * 128:(ch + 1) * 128, :],
                                 fin[:, 0:E].bitcast(BF16), deps=[t_f])
                t_chunk_done = t_dma

        p.finish()
        p.emit()
    return nc


def build_B():
    nc = bass.Bass("TRN2", target_bir_lowering=False)
    x_in = nc.dram_tensor("x_in", [TPC, C], F32, kind="ExternalInput").ap()
    q_in = nc.dram_tensor("q_in", [C, TPC], BF16, kind="ExternalInput").ap()
    vks = [nc.dram_tensor(f"vk{i}", [128, 16 * 33], F32, kind="ExternalInput").ap()
           for i in range(4)]
    wpT = nc.dram_tensor("wpT", [C, C], F32, kind="ExternalInput").ap()
    bproj = nc.dram_tensor("bproj", [C], F32, kind="ExternalInput").ap()
    e4 = nc.dram_tensor("e4", [4, 128], BF16, kind="ExternalInput").ap()
    identf = nc.dram_tensor("identf", [128, 128], F32, kind="ExternalInput").ap()

    x1_o = nc.dram_tensor("x1_o", [C, TPC], F32, kind="ExternalOutput").ap()
    xn2_o = nc.dram_tensor("xn2_o", [C, TPC], BF16, kind="ExternalOutput").ap()

    qvv = q_in.rearrange("(c p) t -> c p t", p=128)
    x1v = x1_o.rearrange("(c p) t -> c p t", p=128)
    xn2v = xn2_o.rearrange("(c p) t -> c p t", p=128)

    with ExitStack() as ctx:
        def sb(nm, shp, dt, stack=None):
            return (stack or ctx).enter_context(nc.sbuf_tensor(nm, shp, dt))
        p = Prog(nc, ctx)

        idf = sb("idf", [128, 128], F32)
        t_idf = p.dma_hw(p.sp, "idf", idf[:], identf)
        xf = sb("xf", [128, 16 * TPC], F32)

        def xfs(c, tt, w=TT):
            return xf[:, c * TPC + tt * w: c * TPC + (tt + 1) * w]

        with ExitStack() as sctx, \
             nc.psum_tensor("pst0", [128, 128], F32) as pst0, \
             nc.psum_tensor("pst1", [128, 128], F32) as pst1:
            xtm = [sb(f"xtm{i}", [128, C], F32, sctx) for i in range(2)]
            psts = [pst0, pst1]
            t_blk_done = [None, None]
            t_evac = [None, None]
            for blk in range(NBLK):
                buf = xtm[blk % 2]
                tl = p.dma_hw(p.sp, f"xtm{blk % 2}", buf[:], x_in[blk * 128:(blk + 1) * 128, :],
                              deps=[t_blk_done[blk % 2]])
                last_tp = None
                for c in range(16):
                    ps = psts[c % 2]
                    last_tp = p.pe.op(lambda t, ps=ps, buf=buf, c=c:
                                      t.transpose(ps[:], buf[:, c * 128:(c + 1) * 128], idf[:]),
                                      deps=[tl, t_idf, t_evac[c % 2]])
                    t_evac[c % 2] = p.act.op(
                        lambda s, ps=ps, c=c, blk=blk:
                        s.copy(xf[:, c * TPC + blk * 128: c * TPC + (blk + 1) * 128], ps[:]),
                        deps=[last_tp])
                t_blk_done[blk % 2] = last_tp
        p.barrier()

        s2 = ctx.enter_context(ExitStack())
        vk_sb = sb("vkB", [128, 16 * 33], F32, s2)
        vk_tmp = sb("vkBt", [128, 16 * 33], F32, s2)
        t_v0 = p.dma_hw(p.sp, "vkl0", vk_sb[:], vks[0])
        tsum = t_v0
        for i in range(1, 4):
            tv = p.dma_hw(p.sp, "vkl1", vk_tmp[:], vks[i], deps=[tsum])
            tsum = p.dve.op(lambda v: v.tensor_add(vk_sb[:], vk_sb[:], vk_tmp[:]),
                            deps=[tv, tsum])
        bd = sb("bd", [128, 16 * 128], BF16, s2)
        ksd = sb("ksd", [128, 16 * 4], BF16, s2)
        t_bd0 = p.dve.op(lambda v: v.memset(bd[:], 0.0), deps=[])
        t_ks0 = p.dve.op(lambda v: v.memset(ksd[:], 0.0), deps=[])
        t_bd = []
        for c in range(16):
            for h in range(4):
                t_bd.append(p.dve.op(
                    lambda v, c=c, h=h: v.tensor_copy(
                        bd[32 * h:32 * h + 32, c * 128 + 32 * h: c * 128 + 32 * h + 32],
                        vk_sb[32 * h:32 * h + 32, c * 33: c * 33 + 32]),
                    deps=[tsum, t_bd0]))
                t_bd.append(p.dve.op(
                    lambda v, c=c, h=h: v.tensor_copy(
                        ksd[32 * h:32 * h + 32, c * 4 + h: c * 4 + h + 1],
                        vk_sb[32 * h:32 * h + 32, c * 33 + 32: c * 33 + 33]),
                    deps=[tsum, t_ks0]))
        t_bdall = p.dve.tag()
        e4b = sb("e4b", [4, 128], BF16, s2)
        t_e4 = p.dma_hw(p.sp, "e4", e4b[:], e4)
        bias_sb = sb("bias_sb", [128, 16], F32, s2)
        t_bias = None
        for oc in range(16):
            t_bias = p.dma_hw(p.sp, "bias", bias_sb[:, oc:oc + 1],
                              bproj[oc * 128:(oc + 1) * 128].unsqueeze(-1), deps=[t_bias])
        p.barrier()

        s3 = ctx.enter_context(ExitStack())
        qt = [sb(f"qt{i}", [128, TT], BF16, s3) for i in range(2)]
        att = sb("att", [128, 16 * TT], BF16, s3)
        denr = sb("denr", [4, TT], F32, s3)
        denb = sb("denb", [4, TT], BF16, s3)
        rbb = sb("rbb", [128, TT], BF16, s3)
        wpb = [sb(f"wpb{i}", [128, 16 * 128], BF16, s3) for i in range(2)]

        def w_cols(wT, oc, width=128):
            return wT.rearrange("(kc p) o -> p kc o", p=128)[:, :, oc * width:(oc + 1) * width]

        with nc.psum_tensor("psaf", [128, TT], F32) as psaf, \
             nc.psum_tensor("psd", [4, TT], F32) as psd, \
             nc.psum_tensor("psbc", [128, TT], F32) as psbc, \
             nc.psum_tensor("psp0", [128, TT], F32) as psp0, \
             nc.psum_tensor("psp1", [128, TT], F32) as psp1:
            psps = [psp0, psp1]
            t_qload = [None, None]
            t_att_mult = None
            t_den_ev = None
            t_rb_ev = None
            t_x1w = {}
            t_wfree = [None, None]
            for tt in range(NTT):
                for c in range(16):
                    tq = p.dma_hw(p.sp, f"qt{c % 2}", qt[c % 2][:],
                                  qvv[c][:, tt * TT:(tt + 1) * TT], deps=[t_qload[c % 2]])
                    t_num = p.pe.op(lambda t, c=c: t.matmul(psaf[:], bd[:, c * 128:(c + 1) * 128],
                                                            qt[c % 2][:], start=True, stop=True),
                                    deps=[tq, t_bdall, t_att_mult])
                    t_den = p.pe.op(lambda t, c=c: t.matmul(psd[:], ksd[:, c * 4:(c + 1) * 4],
                                                            qt[c % 2][:], start=True, stop=True),
                                    deps=[tq, t_den_ev])
                    t_qload[c % 2] = t_den
                    td1 = p.dve.op(lambda v: v.tensor_scalar(denr[:], psd[:], 1e-15, None, ALU.add),
                                   deps=[t_den])
                    t_den_ev = td1
                    td2 = p.dve.op(lambda v: v.reciprocal(denr[:], denr[:]), deps=[td1])
                    td3 = p.dve.op(lambda v: v.tensor_copy(denb[:], denr[:]), deps=[td2])
                    t_bc = p.pe.op(lambda t: t.matmul(psbc[:], e4b[:], denb[:], start=True, stop=True),
                                   deps=[td3, t_e4, t_rb_ev])
                    t_rb = p.act.op(lambda s: s.copy(rbb[:], psbc[:]), deps=[t_bc])
                    t_rb_ev = t_rb
                    t_att_mult = p.dve.op(
                        lambda v, c=c: v.tensor_tensor(att[:, c * TT:(c + 1) * TT], psaf[:],
                                                       rbb[:], ALU.mult),
                        deps=[t_rb, t_num])
                t_att_done = t_att_mult
                for oc in range(16):
                    tw = p.dma_sw(f"wpb{oc % 2}", wpb[oc % 2][:], w_cols(wpT, oc),
                                  deps=[t_wfree[oc % 2]])
                    ps = psps[oc % 2]
                    t_mm = None
                    for kc in range(16):
                        fn = p.pe.op if kc == 15 else p.pe.op_quiet
                        t_mm = fn(lambda t, kc=kc, oc=oc, ps=ps, s=(kc == 0), e=(kc == 15):
                                  t.matmul(ps[:], wpb[oc % 2][:, kc * 128:(kc + 1) * 128],
                                           att[:, kc * TT:(kc + 1) * TT], start=s, stop=e),
                                  deps=([tw, t_att_done, t_x1w.get((oc % 2, tt))] if kc == 0 else ()))
                    t_wfree[oc % 2] = t_mm
                    t_x1 = p.dve.op(
                        lambda v, oc=oc, tt=tt, ps=ps: v.scalar_tensor_tensor(
                            xfs(oc, tt), ps[:], bias_sb[:, oc:oc + 1], xfs(oc, tt),
                            ALU.add, ALU.add),
                        deps=[t_mm, t_bias])
                    t_x1w[(oc % 2, tt)] = t_x1
                    p.dma_hw(p.sp, f"x1st{oc % 2}", x1v[oc][:, tt * TT:(tt + 1) * TT],
                             xfs(oc, tt), deps=[t_x1])
        p.barrier()
        s3.close()
        s2.close()

        s4 = ctx.enter_context(ExitStack())
        onesb = sb("onesbB", [128, 1], BF16, s4)
        onesr = sb("onesrB", [1, 128], BF16, s4)
        t_ob = p.dve.op(lambda v: v.memset(onesb[:], 1.0))
        t_or = p.dve.op(lambda v: v.memset(onesr[:], 1.0))
        casts = [sb(f"castB{i}", [128, TT], BF16, s4) for i in range(2)]
        sqt = [sb(f"sqtB{i}", [128, TT], BF16, s4) for i in range(2)]
        sumr = sb("sumrB", [1, TPC], F32, s4)
        sqr = sb("sqrB", [1, TPC], F32, s4)
        mu_b = sb("mu_bB", [1, TPC], BF16, s4)
        rs_b = sb("rs_bB", [1, TPC], BF16, s4)
        rowa = sb("rowaB", [1, TPC], F32, s4)
        roww = sb("rowwB", [1, TPC], F32, s4)
        with nc.psum_tensor("pssumB", [1, TT], F32) as pssum, \
             nc.psum_tensor("pssqB", [1, TT], F32) as pssq:
            t_sum_evac = None
            t_sq_evac = None
            for tt in range(NTT):
                t_mm = None
                t_mmq = None
                for c in range(16):
                    tcast = p.dve.op(lambda v, c=c, tt=tt, b=c % 2:
                                     v.tensor_copy(casts[b][:], xfs(c, tt)),
                                     deps=[t_mm])
                    fn = p.pe.op
                    t_mm = fn(lambda t, c=c, s=(c == 0), e=(c == 15), b=c % 2:
                              t.matmul(pssum[:], onesb[:], casts[b][:], start=s, stop=e),
                              deps=[tcast, t_ob] + ([t_sum_evac] if c == 0 else []))
                    t_sq = p.act.op(lambda s, c=c, b=c % 2:
                                    s.activation(sqt[b][:], casts[b][:], AF.Square),
                                    deps=[tcast, t_mmq])
                    t_mmq = fn(lambda t, c=c, s=(c == 0), e=(c == 15), b=c % 2:
                               t.matmul(pssq[:], onesb[:], sqt[b][:], start=s, stop=e),
                               deps=[t_sq] + ([t_sq_evac] if c == 0 else []))
                t_sum_evac = p.dve.op(lambda v, tt=tt:
                                      v.tensor_copy(sumr[0:1, tt * TT:(tt + 1) * TT], pssum[:]),
                                      deps=[t_mm])
                t_sq_evac = p.dve.op(lambda v, tt=tt:
                                     v.tensor_copy(sqr[0:1, tt * TT:(tt + 1) * TT], pssq[:]),
                                     deps=[t_mmq])
        t1 = p.dve.op(lambda v: v.tensor_scalar_mul(rowa[:], sumr[:], 1.0 / C), deps=[t_sum_evac])
        t_mub = p.dve.op(lambda v: v.tensor_copy(mu_b[:], rowa[:]), deps=[t1])
        t2 = p.dve.op(lambda v: v.tensor_scalar_mul(roww[:], sqr[:], 1.0 / C), deps=[t_sq_evac])
        t3 = p.dve.op(lambda v: v.tensor_tensor(sumr[:], rowa[:], rowa[:], ALU.mult), deps=[t1, t2])
        t4 = p.dve.op(lambda v: v.tensor_sub(roww[:], roww[:], sumr[:]), deps=[t3])
        c_eps, tg_eps = p.constf(1e-5)
        t5 = p.act.op(lambda s: s.activation(rowa[:], roww[:], AF.Sqrt, bias=c_eps[0:1, :]),
                      deps=[t4, t_mub, tg_eps])
        t6 = p.dve.op(lambda v: v.reciprocal(roww[:], rowa[:]), deps=[t5])
        t_rsb = p.dve.op(lambda v: v.tensor_copy(rs_b[:], roww[:]), deps=[t6])
        p.barrier()

        mub_s = sb("mub_sB", [128, TT], BF16, s4)
        rsb_s = sb("rsb_sB", [128, TT], BF16, s4)
        xnst = [sb(f"xnstB{i}", [128, TT], BF16, s4) for i in range(2)]
        with nc.psum_tensor("psbcB0", [128, TT], F32) as psb0, \
             nc.psum_tensor("psbcB1", [128, TT], F32) as psb1:
            te1 = te2 = None
            tb_last = None
            t_dma = [None, None]
            for tt in range(NTT):
                tm1 = p.pe.op(lambda t, tt=tt: t.matmul(psb0[:], onesr[:],
                                                        mu_b[0:1, tt * TT:(tt + 1) * TT],
                                                        start=True, stop=True),
                              deps=[t_mub, t_or, te1])
                tm2 = p.pe.op(lambda t, tt=tt: t.matmul(psb1[:], onesr[:],
                                                        rs_b[0:1, tt * TT:(tt + 1) * TT],
                                                        start=True, stop=True),
                              deps=[t_rsb, te2])
                te1 = p.dve.op(lambda v: v.tensor_copy(mub_s[:], psb0[:]), deps=[tm1, tb_last])
                te2 = p.dve.op(lambda v: v.tensor_copy(rsb_s[:], psb1[:]), deps=[tm2])
                for c in range(16):
                    b = c % 2
                    ta = p.dve.op(lambda v, c=c, tt=tt, b=b:
                                  v.tensor_sub(xnst[b][:], xfs(c, tt), mub_s[:]),
                                  deps=[te1, t_dma[b]])
                    tb_last = p.dve.op(lambda v, c=c, tt=tt, b=b:
                                       v.tensor_tensor(xnst[b][:], xnst[b][:], rsb_s[:], ALU.mult),
                                       deps=[ta, te2])
                    t_dma[b] = p.dma_hw(p.sp, f"xnst{b}", xn2v[c][:, tt * TT:(tt + 1) * TT],
                                        xnst[b][:], deps=[tb_last])
        p.finish()
        p.emit()
    return nc


def make_B_inmaps(inputs, a_outs):
    x = np.asarray(inputs["L0_attn_x"])
    w_proj = np.asarray(inputs["L0_attn_w_proj"])
    b_proj = np.asarray(inputs["L0_attn_b_proj"])
    wpT = np.ascontiguousarray(w_proj.T)
    identf = np.eye(128, dtype=np.float32)
    e4 = np.zeros((4, 128), dtype=np.float32)
    for h in range(4):
        e4[h, 32 * h:32 * h + 32] = 1.0
    e4 = e4.astype(BF)
    xs = shard_x(x)
    maps = []
    for core in range(NCORE):
        b = core // 4
        peers = [4 * b + i for i in range(4)]
        m = {
            "x_in": xs[core],
            "q_in": np.ascontiguousarray(a_outs[core]["q_o"]),
            "wpT": wpT, "bproj": b_proj, "e4": e4, "identf": identf,
        }
        for i, pc in enumerate(peers):
            m[f"vk{i}"] = np.ascontiguousarray(a_outs[pc]["vk_o"])
        maps.append(m)
    return maps


def build_C():
    IC = 11264
    HC = 5632
    NH = 44
    nc = bass.Bass("TRN2", target_bir_lowering=False)
    x1_in = nc.dram_tensor("x1_in", [C, TPC], F32, kind="ExternalInput").ap()
    xn2_in = nc.dram_tensor("xn2_in", [C, TPC], BF16, kind="ExternalInput").ap()
    wguT = nc.dram_tensor("wguT", [C, IC], BF16, kind="ExternalInput").ap()
    wdnT = nc.dram_tensor("wdnT", [HC, C], BF16, kind="ExternalInput").ap()
    x2_o = nc.dram_tensor("x2_o", [C, TPC], F32, kind="ExternalOutput").ap()
    mu3_o = nc.dram_tensor("mu3_o", [1, TPC], F32, kind="ExternalOutput").ap()
    rs3_o = nc.dram_tensor("rs3_o", [1, TPC], F32, kind="ExternalOutput").ap()

    x1v = x1_in.rearrange("(c p) t -> c p t", p=128)
    x2v = x2_o.rearrange("(c p) t -> c p t", p=128)
    xnv = xn2_in.rearrange("(c p) t -> c p t", p=128)

    with ExitStack() as ctx:
        def sb(nm, shp, dt, stack=None):
            return (stack or ctx).enter_context(nc.sbuf_tensor(nm, shp, dt))
        p = Prog(nc, ctx)

        xn = sb("xn", [128, 16 * TPC], BF16)
        t_xn = []
        for c in range(16):
            t_xn.append(p.dma_hw(p.sp, f"xnld{c % 4}", xn[:, c * TPC:(c + 1) * TPC], xnv[c],
                                 deps=[t_xn[c - 4] if c >= 4 else None]))
        t_xn_all = t_xn

        def xns(kc, tt, w=TT):
            return xn[:, kc * TPC + tt * w: kc * TPC + (tt + 1) * w]

        onesb = sb("onesbC", [128, 1], BF16)
        onesr = sb("onesrC", [1, 128], BF16)
        t_ob = p.dve.op(lambda v: v.memset(onesb[:], 1.0))
        t_or = p.dve.op(lambda v: v.memset(onesr[:], 1.0))

        h_sb = sb("h_sb", [128, NH * TT], BF16)
        y_sb = sb("y_sb", [128, 16 * TT], BF16)
        wgu_b = [sb(f"wgub{i}", [128, 16 * 128], BF16) for i in range(2)]
        wdn_b = [sb("wdnb0", [128, NH * 128], BF16)] * 2
        sg_t = [sb(f"sgC{i}", [128, TT], BF16) for i in range(2)]
        sqt = [sb(f"sqtC{i}", [128, TT], BF16) for i in range(2)]
        x1t = [sb(f"x1tC{i}", [128, TT], F32) for i in range(2)]
        x2t = [sb(f"x2tC{i}", [128, TT], F32) for i in range(2)]
        x2b = [sb(f"x2bC{i}", [128, TT], BF16) for i in range(2)]
        yt = sb("ytC", [128, TT], BF16)
        rr_s = sb("rrC", [128, TT], BF16)
        rowq = sb("rowqC", [1, TT], F32)
        rowqb = sb("rowqbC", [1, TT], BF16)
        sum3 = sb("sum3C", [1, TPC], F32)
        sq3 = sb("sq3C", [1, TPC], F32)
        rowa = sb("rowaC", [1, TPC], F32)
        roww = sb("rowwC", [1, TPC], F32)

        def w_cols(wT, oc, nkc, width=128):
            return wT.rearrange("(kc p) o -> p kc o", p=128)[:, :, oc * width:(oc + 1) * width]

        c_eps6, tg_e6 = p.constf(1e-6)
        c_eps5, tg_e5 = p.constf(1e-5)

        with nc.psum_tensor("psg", [128, TT], F32) as psg, \
             nc.psum_tensor("psu", [128, TT], F32) as psu, \
             nc.psum_tensor("psd0", [128, TT], F32) as psd0, \
             nc.psum_tensor("psd1", [128, TT], F32) as psd1, \
             nc.psum_tensor("psr", [1, TT], F32) as psr, \
             nc.psum_tensor("pss", [1, TT], F32) as pss, \
             nc.psum_tensor("psb", [128, TT], F32) as psb:
            psds = [psd0, psd1]
            t_wg_free = [None, None]
            t_wd_free = [None, None]
            t_hw = None
            t_sg = None
            t_yev = [None, None]
            t_sqv = None
            t_rr = None
            t_x2w = [None, None]
            t_x2c = [None, None]
            t_x1l = [None, None]
            t_sum_ev = None
            t_sq_ev = None
            t_mmy = None
            t_x2read = [[], []]
            for tt in range(NTT):
                for i in range(NH):
                    bg = i % 2
                    twg = p.dma_hw(p.sp, f"wg{bg}", wgu_b[bg][:], w_cols(wguT, i, 16),
                                   deps=[t_wg_free[bg]])
                    t_mmg = None
                    for kc in range(16):
                        fn = p.pe.op if kc == 15 else p.pe.op_quiet
                        t_mmg = fn(lambda t, kc=kc, tt=tt, bg=bg, s=(kc == 0), e=(kc == 15):
                                   t.matmul(psg[:], wgu_b[bg][:, kc * 128:(kc + 1) * 128],
                                            xns(kc, tt), start=s, stop=e),
                                   deps=([twg, t_sg] + t_xn_all if kc == 0 else ()))
                    t_sg = p.act.op(lambda s, bg=bg: s.activation(sg_t[bg][:], psg[:], AF.Silu),
                                    deps=[t_mmg])
                    twu = p.dma_sw(f"wu{bg}", wgu_b[bg][:], w_cols(wguT, NH + i, 16),
                                   deps=[t_mmg])
                    t_mmu = None
                    for kc in range(16):
                        fn = p.pe.op if kc == 15 else p.pe.op_quiet
                        t_mmu = fn(lambda t, kc=kc, tt=tt, bg=bg, s=(kc == 0), e=(kc == 15):
                                   t.matmul(psu[:], wgu_b[bg][:, kc * 128:(kc + 1) * 128],
                                            xns(kc, tt), start=s, stop=e),
                                   deps=([twu, t_hw] if kc == 0 else ()))
                    t_hw = p.dve.op(lambda v, i=i, bg=bg:
                                    v.tensor_tensor(h_sb[:, i * TT:(i + 1) * TT], psu[:],
                                                    sg_t[bg][:], ALU.mult),
                                    deps=[t_mmu, t_sg])
                    t_wg_free[bg] = t_mmu
                t_h_done = t_hw
                t_mmy = None
                for oc in range(16):
                    bo = oc % 2
                    twd = p.dma_hw(p.sp, "wd0", wdn_b[0][:], w_cols(wdnT, oc, NH),
                                   deps=[t_wd_free[0], t_wd_free[1]])
                    ps = psds[bo]
                    t_mmd = None
                    for kc in range(NH):
                        fn = p.pe.op if kc == NH - 1 else p.pe.op_quiet
                        t_mmd = fn(lambda t, kc=kc, oc=oc, ps=ps, s=(kc == 0), e=(kc == NH - 1):
                                   t.matmul(ps[:], wdn_b[0][:, kc * 128:(kc + 1) * 128],
                                            h_sb[:, kc * TT:(kc + 1) * TT], start=s, stop=e),
                                   deps=([twd, t_h_done, t_yev[bo]] if kc == 0 else ()))
                    t_wd_free[bo] = t_mmd
                    t_yev[bo] = p.dve.op(lambda v, oc=oc, ps=ps:
                                         v.tensor_copy(y_sb[:, oc * TT:(oc + 1) * TT], ps[:]),
                                         deps=[t_mmd])
                    t_sqv = p.act.op(lambda s, oc=oc: s.activation(yt[:], y_sb[:, oc * TT:(oc + 1) * TT],
                                                                   AF.Square),
                                     deps=[t_yev[oc % 2], t_mmy])
                    t_mmy = p.pe.op(lambda t, oc=oc, s=(oc == 0), e=(oc == 15):
                                    t.matmul(psr[:], onesb[:], yt[:], start=s, stop=e),
                                    deps=[t_sqv, t_ob])
                tr1 = p.dve.op(lambda v: v.tensor_scalar_mul(rowq[:], psr[:], 1.0 / C), deps=[t_mmy])
                tr2 = p.act.op(lambda s: s.activation(rowq[:], rowq[:], AF.Sqrt, bias=c_eps6[0:1, :]),
                               deps=[tr1, tg_e6])
                tr3 = p.dve.op(lambda v: v.reciprocal(rowq[:], rowq[:]), deps=[tr2])
                tr4 = p.dve.op(lambda v: v.tensor_copy(rowqb[:], rowq[:]), deps=[tr3])
                t_bcm = p.pe.op(lambda t: t.matmul(psb[:], onesr[:], rowqb[:], start=True, stop=True),
                                deps=[tr4, t_or, t_rr])
                t_rr = p.act.op(lambda s: s.copy(rr_s[:], psb[:]), deps=[t_bcm])
                t_mm3 = None
                t_mm3q = None
                for oc in range(16):
                    b2 = oc % 2
                    tx1 = p.dma_hw(p.sp, f"x1l{b2}", x1t[b2][:], x1v[oc][:, tt * TT:(tt + 1) * TT],
                                   deps=[t_x1l[b2]])
                    tyr = p.dve.op(lambda v, oc=oc, b2=b2: v.tensor_tensor(
                        x2b[b2][:], y_sb[:, oc * TT:(oc + 1) * TT], rr_s[:], ALU.mult),
                        deps=[t_rr, t_x2c[b2]] + t_x2read[b2])
                    tx2 = p.dve.op(lambda v, oc=oc, b2=b2: v.tensor_tensor(
                        x2t[b2][:], x1t[b2][:], x2b[b2][:], ALU.add),
                        deps=[tyr, tx1, t_x2w[b2]])
                    t_x1l[b2] = tx2
                    t_x2w[b2] = p.dma_hw(p.sp, f"x2st{b2}", x2v[oc][:, tt * TT:(tt + 1) * TT],
                                         x2t[b2][:], deps=[tx2])
                    t_x2c[b2] = p.dve.op(lambda v, b2=b2: v.tensor_copy(x2b[b2][:], x2t[b2][:]),
                                         deps=[tx2])
                    t_mm3 = p.pe.op(lambda t, oc=oc, b2=b2, s=(oc == 0), e=(oc == 15):
                                    t.matmul(pss[:], onesb[:], x2b[b2][:], start=s, stop=e),
                                    deps=[t_x2c[b2]] + ([t_sum_ev] if oc == 0 else []))
                    t_sq3 = p.act.op(lambda s, b2=b2: s.activation(sqt[b2][:], x2b[b2][:], AF.Square),
                                     deps=[t_x2c[b2], t_mm3q])
                    t_mm3q = p.pe.op(lambda t, oc=oc, b2=b2, s=(oc == 0), e=(oc == 15):
                                     t.matmul(psr[:], onesb[:], sqt[b2][:], start=s, stop=e),
                                     deps=[t_sq3] + ([t_sq_ev, t_mmy, tr1] if oc == 0 else []))
                    t_x2read[b2] = [t_mm3, t_sq3]
                t_sum_ev = p.dve.op(lambda v, tt=tt: v.tensor_copy(sum3[0:1, tt * TT:(tt + 1) * TT],
                                                                   pss[:]), deps=[t_mm3])
                t_sq_ev = p.dve.op(lambda v, tt=tt: v.tensor_copy(sq3[0:1, tt * TT:(tt + 1) * TT],
                                                                  psr[:]), deps=[t_mm3q])
        t1 = p.dve.op(lambda v: v.tensor_scalar_mul(rowa[:], sum3[:], 1.0 / C), deps=[t_sum_ev])
        t2 = p.dve.op(lambda v: v.tensor_scalar_mul(roww[:], sq3[:], 1.0 / C), deps=[t_sq_ev])
        t3 = p.dve.op(lambda v: v.tensor_tensor(sum3[:], rowa[:], rowa[:], ALU.mult), deps=[t1, t2])
        t4 = p.dve.op(lambda v: v.tensor_sub(roww[:], roww[:], sum3[:]), deps=[t3])
        t5 = p.act.op(lambda s: s.activation(roww[:], roww[:], AF.Sqrt, bias=c_eps5[0:1, :]),
                      deps=[t4, tg_e5])
        t6 = p.dve.op(lambda v: v.reciprocal(roww[:], roww[:]), deps=[t5])
        p.dma_hw(p.sp, "mu3", mu3_o, rowa[:], deps=[t1])
        p.dma_hw(p.sp, "rs3", rs3_o, roww[:], deps=[t6])
        p.finish()
        p.emit()
    return nc


def make_C_inmaps(inputs, a_outs, b_outs):
    gu = np.concatenate([np.asarray(a_outs[i]["wgu_o"]).view(np.uint16) for i in range(NCORE)], axis=0)
    dn = np.concatenate([np.asarray(a_outs[i]["wdn_o"]).view(np.uint16) for i in range(NCORE)], axis=0)
    wguT = np.ascontiguousarray(gu.T)
    wdnT = np.ascontiguousarray(dn.T)
    maps = []
    for core in range(NCORE):
        maps.append({
            "x1_in": np.ascontiguousarray(b_outs[core]["x1_o"]),
            "xn2_in": np.ascontiguousarray(b_outs[core]["xn2_o"]),
            "wguT": wguT.view(BF),
            "wdnT": wdnT.view(BF),
        })
    return maps


def build_D():
    QKVD = 8192
    ZD = 4096
    nc = bass.Bass("TRN2", target_bir_lowering=False)
    x2_in = nc.dram_tensor("x2_in", [C, TPC], F32, kind="ExternalInput").ap()
    mu3_in = nc.dram_tensor("mu3_in", [1, TPC], F32, kind="ExternalInput").ap()
    rs3_in = nc.dram_tensor("rs3_in", [1, TPC], F32, kind="ExternalInput").ap()
    wqkvT = nc.dram_tensor("wqkvT", [C, QKVD], F32, kind="ExternalInput").ap()
    wzT = nc.dram_tensor("wzT", [C, ZD], F32, kind="ExternalInput").ap()
    wbaT = nc.dram_tensor("wbaT", [C, 64], F32, kind="ExternalInput").ap()
    qkv_o = nc.dram_tensor("qkv_o", [QKVD, TPC], BF16, kind="ExternalOutput").ap()
    z_o = nc.dram_tensor("z_o", [ZD, TPC], BF16, kind="ExternalOutput").ap()
    ba_o = nc.dram_tensor("ba_o", [64, TPC], F32, kind="ExternalOutput").ap()

    x2v = x2_in.rearrange("(c p) t -> c p t", p=128)
    qkvv = qkv_o.rearrange("(c p) t -> c p t", p=128)
    zv = z_o.rearrange("(c p) t -> c p t", p=128)

    with ExitStack() as ctx:
        def sb(nm, shp, dt, stack=None):
            return (stack or ctx).enter_context(nc.sbuf_tensor(nm, shp, dt))
        p = Prog(nc, ctx)
        xn = sb("xnD", [128, 16 * TPC], BF16)

        def xns(kc, tt, w=TT):
            return xn[:, kc * TPC + tt * w: kc * TPC + (tt + 1) * w]

        onesr = sb("onesrD", [1, 128], BF16)
        t_or = p.dve.op(lambda v: v.memset(onesr[:], 1.0))
        murow = sb("murowD", [1, TPC], BF16)
        rsrow = sb("rsrowD", [1, TPC], BF16)
        rowt = sb("rowtD", [1, TPC], F32)
        t_m = p.dma_hw(p.sp, "mu3", rowt[:], mu3_in)
        t_m2 = p.dve.op(lambda v: v.tensor_copy(murow[:], rowt[:]), deps=[t_m])
        t_r = p.dma_hw(p.sp, "rs3", rowt[:], rs3_in, deps=[t_m2])
        t_r2 = p.dve.op(lambda v: v.tensor_copy(rsrow[:], rowt[:]), deps=[t_r])

        s1 = ctx.enter_context(ExitStack())
        mub_s = sb("mubD", [128, TT], BF16, s1)
        rsb_s = sb("rsbD", [128, TT], BF16, s1)
        x2t = [sb(f"x2tD{i}", [128, TT], F32, s1) for i in range(2)]
        with nc.psum_tensor("psb0D", [128, TT], F32) as psb0, \
             nc.psum_tensor("psb1D", [128, TT], F32) as psb1:
            te1 = te2 = None
            tb_last = None
            t_xl = [None, None]
            for tt in range(NTT):
                tm1 = p.pe.op(lambda t, tt=tt: t.matmul(psb0[:], onesr[:],
                                                        murow[0:1, tt * TT:(tt + 1) * TT],
                                                        start=True, stop=True),
                              deps=[t_m2, t_or, te1])
                tm2 = p.pe.op(lambda t, tt=tt: t.matmul(psb1[:], onesr[:],
                                                        rsrow[0:1, tt * TT:(tt + 1) * TT],
                                                        start=True, stop=True),
                              deps=[t_r2, te2])
                te1 = p.dve.op(lambda v: v.tensor_copy(mub_s[:], psb0[:]), deps=[tm1, tb_last])
                te2 = p.dve.op(lambda v: v.tensor_copy(rsb_s[:], psb1[:]), deps=[tm2])
                for c in range(16):
                    b = c % 2
                    tx = p.dma_hw(p.sp, f"x2l{b}", x2t[b][:], x2v[c][:, tt * TT:(tt + 1) * TT],
                                  deps=[t_xl[b]])
                    ta = p.dve.op(lambda v, c=c, tt=tt, b=b:
                                  v.tensor_sub(xns(c, tt), x2t[b][:], mub_s[:]),
                                  deps=[te1, tx])
                    t_xl[b] = ta
                    tb_last = p.dve.op(lambda v, c=c, tt=tt:
                                       v.tensor_tensor(xns(c, tt), xns(c, tt), rsb_s[:], ALU.mult),
                                       deps=[ta, te2])
        p.barrier()
        s1.close()
        t_xn = p.dve.tag()

        def w_cols(wT, oc, width=128):
            return wT.rearrange("(kc p) o -> p kc o", p=128)[:, :, oc * width:(oc + 1) * width]

        s2 = ctx.enter_context(ExitStack())
        wb_ = [sb(f"wbD{i}", [128, 16 * 128], BF16, s2) for i in range(2)]
        stg = [sb(f"stgD{i}", [128, TT], BF16, s2) for i in range(2)]
        stgf = [sb(f"stgfD{i}", [64, TT], F32, s2) for i in range(2)]
        with nc.psum_tensor("psqD0", [128, TT], F32) as psq0, \
             nc.psum_tensor("psqD1", [128, TT], F32) as psq1:
            psqs = [psq0, psq1]
            t_w_free = [None, None]
            t_dma = [None, None]
            t_ev = [None, None]
            i = 0
            for wsrc, nov, outv in [(wqkvT, 64, qkvv), (wzT, 32, zv)]:
                for oc in range(nov):
                    tw = p.dma_sw(f"wD{oc % 2}", wb_[oc % 2][:], w_cols(wsrc, oc),
                                  deps=[t_w_free[oc % 2]])
                    t_mm = None
                    for tt in range(NTT):
                        ps = psqs[i % 2]
                        st_ = stg[i % 2]
                        for kc in range(16):
                            fn = p.pe.op if kc == 15 else p.pe.op_quiet
                            t_mm = fn(lambda t, kc=kc, tt=tt, oc=oc, ps=ps, s=(kc == 0), e=(kc == 15):
                                      t.matmul(ps[:], wb_[oc % 2][:, kc * 128:(kc + 1) * 128],
                                               xns(kc, tt), start=s, stop=e),
                                      deps=([tw, t_ev[i % 2]] if kc == 0 else ()))
                        t_e = p.act.op(lambda s, ps=ps, st_=st_: s.copy(st_[:], ps[:]),
                                       deps=[t_mm, t_dma[i % 2]])
                        t_ev[i % 2] = t_e
                        t_dma[i % 2] = p.dma_hw(p.sp, f"stD{i % 2}",
                                                outv[oc][:, tt * TT:(tt + 1) * TT], st_[:],
                                                deps=[t_e])
                        i += 1
                    t_w_free[oc % 2] = t_mm
            twba = p.dma_sw("wba", wb_[0][:, 0:16 * 64],
                            wbaT.rearrange("(kc p) o -> p kc o", p=128),
                            deps=[t_w_free[0]])
            for tt in range(NTT):
                ps = psqs[tt % 2]
                sf = stgf[tt % 2]
                t_mm = None
                for kc in range(16):
                    fn = p.pe.op if kc == 15 else p.pe.op_quiet
                    t_mm = fn(lambda t, kc=kc, tt=tt, ps=ps, s=(kc == 0), e=(kc == 15):
                              t.matmul(ps[0:64, :], wb_[0][:, kc * 64:(kc + 1) * 64],
                                       xns(kc, tt), start=s, stop=e),
                              deps=([twba, t_ev[tt % 2]] if kc == 0 else ()))
                t_e = p.dve.op(lambda v, ps=ps, sf=sf: v.tensor_copy(sf[:], ps[0:64, :]),
                               deps=[t_mm, t_dma[tt % 2]])
                t_ev[tt % 2] = t_e
                t_dma[tt % 2] = p.dma_hw(p.sp, f"baD{tt % 2}", ba_o[:, tt * TT:(tt + 1) * TT],
                                         sf[:], deps=[t_e])
        p.finish()
        p.emit()
    return nc


def make_D_inmaps(inputs, c_outs):
    wqkvT = np.ascontiguousarray(np.asarray(inputs["L1_attn_W_qkv"]).T)
    wzT = np.ascontiguousarray(np.asarray(inputs["L1_attn_W_z"]).T)
    wb = np.asarray(inputs["L1_attn_W_b"])
    wa = np.asarray(inputs["L1_attn_W_a"])
    wbaT = np.ascontiguousarray(np.concatenate([wb, wa], axis=0).T)
    maps = []
    for core in range(NCORE):
        maps.append({
            "x2_in": np.ascontiguousarray(c_outs[core]["x2_o"]),
            "mu3_in": np.ascontiguousarray(c_outs[core]["mu3_o"]),
            "rs3_in": np.ascontiguousarray(c_outs[core]["rs3_o"]),
            "wqkvT": wqkvT, "wzT": wzT, "wbaT": wbaT,
        })
    return maps


def build_E(T=8192, NGL=None, CHL=None, KHL=4, PPL=2, PARTS='dbl os e'):
    L = 128
    NCH = T // L
    GRP = 4
    NG = NCH // GRP
    nc = bass.Bass("TRN2", target_bir_lowering=False)
    qh_in = nc.dram_tensor("qh_in", [512, T], BF16, kind="ExternalInput").ap()
    kh_in = nc.dram_tensor("kh_in", [512, T], BF16, kind="ExternalInput").ap()
    vh_in = nc.dram_tensor("vh_in", [1024, T], BF16, kind="ExternalInput").ap()
    zh_in = nc.dram_tensor("zh_in", [T, 1024], BF16, kind="ExternalInput").ap()
    ba_in = nc.dram_tensor("ba_in", [T, 16], F32, kind="ExternalInput").ap()
    cw_in = nc.dram_tensor("cw_in", [2048, 4], F32, kind="ExternalInput").ap()
    dtb_in = nc.dram_tensor("dtb_in", [1, 8], F32, kind="ExternalInput").ap()
    alog_in = nc.dram_tensor("alog_in", [1, 8], F32, kind="ExternalInput").ap()
    nw_in = nc.dram_tensor("nw_in", [1, 128], F32, kind="ExternalInput").ap()
    identb = nc.dram_tensor("identb", [128, 128], BF16, kind="ExternalInput").ap()
    identf = nc.dram_tensor("identf", [128, 128], F32, kind="ExternalInput").ap()
    ident8 = nc.dram_tensor("ident8", [8, 8], F32, kind="ExternalInput").ap()
    ones8x = nc.dram_tensor("ones8x", [8, 128], F32, kind="ExternalInput").ap()
    e8_in = nc.dram_tensor("e8_in", [8, 8 * 128], F32, kind="ExternalInput").ap()
    triu_in = nc.dram_tensor("triu_in", [128, 128], F32, kind="ExternalInput").ap()
    mA_in = nc.dram_tensor("mA_in", [128, 128], F32, kind="ExternalInput").ap()
    mAT_in = nc.dram_tensor("mAT_in", [128, 128], F32, kind="ExternalInput").ap()
    mW_in = nc.dram_tensor("mW_in", [128, 128], F32, kind="ExternalInput").ap()
    og_o = nc.dram_tensor("og_o", [T, 1024], BF16, kind="ExternalOutput").ap()

    with ExitStack() as ctx:
        def sb(nm, shp, dt, stack=None):
            return (stack or ctx).enter_context(nc.sbuf_tensor(nm, shp, dt))
        p = Prog(nc, ctx)

        idb = sb("idb", [128, 128], BF16)
        idf = sb("idf", [128, 128], F32)
        id8 = sb("id8", [8, 8], F32)
        on8 = sb("on8", [8, 128], F32)
        e8 = sb("e8", [8, 8 * 128], F32)
        triu = sb("triu", [128, 128], F32)
        mA = sb("mA", [128, 128], F32)
        mAT = sb("mAT", [128, 128], F32)
        mW = sb("mW", [128, 128], F32)
        t_c = [p.dma_hw(p.sp, "c0", idb[:], identb), p.dma_hw(p.sp, "c1", id8[:], ident8),
               p.dma_hw(p.sp, "c2", on8[:], ones8x), p.dma_hw(p.sp, "c3", e8[:], e8_in),
               p.dma_hw(p.sp, "c4", triu[:], triu_in), p.dma_hw(p.sp, "c5", mA[:], mA_in),
               p.dma_hw(p.sp, "c6", mAT[:], mAT_in), p.dma_hw(p.sp, "c7", mW[:], mW_in),
               p.dma_hw(p.sp, "c8", idf[:], identf)]
        onesb = sb("onesbE", [128, 1], BF16)
        onesr = sb("onesrE", [1, 128], BF16)
        onesrf = sb("onesrfE", [1, 128], F32)
        t_ob = p.dve.op(lambda v: v.memset(onesb[:], 1.0))
        t_or = p.dve.op(lambda v: v.memset(onesr[:], 1.0))
        t_orf = p.dve.op(lambda v: v.memset(onesrf[:], 1.0))
        c_m1E, tg_m1E = p.constf(-1.0)
        c1e, tg1e = p.constf(1.0)

        ba = sb("baE", [128, NCH * 16], F32)
        t_ba = p.dma_hw(p.sp, "ba", ba[:], ba_in.rearrange("(n p) c -> p n c", p=128))
        rows_dt = sb("rows_dt", [1, 8], F32)
        rows_al = sb("rows_al", [1, 8], F32)
        t_d1 = p.dma_hw(p.sp, "dtb", rows_dt[:], dtb_in)
        t_d2 = p.dma_hw(p.sp, "alog", rows_al[:], alog_in)
        nwrow = sb("nwrow", [1, 128], F32)
        t_nw = p.dma_hw(p.sp, "nw", nwrow[:], nw_in)
        nw2row = sb("nw2row", [1, 256], BF16)
        t_nw1 = p.dve.op(lambda v: v.tensor_copy(nw2row[0:1, 0:128], nwrow[:]), deps=[t_nw])
        t_nw2 = p.dve.op(lambda v: v.tensor_copy(nw2row[0:1, 128:256], nwrow[:]), deps=[t_nw1])

        dtb8 = sb("dtb8", [128, 8], F32)
        expA8 = sb("expA8", [128, 8], F32)
        nwt2 = sb("nwt2", [128, 256], BF16)
        cwsb = sb("cwsb", [128, 64], F32)
        t_cw = p.dma_hw(p.sp, "cw", cwsb[:], cw_in.rearrange("(rc p) j -> p rc j", p=128))
        diags = sb("diags", [128, 64 * 128], BF16)

        psGN = ctx.enter_context(nc.psum_tensor("psGN", [128, 16], F32))
        psR = ctx.enter_context(nc.psum_tensor("psR", [8, 128], F32))
        psZ = ctx.enter_context(nc.psum_tensor("psZ", [128, 128], F32))
        psKQ = ctx.enter_context(nc.psum_tensor("psKQ", [128, 256], F32))
        psT = ctx.enter_context(nc.psum_tensor("psT", [128, 256], BF16))
        psBU = ctx.enter_context(nc.psum_tensor("psBU", [128, 128], F32))
        psSq = ctx.enter_context(nc.psum_tensor("psSq", [128, 256], F32))
        psOS = ctx.enter_context(nc.psum_tensor("psOS", [128, 384], F32))

        tb1 = p.pe.op(lambda t: t.matmul(psGN[:, 0:8], onesrf[:], rows_dt[:], start=True, stop=True),
                      deps=[t_d1, t_orf])
        te1 = p.dve.op(lambda v: v.tensor_copy(dtb8[:], psGN[:, 0:8]), deps=[tb1])
        tb2 = p.pe.op(lambda t: t.matmul(psGN[:, 8:16], onesrf[:], rows_al[:], start=True, stop=True),
                      deps=[t_d2, te1])
        te2 = p.act.op(lambda s: s.activation(expA8[:], psGN[:, 8:16], AF.Exp), deps=[tb2])
        tb3 = p.pe.op(lambda t: t.matmul(psKQ[:, 0:256], onesr[:], nw2row[:], start=True, stop=True),
                      deps=[t_nw2, t_or])
        t_nwt2 = p.dve.op(lambda v: v.tensor_copy(nwt2[:], psKQ[:, 0:256]), deps=[tb3])
        t_diag = []
        for rc in range(16):
            for j in range(4):
                t_diag.append(p.dve.op(
                    lambda v, rc=rc, j=j: v.tensor_scalar(
                        diags[:, (rc * 4 + j) * 128:(rc * 4 + j + 1) * 128],
                        idb[:], cwsb[:, rc * 4 + j: rc * 4 + j + 1], None, ALU.mult),
                    deps=[t_cw, t_c[0]]))
        p.barrier()

        convb = [sb(f"convb{i}", [128, 16 * 512], BF16) for i in range(2)]
        sqb = [sb(f"sqb{i}", [128, 8 * 512], BF16) for i in range(2)]
        intile = [sb(f"intile{i}", [128, 515], BF16) for i in range(2)]
        ztile = [sb(f"ztile{i}", [128, 1024], BF16) for i in range(2)]
        S = [sb(f"S{h}", [128, 128], F32) for h in range(8)]
        Sb = [sb(f"Sb{h}", [128, 128], BF16) for h in range(8)]
        t_S = [p.dve.op(lambda v, h=h: v.memset(S[h][:], 0.0)) for h in range(8)]
        t_Sb = [p.dve.op(lambda v, h=h: v.memset(Sb[h][:], 0.0)) for h in range(8)]
        G8 = sb("G8", [128, 8], F32)
        Gp8 = sb("Gp8", [128, 8], F32)
        ld8 = sb("ld8", [128, 8], F32)
        beta8 = sb("beta8", [128, 8], F32)
        spnb = sb("spnb", [128, 8], F32)
        RowA = sb("RowA", [128, 8], F32)
        CqA = sb("CqA", [128, 8], F32)
        CW = sb("CW", [128, 8], F32)
        lr8 = sb("lr8", [128, 8], F32)
        GL8 = sb("GL8", [128, 8], F32)
        gam8 = sb("gam8", [128, 8], F32)
        negE8 = sb("negE8", [128, 8], F32)
        osc8 = sb("osc8", [128, 8], F32)
        eco8 = sb("eco8", [128, 8], F32)
        diag8 = sb("diag8", [8, 8], F32)
        tmp8a = sb("tmp8a", [128, 8], F32)
        tmp8b = sb("tmp8b", [128, 8], F32)
        tmp8c = sb("tmp8c", [128, 8], F32)
        rows_c = {cl: sb(f"rows_{cl}", [8, 128], F32) for cl in ["G", "RowA", "CqA", "CW"]}
        KKt = sb("KKt", [128, 128], BF16)
        QKt = sb("QKt", [128, 128], BF16)
        Ktm = sb("Ktm", [128, 128], BF16)
        Vtm = sb("Vtm", [128, 128], BF16)
        zD = sb("zD", [128, 256], F32)
        DA = sb("DA", [128, 256], BF16)
        DW = sb("DW", [128, 256], BF16)
        Bt = sb("Bt", [128, 128], BF16)
        Ut = sb("Ut", [128, 128], BF16)
        Uf = sb("Uf", [128, 128], BF16)
        ABbuf = sb("ABbuf", [128, 512], BF16)
        tmpB = sb("tmpB", [128, 128], BF16)
        Ot = sb("Ot", [128, 256], BF16)
        ogt = sb("ogt", [128, 256], BF16)
        rmsc = sb("rmsc", [128, 2], F32)
        silz = sb("silz", [128, 256], BF16)

        def bas(n, c0, w):
            return ba[:, n * 16 + c0: n * 16 + c0 + w]

        t_intile = [None, None]
        t_zt = [None, None]
        t_group_consumed = [None, None]
        t_chunk = None
        t_coef_users = None

        for g in range(NGL if NGL is not None else NG):
            gb = g % 2
            t0 = g * 512
            t_conv_last = None
            t_sq_last = None
            t_conv_ps = [None, None]
            for rc in range(16):
                src = qh_in if rc < 4 else (kh_in if rc < 8 else vh_in)
                row0 = (rc % 4) * 128 if rc < 8 else (rc - 8) * 128
                ib = intile[rc % 2]
                if g == 0:
                    tl1 = p.dve.op(lambda v, ib=ib: v.memset(ib[:, 0:3], 0.0),
                                   deps=[t_intile[rc % 2]])
                    tld_ = p.dma_hw(p.sp, f"cin{rc % 2}", ib[:, 3:515],
                                    src[row0:row0 + 128, t0:t0 + 512], deps=[tl1])
                    tl = [tld_, tl1]
                else:
                    tld_ = p.dma_hw(p.sp, f"cin{rc % 2}", ib[:, 0:515],
                                    src[row0:row0 + 128, t0 - 3:t0 + 512],
                                    deps=[t_intile[rc % 2]])
                    tl = [tld_]
                for half in range(2):
                    ps = psKQ if half == 0 else psSq
                    t_mm = None
                    for j in range(4):
                        fn = p.pe.op if j == 3 else p.pe.op_quiet
                        t_mm = fn(lambda t, rc=rc, j=j, half=half, ib=ib, ps=ps,
                                  s=(j == 0), e=(j == 3):
                                  t.matmul(ps[:, 0:256],
                                           diags[:, (rc * 4 + j) * 128:(rc * 4 + j + 1) * 128],
                                           ib[:, half * 256 + j: half * 256 + j + 256],
                                           start=s, stop=e),
                                  deps=(tl + [t_diag[-1], t_group_consumed[gb], t_chunk,
                                              t_conv_ps[half]]
                                        if j == 0 else ()))
                    tev = p.dve.op(lambda v, rc=rc, half=half, gb=gb, ps=ps: v.tensor_copy(
                        convb[gb][:, rc * 512 + half * 256: rc * 512 + half * 256 + 256],
                        ps[:, 0:256]), deps=[t_mm])
                    t_conv_ps[half] = tev
                    t_conv_last = tev
                    if rc < 8:
                        t_sq_last = p.dve.op(lambda v, rc=rc, half=half, gb=gb: v.tensor_tensor(
                            sqb[gb][:, rc * 512 + half * 256: rc * 512 + half * 256 + 256],
                            convb[gb][:, rc * 512 + half * 256: rc * 512 + half * 256 + 256],
                            convb[gb][:, rc * 512 + half * 256: rc * 512 + half * 256 + 256],
                            ALU.mult), deps=[tev])
                t_intile[rc % 2] = t_mm
            t_convdone = t_conv_last
            t_sqdone = t_sq_last

            for o in range(GRP if CHL is None else CHL):
                n = g * GRP + o
                ts0 = o * 128

                def cslice(rc, w=128):
                    return convb[gb][:, rc * 512 + ts0: rc * 512 + ts0 + w]

                t_zl = p.dma_hw(p.sp, f"z{n % 2}", ztile[n % 2][:],
                                zh_in[n * L:(n + 1) * L, :], deps=[t_zt[n % 2]])

                bsl = bas(n, 0, 8)
                asl = bas(n, 8, 8)
                dd = [t_ba, t_coef_users]
                tsg = p.act.op(lambda s, bsl=bsl: s.activation(beta8[:], bsl, AF.Sigmoid), deps=dd)
                tu = p.dve.op(lambda v, bsl=bsl: v.scalar_tensor_tensor(
                    tmp8a[:], bsl, -1.0, bsl, ALU.mult, ALU.max), deps=dd)
                tex = p.act.op(lambda s: s.activation(tmp8b[:], tmp8a[:], AF.Exp, scale=c_m1E),
                               deps=[tu, tg_m1E, tsg])
                tln = p.act.op(lambda s: s.activation(tmp8b[:], tmp8b[:], AF.Ln, bias=c1e),
                               deps=[tex, tg1e])
                tmx = p.dve.op(lambda v, bsl=bsl: v.tensor_scalar(tmp8a[:], bsl, -1.0, 0.0,
                                                                  ALU.mult, ALU.max), deps=[tln])
                tsp = p.dve.op(lambda v: v.tensor_add(spnb[:], tmp8a[:], tmp8b[:]), deps=[tmx])
                td1 = p.dve.op(lambda v, asl=asl: v.tensor_add(tmp8a[:], asl, dtb8[:]),
                               deps=[te1, tsp] + dd)
                td2 = p.dve.op(lambda v: v.scalar_tensor_tensor(
                    tmp8c[:], tmp8a[:], -1.0, tmp8a[:], ALU.mult, ALU.max), deps=[td1])
                td3 = p.act.op(lambda s: s.activation(tmp8b[:], tmp8c[:], AF.Exp, scale=c_m1E),
                               deps=[td2, tsp])
                td4 = p.act.op(lambda s: s.activation(tmp8b[:], tmp8b[:], AF.Ln, bias=c1e),
                               deps=[td3])
                td5 = p.dve.op(lambda v: v.tensor_scalar(tmp8c[:], tmp8a[:], 0.0, None, ALU.max),
                               deps=[td4])
                td6 = p.dve.op(lambda v: v.tensor_add(tmp8a[:], tmp8b[:], tmp8c[:]), deps=[td5, td4])
                tld = p.dve.op(lambda v: v.tensor_tensor(ld8[:], tmp8a[:], expA8[:], ALU.mult),
                               deps=[td6, te2])
                tldn = p.dve.op(lambda v: v.tensor_scalar_mul(ld8[:], ld8[:], -1.0), deps=[tld])
                tgm = p.pe.op(lambda t: t.matmul(psGN[:, 0:8], triu[:], ld8[:], start=True, stop=True),
                              deps=[tldn, t_c[4], te1])
                tge = p.dve.op(lambda v: v.tensor_copy(G8[:], psGN[:, 0:8]), deps=[tgm])
                tgp = p.dve.op(lambda v: v.tensor_sub(Gp8[:], G8[:], ld8[:]), deps=[tge])
                t_nmm = None
                for c8 in range(8):
                    fn = p.pe.op if c8 == 7 else p.pe.op_quiet
                    t_nmm = fn(lambda t, c8=c8, gb=gb, ts0=ts0: t.matmul(
                        psGN[:, 8 + c8:9 + c8],
                        sqb[gb][:, c8 * 512 + ts0: c8 * 512 + ts0 + 128], onesb[:],
                        start=True, stop=True),
                        deps=[t_sqdone, t_ob, tgp] if c8 == 0 else ())
                tn1 = p.dve.op(lambda v: v.tensor_scalar(lr8[:], psGN[:, 8:16], 1e-24, None,
                                                         ALU.max), deps=[t_nmm])
                tn2 = p.act.op(lambda s: s.activation(lr8[:], lr8[:], AF.Ln), deps=[tn1])
                tn3 = p.dve.op(lambda v: v.tensor_scalar_mul(lr8[:], lr8[:], -0.5), deps=[tn2])
                lrq_rep = lr8[:, 0:4].unsqueeze(-1).broadcast_to([128, 4, 2])
                lrk_rep = lr8[:, 4:8].unsqueeze(-1).broadcast_to([128, 4, 2])
                def v3(t):
                    return t[:].rearrange("p (a b) -> p a b", b=2)
                ta1 = p.dve.op(lambda v: v.tensor_sub(tmp8a[:], Gp8[:], spnb[:]), deps=[tgp, tn3])
                ta2 = p.dve.op(lambda v, lrk_rep=lrk_rep: v.tensor_tensor(v3(RowA), v3(tmp8a),
                                                                          lrk_rep, ALU.add),
                               deps=[ta1])
                ta3 = p.dve.op(lambda v, lrk_rep=lrk_rep: v.scalar_tensor_tensor(
                    v3(CqA), v3(G8), -1.0, lrk_rep, ALU.mult, ALU.add), deps=[ta2])
                ta4 = p.dve.op(lambda v, lrq_rep=lrq_rep: v.tensor_tensor(v3(CW), v3(G8), lrq_rep,
                                                                          ALU.add), deps=[ta3])
                t_rows = {}
                tprev_r = None
                for cl, tile, tdep in [("G", G8, tge), ("RowA", RowA, ta2),
                                       ("CqA", CqA, ta3), ("CW", CW, ta4)]:
                    ttp = p.pe.op(lambda t, tile=tile: t.transpose(psR[:], tile[:], idf[:]),
                                  deps=[tdep, tprev_r, t_c[8]])
                    tevr = p.dve.op(lambda v, cl=cl: v.tensor_copy(rows_c[cl][:], psR[:]),
                                    deps=[ttp])
                    t_rows[cl] = tevr
                    tprev_r = tevr
                tdg = p.dve.op(lambda v: v.tensor_scalar(diag8[:], id8[:],
                                                         rows_c["G"][:, 127:128], None, ALU.mult),
                               deps=[t_rows["G"], t_c[1]])
                tglm = p.pe.op(lambda t: t.matmul(psGN[:, 0:8], on8[:], diag8[:], start=True,
                                                  stop=True), deps=[tdg, t_c[2], tn1])
                tgl = p.dve.op(lambda v: v.tensor_copy(GL8[:], psGN[:, 0:8]), deps=[tglm])
                tgam = p.act.op(lambda s: s.activation(gam8[:], GL8[:], AF.Exp), deps=[tgl])
                tne1 = p.act.op(lambda s: s.activation(negE8[:], RowA[:], AF.Exp), deps=[ta2, tgam])
                tne2 = p.dve.op(lambda v: v.tensor_scalar_mul(negE8[:], negE8[:], -1.0), deps=[tne1])
                tosc = p.act.op(lambda s: s.activation(osc8[:], CW[:], AF.Exp), deps=[ta4, tne1])
                tec1 = p.dve.op(lambda v: v.tensor_add(tmp8c[:], CqA[:], GL8[:]), deps=[ta3, tgl])
                tec2 = p.act.op(lambda s: s.activation(eco8[:], tmp8c[:], AF.Exp), deps=[tec1, tosc])

                t_kk_ev = None
                t_qk_ev = None
                t_ktm_ev = None
                t_vtm_last = None
                t_psz_free = None
                t_psbu_free = None
                t_psos_free = [None, None, None]
                tz1_last = None
                if n == 0:
                    t_ogdma = None
                for kh in range(KHL):
                    kc_t = cslice(4 + kh)
                    qc_t = cslice(kh)
                    tkk = p.pe.op(lambda t, kc_t=kc_t: t.matmul(psKQ[:, 0:128], kc_t, kc_t,
                                                                start=True, stop=True),
                                  deps=[t_convdone, t_qk_ev, t_chunk, tec2])
                    t_kk_ev = p.dve.op(lambda v: v.tensor_scalar_mul(KKt[:], psKQ[:, 0:128], -1.0),
                                       deps=[tkk])
                    tqk = p.pe.op(lambda t, kc_t=kc_t, qc_t=qc_t: t.matmul(psKQ[:, 128:256],
                                                                           kc_t, qc_t,
                                                                           start=True, stop=True),
                                  deps=[t_kk_ev])
                    t_qk_ev = p.dve.op(lambda v: v.tensor_copy(QKt[:], psKQ[:, 128:256]),
                                       deps=[tqk])
                    tkt = p.pe.op(lambda t, kc_t=kc_t: t.transpose(psT[:, 0:128], kc_t, idb[:]),
                                  deps=[t_ktm_ev, t_c[0], t_qk_ev, t_vtm_last])
                    t_ktm_ev = p.dve.op(lambda v: v.tensor_copy(Ktm[:], psT[:, 0:128]), deps=[tkt])
                    for pp in range(PPL):
                        h = 2 * kh + pp
                        if 'd' not in PARTS:
                            continue
                        tza = p.pe.op(lambda t, h=h: t.matmul(psZ[:], e8[:, h * 128:(h + 1) * 128],
                                                              rows_c["CqA"][:], start=True, stop=True),
                                      deps=[t_rows["CqA"], t_c[3], t_psz_free, tkt])
                        tsa = p.dve.op(lambda v, h=h: v.scalar_tensor_tensor(
                            zD[:, 0:128], psZ[:], RowA[:, h:h + 1], mA[:], ALU.add, ALU.min),
                            deps=[tza, t_c[5]])
                        tzat = p.pe.op(lambda t, h=h: t.matmul(psZ[:], e8[:, h * 128:(h + 1) * 128],
                                                               rows_c["RowA"][:], start=True,
                                                               stop=True),
                                       deps=[t_rows["RowA"], tsa])
                        tsat = p.dve.op(lambda v, h=h: v.scalar_tensor_tensor(
                            zD[:, 128:256], psZ[:], CqA[:, h:h + 1], mAT[:], ALU.add, ALU.min),
                            deps=[tzat, t_c[6]])
                        texp = p.act.op(lambda s: s.activation(DA[:, 0:256], zD[:, 0:256], AF.Exp),
                                        deps=[tsat])
                        tma = p.dve.op(lambda v: v.tensor_tensor(ABbuf[:, 0:128], DA[:, 0:128],
                                                                 KKt[:], ALU.mult),
                                       deps=[texp, t_kk_ev])
                        tmat = p.dve.op(lambda v: v.tensor_tensor(ABbuf[:, 128:256], DA[:, 128:256],
                                                                  KKt[:], ALU.mult), deps=[tma])
                        tzw = p.pe.op(lambda t, h=h: t.matmul(psZ[:], e8[:, h * 128:(h + 1) * 128],
                                                              rows_c["CW"][:], start=True, stop=True),
                                      deps=[t_rows["CW"], tsat])
                        t_psz_free = tzw
                        tsw = p.dve.op(lambda v, h=h: v.scalar_tensor_tensor(
                            zD[:, 0:128], psZ[:], CqA[:, h:h + 1], mW[:], ALU.add, ALU.min),
                            deps=[tzw, t_c[7], texp])
                        t_psz_free = tsw
                        texw = p.act.op(lambda s: s.activation(DW[:, 0:128], zD[:, 0:128], AF.Exp),
                                        deps=[tsw])
                        tmw = p.dve.op(lambda v: v.tensor_tensor(DW[:, 128:256], DW[:, 0:128],
                                                                 QKt[:], ALU.mult),
                                       deps=[texw, t_qk_ev])
                        if 'b' not in PARTS:
                            continue
                        vc_t = cslice(8 + h)
                        tvt = p.pe.op(lambda t, vc_t=vc_t: t.transpose(psT[:, 128:256], vc_t,
                                                                       idb[:]),
                                      deps=[t_convdone, t_c[0], tsw, tzw])
                        tvte = p.dve.op(lambda v: v.tensor_copy(Vtm[:], psT[:, 128:256]),
                                        deps=[tvt])
                        t_vtm_last = tvte
                        tks = p.pe.op(lambda t, kc_t=kc_t, h=h: t.matmul(psBU[:], kc_t, Sb[h][:],
                                                                         start=True, stop=True),
                                      deps=[t_Sb[h], tvt, t_psbu_free])
                        tbb1 = p.dve.op(lambda v, h=h: v.tensor_scalar(tmpB[:], Vtm[:],
                                                                       beta8[:, h:h + 1], None,
                                                                       ALU.mult), deps=[tvte])
                        tbb2 = p.dve.op(lambda v, h=h: v.scalar_tensor_tensor(
                            Bt[:], psBU[:], negE8[:, h:h + 1], tmpB[:], ALU.mult, ALU.add),
                            deps=[tks, tne2, tbb1])
                        tu_ = p.dve.op(lambda v: v.tensor_copy(Ut[:], Bt[:]), deps=[tbb2])
                        if 'l' not in PARTS:
                            continue
                        tac = tmat
                        curq = 0
                        for lev in range(7):
                            A_c = ABbuf[:, curq * 256: curq * 256 + 128]
                            AT_c = ABbuf[:, curq * 256 + 128: curq * 256 + 256]
                            tap = p.pe.op(lambda t, AT_c=AT_c: t.matmul(psBU[:], AT_c, Ut[:],
                                                                        start=True, stop=True),
                                          deps=[tu_, tac, tks])
                            tu_ = p.dve.op(lambda v: v.scalar_tensor_tensor(
                                Ut[:], psBU[:], 0.0, Ut[:], ALU.add, ALU.add), deps=[tap])
                            t_psbu_free = tu_
                            if lev < 6:
                                nq = 1 - curq
                                tq1_ = p.pe.op(lambda t, A_c=A_c, AT_c=AT_c: t.matmul(
                                    psSq[:, 0:128], AT_c, A_c, start=True, stop=True),
                                    deps=[tac, tap])
                                tq2_ = p.pe.op(lambda t, A_c=A_c, AT_c=AT_c: t.matmul(
                                    psSq[:, 128:256], A_c, AT_c, start=True, stop=True),
                                    deps=[tq1_])
                                tac = p.dve.op(lambda v, nq=nq: v.tensor_copy(
                                    ABbuf[:, nq * 256:(nq + 1) * 256], psSq[:, 0:256]),
                                    deps=[tq2_, tu_])
                                curq = nq
                        if 'os' not in PARTS:
                            continue
                        to1 = p.pe.op(lambda t, qc_t=qc_t, h=h: t.matmul(psOS[:, 0:128], qc_t,
                                                                         Sb[h][:], start=True,
                                                                         stop=True),
                                      deps=[t_Sb[h], tu_, t_psos_free[0]])
                        to2 = p.pe.op(lambda t: t.matmul(psOS[:, 128:256], DW[:, 128:256], Ut[:],
                                                         start=True, stop=True),
                                      deps=[tmw, tu_, to1, t_psos_free[1]])
                        to3 = p.dve.op(lambda v, h=h: v.tensor_scalar(tmpB[:], psOS[:, 0:128],
                                                                      osc8[:, h:h + 1], None,
                                                                      ALU.mult), deps=[to2, tosc])
                        t_psos_free[0] = to3
                        to4 = p.dve.op(lambda v, pp=pp: v.tensor_tensor(
                            Ot[:, pp * 128:(pp + 1) * 128], psOS[:, 128:256], tmpB[:], ALU.add),
                            deps=[to2, to3])
                        t_psos_free[1] = to4
                        tus = p.dve.op(lambda v, h=h: v.tensor_scalar(Uf[:], Ut[:],
                                                                      eco8[:, h:h + 1], None,
                                                                      ALU.mult), deps=[tu_, tec2, to4])
                        tsmm = p.pe.op(lambda t, h=h: t.matmul(psOS[:, 256:384], Ktm[:], Uf[:],
                                                               start=True, stop=True),
                                       deps=[t_ktm_ev, tus, t_psos_free[2]])
                        tsu = p.dve.op(lambda v, h=h: v.scalar_tensor_tensor(
                            S[h][:], S[h][:], gam8[:, h:h + 1], psOS[:, 256:384], ALU.mult,
                            ALU.add), deps=[tsmm, tgam, t_S[h]])
                        t_psos_free[2] = tsu
                        t_S[h] = tsu
                        t_Sb[h] = p.dve.op(lambda v, h=h: v.tensor_copy(Sb[h][:], S[h][:]),
                                           deps=[tsu])
                    if 'e' not in PARTS:
                        continue
                    trms1 = p.dve.op(lambda v: v.tensor_tensor(ogt[:], Ot[:, 0:256], Ot[:, 0:256],
                                                               ALU.mult), deps=[to4, t_ogdma])
                    trms2 = p.dve.op(lambda v: v.tensor_reduce(
                        rmsc[:], ogt[:].rearrange("p (a b) -> p a b", a=2), AX.X, ALU.add),
                        deps=[trms1])
                    trms3 = p.dve.op(lambda v: v.tensor_scalar(rmsc[:], rmsc[:], 1.0 / 128, 1e-6,
                                                               ALU.mult, ALU.add), deps=[trms2])
                    trms4 = p.act.op(lambda s: s.activation(rmsc[:], rmsc[:], AF.Sqrt),
                                     deps=[trms3, tec2])
                    trms5 = p.dve.op(lambda v: v.reciprocal(rmsc[:], rmsc[:]), deps=[trms4])
                    tz1 = p.act.op(lambda s, kh=kh, n=n: s.activation(
                        silz[:], ztile[n % 2][:, kh * 256:(kh + 1) * 256], AF.Silu),
                        deps=[t_zl, trms4])
                    tz1_last = tz1
                    tg1_ = p.dve.op(lambda v: v.tensor_scalar(ogt[:, 0:128], Ot[:, 0:128],
                                                              rmsc[:, 0:1], None, ALU.mult),
                                    deps=[trms5])
                    tg2_ = p.dve.op(lambda v: v.tensor_scalar(ogt[:, 128:256], Ot[:, 128:256],
                                                              rmsc[:, 1:2], None, ALU.mult),
                                    deps=[tg1_])
                    tg3_ = p.dve.op(lambda v: v.tensor_tensor(ogt[:, 0:256], ogt[:, 0:256],
                                                              nwt2[:], ALU.mult),
                                    deps=[tg2_, t_nwt2])
                    tg4_ = p.dve.op(lambda v: v.tensor_tensor(ogt[:, 0:256], ogt[:, 0:256],
                                                              silz[:], ALU.mult),
                                    deps=[tg3_, tz1])
                    t_ogdma = p.dma_hw(p.sp, f"og{kh % 2}",
                                       og_o[n * L:(n + 1) * L, kh * 256:(kh + 1) * 256],
                                       ogt[:, 0:256], deps=[tg4_])
                t_zt[n % 2] = tz1_last
                t_chunk = p.dve.tag()
                t_coef_users = t_chunk
            t_group_consumed[gb] = t_chunk
        p.finish()
        p.emit()
    return nc


def make_E_inmaps(inputs, d_outs):
    T = 8192
    maps = []
    e8 = np.zeros((8, 8 * 128), dtype=np.float32)
    for h in range(8):
        e8[h, h * 128:(h + 1) * 128] = 1.0
    sidx = np.arange(128)
    triu = (sidx[:, None] <= sidx[None, :]).astype(np.float32)
    mA = np.where(sidx[None, :] < sidx[:, None], 1e30, -1e30).astype(np.float32)
    mAT = np.where(sidx[None, :] > sidx[:, None], 1e30, -1e30).astype(np.float32)
    mW = np.where(sidx[None, :] >= sidx[:, None], 1e30, -1e30).astype(np.float32)
    cw = np.asarray(inputs["L1_attn_conv_w"])
    dtb = np.asarray(inputs["L1_attn_dt_bias"])
    alog = np.asarray(inputs["L1_attn_A_log"])
    nw = np.asarray(inputs["L1_attn_norm_w"])
    for bb in range(2):
        cores = [4 * bb + i for i in range(4)]
        qkv_full = np.zeros((8192, T), dtype=BF)
        z_full = np.zeros((4096, T), dtype=BF)
        ba_full = np.zeros((64, T), dtype=np.float32)
        for m, corec in enumerate(cores):
            qkv_c = np.asarray(d_outs[corec]["qkv_o"]).view(BF)
            z_c = np.asarray(d_outs[corec]["z_o"]).view(BF)
            ba_c = np.asarray(d_outs[corec]["ba_o"])
            for j in range(16):
                blk = 4 * j + m
                qkv_full[:, blk * 128:(blk + 1) * 128] = qkv_c[:, j * 128:(j + 1) * 128]
                z_full[:, blk * 128:(blk + 1) * 128] = z_c[:, j * 128:(j + 1) * 128]
                ba_full[:, blk * 128:(blk + 1) * 128] = ba_c[:, j * 128:(j + 1) * 128]
        for m in range(4):
            core = 4 * bb + m
            qh = np.ascontiguousarray(qkv_full[512 * m:512 * (m + 1)])
            kh = np.ascontiguousarray(qkv_full[2048 + 512 * m:2048 + 512 * (m + 1)])
            vh = np.ascontiguousarray(qkv_full[4096 + 1024 * m:4096 + 1024 * (m + 1)])
            zh = np.ascontiguousarray(z_full[1024 * m:1024 * (m + 1)].T)
            bah = np.ascontiguousarray(
                np.concatenate([ba_full[8 * m:8 * m + 8], ba_full[32 + 8 * m:32 + 8 * m + 8]],
                               axis=0).T)
            cwrows = np.concatenate([cw[2048 + 512 * m + 0:2048 + 512 * m + 0][0:0]] ) if False else None
            cw_q = cw[512 * m:512 * (m + 1)]
            cw_k = cw[2048 + 512 * m:2048 + 512 * (m + 1)]
            cw_v = cw[4096 + 1024 * m:4096 + 1024 * (m + 1)]
            cwh = np.ascontiguousarray(np.concatenate([cw_q, cw_k, cw_v], axis=0))
            maps.append({
                "qh_in": qh, "kh_in": kh, "vh_in": vh, "zh_in": zh, "ba_in": bah,
                "cw_in": cwh,
                "dtb_in": np.ascontiguousarray(dtb[8 * m:8 * m + 8][None, :]),
                "alog_in": np.ascontiguousarray(alog[8 * m:8 * m + 8][None, :]),
                "nw_in": np.ascontiguousarray(nw[None, :]),
                "identb": np.eye(128, dtype=np.float32).astype(BF),
                "identf": np.eye(128, dtype=np.float32),
                "ident8": np.eye(8, dtype=np.float32),
                "ones8x": np.ones((8, 128), dtype=np.float32),
                "e8_in": e8, "triu_in": triu, "mA_in": mA, "mAT_in": mAT, "mW_in": mW,
            })
    return maps


def build_F():
    VD = 4096
    nc = bass.Bass("TRN2", target_bir_lowering=False)
    x2_in = nc.dram_tensor("x2_in", [C, TPC], F32, kind="ExternalInput").ap()
    og_in = nc.dram_tensor("og_in", [VD, TPC], BF16, kind="ExternalInput").ap()
    woT = nc.dram_tensor("woT", [VD, C], F32, kind="ExternalInput").ap()
    waqT = nc.dram_tensor("waqT", [C, C], F32, kind="ExternalInput").ap()
    wakT = nc.dram_tensor("wakT", [C, C], F32, kind="ExternalInput").ap()
    waqlT = nc.dram_tensor("waqlT", [C, 128], F32, kind="ExternalInput").ap()
    waklT = nc.dram_tensor("waklT", [C, 128], F32, kind="ExternalInput").ap()
    wavT = nc.dram_tensor("wavT", [C, C], F32, kind="ExternalInput").ap()
    wrec = nc.dram_tensor("wrec", [16, 1], F32, kind="ExternalInput").ap()
    wstd = nc.dram_tensor("wstd", [16, 1], F32, kind="ExternalInput").ap()
    patlow = nc.dram_tensor("patlow", [16, 128], F32, kind="ExternalInput").ap()
    pathigh = nc.dram_tensor("pathigh", [16, 128], F32, kind="ExternalInput").ap()
    ident16 = nc.dram_tensor("ident16", [16, 16], F32, kind="ExternalInput").ap()
    x3_o = nc.dram_tensor("x3_o", [C, TPC], F32, kind="ExternalOutput").ap()
    qf_o = nc.dram_tensor("qf_o", [C, TPC], BF16, kind="ExternalOutput").ap()
    kf_o = nc.dram_tensor("kf_o", [C, TPC], BF16, kind="ExternalOutput").ap()
    v_o = nc.dram_tensor("v_o", [TPC, C], BF16, kind="ExternalOutput").ap()

    x2v = x2_in.rearrange("(c p) t -> c p t", p=128)
    ogv = og_in.rearrange("(c p) t -> c p t", p=128)
    x3v = x3_o.rearrange("(c p) t -> c p t", p=128)
    qfv = qf_o.rearrange("(c p) t -> c p t", p=128)
    kfv = kf_o.rearrange("(c p) t -> c p t", p=128)

    with ExitStack() as ctx:
        def sb(nm, shp, dt, stack=None):
            return (stack or ctx).enter_context(nc.sbuf_tensor(nm, shp, dt))
        p = Prog(nc, ctx)
        onesb = sb("onesbF", [128, 1], BF16)
        onesr = sb("onesrF", [1, 128], BF16)
        t_ob = p.dve.op(lambda v: v.memset(onesb[:], 1.0))
        t_or = p.dve.op(lambda v: v.memset(onesr[:], 1.0))
        c_eps5, tg_e5 = p.constf(1e-5)
        c_eps8, tg_e8 = p.constf(1e-8)

        xf = sb("xfF", [128, 16 * TPC], BF16)
        xn = sb("xnF", [128, 16 * TPC], BF16)

        def xfs(c, tt, w=TT):
            return xf[:, c * TPC + tt * w: c * TPC + (tt + 1) * w]

        s1 = ctx.enter_context(ExitStack())
        ogb = sb("ogb", [128, 32 * TT], BF16, s1)
        wob = [sb(f"wob{i}", [128, 32 * 128], BF16, s1) for i in range(2)]
        x2t = [sb(f"x2tF{i}", [128, TT], F32, s1) for i in range(2)]
        x3t = [sb(f"x3tF{i}", [128, TT], F32, s1) for i in range(2)]

        def w_cols(wT, oc, nkc, width=128):
            return wT.rearrange("(kc p) o -> p kc o", p=128)[:, :, oc * width:(oc + 1) * width]

        with nc.psum_tensor("psf0", [128, TT], F32) as psf0, \
             nc.psum_tensor("psf1", [128, TT], F32) as psf1:
            psfs = [psf0, psf1]
            t_w_free = [None, None]
            t_og_used = None
            t_x2l = [None, None]
            t_xw = [None, None]
            t_x3st = [None, None]
            for tt in range(NTT):
                t_ogl = []
                for c in range(32):
                    t_ogl.append(p.dma_hw(p.sp, f"ogl{c % 4}", ogb[:, c * TT:(c + 1) * TT],
                                          ogv[c][:, tt * TT:(tt + 1) * TT],
                                          deps=[t_og_used, t_ogl[c - 4] if c >= 4 else None]))
                for oc in range(16):
                    bo = oc % 2
                    tw = p.dma_sw(f"woF{bo}", wob[bo][:], w_cols(woT, oc, 32), deps=[t_w_free[bo]])
                    ps = psfs[bo]
                    t_mm = None
                    for kc in range(32):
                        fn = p.pe.op if kc == 31 else p.pe.op_quiet
                        t_mm = fn(lambda t, kc=kc, oc=oc, ps=ps, bo=bo, s=(kc == 0), e=(kc == 31):
                                  t.matmul(ps[:], wob[bo][:, kc * 128:(kc + 1) * 128],
                                           ogb[:, kc * TT:(kc + 1) * TT], start=s, stop=e),
                                  deps=([tw, t_xw[bo]] + t_ogl if kc == 0 else ()))
                    t_w_free[bo] = t_mm
                    tx2 = p.dma_hw(p.sp, f"x2F{bo}", x2t[bo][:], x2v[oc][:, tt * TT:(tt + 1) * TT],
                                   deps=[t_x2l[bo]])
                    txa = p.dve.op(lambda v, bo=bo, ps=ps:
                                   v.tensor_tensor(x3t[bo][:], x2t[bo][:], ps[:], ALU.add),
                                   deps=[t_mm, tx2, t_x3st[bo]])
                    t_x2l[bo] = txa
                    t_xw[bo] = txa
                    txc = p.dve.op(lambda v, oc=oc, tt=tt, bo=bo:
                                   v.tensor_copy(xfs(oc, tt), x3t[bo][:]), deps=[txa])
                    t_x3st[bo] = p.dma_hw(p.sp, f"x3st{bo}", x3v[oc][:, tt * TT:(tt + 1) * TT],
                                          x3t[bo][:], deps=[txc])
                t_og_used = t_mm
        p.barrier()
        s1.close()

        s2 = ctx.enter_context(ExitStack())
        casts = [sb(f"castF{i}", [128, TT], BF16, s2) for i in range(2)]
        sqt = [sb(f"sqtF{i}", [128, TT], BF16, s2) for i in range(2)]
        sumr = sb("sumrF", [1, TPC], F32, s2)
        sqr = sb("sqrF", [1, TPC], F32, s2)
        mu_b = sb("mu_bF", [1, TPC], BF16, s2)
        rs_b = sb("rs_bF", [1, TPC], BF16, s2)
        rowa = sb("rowaF", [1, TPC], F32, s2)
        roww = sb("rowwF", [1, TPC], F32, s2)
        with nc.psum_tensor("pssumF", [1, TT], F32) as pssum, \
             nc.psum_tensor("pssqF", [1, TT], F32) as pssq:
            t_sum_evac = None
            t_sq_evac = None
            for tt in range(NTT):
                t_mm = None
                t_mmq = None
                for c in range(16):
                    tcast = p.dve.op(lambda v, c=c, tt=tt, b=c % 2:
                                     v.tensor_copy(casts[b][:], xfs(c, tt)), deps=[t_mm])
                    t_mm = p.pe.op(lambda t, c=c, s=(c == 0), e=(c == 15), b=c % 2:
                                   t.matmul(pssum[:], onesb[:], casts[b][:], start=s, stop=e),
                                   deps=[tcast, t_ob] + ([t_sum_evac] if c == 0 else []))
                    t_sq = p.act.op(lambda s, c=c, b=c % 2:
                                    s.activation(sqt[b][:], casts[b][:], AF.Square),
                                    deps=[tcast, t_mmq])
                    t_mmq = p.pe.op(lambda t, c=c, s=(c == 0), e=(c == 15), b=c % 2:
                                    t.matmul(pssq[:], onesb[:], sqt[b][:], start=s, stop=e),
                                    deps=[t_sq] + ([t_sq_evac] if c == 0 else []))
                t_sum_evac = p.dve.op(lambda v, tt=tt:
                                      v.tensor_copy(sumr[0:1, tt * TT:(tt + 1) * TT], pssum[:]),
                                      deps=[t_mm])
                t_sq_evac = p.dve.op(lambda v, tt=tt:
                                     v.tensor_copy(sqr[0:1, tt * TT:(tt + 1) * TT], pssq[:]),
                                     deps=[t_mmq])
        t1 = p.dve.op(lambda v: v.tensor_scalar_mul(rowa[:], sumr[:], 1.0 / C), deps=[t_sum_evac])
        t_mub = p.dve.op(lambda v: v.tensor_copy(mu_b[:], rowa[:]), deps=[t1])
        t2 = p.dve.op(lambda v: v.tensor_scalar_mul(roww[:], sqr[:], 1.0 / C), deps=[t_sq_evac])
        t3 = p.dve.op(lambda v: v.tensor_tensor(sumr[:], rowa[:], rowa[:], ALU.mult), deps=[t1, t2])
        t4 = p.dve.op(lambda v: v.tensor_sub(roww[:], roww[:], sumr[:]), deps=[t3])
        t5 = p.act.op(lambda s: s.activation(rowa[:], roww[:], AF.Sqrt, bias=c_eps5[0:1, :]),
                      deps=[t4, t_mub, tg_e5])
        t6 = p.dve.op(lambda v: v.reciprocal(roww[:], rowa[:]), deps=[t5])
        t_rsb = p.dve.op(lambda v: v.tensor_copy(rs_b[:], roww[:]), deps=[t6])
        p.barrier()

        mub_s = sb("mub_sF", [128, TT], BF16, s2)
        rsb_s = sb("rsb_sF", [128, TT], BF16, s2)
        with nc.psum_tensor("psbF0", [128, TT], F32) as psb0, \
             nc.psum_tensor("psbF1", [128, TT], F32) as psb1:
            te1 = te2 = None
            tb_last = None
            for tt in range(NTT):
                tm1 = p.pe.op(lambda t, tt=tt: t.matmul(psb0[:], onesr[:],
                                                        mu_b[0:1, tt * TT:(tt + 1) * TT],
                                                        start=True, stop=True),
                              deps=[t_mub, t_or, te1])
                tm2 = p.pe.op(lambda t, tt=tt: t.matmul(psb1[:], onesr[:],
                                                        rs_b[0:1, tt * TT:(tt + 1) * TT],
                                                        start=True, stop=True),
                              deps=[t_rsb, te2])
                te1 = p.dve.op(lambda v: v.tensor_copy(mub_s[:], psb0[:]), deps=[tm1, tb_last])
                te2 = p.dve.op(lambda v: v.tensor_copy(rsb_s[:], psb1[:]), deps=[tm2])
                for c in range(16):
                    ta = p.dve.op(lambda v, c=c, tt=tt:
                                  v.tensor_sub(xn[:, c * TPC + tt * TT: c * TPC + (tt + 1) * TT],
                                               xfs(c, tt), mub_s[:]),
                                  deps=[te1])
                    tb_last = p.dve.op(lambda v, c=c, tt=tt: v.tensor_tensor(
                        xn[:, c * TPC + tt * TT: c * TPC + (tt + 1) * TT],
                        xn[:, c * TPC + tt * TT: c * TPC + (tt + 1) * TT], rsb_s[:], ALU.mult),
                        deps=[ta, te2])
        p.barrier()
        s2.close()
        t_xn = p.dve.tag()

        def xns(kc, tt, w=TT):
            return xn[:, kc * TPC + tt * w: kc * TPC + (tt + 1) * w]

        s4 = ctx.enter_context(ExitStack())
        w16 = sb("w16", [16, 2], F32, s4)
        m16 = sb("m16", [16, 1], F32, s4)
        s16 = sb("s16d", [16, 16], F32, s4)
        pat_l = sb("pat_l", [16, 128], F32, s4)
        pat_h = sb("pat_h", [16, 128], F32, s4)
        id16 = sb("id16", [16, 16], F32, s4)
        csq = sb("csq", [128, 16], F32, s4)
        mrep = sb("mrep", [128, 16], F32, s4)
        tw1 = p.dma_hw(p.sp, "wrec", w16[:, 0:1], wrec)
        tw2 = p.dma_hw(p.sp, "wstd", w16[:, 1:2], wstd)
        tp1 = p.dma_hw(p.sp, "patl", pat_l[:], patlow)
        tp2 = p.dma_hw(p.sp, "path", pat_h[:], pathigh)
        tp3 = p.dma_hw(p.sp, "id16", id16[:], ident16)
        tm_ = p.dve.op(lambda v: v.tensor_single_scalar(m16[:], w16[:, 0:1], 0.1, ALU.is_gt),
                       deps=[tw1])
        ts1 = p.dve.op(lambda v: v.tensor_scalar(w16[:, 0:1], w16[:, 0:1], 1e-8, None, ALU.max),
                       deps=[tm_])
        ts2 = p.dve.op(lambda v: v.tensor_scalar(w16[:, 1:2], w16[:, 1:2], 1e-8, None, ALU.max),
                       deps=[tw2, ts1])
        ts3 = p.act.op(lambda s: s.activation(w16[:, 0:2], w16[:, 0:2], AF.Sqrt), deps=[ts2])
        with nc.psum_tensor("pscs", [128, 16], F32) as pscs, \
             nc.psum_tensor("psmr", [128, 16], F32) as psmr:
            td1_ = p.dve.op(lambda v: v.tensor_scalar(s16[:], id16[:], w16[:, 0:1], None, ALU.mult),
                            deps=[ts3, tp3])
            tcs1 = p.pe.op(lambda t: t.matmul(pscs[:], pat_l[:], s16[:], start=True, stop=False),
                           deps=[td1_, tp1])
            td2_ = p.dve.op(lambda v: v.tensor_scalar(s16[:], id16[:], w16[:, 1:2], None, ALU.mult),
                            deps=[tcs1])
            tcs2 = p.pe.op(lambda t: t.matmul(pscs[:], pat_h[:], s16[:], start=False, stop=True),
                           deps=[td2_, tp2])
            tcse = p.dve.op(lambda v: v.tensor_copy(csq[:], pscs[:]), deps=[tcs2])
            td3_ = p.dve.op(lambda v: v.tensor_scalar(s16[:], id16[:], m16[:], None, ALU.mult),
                            deps=[tcse, tm_])
            tmr1 = p.pe.op(lambda t: t.matmul(psmr[:], pat_l[:], s16[:], start=True, stop=False),
                           deps=[td3_, tcse])
            tmr2 = p.pe.op(lambda t: t.matmul(psmr[:], pat_h[:], s16[:], start=False, stop=True),
                           deps=[tmr1])
            tmre = p.dve.op(lambda v: v.tensor_copy(mrep[:], psmr[:]), deps=[tmr2])
        p.barrier()

        wb_ = [sb(f"wbF{i}", [128, 16 * 128], BF16, s4) for i in range(2)]
        wlo = [sb(f"wlo{i}", [128, 16 * 8], BF16, s4) for i in range(2)]
        stg = [sb(f"stgF{i}", [128, TT], BF16, s4) for i in range(2)]
        with nc.psum_tensor("psqF0", [128, TT], F32) as psq0, \
             nc.psum_tensor("psqF1", [128, TT], F32) as psq1:
            psqs = [psq0, psq1]
            t_w_free = [None, None]
            t_lo_free = [None, None]
            t_dma = [None, None]
            t_ev = [None, None]
            i = 0
            for which, wmain, walt, outv in [("q", waqT, waklT, qfv), ("k", wakT, waqlT, kfv)]:
                for hh in range(16):
                    bo = hh % 2
                    tw = p.dma_sw(f"wF{bo}", wb_[bo][:], w_cols(wmain, hh, 16),
                                  deps=[t_w_free[bo]])
                    twl = p.dma_sw(f"wl{bo}", wlo[bo][:],
                                   walt.rearrange("(kc p) o -> p kc o", p=128)[:, :, hh * 8:(hh + 1) * 8],
                                   deps=[t_lo_free[bo]])
                    t_bl = None
                    for kc in range(16):
                        tdiff = p.dve.op(lambda v, kc=kc, bo=bo: v.tensor_sub(
                            wlo[bo][:, kc * 8:(kc + 1) * 8], wlo[bo][:, kc * 8:(kc + 1) * 8],
                            wb_[bo][:, kc * 128:kc * 128 + 8]), deps=[tw, twl])
                        t_bl = p.dve.op(lambda v, kc=kc, bo=bo, hh=hh: v.scalar_tensor_tensor(
                            wb_[bo][:, kc * 128:kc * 128 + 8], wlo[bo][:, kc * 8:(kc + 1) * 8],
                            mrep[:, hh:hh + 1], wb_[bo][:, kc * 128:kc * 128 + 8],
                            ALU.mult, ALU.add), deps=[tdiff, tmre])
                    t_lo_free[bo] = t_bl
                    t_mm = None
                    for tt in range(NTT):
                        ps = psqs[i % 2]
                        st_ = stg[i % 2]
                        for kc in range(16):
                            fn = p.pe.op if kc == 15 else p.pe.op_quiet
                            t_mm = fn(lambda t, kc=kc, tt=tt, ps=ps, bo=bo, s=(kc == 0), e=(kc == 15):
                                      t.matmul(ps[:], wb_[bo][:, kc * 128:(kc + 1) * 128],
                                               xns(kc, tt), start=s, stop=e),
                                      deps=([tw, t_bl, t_ev[i % 2]] if kc == 0 else ()))
                        t_e = p.dve.op(lambda v, ps=ps, st_=st_, hh=hh: v.tensor_scalar(
                            st_[:], ps[:], csq[:, hh:hh + 1], None, ALU.mult),
                            deps=[t_mm, t_dma[i % 2], tcse])
                        t_ev[i % 2] = t_e
                        t_dma[i % 2] = p.dma_hw(p.sp, f"sF{i % 2}",
                                                outv[hh][:, tt * TT:(tt + 1) * TT], st_[:],
                                                deps=[t_e])
                        i += 1
                    t_w_free[bo] = t_mm
            wv_sb = sb("wv_sb", [128, 16 * TT], BF16, s4)
            t_vw_free = None
            for vg in range(4):
                t_vw = p.dma_sw("wvF", wv_sb[:],
                                wavT.rearrange("(kc p) o -> p kc o", p=128)[:, :, vg * TT:(vg + 1) * TT],
                                deps=[t_vw_free])
                t_mm = None
                for tsb in range(16):
                    ps = psqs[tsb % 2]
                    st_ = stg[tsb % 2]
                    for kc in range(16):
                        fn = p.pe.op if kc == 15 else p.pe.op_quiet
                        t_mm = fn(lambda t, kc=kc, tsb=tsb, vg=vg, ps=ps, s=(kc == 0), e=(kc == 15):
                                  t.matmul(ps[:], xn[:, kc * TPC + tsb * 128:kc * TPC + (tsb + 1) * 128],
                                           wv_sb[:, kc * TT:(kc + 1) * TT], start=s, stop=e),
                                  deps=([t_vw, t_ev[tsb % 2]] if kc == 0 else ()))
                    t_e = p.act.op(lambda s, ps=ps, st_=st_: s.copy(st_[:], ps[:]),
                                   deps=[t_mm, t_dma[tsb % 2]])
                    t_ev[tsb % 2] = t_e
                    t_dma[tsb % 2] = p.dma_hw(p.sp, f"vF{tsb % 2}",
                                              v_o[tsb * 128:(tsb + 1) * 128, vg * TT:(vg + 1) * TT],
                                              st_[:], deps=[t_e])
                t_vw_free = t_mm
        p.finish()
        p.emit()
    return nc


def build_G1():
    T = 8192
    nc = bass.Bass("TRN2", target_bir_lowering=False)
    qh4 = nc.dram_tensor("qh4", [512, T], BF16, kind="ExternalInput").ap()
    kh4 = nc.dram_tensor("kh4", [512, T], BF16, kind="ExternalInput").ap()
    vh4 = nc.dram_tensor("vh4", [T, 512], BF16, kind="ExternalInput").ap()
    trineg = nc.dram_tensor("trineg", [128, 128], F32, kind="ExternalInput").ap()
    identf = nc.dram_tensor("identf", [128, 128], F32, kind="ExternalInput").ap()
    outF_o = nc.dram_tensor("outF_o", [512, T], BF16, kind="ExternalOutput").ap()
    SCALE = 1.0 / np.sqrt(128.0)

    with ExitStack() as ctx:
        def sb(nm, shp, dt, stack=None):
            return (stack or ctx).enter_context(nc.sbuf_tensor(nm, shp, dt))
        p = Prog(nc, ctx)
        idf = sb("idfG", [128, 128], F32)
        tri = sb("triG", [128, 128], F32)
        onesb = sb("onesbG", [128, 1], BF16)
        onesr = sb("onesrG", [1, 128], BF16)
        onesrf = sb("onesrfG", [1, 128], F32)
        t_ob = p.dve.op(lambda v: v.memset(onesb[:], 1.0))
        t_or = p.dve.op(lambda v: v.memset(onesr[:], 1.0))
        t_orf = p.dve.op(lambda v: v.memset(onesrf[:], 1.0))
        t_idf = p.dma_hw(p.sp, "idf", idf[:], identf)
        t_tri = p.dma_hw(p.sp, "tri", tri[:], trineg)

        Kh = sb("Kh", [128, T], BF16)
        Vh = sb("Vh", [128, T], BF16)
        Qall = sb("Qall", [128, T], BF16)
        sqh = sb("sqh", [128, T], BF16)
        Pb = [sb(f"Pb{i}", [128, TT], BF16) for i in range(2)]
        lrow = sb("lrowG", [1, TT], F32)
        lrowb = sb("lrowbG", [1, TT], BF16)
        rb = sb("rbG", [128, TT], BF16)
        mrow = sb("mrowG", [1, 4], F32)
        mcol = sb("mcolG", [128, 1], F32)
        negMh = sb("negMhG", [128, 1], F32)
        oste = sb("osteG", [128, TT], BF16)

        psS = [ctx.enter_context(nc.psum_tensor(f"psSG{i}", [128, TT], F32)) for i in range(2)]
        psO = ctx.enter_context(nc.psum_tensor("psOG", [128, TT], F32))
        psL = ctx.enter_context(nc.psum_tensor("psLG", [1, TT], F32))
        psM = ctx.enter_context(nc.psum_tensor("psMG", [128, 64], F32))
        psB = ctx.enter_context(nc.psum_tensor("psBG", [128, TT], F32))

        t_head_done = None
        t_l_ev = None
        t_o_ev = None
        t_rb_ev = None
        t_ofdma = None
        for i4 in range(4):
            tkl = p.dma_hw(p.sp, "khl", Kh[:], kh4[i4 * 128:(i4 + 1) * 128, :],
                           deps=[t_head_done])
            tvl = p.dma_hw(p.sp, "vhl", Vh[:],
                           vh4.rearrange("(kb p) d -> p kb d", p=128)[:, :, i4 * 128:(i4 + 1) * 128],
                           deps=[t_head_done])
            tq0 = p.dma_hw(p.sp, "qhl", Qall[:], qh4[i4 * 128:(i4 + 1) * 128, :],
                           deps=[t_head_done])
            tql = p.dve.op(lambda v: v.tensor_scalar_mul(Qall[:], Qall[:], SCALE),
                           deps=[tq0, t_head_done])
            tsq = p.act.op(lambda s: s.activation(sqh[:], Kh[:], AF.Square),
                           deps=[tkl, t_head_done])
            t_mx = None
            for kb in range(64):
                fn = p.pe.op if kb == 63 else p.pe.op_quiet
                t_mx = fn(lambda t, kb=kb: t.matmul(psM[:, kb:kb + 1],
                                                    sqh[:, kb * 128:(kb + 1) * 128], onesb[:],
                                                    start=True, stop=True),
                          deps=[tsq, t_ob, t_head_done] if kb == 0 else ())
            tmk1 = p.dve.op(lambda v: v.tensor_reduce(mcol[:], psM[:, 0:64], AX.X, ALU.max),
                            deps=[t_mx])
            ttk = p.pe.op(lambda t: t.transpose(psS[0][0:1, 0:128], mcol[:], idf[:]),
                          deps=[tmk1, t_idf])
            tmk2 = p.dve.op(lambda v: v.tensor_reduce(mrow[0:1, 1:2], psS[0][0:1, 0:128], AX.X,
                                                      ALU.max), deps=[ttk])
            tsq2 = p.act.op(lambda s: s.activation(sqh[:], Qall[:], AF.Square),
                            deps=[tql, t_mx])
            t_mq = None
            for kb in range(64):
                fn = p.pe.op if kb == 63 else p.pe.op_quiet
                t_mq = fn(lambda t, kb=kb: t.matmul(psM[:, kb:kb + 1],
                                                    sqh[:, kb * 128:(kb + 1) * 128], onesb[:],
                                                    start=True, stop=True),
                          deps=[tsq2, tmk1] if kb == 0 else ())
            tmq1 = p.dve.op(lambda v: v.tensor_reduce(mcol[:], psM[:, 0:64], AX.X, ALU.max),
                            deps=[t_mq, tmk2])
            ttq = p.pe.op(lambda t: t.transpose(psS[0][0:1, 0:128], mcol[:], idf[:]),
                          deps=[tmq1, tmk2])
            tmq2 = p.dve.op(lambda v: v.tensor_reduce(mrow[0:1, 0:1], psS[0][0:1, 0:128], AX.X,
                                                      ALU.max), deps=[ttq])
            tm3 = p.dve.op(lambda v: v.tensor_tensor(mrow[0:1, 2:3], mrow[0:1, 0:1],
                                                     mrow[0:1, 1:2], ALU.mult), deps=[tmq2])
            tm4 = p.act.op(lambda s: s.activation(mrow[0:1, 2:3], mrow[0:1, 2:3], AF.Sqrt),
                           deps=[tm3])
            tm5 = p.dve.op(lambda v: v.tensor_scalar_mul(mrow[0:1, 2:3], mrow[0:1, 2:3], -1.0),
                           deps=[tm4])
            tbc = p.pe.op(lambda t: t.matmul(psB[:, 0:1], onesrf[:], mrow[0:1, 2:3],
                                             start=True, stop=True), deps=[tm5, t_orf, t_rb_ev])
            t_nm = p.dve.op(lambda v: v.tensor_copy(negMh[:], psB[:, 0:1]), deps=[tbc])

            for qt in range(16):
                nkb = 4 * qt + 4
                t_exp = [None, None]
                t_av = [None, None, None]
                for kb in range(nkb):
                    sb_ = kb % 2
                    tmm = p.pe.op(lambda t, kb=kb, qt=qt, sb_=sb_:
                                  t.matmul(psS[sb_][:], Kh[:, kb * 128:(kb + 1) * 128],
                                           Qall[:, qt * TT:(qt + 1) * TT], start=True, stop=True),
                                  deps=[tql, t_nm, t_exp[sb_]])
                    dsub = kb - 4 * qt
                    if dsub < 0:
                        texp = p.act.op(lambda s, sb_=sb_: s.activation(
                            Pb[sb_][:], psS[sb_][:], AF.Exp, bias=negMh[:]),
                            deps=[tmm, t_av[sb_]])
                    else:
                        ops = []
                        if dsub > 0:
                            ops.append(p.dve.op(lambda v, sb_=sb_, dsub=dsub: v.memset(
                                Pb[sb_][:, 0:dsub * 128], 0.0), deps=[t_av[sb_]]))
                        td = p.dve.op(lambda v, sb_=sb_, dsub=dsub: v.tensor_tensor(
                            psS[sb_][:, dsub * 128:(dsub + 1) * 128],
                            psS[sb_][:, dsub * 128:(dsub + 1) * 128], tri[:], ALU.add),
                            deps=[tmm, t_tri, t_av[sb_]])
                        ops.append(td)
                        texp = p.act.op(lambda s, sb_=sb_, dsub=dsub: s.activation(
                            Pb[sb_][:, dsub * 128:], psS[sb_][:, dsub * 128:], AF.Exp,
                            bias=negMh[:]),
                            deps=ops)
                    tlm = p.pe.op(lambda t, kb=kb, s=(kb == 0), e=(kb == nkb - 1):
                                  t.matmul(psL[:], onesb[:], Pb[kb % 2][:], start=s, stop=e),
                                  deps=[texp] + ([t_l_ev] if kb == 0 else []))
                    tom = p.pe.op(lambda t, kb=kb, s=(kb == 0), e=(kb == nkb - 1):
                                  t.matmul(psO[:], Vh[:, kb * 128:(kb + 1) * 128], Pb[kb % 2][:],
                                           start=s, stop=e),
                                  deps=[tvl] + ([t_o_ev] if kb == 0 else []))
                    t_exp[sb_] = texp
                    t_av[sb_] = tom
                t_l1 = p.dve.op(lambda v: v.tensor_copy(lrow[:], psL[:]), deps=[tlm])
                t_l_ev = t_l1
                t_l2 = p.dve.op(lambda v: v.reciprocal(lrow[:], lrow[:]), deps=[t_l1])
                t_l3 = p.dve.op(lambda v: v.tensor_copy(lrowb[:], lrow[:]), deps=[t_l2])
                t_bc2 = p.pe.op(lambda t: t.matmul(psB[:], onesr[:], lrowb[:], start=True,
                                                   stop=True), deps=[t_l3, t_or, t_rb_ev])
                t_rb = p.act.op(lambda s: s.copy(rb[:], psB[:]), deps=[t_bc2])
                t_rb_ev = t_rb
                t_of = p.dve.op(lambda v: v.tensor_tensor(oste[:], psO[:], rb[:], ALU.mult),
                                deps=[t_rb, tom, t_ofdma])
                t_o_ev = t_of
                t_ofdma = p.dma_hw(p.sp, f"of{qt % 2}",
                                   outF_o[i4 * 128:(i4 + 1) * 128, qt * TT:(qt + 1) * TT],
                                   oste[:], deps=[t_of])
            t_head_done = t_of
        p.finish()
        p.emit()
    return nc


def build_G2():
    nc = bass.Bass("TRN2", target_bir_lowering=False)
    x3_in = nc.dram_tensor("x3_in", [C, TPC], F32, kind="ExternalInput").ap()
    outF_in = nc.dram_tensor("outF_in", [C, TPC], BF16, kind="ExternalInput").ap()
    wp_in = nc.dram_tensor("wp_in", [C, C], F32, kind="ExternalInput").ap()
    identf = nc.dram_tensor("identf", [128, 128], F32, kind="ExternalInput").ap()
    out_o = nc.dram_tensor("out_o", [TPC, C], F32, kind="ExternalOutput").ap()
    x3v = x3_in.rearrange("(c p) t -> c p t", p=128)
    ofv = outF_in.rearrange("(c p) t -> c p t", p=128)

    with ExitStack() as ctx:
        def sb(nm, shp, dt, stack=None):
            return (stack or ctx).enter_context(nc.sbuf_tensor(nm, shp, dt))
        p = Prog(nc, ctx)
        idf = sb("idfG2", [128, 128], F32)
        t_idf = p.dma_hw(p.sp, "idf", idf[:], identf)
        ofb = sb("ofbG2", [128, 16 * TT], BF16)
        wpb = [sb(f"wpbG2{i}", [128, 16 * 128], BF16) for i in range(2)]
        x3t = [sb(f"x3tG2{i}", [128, TT], F32) for i in range(2)]
        x4b = sb("x4bG2", [128, 16 * TT], F32)
        tmt = sb("tmtG2", [128, C], F32)

        def w_cols(wT, oc, width=128):
            return wT.rearrange("(kc p) o -> p kc o", p=128)[:, :, oc * width:(oc + 1) * width]

        with nc.psum_tensor("psp0G2", [128, TT], F32) as psp0, \
             nc.psum_tensor("psp1G2", [128, TT], F32) as psp1, \
             nc.psum_tensor("pst0G2", [128, 128], F32) as pst0, \
             nc.psum_tensor("pst1G2", [128, 128], F32) as pst1:
            psps = [psp0, psp1]
            psts = [pst0, pst1]
            t_wfree = [None, None]
            t_x3l = [None, None]
            t_x4w = [None, None]
            t_tp_ev = [None, None]
            t_tmdma = None
            t_of_used = None
            t_x4u = None
            for qt in range(NTT):
                t_ofl = []
                for c in range(16):
                    t_ofl.append(p.dma_hw(p.sp, f"ofl{c % 4}", ofb[:, c * TT:(c + 1) * TT],
                                          ofv[c][:, qt * TT:(qt + 1) * TT],
                                          deps=[t_of_used, t_ofl[c - 4] if c >= 4 else None]))
                for oc in range(16):
                    bo = oc % 2
                    tw = p.dma_sw(f"wpG{bo}", wpb[bo][:], w_cols(wp_in, oc), deps=[t_wfree[bo]])
                    ps = psps[bo]
                    t_mm = None
                    for kc in range(16):
                        fn = p.pe.op if kc == 15 else p.pe.op_quiet
                        t_mm = fn(lambda t, kc=kc, oc=oc, ps=ps, bo=bo, s=(kc == 0), e=(kc == 15):
                                  t.matmul(ps[:], wpb[bo][:, kc * 128:(kc + 1) * 128],
                                           ofb[:, kc * TT:(kc + 1) * TT], start=s, stop=e),
                                  deps=([tw, t_x4w[bo]] + t_ofl if kc == 0 else ()))
                    t_wfree[bo] = t_mm
                    t_of_used = t_mm
                    tx3 = p.dma_hw(p.sp, f"x3G{bo}", x3t[bo][:],
                                   x3v[oc][:, qt * TT:(qt + 1) * TT], deps=[t_x3l[bo]])
                    txa = p.dve.op(lambda v, oc=oc, bo=bo, ps=ps: v.tensor_tensor(
                        x4b[:, oc * TT:(oc + 1) * TT], x3t[bo][:], ps[:], ALU.add),
                        deps=[t_mm, tx3, t_x4u])
                    t_x3l[bo] = txa
                    t_x4w[bo] = txa
                t_x4done = txa
                for tb in range(4):
                    for oc in range(16):
                        bo2 = oc % 2
                        ttp = p.pe.op(lambda t, oc=oc, tb=tb, bo2=bo2:
                                      t.transpose(psts[bo2][:],
                                                  x4b[:, oc * TT + tb * 128: oc * TT + (tb + 1) * 128],
                                                  idf[:]),
                                      deps=[t_x4done, t_idf, t_tp_ev[bo2]])
                        t_tp_ev[bo2] = p.act.op(lambda s, oc=oc, bo2=bo2:
                                                s.copy(tmt[:, oc * 128:(oc + 1) * 128],
                                                       psts[bo2][:]),
                                                deps=[ttp, t_tmdma])
                    t_tmdma = p.dma_hw(p.sp, f"outG{tb % 2}",
                                       out_o[qt * TT + tb * 128: qt * TT + (tb + 1) * 128, :],
                                       tmt[:], deps=[t_tp_ev[0], t_tp_ev[1]])
                    t_x4u = t_tmdma
        p.finish()
        p.emit()
    return nc


def blocks_for_core(core):
    b, m = divmod(core, 4)
    return b, [4 * j + m for j in range(16)]


def shard_x(x):
    out = []
    for core in range(NCORE):
        b, blks = blocks_for_core(core)
        rows = np.concatenate([np.asarray(x[b, blk * 128:(blk + 1) * 128, :]) for blk in blks], axis=0)
        out.append(np.ascontiguousarray(rows, dtype=np.float32))
    return out


def make_A_inmaps(inputs):
    x = np.asarray(inputs["L0_attn_x"])
    w_qkv = np.asarray(inputs["L0_attn_w_qkv"])
    wqT = np.ascontiguousarray(w_qkv[0:C].T)
    wkT = np.ascontiguousarray(w_qkv[C:2 * C].T)
    wvT = np.ascontiguousarray(w_qkv[2 * C:].T)
    identf = np.eye(128, dtype=np.float32)
    identb = np.eye(128, dtype=np.float32).astype(BF)
    gup = np.asarray(inputs["L0_ffn_gate_up_packed"])
    gsc = np.asarray(inputs["L0_ffn_gate_up_scales"])
    dwp = np.asarray(inputs["L0_ffn_down_packed"])
    dsc = np.asarray(inputs["L0_ffn_down_scales"])
    xs = shard_x(x)
    maps = []
    for core in range(NCORE):
        r0, r1 = core * 1408, (core + 1) * 1408
        d0, d1 = core * 256, (core + 1) * 256
        maps.append({
            "x_in": xs[core],
            "wqT": wqT, "wkT": wkT, "wvT": wvT,
            "identf": identf, "identb": identb,
            "gup": np.ascontiguousarray(gup[r0:r1]),
            "gup_sc": np.ascontiguousarray(gsc[:, r0:r1].T),
            "dwp": np.ascontiguousarray(dwp[d0:d1]),
            "dwp_sc": np.ascontiguousarray(dsc[:, d0:d1].T),
        })
    return maps


def make_F_inmaps(inputs, c_outs, e_outs):
    Wa = np.asarray(inputs["L1_ffn_Wa"])
    W_out = np.asarray(inputs["L1_attn_W_out"])
    w_std = np.asarray(inputs["L1_ffn_w_std"])
    w_rec = np.asarray(inputs["L1_ffn_w_rec"])
    H, Ds, R = 16, 120, 8
    q_std = Wa[:, :H * Ds]
    k_std = Wa[:, H * Ds:2 * H * Ds]
    v_w = Wa[:, 2 * H * Ds:2 * H * Ds + 2048]
    q_low = Wa[:, 2 * H * Ds + 2048:2 * H * Ds + 2048 + 128]
    k_low = Wa[:, 2 * H * Ds + 2048 + 128:]
    waq = np.concatenate([np.concatenate([q_low[:, h * 8:(h + 1) * 8],
                                          q_std[:, h * Ds:(h + 1) * Ds]], 1) for h in range(H)], 1)
    wak = np.concatenate([np.concatenate([k_low[:, h * 8:(h + 1) * 8],
                                          k_std[:, h * Ds:(h + 1) * Ds]], 1) for h in range(H)], 1)
    patlow = np.broadcast_to((np.arange(128) < 8).astype(np.float32), (16, 128)).copy()
    pathigh = np.broadcast_to((np.arange(128) >= 8).astype(np.float32), (16, 128)).copy()
    maps = []
    og_fm = {}
    for bb in range(2):
        og_full = np.concatenate([np.asarray(e_outs[4 * bb + mm2]["og_o"]).view(BF)
                                  for mm2 in range(4)], axis=1)
        og_fm[bb] = np.ascontiguousarray(og_full.T)
    for core in range(NCORE):
        bb, blks = blocks_for_core(core)
        og_c = np.concatenate([og_fm[bb][:, blk * 128:(blk + 1) * 128] for blk in blks], axis=1)
        maps.append({
            "x2_in": np.ascontiguousarray(c_outs[core]["x2_o"]),
            "og_in": np.ascontiguousarray(og_c),
            "woT": np.ascontiguousarray(W_out.T),
            "waqT": np.ascontiguousarray(waq), "wakT": np.ascontiguousarray(wak),
            "waqlT": np.ascontiguousarray(q_low), "waklT": np.ascontiguousarray(k_low),
            "wavT": np.ascontiguousarray(v_w),
            "wrec": np.ascontiguousarray(w_rec[:, None]),
            "wstd": np.ascontiguousarray(w_std[:, None]),
            "patlow": patlow, "pathigh": pathigh,
            "ident16": np.eye(16, dtype=np.float32),
        })
    return maps


def make_G1_inmaps(inputs, f_outs):
    sidx = np.arange(128)
    trineg = np.where(sidx[:, None] <= sidx[None, :], 0.0, -1e30).astype(np.float32)
    full = {}
    for bb in range(2):
        kf = np.zeros((2048, 8192), dtype=BF)
        qf = np.zeros((2048, 8192), dtype=BF)
        vf = np.zeros((8192, 2048), dtype=BF)
        for mm2 in range(4):
            core = 4 * bb + mm2
            kc = np.asarray(f_outs[core]["kf_o"]).view(BF)
            qc = np.asarray(f_outs[core]["qf_o"]).view(BF)
            vc = np.asarray(f_outs[core]["v_o"]).view(BF)
            for j in range(16):
                blk = 4 * j + mm2
                kf[:, blk * 128:(blk + 1) * 128] = kc[:, j * 128:(j + 1) * 128]
                qf[:, blk * 128:(blk + 1) * 128] = qc[:, j * 128:(j + 1) * 128]
                vf[blk * 128:(blk + 1) * 128, :] = vc[j * 128:(j + 1) * 128, :]
        full[bb] = (qf, kf, vf)
    maps = []
    for core in range(NCORE):
        bb, mm2 = divmod(core, 4)
        qf, kf, vf = full[bb]
        maps.append({
            "qh4": np.ascontiguousarray(qf[mm2 * 512:(mm2 + 1) * 512]),
            "kh4": np.ascontiguousarray(kf[mm2 * 512:(mm2 + 1) * 512]),
            "vh4": np.ascontiguousarray(vf[:, mm2 * 512:(mm2 + 1) * 512]),
            "trineg": trineg,
            "identf": np.eye(128, dtype=np.float32),
        })
    return maps


def make_G2_inmaps(inputs, f_outs, g1_outs):
    Wp = np.asarray(inputs["L1_ffn_Wp"])
    maps = []
    ofm = {}
    for bb in range(2):
        ofm[bb] = np.concatenate([np.asarray(g1_outs[4 * bb + mm2]["outF_o"]).view(BF)
                                  for mm2 in range(4)], axis=0)
    for core in range(NCORE):
        bb, blks = blocks_for_core(core)
        of_c = np.concatenate([ofm[bb][:, blk * 128:(blk + 1) * 128] for blk in blks], axis=1)
        maps.append({
            "x3_in": np.ascontiguousarray(f_outs[core]["x3_o"]),
            "outF_in": np.ascontiguousarray(of_c),
            "wp_in": Wp,
            "identf": np.eye(128, dtype=np.float32),
        })
    return maps


def kernel(**inputs):
    a_maps = make_A_inmaps(inputs)
    ncA = build_A()
    a_outs = run_bass_kernel_spmd(ncA, a_maps, core_ids=list(range(NCORE))).results
    b_maps = make_B_inmaps(inputs, a_outs)
    ncB = build_B()
    b_outs = run_bass_kernel_spmd(ncB, b_maps, core_ids=list(range(NCORE))).results
    c_maps = make_C_inmaps(inputs, a_outs, b_outs)
    ncC = build_C()
    c_outs = run_bass_kernel_spmd(ncC, c_maps, core_ids=list(range(NCORE))).results
    d_maps = make_D_inmaps(inputs, c_outs)
    ncD = build_D()
    d_outs = run_bass_kernel_spmd(ncD, d_maps, core_ids=list(range(NCORE))).results
    e_maps = make_E_inmaps(inputs, d_outs)
    ncE = build_E()
    e_outs = run_bass_kernel_spmd(ncE, e_maps, core_ids=list(range(NCORE))).results
    f_maps = make_F_inmaps(inputs, c_outs, e_outs)
    ncF = build_F()
    f_outs = run_bass_kernel_spmd(ncF, f_maps, core_ids=list(range(NCORE))).results
    g1_maps = make_G1_inmaps(inputs, f_outs)
    ncG1 = build_G1()
    g1_outs = run_bass_kernel_spmd(ncG1, g1_maps, core_ids=list(range(NCORE))).results
    g2_maps = make_G2_inmaps(inputs, f_outs, g1_outs)
    ncG2 = build_G2()
    g2_outs = run_bass_kernel_spmd(ncG2, g2_maps, core_ids=list(range(NCORE))).results
    out = np.zeros((2, 8192, C), dtype=np.float32)
    for core in range(NCORE):
        bb, blks = blocks_for_core(core)
        shard = np.asarray(g2_outs[core]["out_o"])
        for j, blk in enumerate(blks):
            out[bb, blk * 128:(blk + 1) * 128, :] = shard[j * 128:(j + 1) * 128, :]
    return out


# version 03
# speedup vs baseline: 2.3794x; 1.0757x over previous
import numpy as np
import ml_dtypes
from contextlib import ExitStack

import concourse.bass as bass
import concourse.mybir as mybir
from concourse.bass_utils import run_bass_kernel_spmd

F32 = mybir.dt.float32
F16 = mybir.dt.float16
BF16 = mybir.dt.bfloat16
I32 = mybir.dt.int32
U16 = mybir.dt.uint16
AF = mybir.ActivationFunctionType
ALU = mybir.AluOpType
AX = mybir.AxisListType

BF = ml_dtypes.bfloat16

C = 2048
TPC = 2048
NCORE = 8
NBLK = 16
TT = 512
NTT = TPC // TT


class EQ:
    SERIALIZED = {"q_dve", "q_act", "q_pool"}

    def __init__(self, prog, name):
        self.prog = prog
        self.name = name
        self.cnt = 0
        self.waited = {}
        self.items = []
        self.serialize = name in self.SERIALIZED

    def _wait(self, tag):
        if tag is None:
            return
        sem_name, val = tag
        if val <= 0:
            return
        if self.waited.get(sem_name, 0) >= val:
            return
        self.waited[sem_name] = val

        def w(e, sn=sem_name, v=val):
            e.wait_ge(self.prog.sems[sn], v)
        self.items.append(w)

    def op(self, fn, deps=()):
        if self.serialize and self.cnt > 0:
            self._wait((self.name, self.cnt))
        for d in deps:
            self._wait(d)
        self.cnt += 1

        def g(e, fn=fn):
            fn(e).then_inc(self.prog.sems[self.name], 1)
        self.items.append(g)
        return (self.name, self.cnt)

    def op_quiet(self, fn, deps=()):
        for d in deps:
            self._wait(d)
        self.items.append(lambda e, fn=fn: fn(e))
        return None

    def tag(self):
        return (self.name, self.cnt)


class Prog:
    def __init__(self, nc, ctx):
        self.nc = nc
        self.ctx = ctx
        self.sems = {}
        self.pe = EQ(self, "q_pe")
        self.act = EQ(self, "q_act")
        self.dve = EQ(self, "q_dve")
        self.pool = EQ(self, "q_pool")
        self.sp = EQ(self, "q_sp")
        self.queues = [self.pe, self.act, self.dve, self.pool, self.sp]
        for q in self.queues:
            self.sems[q.name] = ctx.enter_context(nc.semaphore(q.name))
        self.dma_cnt = {}
        self._consts = {}
        self._const_tile = ctx.enter_context(nc.sbuf_tensor("constf_tile", [128, 16], F32))
        self._const_n = 0

    def constf(self, val):
        if val in self._consts:
            return self._consts[val]
        i = self._const_n
        self._const_n += 1
        assert i < 16
        ap = self._const_tile[:, i:i + 1]
        tag = self.dve.op(lambda v, ap=ap, val=val: v.memset(ap, float(val)))
        self._consts[val] = (ap, tag)
        return ap, tag

    def _dma_chan(self, ch):
        key = "dch_" + ch
        if key not in self.sems:
            self.sems[key] = self.ctx.enter_context(self.nc.semaphore(key))
            self.dma_cnt[key] = 0
        return key

    def dma_hw(self, q, ch, out, in_, deps=(), transpose=False):
        key = self._dma_chan(ch)
        for d in deps:
            q._wait(d)
        self.dma_cnt[key] += 16
        val = self.dma_cnt[key]

        def g(e, out=out, in_=in_, t=transpose, key=key):
            e.dma_start(out=out, in_=in_, transpose=t).then_inc(self.sems[key], 16)
        q.items.append(g)
        return (key, val)

    def dma_sw(self, ch, out, in_, deps=()):
        key = self._dma_chan(ch)
        for d in deps:
            self.pool._wait(d)
        self.dma_cnt[key] += 16
        val = self.dma_cnt[key]

        def g(e, out=out, in_=in_, key=key):
            e.dma_start(out=out, in_=in_).then_inc(self.sems[key], 16)
        self.pool.items.append(g)
        return (key, val)

    def barrier(self):
        tags = [q.tag() for q in self.queues]
        tags += [(k, v) for k, v in self.dma_cnt.items()]
        for q in self.queues:
            for t in tags:
                if t[0] != q.name:
                    q._wait(t)

    def finish(self):
        for k, v in self.dma_cnt.items():
            self.sp._wait((k, v))

    def emit(self):
        block = self.ctx.enter_context(self.nc.Block())

        def runner(lst):
            def f(eng):
                for it in lst:
                    it(eng)
            return f
        block.tensor(runner(self.pe.items))
        block.scalar(runner(self.act.items))
        block.vector(runner(self.dve.items))
        block.gpsimd(runner(self.pool.items))
        block.sync(runner(self.sp.items))


def build_A():
    nc = bass.Bass("TRN2", target_bir_lowering=False)
    x_in = nc.dram_tensor("x_in", [TPC, C], F32, kind="ExternalInput").ap()
    wqT = nc.dram_tensor("wqT", [C, C], F32, kind="ExternalInput").ap()
    wkT = nc.dram_tensor("wkT", [C, C], F32, kind="ExternalInput").ap()
    wvT = nc.dram_tensor("wvT", [C, C], F32, kind="ExternalInput").ap()
    identf = nc.dram_tensor("identf", [128, 128], F32, kind="ExternalInput").ap()
    identb = nc.dram_tensor("identb", [128, 128], BF16, kind="ExternalInput").ap()
    gup = nc.dram_tensor("gup", [1408, 256], I32, kind="ExternalInput").ap()
    gup_sc = nc.dram_tensor("gup_sc", [1408, 16], F16, kind="ExternalInput").ap()
    dwp = nc.dram_tensor("dwp", [256, 704], I32, kind="ExternalInput").ap()
    dwp_sc = nc.dram_tensor("dwp_sc", [256, 44], F16, kind="ExternalInput").ap()

    q_o = nc.dram_tensor("q_o", [C, TPC], BF16, kind="ExternalOutput").ap()
    vk_o = nc.dram_tensor("vk_o", [128, 16 * 33], F32, kind="ExternalOutput").ap()
    wgu_o = nc.dram_tensor("wgu_o", [1408, C], BF16, kind="ExternalOutput").ap()
    wdn_o = nc.dram_tensor("wdn_o", [256, 5632], BF16, kind="ExternalOutput").ap()

    qv = q_o.rearrange("(oc p) t -> oc p t", p=128)

    with ExitStack() as ctx:
        def sb(nm, shp, dt, stack=None):
            return (stack or ctx).enter_context(nc.sbuf_tensor(nm, shp, dt))
        p = Prog(nc, ctx)

        idf = sb("idf", [128, 128], F32)
        idb = sb("idb", [128, 128], BF16)
        t_idf = p.dma_hw(p.sp, "idf", idf[:], identf)
        t_idb = p.dma_hw(p.sp, "idb", idb[:], identb)

        xb = sb("xb", [128, 16 * TPC], BF16)

        def xbs(c, tt, w=TT):
            return xb[:, c * TPC + tt * w: c * TPC + (tt + 1) * w]

        with ExitStack() as sctx, \
             nc.psum_tensor("pst0", [128, 128], F32) as pst0, \
             nc.psum_tensor("pst1", [128, 128], F32) as pst1:
            xtm = [sb(f"xtm{i}", [128, C], F32, sctx) for i in range(2)]
            psts = [pst0, pst1]
            t_blk_done = [None, None]
            t_evac = [None, None]
            for blk in range(NBLK):
                buf = xtm[blk % 2]
                tl = p.dma_hw(p.sp, f"xtm{blk % 2}", buf[:], x_in[blk * 128:(blk + 1) * 128, :],
                              deps=[t_blk_done[blk % 2]])
                last_tp = None
                for c in range(16):
                    ps = psts[c % 2]
                    last_tp = p.pe.op(lambda t, ps=ps, buf=buf, c=c:
                                      t.transpose(ps[:], buf[:, c * 128:(c + 1) * 128], idf[:]),
                                      deps=[tl, t_idf, t_evac[c % 2]])
                    t_evac[c % 2] = p.dve.op(
                        lambda v, ps=ps, c=c, blk=blk:
                        v.tensor_copy(xb[:, c * TPC + blk * 128: c * TPC + (blk + 1) * 128], ps[:]),
                        deps=[last_tp])
                t_blk_done[blk % 2] = last_tp
        p.barrier()

        onesb = sb("onesb", [128, 1], BF16)
        onesr = sb("onesr", [1, 128], BF16)
        t_ob = p.dve.op(lambda v: v.memset(onesb[:], 1.0))
        t_or = p.dve.op(lambda v: v.memset(onesr[:], 1.0))
        s23 = ctx.enter_context(ExitStack())
        sqt = [sb(f"sqt{i}", [128, TT], BF16, s23) for i in range(2)]
        sumr = sb("sumr", [1, TPC], F32, s23)
        sqr = sb("sqr", [1, TPC], F32, s23)
        mu_b = sb("mu_b", [1, TPC], BF16, s23)
        rs_b = sb("rs_b", [1, TPC], BF16, s23)
        rowa = sb("rowa", [1, TPC], F32, s23)
        roww = sb("roww", [1, TPC], F32, s23)

        with nc.psum_tensor("pssum", [1, TT], F32) as pssum, \
             nc.psum_tensor("pssq", [1, TT], F32) as pssq:
            t_sum_evac = None
            t_sq_evac = None
            for tt in range(NTT):
                t_mm = None
                for c in range(16):
                    fn = p.pe.op if c == 15 else p.pe.op_quiet
                    t_mm = fn(lambda t, c=c, tt=tt, s=(c == 0), e=(c == 15):
                              t.matmul(pssum[:], onesb[:], xbs(c, tt), start=s, stop=e),
                              deps=[t_ob, t_sum_evac] if c == 0 else ())
                t_sum_evac = p.dve.op(lambda v, tt=tt:
                                      v.tensor_copy(sumr[0:1, tt * TT:(tt + 1) * TT], pssum[:]),
                                      deps=[t_mm])
                t_sqa = [None, None]
                t_mmq = None
                for c in range(16):
                    t_sqa[c % 2] = p.act.op(lambda s, c=c, tt=tt, b=c % 2:
                                            s.activation(sqt[b][:], xbs(c, tt), AF.Square),
                                            deps=[t_mmq])
                    t_mmq = p.pe.op(lambda t, c=c, s=(c == 0), e=(c == 15), b=c % 2:
                                    t.matmul(pssq[:], onesb[:], sqt[b][:], start=s, stop=e),
                                    deps=[t_sqa[c % 2]] + ([t_sq_evac] if c == 0 else []))
                t_sq_evac = p.dve.op(lambda v, tt=tt:
                                     v.tensor_copy(sqr[0:1, tt * TT:(tt + 1) * TT], pssq[:]),
                                     deps=[t_mmq])
        t1 = p.dve.op(lambda v: v.tensor_scalar_mul(rowa[:], sumr[:], 1.0 / C), deps=[t_sum_evac])
        t_mub = p.dve.op(lambda v: v.tensor_copy(mu_b[:], rowa[:]), deps=[t1])
        t2 = p.dve.op(lambda v: v.tensor_scalar_mul(roww[:], sqr[:], 1.0 / C), deps=[t_sq_evac])
        t3 = p.dve.op(lambda v: v.tensor_tensor(sumr[:], rowa[:], rowa[:], ALU.mult), deps=[t1, t2])
        t4 = p.dve.op(lambda v: v.tensor_sub(roww[:], roww[:], sumr[:]), deps=[t3])
        c_eps, tg_eps = p.constf(1e-5)
        t5 = p.act.op(lambda s: s.activation(rowa[:], roww[:], AF.Sqrt, bias=c_eps[0:1, :]),
                      deps=[t4, t_mub, tg_eps])
        t6 = p.dve.op(lambda v: v.reciprocal(roww[:], rowa[:]), deps=[t5])
        t_rsb = p.dve.op(lambda v: v.tensor_copy(rs_b[:], roww[:]), deps=[t6])
        p.barrier()

        mub_s = sb("mub_s", [128, TT], BF16, s23)
        rsb_s = sb("rsb_s", [128, TT], BF16, s23)
        tmp3 = sb("tmp3", [128, TT], BF16, s23)
        with nc.psum_tensor("psbc0", [128, TT], F32) as psbc0, \
             nc.psum_tensor("psbc1", [128, TT], F32) as psbc1:
            te1 = te2 = None
            tb = None
            for tt in range(NTT):
                tm1 = p.pe.op(lambda t, tt=tt: t.matmul(psbc0[:], onesr[:],
                                                        mu_b[0:1, tt * TT:(tt + 1) * TT],
                                                        start=True, stop=True),
                              deps=[t_mub, t_or, te1])
                tm2 = p.pe.op(lambda t, tt=tt: t.matmul(psbc1[:], onesr[:],
                                                        rs_b[0:1, tt * TT:(tt + 1) * TT],
                                                        start=True, stop=True),
                              deps=[t_rsb, te2])
                te1 = p.dve.op(lambda v: v.tensor_copy(mub_s[:], psbc0[:]), deps=[tm1, tb])
                te2 = p.dve.op(lambda v: v.tensor_copy(rsb_s[:], psbc1[:]), deps=[tm2])
                for c in range(16):
                    ta = p.dve.op(lambda v, c=c, tt=tt: v.tensor_sub(tmp3[:], xbs(c, tt), mub_s[:]),
                                  deps=[te1])
                    tb = p.dve.op(lambda v, c=c, tt=tt:
                                  v.tensor_tensor(xbs(c, tt), tmp3[:], rsb_s[:], ALU.mult),
                                  deps=[ta, te2])
        p.barrier()
        t_xn = p.dve.tag()
        s23.close()

        def w_cols(wT, oc, width=128):
            return wT.rearrange("(kc p) o -> p kc o", p=128)[:, :, oc * width:(oc + 1) * width]

        with ExitStack() as s4, \
             nc.psum_tensor("psq0", [128, TT], F32) as psq0, \
             nc.psum_tensor("psq1", [128, TT], F32) as psq1:
            wqb = [sb(f"wqb{i}", [128, 16 * 128], BF16, s4) for i in range(2)]
            qstg = [sb(f"qstg{i}", [128, TT], BF16, s4) for i in range(2)]
            psqs = [psq0, psq1]
            t_wq_free = [None, None]
            t_qdma = [None, None]
            t_evq = [None, None]
            i = 0
            for oc in range(16):
                tw = p.dma_sw(f"wqb{oc % 2}", wqb[oc % 2][:], w_cols(wqT, oc), deps=[t_wq_free[oc % 2]])
                t_mm = None
                for tt in range(NTT):
                    ps = psqs[i % 2]
                    stg = qstg[i % 2]
                    for kc in range(16):
                        fn = p.pe.op if kc == 15 else p.pe.op_quiet
                        t_mm = fn(lambda t, kc=kc, tt=tt, oc=oc, ps=ps, s=(kc == 0), e=(kc == 15):
                                  t.matmul(ps[:], wqb[oc % 2][:, kc * 128:(kc + 1) * 128],
                                           xbs(kc, tt), start=s, stop=e),
                                  deps=([tw, t_evq[i % 2]] if kc == 0 else ()))
                    t_ev = p.act.op(lambda s, ps=ps, stg=stg: s.activation(stg[:], ps[:], AF.Relu),
                                    deps=[t_mm, t_qdma[i % 2]])
                    t_evq[i % 2] = t_ev
                    t_qdma[i % 2] = p.dma_hw(p.sp, f"qst{i % 2}", qv[oc][:, tt * TT:(tt + 1) * TT],
                                             stg[:], deps=[t_ev])
                    i += 1
                t_wq_free[oc % 2] = t_mm
        p.barrier()

        s5 = ctx.enter_context(ExitStack())
        wkb = [sb(f"wkb{i}", [128, 16 * 128], BF16, s5) for i in range(2)]
        wvb = [sb(f"wvb{i}", [128, 16 * 128], BF16, s5) for i in range(2)]
        ktile = [sb(f"ktile{i}", [128, TT], BF16, s5) for i in range(2)]
        vtile = [sb(f"vtile{i}", [128, TT], BF16, s5) for i in range(2)]
        ktm = [sb(f"ktm{i}", [128, 128], BF16, s5) for i in range(2)]
        vtm = [sb(f"vtm{i}", [128, 132], BF16, s5) for i in range(2)]
        vk_sb = sb("vk_sb", [128, 16 * 33], F32, s5)
        t_vob = [p.dve.op(lambda v, b=b: v.memset(
            vtm[b][:].rearrange("p (h e) -> p h e", e=33)[:, :, 32:33], 1.0)) for b in range(2)]

        with nc.psum_tensor("psk", [128, TT], F32) as psk, \
             nc.psum_tensor("psv", [128, TT], F32) as psv, \
             nc.psum_tensor("pstk", [128, 128], BF16) as pstk, \
             nc.psum_tensor("pstv", [128, 128], BF16) as pstv, \
             nc.psum_tensor("psvk0", [128, 33], F32) as psvk0, \
             nc.psum_tensor("psvk1", [128, 33], F32) as psvk1:
            psvks = [psvk0, psvk1]
            t_wk_free = [None, None]
            t_wv_free = [None, None]
            t_kev = None
            t_vev = None
            t_ktm_ev_prev = None
            t_vtm_ev_prev = None
            t_vk_evac = [None, None]
            t_vk_acc = None
            jj = 0
            for cc in range(16):
                twk = p.dma_sw(f"wkb{cc % 2}", wkb[cc % 2][:], w_cols(wkT, cc), deps=[t_wk_free[cc % 2]])
                twv = p.dma_sw(f"wvb{cc % 2}", wvb[cc % 2][:], w_cols(wvT, cc), deps=[t_wv_free[cc % 2]])
                psvk = psvks[cc % 2]
                t_mmk = t_mmv = None
                first_vk = True
                tvk = None
                for tt in range(NTT):
                    kb = jj % 2
                    for kc in range(16):
                        fn = p.pe.op if kc == 15 else p.pe.op_quiet
                        t_mmk = fn(lambda t, kc=kc, tt=tt, cc=cc, s=(kc == 0), e=(kc == 15):
                                   t.matmul(psk[:], wkb[cc % 2][:, kc * 128:(kc + 1) * 128],
                                            xbs(kc, tt), start=s, stop=e),
                                   deps=([twk, t_kev] if kc == 0 else ()))
                    t_kev = p.act.op(lambda s, b=kb: s.activation(ktile[b][:], psk[:], AF.Relu),
                                     deps=[t_mmk])
                    for kc in range(16):
                        fn = p.pe.op if kc == 15 else p.pe.op_quiet
                        t_mmv = fn(lambda t, kc=kc, tt=tt, cc=cc, s=(kc == 0), e=(kc == 15):
                                   t.matmul(psv[:], wvb[cc % 2][:, kc * 128:(kc + 1) * 128],
                                            xbs(kc, tt), start=s, stop=e),
                                   deps=([twv, t_vev] if kc == 0 else ()))
                    t_vev = p.act.op(lambda s, b=kb: s.activation(vtile[b][:], psv[:], AF.Identity),
                                     deps=[t_mmv])
                    for j in range(4):
                        jb = j % 2
                        t_ktp = p.pe.op(lambda t, j=j, b=kb:
                                        t.transpose(pstk[:], ktile[b][:, j * 128:(j + 1) * 128], idb[:]),
                                        deps=[t_kev, t_idb, t_ktm_ev_prev])
                        t_ktm_ev = p.dve.op(lambda v, jb=jb: v.tensor_copy(ktm[jb][:], pstk[:]),
                                            deps=[t_ktp])
                        t_ktm_ev_prev = t_ktm_ev
                        t_vtp = p.pe.op(lambda t, j=j, b=kb:
                                        t.transpose(pstv[:], vtile[b][:, j * 128:(j + 1) * 128], idb[:]),
                                        deps=[t_vev, t_vtm_ev_prev])
                        t_vtm_ev = p.dve.op(
                            lambda v, jb=jb: v.tensor_copy(
                                vtm[jb][:].rearrange("p (h e) -> p h e", e=33)[:, :, 0:32],
                                pstv[:].rearrange("p (h e) -> p h e", e=32)),
                            deps=[t_vtp, t_vob[jb]])
                        t_vtm_ev_prev = t_vtm_ev
                        for h in range(4):
                            fn = p.pe.op if h == 3 else p.pe.op_quiet
                            tvk = fn(lambda t, h=h, jb=jb, psvk=psvk:
                                     t.matmul(psvk[32 * h:32 * h + 32, :],
                                              ktm[jb][:, 32 * h:32 * h + 32],
                                              vtm[jb][:, 33 * h:33 * h + 33],
                                              start=True, stop=True, tile_position=(0, 32 * h)),
                                     deps=([t_ktm_ev, t_vtm_ev] +
                                           ([t_vk_acc] if h == 0 else [])))
                        if first_vk:
                            t_vk_acc = p.dve.op(
                                lambda v, cc=cc, psvk=psvk:
                                v.tensor_copy(vk_sb[:, cc * 33:(cc + 1) * 33], psvk[:]),
                                deps=[tvk])
                        else:
                            t_vk_acc = p.dve.op(
                                lambda v, cc=cc, psvk=psvk:
                                v.scalar_tensor_tensor(vk_sb[:, cc * 33:(cc + 1) * 33],
                                                       psvk[:], 0.0, vk_sb[:, cc * 33:(cc + 1) * 33],
                                                       ALU.add, ALU.add),
                                deps=[tvk])
                        first_vk = False
                    jj += 1
                t_wk_free[cc % 2] = t_mmk
                t_wv_free[cc % 2] = t_mmv
                t_vk_evac[cc % 2] = t_vk_acc
            p.dma_hw(p.sp, "vko", vk_o, vk_sb[:], deps=[t_vk_evac[0], t_vk_evac[1]])
        p.barrier()
        s5.close()

        s6 = ctx.enter_context(ExitStack())
        wI = sb("wI", [128, 704], I32, s6)
        vj_t = sb("vj_t", [128, 704], I32, s6)
        sg_t = sb("sg_t", [128, 704], I32, s6)
        ub_mag = sb("ub_mag", [128, 5632], U16, s6)
        ub_sgn = sb("ub_sgn", [128, 5632], U16, s6)
        tq1 = sb("tq1", [128, 5632], BF16, s6)
        gq = sb("gq", [128, 5632], BF16, s6)
        fin = sb("fin", [128, 5632], U16, s6)
        sclh = sb("sclh", [128, 44], F16, s6)
        scl32 = sb("scl32", [128, 44], F32, s6)
        t_chunk_done = None
        for packed, scales, out_d, nch, W, G in [
            (gup, gup_sc, wgu_o, 11, 256, 16),
            (dwp, dwp_sc, wdn_o, 2, 704, 44),
        ]:
            E = W * 8
            for ch in range(nch):
                tw = p.dma_hw(p.sp, "wI", wI[:, 0:W], packed[ch * 128:(ch + 1) * 128, :],
                              deps=[t_chunk_done])
                tsc = p.dma_hw(p.sp, "sclh", sclh[:, 0:G], scales[ch * 128:(ch + 1) * 128, :],
                               deps=[t_chunk_done])
                tsc32 = p.dve.op(lambda v, G=G: v.tensor_copy(scl32[:, 0:G], sclh[:, 0:G]),
                                 deps=[tsc, t_chunk_done])
                t_mag = []
                t_sgn = []
                for j in range(8):
                    sh = 4 * j - 6
                    op0 = ALU.logical_shift_right if sh >= 0 else ALU.logical_shift_left
                    tv = p.dve.op(lambda v, W=W, sh=abs(sh), op0=op0:
                                  v.tensor_scalar(vj_t[:, 0:W], wI[:, 0:W], sh, 0x1C0,
                                                  op0, ALU.bitwise_and),
                                  deps=[tw, t_chunk_done])
                    t_mag.append(p.dve.op(
                        lambda v, j=j, W=W, E=E: v.tensor_scalar(
                            ub_mag[:, 0:E].rearrange("p (w j) -> p w j", j=8)[:, :, j],
                            vj_t[:, 0:W], 0x3F00, None, ALU.add),
                        deps=[tv]))
                    sh2 = 4 * j - 12
                    op2 = ALU.logical_shift_right if sh2 >= 0 else ALU.logical_shift_left
                    ts2 = p.dve.op(
                        lambda v, W=W, sh2=abs(sh2), op2=op2: v.tensor_scalar(
                            sg_t[:, 0:W], wI[:, 0:W], sh2, 0x8000, op2, ALU.bitwise_and),
                        deps=[tw])
                    t_sgn.append(p.dve.op(
                        lambda v, j=j, W=W, E=E: v.tensor_scalar(
                            ub_sgn[:, 0:E].rearrange("p (w j) -> p w j", j=8)[:, :, j],
                            sg_t[:, 0:W], 0, None, ALU.add),
                        deps=[ts2]))
                c_m1, tg_m1 = p.constf(-1.0)
                c_2, tg_2 = p.constf(2.0)
                t_t1 = p.act.op(lambda s, E=E, c_m1=c_m1, c_2=c_2:
                                s.activation(tq1[:, 0:E], ub_mag[:, 0:E].bitcast(BF16),
                                             AF.Identity, bias=c_m1, scale=c_2),
                                deps=t_mag + [t_chunk_done, tg_m1, tg_2])
                t_g = p.dve.op(lambda v, E=E: v.scalar_tensor_tensor(
                    gq[:, 0:E], tq1[:, 0:E], 0.0, ub_mag[:, 0:E].bitcast(BF16), ALU.max, ALU.min),
                    deps=[t_t1] + t_mag)
                t_s = p.dve.op(lambda v, E=E, G=G: v.tensor_tensor(
                    tq1[:, 0:E].rearrange("p (g k) -> p g k", g=G),
                    gq[:, 0:E].rearrange("p (g k) -> p g k", g=G),
                    scl32[:, 0:G].unsqueeze(-1).broadcast_to([128, G, 128]), ALU.mult),
                    deps=[t_g, tsc32])
                t_f = p.dve.op(lambda v, E=E: v.tensor_tensor(
                    fin[:, 0:E], tq1[:, 0:E].bitcast(U16), ub_sgn[:, 0:E], ALU.bitwise_or),
                    deps=[t_s] + t_sgn)
                t_dma = p.dma_hw(p.sp, "fino", out_d[ch * 128:(ch + 1) * 128, :],
                                 fin[:, 0:E].bitcast(BF16), deps=[t_f])
                t_chunk_done = t_dma

        p.finish()
        p.emit()
    return nc


def build_B():
    nc = bass.Bass("TRN2", target_bir_lowering=False)
    x_in = nc.dram_tensor("x_in", [TPC, C], F32, kind="ExternalInput").ap()
    q_in = nc.dram_tensor("q_in", [C, TPC], BF16, kind="ExternalInput").ap()
    vks = [nc.dram_tensor(f"vk{i}", [128, 16 * 33], F32, kind="ExternalInput").ap()
           for i in range(4)]
    wpT = nc.dram_tensor("wpT", [C, C], F32, kind="ExternalInput").ap()
    bproj = nc.dram_tensor("bproj", [C], F32, kind="ExternalInput").ap()
    e4 = nc.dram_tensor("e4", [4, 128], BF16, kind="ExternalInput").ap()
    identf = nc.dram_tensor("identf", [128, 128], F32, kind="ExternalInput").ap()

    x1_o = nc.dram_tensor("x1_o", [C, TPC], F32, kind="ExternalOutput").ap()
    xn2_o = nc.dram_tensor("xn2_o", [C, TPC], BF16, kind="ExternalOutput").ap()

    qvv = q_in.rearrange("(c p) t -> c p t", p=128)
    x1v = x1_o.rearrange("(c p) t -> c p t", p=128)
    xn2v = xn2_o.rearrange("(c p) t -> c p t", p=128)

    with ExitStack() as ctx:
        def sb(nm, shp, dt, stack=None):
            return (stack or ctx).enter_context(nc.sbuf_tensor(nm, shp, dt))
        p = Prog(nc, ctx)

        idf = sb("idf", [128, 128], F32)
        t_idf = p.dma_hw(p.sp, "idf", idf[:], identf)
        xf = sb("xf", [128, 16 * TPC], F32)

        def xfs(c, tt, w=TT):
            return xf[:, c * TPC + tt * w: c * TPC + (tt + 1) * w]

        with ExitStack() as sctx, \
             nc.psum_tensor("pst0", [128, 128], F32) as pst0, \
             nc.psum_tensor("pst1", [128, 128], F32) as pst1:
            xtm = [sb(f"xtm{i}", [128, C], F32, sctx) for i in range(2)]
            psts = [pst0, pst1]
            t_blk_done = [None, None]
            t_evac = [None, None]
            for blk in range(NBLK):
                buf = xtm[blk % 2]
                tl = p.dma_hw(p.sp, f"xtm{blk % 2}", buf[:], x_in[blk * 128:(blk + 1) * 128, :],
                              deps=[t_blk_done[blk % 2]])
                last_tp = None
                for c in range(16):
                    ps = psts[c % 2]
                    last_tp = p.pe.op(lambda t, ps=ps, buf=buf, c=c:
                                      t.transpose(ps[:], buf[:, c * 128:(c + 1) * 128], idf[:]),
                                      deps=[tl, t_idf, t_evac[c % 2]])
                    t_evac[c % 2] = p.act.op(
                        lambda s, ps=ps, c=c, blk=blk:
                        s.copy(xf[:, c * TPC + blk * 128: c * TPC + (blk + 1) * 128], ps[:]),
                        deps=[last_tp])
                t_blk_done[blk % 2] = last_tp
        p.barrier()

        s2 = ctx.enter_context(ExitStack())
        vk_sb = sb("vkB", [128, 16 * 33], F32, s2)
        vk_tmp = sb("vkBt", [128, 16 * 33], F32, s2)
        t_v0 = p.dma_hw(p.sp, "vkl0", vk_sb[:], vks[0])
        tsum = t_v0
        for i in range(1, 4):
            tv = p.dma_hw(p.sp, "vkl1", vk_tmp[:], vks[i], deps=[tsum])
            tsum = p.dve.op(lambda v: v.tensor_add(vk_sb[:], vk_sb[:], vk_tmp[:]),
                            deps=[tv, tsum])
        bd = sb("bd", [128, 16 * 128], BF16, s2)
        ksd = sb("ksd", [128, 16 * 4], BF16, s2)
        t_bd0 = p.dve.op(lambda v: v.memset(bd[:], 0.0), deps=[])
        t_ks0 = p.dve.op(lambda v: v.memset(ksd[:], 0.0), deps=[])
        t_bd = []
        for c in range(16):
            for h in range(4):
                t_bd.append(p.dve.op(
                    lambda v, c=c, h=h: v.tensor_copy(
                        bd[32 * h:32 * h + 32, c * 128 + 32 * h: c * 128 + 32 * h + 32],
                        vk_sb[32 * h:32 * h + 32, c * 33: c * 33 + 32]),
                    deps=[tsum, t_bd0]))
                t_bd.append(p.dve.op(
                    lambda v, c=c, h=h: v.tensor_copy(
                        ksd[32 * h:32 * h + 32, c * 4 + h: c * 4 + h + 1],
                        vk_sb[32 * h:32 * h + 32, c * 33 + 32: c * 33 + 33]),
                    deps=[tsum, t_ks0]))
        t_bdall = p.dve.tag()
        e4b = sb("e4b", [4, 128], BF16, s2)
        t_e4 = p.dma_hw(p.sp, "e4", e4b[:], e4)
        bias_sb = sb("bias_sb", [128, 16], F32, s2)
        t_bias = None
        for oc in range(16):
            t_bias = p.dma_hw(p.sp, "bias", bias_sb[:, oc:oc + 1],
                              bproj[oc * 128:(oc + 1) * 128].unsqueeze(-1), deps=[t_bias])
        p.barrier()

        s3 = ctx.enter_context(ExitStack())
        qt = [sb(f"qt{i}", [128, TT], BF16, s3) for i in range(2)]
        att = sb("att", [128, 16 * TT], BF16, s3)
        denr = sb("denr", [4, TT], F32, s3)
        denb = sb("denb", [4, TT], BF16, s3)
        rbb = sb("rbb", [128, TT], BF16, s3)
        wpb = [sb(f"wpb{i}", [128, 16 * 128], BF16, s3) for i in range(2)]

        def w_cols(wT, oc, width=128):
            return wT.rearrange("(kc p) o -> p kc o", p=128)[:, :, oc * width:(oc + 1) * width]

        with nc.psum_tensor("psaf", [128, TT], F32) as psaf, \
             nc.psum_tensor("psd", [4, TT], F32) as psd, \
             nc.psum_tensor("psbc", [128, TT], F32) as psbc, \
             nc.psum_tensor("psp0", [128, TT], F32) as psp0, \
             nc.psum_tensor("psp1", [128, TT], F32) as psp1:
            psps = [psp0, psp1]
            t_qload = [None, None]
            t_att_mult = None
            t_den_ev = None
            t_rb_ev = None
            t_x1w = {}
            t_wfree = [None, None]
            for tt in range(NTT):
                for c in range(16):
                    tq = p.dma_hw(p.sp, f"qt{c % 2}", qt[c % 2][:],
                                  qvv[c][:, tt * TT:(tt + 1) * TT], deps=[t_qload[c % 2]])
                    t_num = p.pe.op(lambda t, c=c: t.matmul(psaf[:], bd[:, c * 128:(c + 1) * 128],
                                                            qt[c % 2][:], start=True, stop=True),
                                    deps=[tq, t_bdall, t_att_mult])
                    t_den = p.pe.op(lambda t, c=c: t.matmul(psd[:], ksd[:, c * 4:(c + 1) * 4],
                                                            qt[c % 2][:], start=True, stop=True),
                                    deps=[tq, t_den_ev])
                    t_qload[c % 2] = t_den
                    td1 = p.dve.op(lambda v: v.tensor_scalar(denr[:], psd[:], 1e-15, None, ALU.add),
                                   deps=[t_den])
                    t_den_ev = td1
                    td2 = p.dve.op(lambda v: v.reciprocal(denr[:], denr[:]), deps=[td1])
                    td3 = p.dve.op(lambda v: v.tensor_copy(denb[:], denr[:]), deps=[td2])
                    t_bc = p.pe.op(lambda t: t.matmul(psbc[:], e4b[:], denb[:], start=True, stop=True),
                                   deps=[td3, t_e4, t_rb_ev])
                    t_rb = p.act.op(lambda s: s.copy(rbb[:], psbc[:]), deps=[t_bc])
                    t_rb_ev = t_rb
                    t_att_mult = p.dve.op(
                        lambda v, c=c: v.tensor_tensor(att[:, c * TT:(c + 1) * TT], psaf[:],
                                                       rbb[:], ALU.mult),
                        deps=[t_rb, t_num])
                t_att_done = t_att_mult
                for oc in range(16):
                    tw = p.dma_sw(f"wpb{oc % 2}", wpb[oc % 2][:], w_cols(wpT, oc),
                                  deps=[t_wfree[oc % 2]])
                    ps = psps[oc % 2]
                    t_mm = None
                    for kc in range(16):
                        fn = p.pe.op if kc == 15 else p.pe.op_quiet
                        t_mm = fn(lambda t, kc=kc, oc=oc, ps=ps, s=(kc == 0), e=(kc == 15):
                                  t.matmul(ps[:], wpb[oc % 2][:, kc * 128:(kc + 1) * 128],
                                           att[:, kc * TT:(kc + 1) * TT], start=s, stop=e),
                                  deps=([tw, t_att_done, t_x1w.get((oc % 2, tt))] if kc == 0 else ()))
                    t_wfree[oc % 2] = t_mm
                    t_x1 = p.dve.op(
                        lambda v, oc=oc, tt=tt, ps=ps: v.scalar_tensor_tensor(
                            xfs(oc, tt), ps[:], bias_sb[:, oc:oc + 1], xfs(oc, tt),
                            ALU.add, ALU.add),
                        deps=[t_mm, t_bias])
                    t_x1w[(oc % 2, tt)] = t_x1
                    p.dma_hw(p.sp, f"x1st{oc % 2}", x1v[oc][:, tt * TT:(tt + 1) * TT],
                             xfs(oc, tt), deps=[t_x1])
        p.barrier()
        s3.close()
        s2.close()

        s4 = ctx.enter_context(ExitStack())
        onesb = sb("onesbB", [128, 1], BF16, s4)
        onesr = sb("onesrB", [1, 128], BF16, s4)
        t_ob = p.dve.op(lambda v: v.memset(onesb[:], 1.0))
        t_or = p.dve.op(lambda v: v.memset(onesr[:], 1.0))
        casts = [sb(f"castB{i}", [128, TT], BF16, s4) for i in range(2)]
        sqt = [sb(f"sqtB{i}", [128, TT], BF16, s4) for i in range(2)]
        sumr = sb("sumrB", [1, TPC], F32, s4)
        sqr = sb("sqrB", [1, TPC], F32, s4)
        mu_b = sb("mu_bB", [1, TPC], BF16, s4)
        rs_b = sb("rs_bB", [1, TPC], BF16, s4)
        rowa = sb("rowaB", [1, TPC], F32, s4)
        roww = sb("rowwB", [1, TPC], F32, s4)
        with nc.psum_tensor("pssumB", [1, TT], F32) as pssum, \
             nc.psum_tensor("pssqB", [1, TT], F32) as pssq:
            t_sum_evac = None
            t_sq_evac = None
            for tt in range(NTT):
                t_mm = None
                t_mmq = None
                for c in range(16):
                    tcast = p.dve.op(lambda v, c=c, tt=tt, b=c % 2:
                                     v.tensor_copy(casts[b][:], xfs(c, tt)),
                                     deps=[t_mm])
                    fn = p.pe.op
                    t_mm = fn(lambda t, c=c, s=(c == 0), e=(c == 15), b=c % 2:
                              t.matmul(pssum[:], onesb[:], casts[b][:], start=s, stop=e),
                              deps=[tcast, t_ob] + ([t_sum_evac] if c == 0 else []))
                    t_sq = p.act.op(lambda s, c=c, b=c % 2:
                                    s.activation(sqt[b][:], casts[b][:], AF.Square),
                                    deps=[tcast, t_mmq])
                    t_mmq = fn(lambda t, c=c, s=(c == 0), e=(c == 15), b=c % 2:
                               t.matmul(pssq[:], onesb[:], sqt[b][:], start=s, stop=e),
                               deps=[t_sq] + ([t_sq_evac] if c == 0 else []))
                t_sum_evac = p.dve.op(lambda v, tt=tt:
                                      v.tensor_copy(sumr[0:1, tt * TT:(tt + 1) * TT], pssum[:]),
                                      deps=[t_mm])
                t_sq_evac = p.dve.op(lambda v, tt=tt:
                                     v.tensor_copy(sqr[0:1, tt * TT:(tt + 1) * TT], pssq[:]),
                                     deps=[t_mmq])
        t1 = p.dve.op(lambda v: v.tensor_scalar_mul(rowa[:], sumr[:], 1.0 / C), deps=[t_sum_evac])
        t_mub = p.dve.op(lambda v: v.tensor_copy(mu_b[:], rowa[:]), deps=[t1])
        t2 = p.dve.op(lambda v: v.tensor_scalar_mul(roww[:], sqr[:], 1.0 / C), deps=[t_sq_evac])
        t3 = p.dve.op(lambda v: v.tensor_tensor(sumr[:], rowa[:], rowa[:], ALU.mult), deps=[t1, t2])
        t4 = p.dve.op(lambda v: v.tensor_sub(roww[:], roww[:], sumr[:]), deps=[t3])
        c_eps, tg_eps = p.constf(1e-5)
        t5 = p.act.op(lambda s: s.activation(rowa[:], roww[:], AF.Sqrt, bias=c_eps[0:1, :]),
                      deps=[t4, t_mub, tg_eps])
        t6 = p.dve.op(lambda v: v.reciprocal(roww[:], rowa[:]), deps=[t5])
        t_rsb = p.dve.op(lambda v: v.tensor_copy(rs_b[:], roww[:]), deps=[t6])
        p.barrier()

        mub_s = sb("mub_sB", [128, TT], BF16, s4)
        rsb_s = sb("rsb_sB", [128, TT], BF16, s4)
        xnst = [sb(f"xnstB{i}", [128, TT], BF16, s4) for i in range(2)]
        with nc.psum_tensor("psbcB0", [128, TT], F32) as psb0, \
             nc.psum_tensor("psbcB1", [128, TT], F32) as psb1:
            te1 = te2 = None
            tb_last = None
            t_dma = [None, None]
            for tt in range(NTT):
                tm1 = p.pe.op(lambda t, tt=tt: t.matmul(psb0[:], onesr[:],
                                                        mu_b[0:1, tt * TT:(tt + 1) * TT],
                                                        start=True, stop=True),
                              deps=[t_mub, t_or, te1])
                tm2 = p.pe.op(lambda t, tt=tt: t.matmul(psb1[:], onesr[:],
                                                        rs_b[0:1, tt * TT:(tt + 1) * TT],
                                                        start=True, stop=True),
                              deps=[t_rsb, te2])
                te1 = p.dve.op(lambda v: v.tensor_copy(mub_s[:], psb0[:]), deps=[tm1, tb_last])
                te2 = p.dve.op(lambda v: v.tensor_copy(rsb_s[:], psb1[:]), deps=[tm2])
                for c in range(16):
                    b = c % 2
                    ta = p.dve.op(lambda v, c=c, tt=tt, b=b:
                                  v.tensor_sub(xnst[b][:], xfs(c, tt), mub_s[:]),
                                  deps=[te1, t_dma[b]])
                    tb_last = p.dve.op(lambda v, c=c, tt=tt, b=b:
                                       v.tensor_tensor(xnst[b][:], xnst[b][:], rsb_s[:], ALU.mult),
                                       deps=[ta, te2])
                    t_dma[b] = p.dma_hw(p.sp, f"xnst{b}", xn2v[c][:, tt * TT:(tt + 1) * TT],
                                        xnst[b][:], deps=[tb_last])
        p.finish()
        p.emit()
    return nc


def make_B_inmaps(inputs, a_outs):
    x = np.asarray(inputs["L0_attn_x"])
    w_proj = np.asarray(inputs["L0_attn_w_proj"])
    b_proj = np.asarray(inputs["L0_attn_b_proj"])
    wpT = np.ascontiguousarray(w_proj.T)
    identf = np.eye(128, dtype=np.float32)
    e4 = np.zeros((4, 128), dtype=np.float32)
    for h in range(4):
        e4[h, 32 * h:32 * h + 32] = 1.0
    e4 = e4.astype(BF)
    xs = shard_x(x)
    maps = []
    for core in range(NCORE):
        b = core // 4
        peers = [4 * b + i for i in range(4)]
        m = {
            "x_in": xs[core],
            "q_in": np.ascontiguousarray(a_outs[core]["q_o"]),
            "wpT": wpT, "bproj": b_proj, "e4": e4, "identf": identf,
        }
        for i, pc in enumerate(peers):
            m[f"vk{i}"] = np.ascontiguousarray(a_outs[pc]["vk_o"])
        maps.append(m)
    return maps


def build_C():
    IC = 11264
    HC = 5632
    NH = 44
    nc = bass.Bass("TRN2", target_bir_lowering=False)
    x1_in = nc.dram_tensor("x1_in", [C, TPC], F32, kind="ExternalInput").ap()
    xn2_in = nc.dram_tensor("xn2_in", [C, TPC], BF16, kind="ExternalInput").ap()
    wguT = nc.dram_tensor("wguT", [C, IC], BF16, kind="ExternalInput").ap()
    wdnT = nc.dram_tensor("wdnT", [HC, C], BF16, kind="ExternalInput").ap()
    x2_o = nc.dram_tensor("x2_o", [C, TPC], F32, kind="ExternalOutput").ap()
    mu3_o = nc.dram_tensor("mu3_o", [1, TPC], F32, kind="ExternalOutput").ap()
    rs3_o = nc.dram_tensor("rs3_o", [1, TPC], F32, kind="ExternalOutput").ap()

    x1v = x1_in.rearrange("(c p) t -> c p t", p=128)
    x2v = x2_o.rearrange("(c p) t -> c p t", p=128)
    xnv = xn2_in.rearrange("(c p) t -> c p t", p=128)

    with ExitStack() as ctx:
        def sb(nm, shp, dt, stack=None):
            return (stack or ctx).enter_context(nc.sbuf_tensor(nm, shp, dt))
        p = Prog(nc, ctx)

        xn = sb("xn", [128, 16 * TPC], BF16)
        t_xn = []
        for c in range(16):
            t_xn.append(p.dma_hw(p.sp, f"xnld{c % 4}", xn[:, c * TPC:(c + 1) * TPC], xnv[c],
                                 deps=[t_xn[c - 4] if c >= 4 else None]))
        t_xn_all = t_xn

        def xns(kc, tt, w=TT):
            return xn[:, kc * TPC + tt * w: kc * TPC + (tt + 1) * w]

        onesb = sb("onesbC", [128, 1], BF16)
        onesr = sb("onesrC", [1, 128], BF16)
        t_ob = p.dve.op(lambda v: v.memset(onesb[:], 1.0))
        t_or = p.dve.op(lambda v: v.memset(onesr[:], 1.0))

        h_sb = sb("h_sb", [128, NH * TT], BF16)
        y_sb = sb("y_sb", [128, 16 * TT], BF16)
        wgu_b = [sb(f"wgub{i}", [128, 16 * 128], BF16) for i in range(2)]
        wdn_b = [sb("wdnb0", [128, NH * 128], BF16)] * 2
        sg_t = [sb(f"sgC{i}", [128, TT], BF16) for i in range(2)]
        sqt = [sb(f"sqtC{i}", [128, TT], BF16) for i in range(2)]
        x1t = [sb(f"x1tC{i}", [128, TT], F32) for i in range(2)]
        x2t = [sb(f"x2tC{i}", [128, TT], F32) for i in range(2)]
        x2b = [sb(f"x2bC{i}", [128, TT], BF16) for i in range(2)]
        yt = sb("ytC", [128, TT], BF16)
        rr_s = sb("rrC", [128, TT], BF16)
        rowq = sb("rowqC", [1, TT], F32)
        rowqb = sb("rowqbC", [1, TT], BF16)
        sum3 = sb("sum3C", [1, TPC], F32)
        sq3 = sb("sq3C", [1, TPC], F32)
        rowa = sb("rowaC", [1, TPC], F32)
        roww = sb("rowwC", [1, TPC], F32)

        def w_cols(wT, oc, nkc, width=128):
            return wT.rearrange("(kc p) o -> p kc o", p=128)[:, :, oc * width:(oc + 1) * width]

        c_eps6, tg_e6 = p.constf(1e-6)
        c_eps5, tg_e5 = p.constf(1e-5)

        with nc.psum_tensor("psg", [128, TT], F32) as psg, \
             nc.psum_tensor("psu", [128, TT], F32) as psu, \
             nc.psum_tensor("psd0", [128, TT], F32) as psd0, \
             nc.psum_tensor("psd1", [128, TT], F32) as psd1, \
             nc.psum_tensor("psr", [1, TT], F32) as psr, \
             nc.psum_tensor("pss", [1, TT], F32) as pss, \
             nc.psum_tensor("psb", [128, TT], F32) as psb:
            psds = [psd0, psd1]
            t_wg_free = [None, None]
            t_wd_free = [None, None]
            t_hw = None
            t_sg = None
            t_yev = [None, None]
            t_sqv = None
            t_rr = None
            t_x2w = [None, None]
            t_x2c = [None, None]
            t_x1l = [None, None]
            t_sum_ev = None
            t_sq_ev = None
            t_mmy = None
            t_x2read = [[], []]
            for tt in range(NTT):
                for i in range(NH):
                    bg = i % 2
                    twg = p.dma_hw(p.sp, f"wg{bg}", wgu_b[bg][:], w_cols(wguT, i, 16),
                                   deps=[t_wg_free[bg]])
                    t_mmg = None
                    for kc in range(16):
                        fn = p.pe.op if kc == 15 else p.pe.op_quiet
                        t_mmg = fn(lambda t, kc=kc, tt=tt, bg=bg, s=(kc == 0), e=(kc == 15):
                                   t.matmul(psg[:], wgu_b[bg][:, kc * 128:(kc + 1) * 128],
                                            xns(kc, tt), start=s, stop=e),
                                   deps=([twg, t_sg] + t_xn_all if kc == 0 else ()))
                    t_sg = p.act.op(lambda s, bg=bg: s.activation(sg_t[bg][:], psg[:], AF.Silu),
                                    deps=[t_mmg])
                    twu = p.dma_sw(f"wu{bg}", wgu_b[bg][:], w_cols(wguT, NH + i, 16),
                                   deps=[t_mmg])
                    t_mmu = None
                    for kc in range(16):
                        fn = p.pe.op if kc == 15 else p.pe.op_quiet
                        t_mmu = fn(lambda t, kc=kc, tt=tt, bg=bg, s=(kc == 0), e=(kc == 15):
                                   t.matmul(psu[:], wgu_b[bg][:, kc * 128:(kc + 1) * 128],
                                            xns(kc, tt), start=s, stop=e),
                                   deps=([twu, t_hw] if kc == 0 else ()))
                    t_hw = p.dve.op(lambda v, i=i, bg=bg:
                                    v.tensor_tensor(h_sb[:, i * TT:(i + 1) * TT], psu[:],
                                                    sg_t[bg][:], ALU.mult),
                                    deps=[t_mmu, t_sg])
                    t_wg_free[bg] = t_mmu
                t_h_done = t_hw
                t_mmy = None
                for oc in range(16):
                    bo = oc % 2
                    twd = p.dma_hw(p.sp, "wd0", wdn_b[0][:], w_cols(wdnT, oc, NH),
                                   deps=[t_wd_free[0], t_wd_free[1]])
                    ps = psds[bo]
                    t_mmd = None
                    for kc in range(NH):
                        fn = p.pe.op if kc == NH - 1 else p.pe.op_quiet
                        t_mmd = fn(lambda t, kc=kc, oc=oc, ps=ps, s=(kc == 0), e=(kc == NH - 1):
                                   t.matmul(ps[:], wdn_b[0][:, kc * 128:(kc + 1) * 128],
                                            h_sb[:, kc * TT:(kc + 1) * TT], start=s, stop=e),
                                   deps=([twd, t_h_done, t_yev[bo]] if kc == 0 else ()))
                    t_wd_free[bo] = t_mmd
                    t_yev[bo] = p.dve.op(lambda v, oc=oc, ps=ps:
                                         v.tensor_copy(y_sb[:, oc * TT:(oc + 1) * TT], ps[:]),
                                         deps=[t_mmd])
                    t_sqv = p.act.op(lambda s, oc=oc: s.activation(yt[:], y_sb[:, oc * TT:(oc + 1) * TT],
                                                                   AF.Square),
                                     deps=[t_yev[oc % 2], t_mmy])
                    t_mmy = p.pe.op(lambda t, oc=oc, s=(oc == 0), e=(oc == 15):
                                    t.matmul(psr[:], onesb[:], yt[:], start=s, stop=e),
                                    deps=[t_sqv, t_ob])
                tr1 = p.dve.op(lambda v: v.tensor_scalar_mul(rowq[:], psr[:], 1.0 / C), deps=[t_mmy])
                tr2 = p.act.op(lambda s: s.activation(rowq[:], rowq[:], AF.Sqrt, bias=c_eps6[0:1, :]),
                               deps=[tr1, tg_e6])
                tr3 = p.dve.op(lambda v: v.reciprocal(rowq[:], rowq[:]), deps=[tr2])
                tr4 = p.dve.op(lambda v: v.tensor_copy(rowqb[:], rowq[:]), deps=[tr3])
                t_bcm = p.pe.op(lambda t: t.matmul(psb[:], onesr[:], rowqb[:], start=True, stop=True),
                                deps=[tr4, t_or, t_rr])
                t_rr = p.act.op(lambda s: s.copy(rr_s[:], psb[:]), deps=[t_bcm])
                t_mm3 = None
                t_mm3q = None
                for oc in range(16):
                    b2 = oc % 2
                    tx1 = p.dma_hw(p.sp, f"x1l{b2}", x1t[b2][:], x1v[oc][:, tt * TT:(tt + 1) * TT],
                                   deps=[t_x1l[b2]])
                    tyr = p.dve.op(lambda v, oc=oc, b2=b2: v.tensor_tensor(
                        x2b[b2][:], y_sb[:, oc * TT:(oc + 1) * TT], rr_s[:], ALU.mult),
                        deps=[t_rr, t_x2c[b2]] + t_x2read[b2])
                    tx2 = p.dve.op(lambda v, oc=oc, b2=b2: v.tensor_tensor(
                        x2t[b2][:], x1t[b2][:], x2b[b2][:], ALU.add),
                        deps=[tyr, tx1, t_x2w[b2]])
                    t_x1l[b2] = tx2
                    t_x2w[b2] = p.dma_hw(p.sp, f"x2st{b2}", x2v[oc][:, tt * TT:(tt + 1) * TT],
                                         x2t[b2][:], deps=[tx2])
                    t_x2c[b2] = p.dve.op(lambda v, b2=b2: v.tensor_copy(x2b[b2][:], x2t[b2][:]),
                                         deps=[tx2])
                    t_mm3 = p.pe.op(lambda t, oc=oc, b2=b2, s=(oc == 0), e=(oc == 15):
                                    t.matmul(pss[:], onesb[:], x2b[b2][:], start=s, stop=e),
                                    deps=[t_x2c[b2]] + ([t_sum_ev] if oc == 0 else []))
                    t_sq3 = p.act.op(lambda s, b2=b2: s.activation(sqt[b2][:], x2b[b2][:], AF.Square),
                                     deps=[t_x2c[b2], t_mm3q])
                    t_mm3q = p.pe.op(lambda t, oc=oc, b2=b2, s=(oc == 0), e=(oc == 15):
                                     t.matmul(psr[:], onesb[:], sqt[b2][:], start=s, stop=e),
                                     deps=[t_sq3] + ([t_sq_ev, t_mmy, tr1] if oc == 0 else []))
                    t_x2read[b2] = [t_mm3, t_sq3]
                t_sum_ev = p.dve.op(lambda v, tt=tt: v.tensor_copy(sum3[0:1, tt * TT:(tt + 1) * TT],
                                                                   pss[:]), deps=[t_mm3])
                t_sq_ev = p.dve.op(lambda v, tt=tt: v.tensor_copy(sq3[0:1, tt * TT:(tt + 1) * TT],
                                                                  psr[:]), deps=[t_mm3q])
        t1 = p.dve.op(lambda v: v.tensor_scalar_mul(rowa[:], sum3[:], 1.0 / C), deps=[t_sum_ev])
        t2 = p.dve.op(lambda v: v.tensor_scalar_mul(roww[:], sq3[:], 1.0 / C), deps=[t_sq_ev])
        t3 = p.dve.op(lambda v: v.tensor_tensor(sum3[:], rowa[:], rowa[:], ALU.mult), deps=[t1, t2])
        t4 = p.dve.op(lambda v: v.tensor_sub(roww[:], roww[:], sum3[:]), deps=[t3])
        t5 = p.act.op(lambda s: s.activation(roww[:], roww[:], AF.Sqrt, bias=c_eps5[0:1, :]),
                      deps=[t4, tg_e5])
        t6 = p.dve.op(lambda v: v.reciprocal(roww[:], roww[:]), deps=[t5])
        p.dma_hw(p.sp, "mu3", mu3_o, rowa[:], deps=[t1])
        p.dma_hw(p.sp, "rs3", rs3_o, roww[:], deps=[t6])
        p.finish()
        p.emit()
    return nc


def make_C_inmaps(inputs, a_outs, b_outs):
    gu = np.concatenate([np.asarray(a_outs[i]["wgu_o"]).view(np.uint16) for i in range(NCORE)], axis=0)
    dn = np.concatenate([np.asarray(a_outs[i]["wdn_o"]).view(np.uint16) for i in range(NCORE)], axis=0)
    wguT = np.ascontiguousarray(gu.T)
    wdnT = np.ascontiguousarray(dn.T)
    maps = []
    for core in range(NCORE):
        maps.append({
            "x1_in": np.ascontiguousarray(b_outs[core]["x1_o"]),
            "xn2_in": np.ascontiguousarray(b_outs[core]["xn2_o"]),
            "wguT": wguT.view(BF),
            "wdnT": wdnT.view(BF),
        })
    return maps


def build_D():
    QKVD = 8192
    ZD = 4096
    nc = bass.Bass("TRN2", target_bir_lowering=False)
    x2_in = nc.dram_tensor("x2_in", [C, TPC], F32, kind="ExternalInput").ap()
    mu3_in = nc.dram_tensor("mu3_in", [1, TPC], F32, kind="ExternalInput").ap()
    rs3_in = nc.dram_tensor("rs3_in", [1, TPC], F32, kind="ExternalInput").ap()
    wqkvT = nc.dram_tensor("wqkvT", [C, QKVD], F32, kind="ExternalInput").ap()
    wzT = nc.dram_tensor("wzT", [C, ZD], F32, kind="ExternalInput").ap()
    wbaT = nc.dram_tensor("wbaT", [C, 64], F32, kind="ExternalInput").ap()
    qkv_o = nc.dram_tensor("qkv_o", [QKVD, TPC], BF16, kind="ExternalOutput").ap()
    z_o = nc.dram_tensor("z_o", [ZD, TPC], BF16, kind="ExternalOutput").ap()
    ba_o = nc.dram_tensor("ba_o", [64, TPC], F32, kind="ExternalOutput").ap()

    x2v = x2_in.rearrange("(c p) t -> c p t", p=128)
    qkvv = qkv_o.rearrange("(c p) t -> c p t", p=128)
    zv = z_o.rearrange("(c p) t -> c p t", p=128)

    with ExitStack() as ctx:
        def sb(nm, shp, dt, stack=None):
            return (stack or ctx).enter_context(nc.sbuf_tensor(nm, shp, dt))
        p = Prog(nc, ctx)
        xn = sb("xnD", [128, 16 * TPC], BF16)

        def xns(kc, tt, w=TT):
            return xn[:, kc * TPC + tt * w: kc * TPC + (tt + 1) * w]

        onesr = sb("onesrD", [1, 128], BF16)
        t_or = p.dve.op(lambda v: v.memset(onesr[:], 1.0))
        murow = sb("murowD", [1, TPC], BF16)
        rsrow = sb("rsrowD", [1, TPC], BF16)
        rowt = sb("rowtD", [1, TPC], F32)
        t_m = p.dma_hw(p.sp, "mu3", rowt[:], mu3_in)
        t_m2 = p.dve.op(lambda v: v.tensor_copy(murow[:], rowt[:]), deps=[t_m])
        t_r = p.dma_hw(p.sp, "rs3", rowt[:], rs3_in, deps=[t_m2])
        t_r2 = p.dve.op(lambda v: v.tensor_copy(rsrow[:], rowt[:]), deps=[t_r])

        s1 = ctx.enter_context(ExitStack())
        mub_s = sb("mubD", [128, TT], BF16, s1)
        rsb_s = sb("rsbD", [128, TT], BF16, s1)
        x2t = [sb(f"x2tD{i}", [128, TT], F32, s1) for i in range(2)]
        with nc.psum_tensor("psb0D", [128, TT], F32) as psb0, \
             nc.psum_tensor("psb1D", [128, TT], F32) as psb1:
            te1 = te2 = None
            tb_last = None
            t_xl = [None, None]
            for tt in range(NTT):
                tm1 = p.pe.op(lambda t, tt=tt: t.matmul(psb0[:], onesr[:],
                                                        murow[0:1, tt * TT:(tt + 1) * TT],
                                                        start=True, stop=True),
                              deps=[t_m2, t_or, te1])
                tm2 = p.pe.op(lambda t, tt=tt: t.matmul(psb1[:], onesr[:],
                                                        rsrow[0:1, tt * TT:(tt + 1) * TT],
                                                        start=True, stop=True),
                              deps=[t_r2, te2])
                te1 = p.dve.op(lambda v: v.tensor_copy(mub_s[:], psb0[:]), deps=[tm1, tb_last])
                te2 = p.dve.op(lambda v: v.tensor_copy(rsb_s[:], psb1[:]), deps=[tm2])
                for c in range(16):
                    b = c % 2
                    tx = p.dma_hw(p.sp, f"x2l{b}", x2t[b][:], x2v[c][:, tt * TT:(tt + 1) * TT],
                                  deps=[t_xl[b]])
                    ta = p.dve.op(lambda v, c=c, tt=tt, b=b:
                                  v.tensor_sub(xns(c, tt), x2t[b][:], mub_s[:]),
                                  deps=[te1, tx])
                    t_xl[b] = ta
                    tb_last = p.dve.op(lambda v, c=c, tt=tt:
                                       v.tensor_tensor(xns(c, tt), xns(c, tt), rsb_s[:], ALU.mult),
                                       deps=[ta, te2])
        p.barrier()
        s1.close()
        t_xn = p.dve.tag()

        def w_cols(wT, oc, width=128):
            return wT.rearrange("(kc p) o -> p kc o", p=128)[:, :, oc * width:(oc + 1) * width]

        s2 = ctx.enter_context(ExitStack())
        wb_ = [sb(f"wbD{i}", [128, 16 * 128], BF16, s2) for i in range(2)]
        stg = [sb(f"stgD{i}", [128, TT], BF16, s2) for i in range(2)]
        stgf = [sb(f"stgfD{i}", [64, TT], F32, s2) for i in range(2)]
        with nc.psum_tensor("psqD0", [128, TT], F32) as psq0, \
             nc.psum_tensor("psqD1", [128, TT], F32) as psq1:
            psqs = [psq0, psq1]
            t_w_free = [None, None]
            t_dma = [None, None]
            t_ev = [None, None]
            i = 0
            for wsrc, nov, outv in [(wqkvT, 64, qkvv), (wzT, 32, zv)]:
                for oc in range(nov):
                    tw = p.dma_sw(f"wD{oc % 2}", wb_[oc % 2][:], w_cols(wsrc, oc),
                                  deps=[t_w_free[oc % 2]])
                    t_mm = None
                    for tt in range(NTT):
                        ps = psqs[i % 2]
                        st_ = stg[i % 2]
                        for kc in range(16):
                            fn = p.pe.op if kc == 15 else p.pe.op_quiet
                            t_mm = fn(lambda t, kc=kc, tt=tt, oc=oc, ps=ps, s=(kc == 0), e=(kc == 15):
                                      t.matmul(ps[:], wb_[oc % 2][:, kc * 128:(kc + 1) * 128],
                                               xns(kc, tt), start=s, stop=e),
                                      deps=([tw, t_ev[i % 2]] if kc == 0 else ()))
                        t_e = p.act.op(lambda s, ps=ps, st_=st_: s.copy(st_[:], ps[:]),
                                       deps=[t_mm, t_dma[i % 2]])
                        t_ev[i % 2] = t_e
                        t_dma[i % 2] = p.dma_hw(p.sp, f"stD{i % 2}",
                                                outv[oc][:, tt * TT:(tt + 1) * TT], st_[:],
                                                deps=[t_e])
                        i += 1
                    t_w_free[oc % 2] = t_mm
            twba = p.dma_sw("wba", wb_[0][:, 0:16 * 64],
                            wbaT.rearrange("(kc p) o -> p kc o", p=128),
                            deps=[t_w_free[0]])
            for tt in range(NTT):
                ps = psqs[tt % 2]
                sf = stgf[tt % 2]
                t_mm = None
                for kc in range(16):
                    fn = p.pe.op if kc == 15 else p.pe.op_quiet
                    t_mm = fn(lambda t, kc=kc, tt=tt, ps=ps, s=(kc == 0), e=(kc == 15):
                              t.matmul(ps[0:64, :], wb_[0][:, kc * 64:(kc + 1) * 64],
                                       xns(kc, tt), start=s, stop=e),
                              deps=([twba, t_ev[tt % 2]] if kc == 0 else ()))
                t_e = p.dve.op(lambda v, ps=ps, sf=sf: v.tensor_copy(sf[:], ps[0:64, :]),
                               deps=[t_mm, t_dma[tt % 2]])
                t_ev[tt % 2] = t_e
                t_dma[tt % 2] = p.dma_hw(p.sp, f"baD{tt % 2}", ba_o[:, tt * TT:(tt + 1) * TT],
                                         sf[:], deps=[t_e])
        p.finish()
        p.emit()
    return nc


def make_D_inmaps(inputs, c_outs):
    wqkvT = np.ascontiguousarray(np.asarray(inputs["L1_attn_W_qkv"]).T)
    wzT = np.ascontiguousarray(np.asarray(inputs["L1_attn_W_z"]).T)
    wb = np.asarray(inputs["L1_attn_W_b"])
    wa = np.asarray(inputs["L1_attn_W_a"])
    wbaT = np.ascontiguousarray(np.concatenate([wb, wa], axis=0).T)
    maps = []
    for core in range(NCORE):
        maps.append({
            "x2_in": np.ascontiguousarray(c_outs[core]["x2_o"]),
            "mu3_in": np.ascontiguousarray(c_outs[core]["mu3_o"]),
            "rs3_in": np.ascontiguousarray(c_outs[core]["rs3_o"]),
            "wqkvT": wqkvT, "wzT": wzT, "wbaT": wbaT,
        })
    return maps


def build_E(T=8192, NGL=None, CHL=None, KHL=4, PPL=2, PARTS='dbl os e'):
    L = 128
    NCH = T // L
    GRP = 4
    NG = NCH // GRP
    nc = bass.Bass("TRN2", target_bir_lowering=False)
    qh_in = nc.dram_tensor("qh_in", [512, T], BF16, kind="ExternalInput").ap()
    kh_in = nc.dram_tensor("kh_in", [512, T], BF16, kind="ExternalInput").ap()
    vh_in = nc.dram_tensor("vh_in", [1024, T], BF16, kind="ExternalInput").ap()
    zh_in = nc.dram_tensor("zh_in", [T, 1024], BF16, kind="ExternalInput").ap()
    ba_in = nc.dram_tensor("ba_in", [T, 16], F32, kind="ExternalInput").ap()
    cw_in = nc.dram_tensor("cw_in", [2048, 4], F32, kind="ExternalInput").ap()
    dtb_in = nc.dram_tensor("dtb_in", [1, 8], F32, kind="ExternalInput").ap()
    alog_in = nc.dram_tensor("alog_in", [1, 8], F32, kind="ExternalInput").ap()
    nw_in = nc.dram_tensor("nw_in", [1, 128], F32, kind="ExternalInput").ap()
    identb = nc.dram_tensor("identb", [128, 128], BF16, kind="ExternalInput").ap()
    identf = nc.dram_tensor("identf", [128, 128], F32, kind="ExternalInput").ap()
    ident8 = nc.dram_tensor("ident8", [8, 8], F32, kind="ExternalInput").ap()
    ones8x = nc.dram_tensor("ones8x", [8, 128], F32, kind="ExternalInput").ap()
    e8_in = nc.dram_tensor("e8_in", [8, 8 * 128], F32, kind="ExternalInput").ap()
    triu_in = nc.dram_tensor("triu_in", [128, 128], F32, kind="ExternalInput").ap()
    mA_in = nc.dram_tensor("mA_in", [128, 128], F32, kind="ExternalInput").ap()
    mAT_in = nc.dram_tensor("mAT_in", [128, 128], F32, kind="ExternalInput").ap()
    mW_in = nc.dram_tensor("mW_in", [128, 128], F32, kind="ExternalInput").ap()
    og_o = nc.dram_tensor("og_o", [T, 1024], BF16, kind="ExternalOutput").ap()

    with ExitStack() as ctx:
        def sb(nm, shp, dt, stack=None):
            return (stack or ctx).enter_context(nc.sbuf_tensor(nm, shp, dt))
        p = Prog(nc, ctx)

        idb = sb("idb", [128, 128], BF16)
        idf = sb("idf", [128, 128], F32)
        id8 = sb("id8", [8, 8], F32)
        on8 = sb("on8", [8, 128], F32)
        e8 = sb("e8", [8, 8 * 128], F32)
        triu = sb("triu", [128, 128], F32)
        mA = sb("mA", [128, 128], F32)
        mAT = sb("mAT", [128, 128], F32)
        mW = sb("mW", [128, 128], F32)
        t_c = [p.dma_hw(p.sp, "c0", idb[:], identb), p.dma_hw(p.sp, "c1", id8[:], ident8),
               p.dma_hw(p.sp, "c2", on8[:], ones8x), p.dma_hw(p.sp, "c3", e8[:], e8_in),
               p.dma_hw(p.sp, "c4", triu[:], triu_in), p.dma_hw(p.sp, "c5", mA[:], mA_in),
               p.dma_hw(p.sp, "c6", mAT[:], mAT_in), p.dma_hw(p.sp, "c7", mW[:], mW_in),
               p.dma_hw(p.sp, "c8", idf[:], identf)]
        onesb = sb("onesbE", [128, 1], BF16)
        onesr = sb("onesrE", [1, 128], BF16)
        onesrf = sb("onesrfE", [1, 128], F32)
        t_ob = p.dve.op(lambda v: v.memset(onesb[:], 1.0))
        t_or = p.dve.op(lambda v: v.memset(onesr[:], 1.0))
        t_orf = p.dve.op(lambda v: v.memset(onesrf[:], 1.0))
        c_m1E, tg_m1E = p.constf(-1.0)
        c1e, tg1e = p.constf(1.0)

        ba = sb("baE", [128, NCH * 16], F32)
        t_ba = p.dma_hw(p.sp, "ba", ba[:], ba_in.rearrange("(n p) c -> p n c", p=128))
        rows_dt = sb("rows_dt", [1, 8], F32)
        rows_al = sb("rows_al", [1, 8], F32)
        t_d1 = p.dma_hw(p.sp, "dtb", rows_dt[:], dtb_in)
        t_d2 = p.dma_hw(p.sp, "alog", rows_al[:], alog_in)
        nwrow = sb("nwrow", [1, 128], F32)
        t_nw = p.dma_hw(p.sp, "nw", nwrow[:], nw_in)
        nw2row = sb("nw2row", [1, 256], BF16)
        t_nw1 = p.dve.op(lambda v: v.tensor_copy(nw2row[0:1, 0:128], nwrow[:]), deps=[t_nw])
        t_nw2 = p.dve.op(lambda v: v.tensor_copy(nw2row[0:1, 128:256], nwrow[:]), deps=[t_nw1])

        dtb8 = sb("dtb8", [128, 8], F32)
        expA8 = sb("expA8", [128, 8], F32)
        nwt2 = sb("nwt2", [128, 256], BF16)
        cwsb = sb("cwsb", [128, 64], F32)
        t_cw = p.dma_hw(p.sp, "cw", cwsb[:], cw_in.rearrange("(rc p) j -> p rc j", p=128))
        diags = sb("diags", [128, 64 * 128], BF16)

        psGN = ctx.enter_context(nc.psum_tensor("psGN", [128, 16], F32))
        psR = ctx.enter_context(nc.psum_tensor("psR", [8, 128], F32))
        psZ = ctx.enter_context(nc.psum_tensor("psZ", [128, 128], F32))
        psKQ = ctx.enter_context(nc.psum_tensor("psKQ", [128, 256], F32))
        psT = ctx.enter_context(nc.psum_tensor("psT", [128, 256], BF16))
        psBU = ctx.enter_context(nc.psum_tensor("psBU", [128, 128], F32))
        psSq = ctx.enter_context(nc.psum_tensor("psSq", [128, 256], F32))
        psOS = ctx.enter_context(nc.psum_tensor("psOS", [128, 384], F32))

        tb1 = p.pe.op(lambda t: t.matmul(psGN[:, 0:8], onesrf[:], rows_dt[:], start=True, stop=True),
                      deps=[t_d1, t_orf])
        te1 = p.dve.op(lambda v: v.tensor_copy(dtb8[:], psGN[:, 0:8]), deps=[tb1])
        tb2 = p.pe.op(lambda t: t.matmul(psGN[:, 8:16], onesrf[:], rows_al[:], start=True, stop=True),
                      deps=[t_d2, te1])
        te2 = p.act.op(lambda s: s.activation(expA8[:], psGN[:, 8:16], AF.Exp), deps=[tb2])
        tb3 = p.pe.op(lambda t: t.matmul(psKQ[:, 0:256], onesr[:], nw2row[:], start=True, stop=True),
                      deps=[t_nw2, t_or])
        t_nwt2 = p.dve.op(lambda v: v.tensor_copy(nwt2[:], psKQ[:, 0:256]), deps=[tb3])
        t_diag = []
        for rc in range(16):
            for j in range(4):
                t_diag.append(p.dve.op(
                    lambda v, rc=rc, j=j: v.tensor_scalar(
                        diags[:, (rc * 4 + j) * 128:(rc * 4 + j + 1) * 128],
                        idb[:], cwsb[:, rc * 4 + j: rc * 4 + j + 1], None, ALU.mult),
                    deps=[t_cw, t_c[0]]))
        p.barrier()

        convb = [sb(f"convb{i}", [128, 16 * 512], BF16) for i in range(2)]
        sqb = [sb(f"sqb{i}", [128, 8 * 512], BF16) for i in range(2)]
        intile = [sb(f"intile{i}", [128, 515], BF16) for i in range(2)]
        ztile = [sb(f"ztile{i}", [128, 1024], BF16) for i in range(2)]
        S = [sb(f"S{h}", [128, 128], F32) for h in range(8)]
        Sb = [sb(f"Sb{h}", [128, 128], BF16) for h in range(8)]
        t_S = [p.dve.op(lambda v, h=h: v.memset(S[h][:], 0.0)) for h in range(8)]
        t_Sb = [p.dve.op(lambda v, h=h: v.memset(Sb[h][:], 0.0)) for h in range(8)]
        G8 = sb("G8", [128, 8], F32)
        Gp8 = sb("Gp8", [128, 8], F32)
        ld8 = sb("ld8", [128, 8], F32)
        beta8 = sb("beta8", [128, 8], F32)
        spnb = sb("spnb", [128, 8], F32)
        RowA = sb("RowA", [128, 8], F32)
        CqA = sb("CqA", [128, 8], F32)
        CW = sb("CW", [128, 8], F32)
        lr8 = sb("lr8", [128, 8], F32)
        GL8 = sb("GL8", [128, 8], F32)
        gam8 = sb("gam8", [128, 8], F32)
        negE8 = sb("negE8", [128, 8], F32)
        osc8 = sb("osc8", [128, 8], F32)
        eco8 = sb("eco8", [128, 8], F32)
        diag8 = sb("diag8", [8, 8], F32)
        tmp8a = sb("tmp8a", [128, 8], F32)
        tmp8b = sb("tmp8b", [128, 8], F32)
        tmp8c = sb("tmp8c", [128, 8], F32)
        rows_c = {cl: sb(f"rows_{cl}", [8, 128], F32) for cl in ["G", "RowA", "CqA", "CW"]}
        KKt = [sb(f"KKt{i}", [128, 128], BF16) for i in range(4)]
        QKt = [sb(f"QKt{i}", [128, 128], BF16) for i in range(4)]
        Ktm = [sb(f"Ktm{i}", [128, 128], BF16) for i in range(4)]
        Vtmh = [sb(f"Vtmh{i}", [128, 128], BF16) for i in range(2)]
        zD = sb("zD", [128, 256], F32)
        DA2 = sb("DA2", [128, 256], BF16)
        DAh = [sb(f"DAh{i}", [128, 128], BF16) for i in range(8)]
        DATh = [sb(f"DATh{i}", [128, 128], BF16) for i in range(8)]
        DWh = [sb(f"DWh{i}", [128, 128], BF16) for i in range(8)]
        Bth = [sb(f"Bth{i}", [128, 128], BF16) for i in range(8)]
        Uth = [sb(f"Uth{i}", [128, 128], BF16) for i in range(8)]
        ABh = [sb(f"ABh{i}", [128, 512], BF16) for i in range(8)]
        Oth = [sb(f"Oth{i}", [128, 128], BF16) for i in range(8)]
        tmpB = sb("tmpB", [128, 128], BF16)
        ogt = sb("ogt", [128, 256], BF16)
        rmsc = sb("rmsc", [128, 2], F32)
        silz = sb("silz", [128, 256], BF16)

        def bas(n, c0, w):
            return ba[:, n * 16 + c0: n * 16 + c0 + w]

        t_intile = [None, None]
        t_zt = [None, None]
        t_group_consumed = [None, None]
        t_chunk = None
        t_coef_users = None
        t_ogdma = None

        for g in range(NGL if NGL is not None else NG):
            gb = g % 2
            t0 = g * 512
            t_conv_last = None
            t_sq_last = None
            t_conv_ps = [None, None]
            for rc in range(16):
                src = qh_in if rc < 4 else (kh_in if rc < 8 else vh_in)
                row0 = (rc % 4) * 128 if rc < 8 else (rc - 8) * 128
                ib = intile[rc % 2]
                if g == 0:
                    tl1 = p.dve.op(lambda v, ib=ib: v.memset(ib[:, 0:3], 0.0),
                                   deps=[t_intile[rc % 2]])
                    tld_ = p.dma_hw(p.sp, f"cin{rc % 2}", ib[:, 3:515],
                                    src[row0:row0 + 128, t0:t0 + 512], deps=[tl1])
                    tl = [tld_, tl1]
                else:
                    tld_ = p.dma_hw(p.sp, f"cin{rc % 2}", ib[:, 0:515],
                                    src[row0:row0 + 128, t0 - 3:t0 + 512],
                                    deps=[t_intile[rc % 2]])
                    tl = [tld_]
                for half in range(2):
                    ps = psKQ if half == 0 else psSq
                    t_mm = None
                    for j in range(4):
                        fn = p.pe.op if j == 3 else p.pe.op_quiet
                        t_mm = fn(lambda t, rc=rc, j=j, half=half, ib=ib, ps=ps,
                                  s=(j == 0), e=(j == 3):
                                  t.matmul(ps[:, 0:256],
                                           diags[:, (rc * 4 + j) * 128:(rc * 4 + j + 1) * 128],
                                           ib[:, half * 256 + j: half * 256 + j + 256],
                                           start=s, stop=e),
                                  deps=(tl + [t_diag[-1], t_group_consumed[gb], t_chunk,
                                              t_conv_ps[half]]
                                        if j == 0 else ()))
                    tev = p.dve.op(lambda v, rc=rc, half=half, gb=gb, ps=ps: v.tensor_copy(
                        convb[gb][:, rc * 512 + half * 256: rc * 512 + half * 256 + 256],
                        ps[:, 0:256]), deps=[t_mm])
                    t_conv_ps[half] = tev
                    t_conv_last = tev
                    if rc < 8:
                        t_sq_last = p.dve.op(lambda v, rc=rc, half=half, gb=gb: v.tensor_tensor(
                            sqb[gb][:, rc * 512 + half * 256: rc * 512 + half * 256 + 256],
                            convb[gb][:, rc * 512 + half * 256: rc * 512 + half * 256 + 256],
                            convb[gb][:, rc * 512 + half * 256: rc * 512 + half * 256 + 256],
                            ALU.mult), deps=[tev])
                t_intile[rc % 2] = t_mm
            t_convdone = t_conv_last
            t_sqdone = t_sq_last

            for o in range(GRP if CHL is None else CHL):
                n = g * GRP + o
                ts0 = o * 128

                def cslice(rc, w=128):
                    return convb[gb][:, rc * 512 + ts0: rc * 512 + ts0 + w]

                t_zl = p.dma_hw(p.sp, f"z{n % 2}", ztile[n % 2][:],
                                zh_in[n * L:(n + 1) * L, :], deps=[t_zt[n % 2]])

                bsl = bas(n, 0, 8)
                asl = bas(n, 8, 8)
                dd = [t_ba, t_coef_users]
                tsg = p.act.op(lambda s, bsl=bsl: s.activation(beta8[:], bsl, AF.Sigmoid), deps=dd)
                tu = p.dve.op(lambda v, bsl=bsl: v.scalar_tensor_tensor(
                    tmp8a[:], bsl, -1.0, bsl, ALU.mult, ALU.max), deps=dd)
                tex = p.act.op(lambda s: s.activation(tmp8b[:], tmp8a[:], AF.Exp, scale=c_m1E),
                               deps=[tu, tg_m1E, tsg])
                tln = p.act.op(lambda s: s.activation(tmp8b[:], tmp8b[:], AF.Ln, bias=c1e),
                               deps=[tex, tg1e])
                tmx = p.dve.op(lambda v, bsl=bsl: v.tensor_scalar(tmp8a[:], bsl, -1.0, 0.0,
                                                                  ALU.mult, ALU.max), deps=[tln])
                tsp = p.dve.op(lambda v: v.tensor_add(spnb[:], tmp8a[:], tmp8b[:]), deps=[tmx])
                td1 = p.dve.op(lambda v, asl=asl: v.tensor_add(tmp8a[:], asl, dtb8[:]),
                               deps=[te1, tsp] + dd)
                td2 = p.dve.op(lambda v: v.scalar_tensor_tensor(
                    tmp8c[:], tmp8a[:], -1.0, tmp8a[:], ALU.mult, ALU.max), deps=[td1])
                td3 = p.act.op(lambda s: s.activation(tmp8b[:], tmp8c[:], AF.Exp, scale=c_m1E),
                               deps=[td2, tsp])
                td4 = p.act.op(lambda s: s.activation(tmp8b[:], tmp8b[:], AF.Ln, bias=c1e),
                               deps=[td3])
                td5 = p.dve.op(lambda v: v.tensor_scalar(tmp8c[:], tmp8a[:], 0.0, None, ALU.max),
                               deps=[td4])
                td6 = p.dve.op(lambda v: v.tensor_add(tmp8a[:], tmp8b[:], tmp8c[:]), deps=[td5, td4])
                tld = p.dve.op(lambda v: v.tensor_tensor(ld8[:], tmp8a[:], expA8[:], ALU.mult),
                               deps=[td6, te2])
                tldn = p.dve.op(lambda v: v.tensor_scalar_mul(ld8[:], ld8[:], -1.0), deps=[tld])
                tgm = p.pe.op(lambda t: t.matmul(psGN[:, 0:8], triu[:], ld8[:], start=True, stop=True),
                              deps=[tldn, t_c[4], te1, te2])
                tge = p.dve.op(lambda v: v.tensor_copy(G8[:], psGN[:, 0:8]), deps=[tgm])
                tgp = p.dve.op(lambda v: v.tensor_sub(Gp8[:], G8[:], ld8[:]), deps=[tge])
                t_nmm = None
                for c8 in range(8):
                    fn = p.pe.op if c8 == 7 else p.pe.op_quiet
                    t_nmm = fn(lambda t, c8=c8, gb=gb, ts0=ts0: t.matmul(
                        psGN[:, 8 + c8:9 + c8],
                        sqb[gb][:, c8 * 512 + ts0: c8 * 512 + ts0 + 128], onesb[:],
                        start=True, stop=True),
                        deps=[t_sqdone, t_ob, tgp] if c8 == 0 else ())
                tn1 = p.dve.op(lambda v: v.tensor_scalar(lr8[:], psGN[:, 8:16], 1e-24, None,
                                                         ALU.max), deps=[t_nmm])
                tn2 = p.act.op(lambda s: s.activation(lr8[:], lr8[:], AF.Ln), deps=[tn1])
                tn3 = p.dve.op(lambda v: v.tensor_scalar_mul(lr8[:], lr8[:], -0.5), deps=[tn2])
                lrq_rep = lr8[:, 0:4].unsqueeze(-1).broadcast_to([128, 4, 2])
                lrk_rep = lr8[:, 4:8].unsqueeze(-1).broadcast_to([128, 4, 2])
                def v3(t):
                    return t[:].rearrange("p (a b) -> p a b", b=2)
                ta1 = p.dve.op(lambda v: v.tensor_sub(tmp8a[:], Gp8[:], spnb[:]), deps=[tgp, tn3])
                ta2 = p.dve.op(lambda v, lrk_rep=lrk_rep: v.tensor_tensor(v3(RowA), v3(tmp8a),
                                                                          lrk_rep, ALU.add),
                               deps=[ta1])
                ta3 = p.dve.op(lambda v, lrk_rep=lrk_rep: v.scalar_tensor_tensor(
                    v3(CqA), v3(G8), -1.0, lrk_rep, ALU.mult, ALU.add), deps=[ta2])
                ta4 = p.dve.op(lambda v, lrq_rep=lrq_rep: v.tensor_tensor(v3(CW), v3(G8), lrq_rep,
                                                                          ALU.add), deps=[ta3])
                t_rows = {}
                tprev_r = None
                for cl, tile, tdep in [("G", G8, tge), ("RowA", RowA, ta2),
                                       ("CqA", CqA, ta3), ("CW", CW, ta4)]:
                    ttp = p.pe.op(lambda t, tile=tile: t.transpose(psR[:], tile[:], idf[:]),
                                  deps=[tdep, tprev_r, t_c[8]])
                    tevr = p.dve.op(lambda v, cl=cl: v.tensor_copy(rows_c[cl][:], psR[:]),
                                    deps=[ttp])
                    t_rows[cl] = tevr
                    tprev_r = tevr
                tdg = p.dve.op(lambda v: v.tensor_scalar(diag8[:], id8[:],
                                                         rows_c["G"][:, 127:128], None, ALU.mult),
                               deps=[t_rows["G"], t_c[1]])
                tglm = p.pe.op(lambda t: t.matmul(psGN[:, 0:8], on8[:], diag8[:], start=True,
                                                  stop=True), deps=[tdg, t_c[2], tn1])
                tgl = p.dve.op(lambda v: v.tensor_copy(GL8[:], psGN[:, 0:8]), deps=[tglm])
                tgam = p.act.op(lambda s: s.activation(gam8[:], GL8[:], AF.Exp), deps=[tgl])
                tne1 = p.act.op(lambda s: s.activation(negE8[:], RowA[:], AF.Exp), deps=[ta2, tgam])
                tne2 = p.dve.op(lambda v: v.tensor_scalar_mul(negE8[:], negE8[:], -1.0), deps=[tne1])
                tosc = p.act.op(lambda s: s.activation(osc8[:], CW[:], AF.Exp), deps=[ta4, tne1])
                tec1 = p.dve.op(lambda v: v.tensor_add(tmp8c[:], CqA[:], GL8[:]), deps=[ta3, tgl])
                tec2 = p.act.op(lambda s: s.activation(eco8[:], tmp8c[:], AF.Exp), deps=[tec1, tosc])

                t_kk = [None] * 4
                t_qk = [None] * 4
                t_ktm = [None] * 4
                t_pkq_free = [t_chunk, t_chunk]
                t_pt_free = [t_chunk, t_chunk]
                for kh in range(KHL):
                    kc_t = cslice(4 + kh)
                    qc_t = cslice(kh)
                    bk = kh % 2
                    tkk = p.pe.op(lambda t, kc_t=kc_t, bk=bk: t.matmul(psKQ[:, bk * 128:bk * 128 + 128],
                                                                       kc_t, kc_t, start=True, stop=True),
                                  deps=[t_convdone, t_pkq_free[bk], tec2,
                                        t_kk[kh - 1] if kh > 0 else None])
                    t_kk[kh] = p.dve.op(lambda v, kh=kh, bk=bk: v.tensor_scalar_mul(
                        KKt[kh][:], psKQ[:, bk * 128:bk * 128 + 128], -1.0), deps=[tkk])
                    tqk = p.pe.op(lambda t, kc_t=kc_t, qc_t=qc_t, bk=bk: t.matmul(
                        psSq[:, bk * 128:bk * 128 + 128], kc_t, qc_t, start=True, stop=True),
                        deps=[t_pt_free[bk], tkk, t_qk[kh - 1] if kh > 0 else None])
                    t_qk[kh] = p.dve.op(lambda v, kh=kh, bk=bk: v.tensor_copy(
                        QKt[kh][:], psSq[:, bk * 128:bk * 128 + 128]), deps=[tqk])
                    t_pkq_free[bk] = t_kk[kh]
                    t_pt_free[bk] = t_qk[kh]
                    tkt = p.pe.op(lambda t, kc_t=kc_t, bk=bk: t.transpose(
                        psT[:, bk * 128:bk * 128 + 128], kc_t, idb[:]),
                        deps=[t_ktm[kh - 1] if kh > 0 else t_chunk, t_c[0], tqk])
                    t_ktm[kh] = p.dve.op(lambda v, kh=kh, bk=bk: v.tensor_copy(
                        Ktm[kh][:], psT[:, bk * 128:bk * 128 + 128]), deps=[tkt])

                t_DA = [None] * 8
                t_DAT = [None] * 8
                t_DW = [None] * 8
                t_psz_r = [t_chunk, t_chunk]
                for h in range(2 * KHL):
                    kh = h // 2
                    bz = h % 2
                    tza = p.pe.op(lambda t, h=h, bz=bz: t.matmul(
                        psZ[:, bz * 128:bz * 128 + 128] if False else psZ[:],
                        e8[:, h * 128:(h + 1) * 128], rows_c["CqA"][:], start=True, stop=True),
                        deps=[t_rows["CqA"], t_c[3], t_psz_r[0]])
                    tsa = p.dve.op(lambda v, h=h: v.scalar_tensor_tensor(
                        zD[:, 0:128], psZ[:], RowA[:, h:h + 1], mA[:], ALU.add, ALU.min),
                        deps=[tza, t_c[5]])
                    tzat = p.pe.op(lambda t, h=h: t.matmul(psZ[:], e8[:, h * 128:(h + 1) * 128],
                                                           rows_c["RowA"][:], start=True, stop=True),
                                   deps=[t_rows["RowA"], tsa])
                    tsat = p.dve.op(lambda v, h=h: v.scalar_tensor_tensor(
                        zD[:, 128:256], psZ[:], CqA[:, h:h + 1], mAT[:], ALU.add, ALU.min),
                        deps=[tzat, t_c[6]])
                    texp = p.act.op(lambda s: s.activation(DA2[:, 0:256], zD[:, 0:256], AF.Exp),
                                    deps=[tsat])
                    t_DA[h] = p.dve.op(lambda v, h=h, kh=kh: v.tensor_tensor(
                        DAh[h][:], DA2[:, 0:128], KKt[kh][:], ALU.mult), deps=[texp, t_kk[kh]])
                    t_DAT[h] = p.dve.op(lambda v, h=h, kh=kh: v.tensor_tensor(
                        DATh[h][:], DA2[:, 128:256], KKt[kh][:], ALU.mult), deps=[t_DA[h]])
                    tzw = p.pe.op(lambda t, h=h: t.matmul(psZ[:], e8[:, h * 128:(h + 1) * 128],
                                                          rows_c["CW"][:], start=True, stop=True),
                                  deps=[t_rows["CW"], tsat])
                    t_psz_r[0] = tzw
                    tsw = p.dve.op(lambda v, h=h: v.scalar_tensor_tensor(
                        zD[:, 0:128], psZ[:], CqA[:, h:h + 1], mW[:], ALU.add, ALU.min),
                        deps=[tzw, t_c[7], texp])
                    t_psz_r[0] = tsw
                    texw = p.act.op(lambda s: s.activation(DA2[:, 0:128], zD[:, 0:128], AF.Exp),
                                    deps=[tsw])
                    t_DW[h] = p.dve.op(lambda v, h=h, kh=kh: v.tensor_tensor(
                        DWh[h][:], DA2[:, 0:128], QKt[kh][:], ALU.mult), deps=[texw, t_qk[kh]])

                t_B = [None] * 8
                t_vtm = [None, None]
                t_vt_last = None
                t_psbu_r = [t_chunk, t_chunk]
                for h in range(2 * KHL):
                    kh = h // 2
                    bv = h % 2
                    vc_t = cslice(8 + h)
                    tvt = p.pe.op(lambda t, vc_t=vc_t, bv=bv: t.transpose(
                        psT[:, bv * 128:bv * 128 + 128], vc_t, idb[:]),
                        deps=[t_convdone, t_c[0], t_vt_last or t_ktm[KHL - 1]])
                    tvte = p.dve.op(lambda v, h=h, bv=bv: v.tensor_copy(
                        Vtmh[bv][:], psT[:, bv * 128:bv * 128 + 128]), deps=[tvt])
                    t_vt_last = tvte
                    kc_t2 = cslice(4 + kh)
                    tks = p.pe.op(lambda t, h=h, kc_t2=kc_t2: t.matmul(
                        psBU[:], kc_t2, Sb[h][:], start=True, stop=True),
                        deps=[t_Sb[h], tvt, t_psbu_r[0]])
                    tbb1 = p.dve.op(lambda v, h=h, bv=bv: v.tensor_scalar(
                        tmpB[:], Vtmh[bv][:], beta8[:, h:h + 1], None, ALU.mult),
                        deps=[tvte, tsg])
                    t_B[h] = p.dve.op(lambda v, h=h: v.scalar_tensor_tensor(
                        Bth[h][:], psBU[:], negE8[:, h:h + 1], tmpB[:], ALU.mult, ALU.add),
                        deps=[tks, tne2, tbb1])
                    t_vtm[bv] = t_B[h]
                    t_psbu_r[0] = t_B[h]

                t_U = [None] * 8
                for h in range(2 * KHL):
                    t_U[h] = p.dve.op(lambda v, h=h: v.tensor_copy(Uth[h][:], Bth[h][:]),
                                      deps=[t_B[h]])
                t_ac = list(t_DAT)
                t_acA = list(t_DA)
                curA = [DAh[h][:] for h in range(8)]
                curAT = [DATh[h][:] for h in range(8)]
                curq = [1] * 8
                t_psbu_r = [t_psbu_r[0], t_psbu_r[0]]
                t_psq_r = [t_chunk, t_chunk]
                for lev in range(7):
                    for h in range(2 * KHL):
                        bu = h % 2
                        AT_c = curAT[h]
                        tap = p.pe.op(lambda t, h=h, AT_c=AT_c, bu=bu: t.matmul(
                            psBU[:] if bu == 0 else psZ[:], AT_c, Uth[h][:],
                            start=True, stop=True),
                            deps=[t_U[h], t_ac[h], t_acA[h], t_psbu_r[bu]])
                        t_U[h] = p.dve.op(lambda v, h=h, bu=bu: v.scalar_tensor_tensor(
                            Uth[h][:], psBU[:] if bu == 0 else psZ[:], 0.0, Uth[h][:],
                            ALU.add, ALU.add), deps=[tap])
                        t_psbu_r[bu] = t_U[h]
                    if lev < 6:
                        for h in range(2 * KHL):
                            bq = h % 2
                            A_c = curA[h]
                            AT_c = curAT[h]
                            nq = curq[h]
                            tq1_ = p.pe.op(lambda t, A_c=A_c, AT_c=AT_c, bq=bq: t.matmul(
                                psSq[:, 0:128] if bq == 0 else psKQ[:, 0:128], AT_c, A_c,
                                start=True, stop=True),
                                deps=[t_ac[h], t_acA[h], t_psq_r[bq], t_U[h]])
                            tq2_ = p.pe.op(lambda t, A_c=A_c, AT_c=AT_c, bq=bq: t.matmul(
                                psSq[:, 128:256] if bq == 0 else psKQ[:, 128:256], A_c, AT_c,
                                start=True, stop=True), deps=[tq1_])
                            tqe = p.dve.op(lambda v, h=h, nq=nq, bq=bq: v.tensor_copy(
                                ABh[h][:, nq * 256:(nq + 1) * 256],
                                psSq[:, 0:256] if bq == 0 else psKQ[:, 0:256]),
                                deps=[tq2_])
                            t_ac[h] = tqe
                            t_acA[h] = tqe
                            t_psq_r[bq] = tqe
                            curA[h] = ABh[h][:, nq * 256: nq * 256 + 128]
                            curAT[h] = ABh[h][:, nq * 256 + 128: nq * 256 + 256]
                            curq[h] = 1 - nq

                t_O = [None] * 8
                t_pso_r = [t_chunk, t_chunk, t_chunk]
                for h in range(2 * KHL):
                    kh = h // 2
                    qc_t = cslice(kh)
                    to1 = p.pe.op(lambda t, qc_t=qc_t, h=h: t.matmul(psOS[:, 0:128], qc_t,
                                                                     Sb[h][:], start=True, stop=True),
                                  deps=[t_Sb[h], t_U[h], t_pso_r[0], t_pso_r[2]])
                    to2 = p.pe.op(lambda t, h=h: t.matmul(psOS[:, 128:256], DWh[h][:], Uth[h][:],
                                                          start=True, stop=True),
                                  deps=[t_DW[h], t_U[h], to1, t_pso_r[1]])
                    to3 = p.dve.op(lambda v, h=h: v.tensor_scalar(tmpB[:], psOS[:, 0:128],
                                                                  osc8[:, h:h + 1], None,
                                                                  ALU.mult), deps=[to2, tosc])
                    t_pso_r[0] = to3
                    to4 = p.dve.op(lambda v, h=h: v.tensor_tensor(
                        Oth[h][:], psOS[:, 128:256], tmpB[:], ALU.add), deps=[to2, to3])
                    t_pso_r[1] = to4
                    t_O[h] = to4
                    tus = p.dve.op(lambda v, h=h: v.tensor_scalar(Uth[h][:], Uth[h][:],
                                                                  eco8[:, h:h + 1], None,
                                                                  ALU.mult), deps=[t_U[h], tec2, to4])
                    tsmm = p.pe.op(lambda t, h=h, kh=kh: t.matmul(psOS[:, 256:384], Ktm[kh][:],
                                                                  Uth[h][:], start=True, stop=True),
                                   deps=[t_ktm[kh], tus, t_pso_r[2]])
                    tsu = p.dve.op(lambda v, h=h: v.scalar_tensor_tensor(
                        S[h][:], S[h][:], gam8[:, h:h + 1], psOS[:, 256:384], ALU.mult,
                        ALU.add), deps=[tsmm, tgam, t_S[h]])
                    t_pso_r[2] = tsu
                    t_S[h] = tsu
                    t_Sb[h] = p.dve.op(lambda v, h=h: v.tensor_copy(Sb[h][:], S[h][:]),
                                       deps=[tsu])

                tz1_last = None
                for kh in range(KHL):
                    trms1 = p.dve.op(lambda v, kh=kh: v.tensor_tensor(
                        ogt[:, 0:128], Oth[2 * kh][:], Oth[2 * kh][:], ALU.mult),
                        deps=[t_O[2 * kh], t_ogdma])
                    trms1b = p.dve.op(lambda v, kh=kh: v.tensor_tensor(
                        ogt[:, 128:256], Oth[2 * kh + 1][:], Oth[2 * kh + 1][:], ALU.mult),
                        deps=[t_O[2 * kh + 1], trms1])
                    trms2 = p.dve.op(lambda v: v.tensor_reduce(
                        rmsc[:], ogt[:].rearrange("p (a b) -> p a b", a=2), AX.X, ALU.add),
                        deps=[trms1b])
                    trms3 = p.dve.op(lambda v: v.tensor_scalar(rmsc[:], rmsc[:], 1.0 / 128, 1e-6,
                                                               ALU.mult, ALU.add), deps=[trms2])
                    trms4 = p.act.op(lambda s: s.activation(rmsc[:], rmsc[:], AF.Sqrt),
                                     deps=[trms3, tec2])
                    trms5 = p.dve.op(lambda v: v.reciprocal(rmsc[:], rmsc[:]), deps=[trms4])
                    tz1 = p.act.op(lambda s, kh=kh, n=n: s.activation(
                        silz[:], ztile[n % 2][:, kh * 256:(kh + 1) * 256], AF.Silu),
                        deps=[t_zl, trms4])
                    tz1_last = tz1
                    tg1_ = p.dve.op(lambda v, kh=kh: v.tensor_scalar(
                        ogt[:, 0:128], Oth[2 * kh][:], rmsc[:, 0:1], None, ALU.mult),
                        deps=[trms5])
                    tg2_ = p.dve.op(lambda v, kh=kh: v.tensor_scalar(
                        ogt[:, 128:256], Oth[2 * kh + 1][:], rmsc[:, 1:2], None, ALU.mult),
                        deps=[tg1_])
                    tg3_ = p.dve.op(lambda v: v.tensor_tensor(ogt[:, 0:256], ogt[:, 0:256],
                                                              nwt2[:], ALU.mult),
                                    deps=[tg2_, t_nwt2])
                    tg4_ = p.dve.op(lambda v: v.tensor_tensor(ogt[:, 0:256], ogt[:, 0:256],
                                                              silz[:], ALU.mult),
                                    deps=[tg3_, tz1])
                    t_ogdma = p.dma_hw(p.sp, f"og{kh % 2}",
                                       og_o[n * L:(n + 1) * L, kh * 256:(kh + 1) * 256],
                                       ogt[:, 0:256], deps=[tg4_])
                t_zt[n % 2] = tz1_last
                t_chunk = p.dve.tag()
                t_coef_users = t_chunk
            t_group_consumed[gb] = t_chunk
        p.finish()
        p.emit()
    return nc


def make_E_inmaps(inputs, d_outs):
    T = 8192
    maps = []
    e8 = np.zeros((8, 8 * 128), dtype=np.float32)
    for h in range(8):
        e8[h, h * 128:(h + 1) * 128] = 1.0
    sidx = np.arange(128)
    triu = (sidx[:, None] <= sidx[None, :]).astype(np.float32)
    mA = np.where(sidx[None, :] < sidx[:, None], 1e30, -1e30).astype(np.float32)
    mAT = np.where(sidx[None, :] > sidx[:, None], 1e30, -1e30).astype(np.float32)
    mW = np.where(sidx[None, :] >= sidx[:, None], 1e30, -1e30).astype(np.float32)
    cw = np.asarray(inputs["L1_attn_conv_w"])
    dtb = np.asarray(inputs["L1_attn_dt_bias"])
    alog = np.asarray(inputs["L1_attn_A_log"])
    nw = np.asarray(inputs["L1_attn_norm_w"])
    for bb in range(2):
        cores = [4 * bb + i for i in range(4)]
        qkv_full = np.zeros((8192, T), dtype=BF)
        z_full = np.zeros((4096, T), dtype=BF)
        ba_full = np.zeros((64, T), dtype=np.float32)
        for m, corec in enumerate(cores):
            qkv_c = np.asarray(d_outs[corec]["qkv_o"]).view(BF)
            z_c = np.asarray(d_outs[corec]["z_o"]).view(BF)
            ba_c = np.asarray(d_outs[corec]["ba_o"])
            for j in range(16):
                blk = 4 * j + m
                qkv_full[:, blk * 128:(blk + 1) * 128] = qkv_c[:, j * 128:(j + 1) * 128]
                z_full[:, blk * 128:(blk + 1) * 128] = z_c[:, j * 128:(j + 1) * 128]
                ba_full[:, blk * 128:(blk + 1) * 128] = ba_c[:, j * 128:(j + 1) * 128]
        for m in range(4):
            core = 4 * bb + m
            qh = np.ascontiguousarray(qkv_full[512 * m:512 * (m + 1)])
            kh = np.ascontiguousarray(qkv_full[2048 + 512 * m:2048 + 512 * (m + 1)])
            vh = np.ascontiguousarray(qkv_full[4096 + 1024 * m:4096 + 1024 * (m + 1)])
            zh = np.ascontiguousarray(z_full[1024 * m:1024 * (m + 1)].T)
            bah = np.ascontiguousarray(
                np.concatenate([ba_full[8 * m:8 * m + 8], ba_full[32 + 8 * m:32 + 8 * m + 8]],
                               axis=0).T)
            cwrows = np.concatenate([cw[2048 + 512 * m + 0:2048 + 512 * m + 0][0:0]] ) if False else None
            cw_q = cw[512 * m:512 * (m + 1)]
            cw_k = cw[2048 + 512 * m:2048 + 512 * (m + 1)]
            cw_v = cw[4096 + 1024 * m:4096 + 1024 * (m + 1)]
            cwh = np.ascontiguousarray(np.concatenate([cw_q, cw_k, cw_v], axis=0))
            maps.append({
                "qh_in": qh, "kh_in": kh, "vh_in": vh, "zh_in": zh, "ba_in": bah,
                "cw_in": cwh,
                "dtb_in": np.ascontiguousarray(dtb[8 * m:8 * m + 8][None, :]),
                "alog_in": np.ascontiguousarray(alog[8 * m:8 * m + 8][None, :]),
                "nw_in": np.ascontiguousarray(nw[None, :]),
                "identb": np.eye(128, dtype=np.float32).astype(BF),
                "identf": np.eye(128, dtype=np.float32),
                "ident8": np.eye(8, dtype=np.float32),
                "ones8x": np.ones((8, 128), dtype=np.float32),
                "e8_in": e8, "triu_in": triu, "mA_in": mA, "mAT_in": mAT, "mW_in": mW,
            })
    return maps


def build_F():
    VD = 4096
    nc = bass.Bass("TRN2", target_bir_lowering=False)
    x2_in = nc.dram_tensor("x2_in", [C, TPC], F32, kind="ExternalInput").ap()
    og_in = nc.dram_tensor("og_in", [VD, TPC], BF16, kind="ExternalInput").ap()
    woT = nc.dram_tensor("woT", [VD, C], F32, kind="ExternalInput").ap()
    waqT = nc.dram_tensor("waqT", [C, C], F32, kind="ExternalInput").ap()
    wakT = nc.dram_tensor("wakT", [C, C], F32, kind="ExternalInput").ap()
    waqlT = nc.dram_tensor("waqlT", [C, 128], F32, kind="ExternalInput").ap()
    waklT = nc.dram_tensor("waklT", [C, 128], F32, kind="ExternalInput").ap()
    wavT = nc.dram_tensor("wavT", [C, C], F32, kind="ExternalInput").ap()
    wrec = nc.dram_tensor("wrec", [16, 1], F32, kind="ExternalInput").ap()
    wstd = nc.dram_tensor("wstd", [16, 1], F32, kind="ExternalInput").ap()
    patlow = nc.dram_tensor("patlow", [16, 128], F32, kind="ExternalInput").ap()
    pathigh = nc.dram_tensor("pathigh", [16, 128], F32, kind="ExternalInput").ap()
    ident16 = nc.dram_tensor("ident16", [16, 16], F32, kind="ExternalInput").ap()
    x3_o = nc.dram_tensor("x3_o", [C, TPC], F32, kind="ExternalOutput").ap()
    qf_o = nc.dram_tensor("qf_o", [C, TPC], BF16, kind="ExternalOutput").ap()
    kf_o = nc.dram_tensor("kf_o", [C, TPC], BF16, kind="ExternalOutput").ap()
    v_o = nc.dram_tensor("v_o", [TPC, C], BF16, kind="ExternalOutput").ap()

    x2v = x2_in.rearrange("(c p) t -> c p t", p=128)
    ogv = og_in.rearrange("(c p) t -> c p t", p=128)
    x3v = x3_o.rearrange("(c p) t -> c p t", p=128)
    qfv = qf_o.rearrange("(c p) t -> c p t", p=128)
    kfv = kf_o.rearrange("(c p) t -> c p t", p=128)

    with ExitStack() as ctx:
        def sb(nm, shp, dt, stack=None):
            return (stack or ctx).enter_context(nc.sbuf_tensor(nm, shp, dt))
        p = Prog(nc, ctx)
        onesb = sb("onesbF", [128, 1], BF16)
        onesr = sb("onesrF", [1, 128], BF16)
        t_ob = p.dve.op(lambda v: v.memset(onesb[:], 1.0))
        t_or = p.dve.op(lambda v: v.memset(onesr[:], 1.0))
        c_eps5, tg_e5 = p.constf(1e-5)
        c_eps8, tg_e8 = p.constf(1e-8)

        xf = sb("xfF", [128, 16 * TPC], BF16)
        xn = sb("xnF", [128, 16 * TPC], BF16)

        def xfs(c, tt, w=TT):
            return xf[:, c * TPC + tt * w: c * TPC + (tt + 1) * w]

        s1 = ctx.enter_context(ExitStack())
        ogb = sb("ogb", [128, 32 * TT], BF16, s1)
        wob = [sb(f"wob{i}", [128, 32 * 128], BF16, s1) for i in range(2)]
        x2t = [sb(f"x2tF{i}", [128, TT], F32, s1) for i in range(2)]
        x3t = [sb(f"x3tF{i}", [128, TT], F32, s1) for i in range(2)]

        def w_cols(wT, oc, nkc, width=128):
            return wT.rearrange("(kc p) o -> p kc o", p=128)[:, :, oc * width:(oc + 1) * width]

        with nc.psum_tensor("psf0", [128, TT], F32) as psf0, \
             nc.psum_tensor("psf1", [128, TT], F32) as psf1:
            psfs = [psf0, psf1]
            t_w_free = [None, None]
            t_og_used = None
            t_x2l = [None, None]
            t_xw = [None, None]
            t_x3st = [None, None]
            for tt in range(NTT):
                t_ogl = []
                for c in range(32):
                    t_ogl.append(p.dma_hw(p.sp, f"ogl{c % 4}", ogb[:, c * TT:(c + 1) * TT],
                                          ogv[c][:, tt * TT:(tt + 1) * TT],
                                          deps=[t_og_used, t_ogl[c - 4] if c >= 4 else None]))
                for oc in range(16):
                    bo = oc % 2
                    tw = p.dma_sw(f"woF{bo}", wob[bo][:], w_cols(woT, oc, 32), deps=[t_w_free[bo]])
                    ps = psfs[bo]
                    t_mm = None
                    for kc in range(32):
                        fn = p.pe.op if kc == 31 else p.pe.op_quiet
                        t_mm = fn(lambda t, kc=kc, oc=oc, ps=ps, bo=bo, s=(kc == 0), e=(kc == 31):
                                  t.matmul(ps[:], wob[bo][:, kc * 128:(kc + 1) * 128],
                                           ogb[:, kc * TT:(kc + 1) * TT], start=s, stop=e),
                                  deps=([tw, t_xw[bo]] + t_ogl if kc == 0 else ()))
                    t_w_free[bo] = t_mm
                    tx2 = p.dma_hw(p.sp, f"x2F{bo}", x2t[bo][:], x2v[oc][:, tt * TT:(tt + 1) * TT],
                                   deps=[t_x2l[bo]])
                    txa = p.dve.op(lambda v, bo=bo, ps=ps:
                                   v.tensor_tensor(x3t[bo][:], x2t[bo][:], ps[:], ALU.add),
                                   deps=[t_mm, tx2, t_x3st[bo]])
                    t_x2l[bo] = txa
                    t_xw[bo] = txa
                    txc = p.dve.op(lambda v, oc=oc, tt=tt, bo=bo:
                                   v.tensor_copy(xfs(oc, tt), x3t[bo][:]), deps=[txa])
                    t_x3st[bo] = p.dma_hw(p.sp, f"x3st{bo}", x3v[oc][:, tt * TT:(tt + 1) * TT],
                                          x3t[bo][:], deps=[txc])
                t_og_used = t_mm
        p.barrier()
        s1.close()

        s2 = ctx.enter_context(ExitStack())
        casts = [sb(f"castF{i}", [128, TT], BF16, s2) for i in range(2)]
        sqt = [sb(f"sqtF{i}", [128, TT], BF16, s2) for i in range(2)]
        sumr = sb("sumrF", [1, TPC], F32, s2)
        sqr = sb("sqrF", [1, TPC], F32, s2)
        mu_b = sb("mu_bF", [1, TPC], BF16, s2)
        rs_b = sb("rs_bF", [1, TPC], BF16, s2)
        rowa = sb("rowaF", [1, TPC], F32, s2)
        roww = sb("rowwF", [1, TPC], F32, s2)
        with nc.psum_tensor("pssumF", [1, TT], F32) as pssum, \
             nc.psum_tensor("pssqF", [1, TT], F32) as pssq:
            t_sum_evac = None
            t_sq_evac = None
            for tt in range(NTT):
                t_mm = None
                t_mmq = None
                for c in range(16):
                    tcast = p.dve.op(lambda v, c=c, tt=tt, b=c % 2:
                                     v.tensor_copy(casts[b][:], xfs(c, tt)), deps=[t_mm])
                    t_mm = p.pe.op(lambda t, c=c, s=(c == 0), e=(c == 15), b=c % 2:
                                   t.matmul(pssum[:], onesb[:], casts[b][:], start=s, stop=e),
                                   deps=[tcast, t_ob] + ([t_sum_evac] if c == 0 else []))
                    t_sq = p.act.op(lambda s, c=c, b=c % 2:
                                    s.activation(sqt[b][:], casts[b][:], AF.Square),
                                    deps=[tcast, t_mmq])
                    t_mmq = p.pe.op(lambda t, c=c, s=(c == 0), e=(c == 15), b=c % 2:
                                    t.matmul(pssq[:], onesb[:], sqt[b][:], start=s, stop=e),
                                    deps=[t_sq] + ([t_sq_evac] if c == 0 else []))
                t_sum_evac = p.dve.op(lambda v, tt=tt:
                                      v.tensor_copy(sumr[0:1, tt * TT:(tt + 1) * TT], pssum[:]),
                                      deps=[t_mm])
                t_sq_evac = p.dve.op(lambda v, tt=tt:
                                     v.tensor_copy(sqr[0:1, tt * TT:(tt + 1) * TT], pssq[:]),
                                     deps=[t_mmq])
        t1 = p.dve.op(lambda v: v.tensor_scalar_mul(rowa[:], sumr[:], 1.0 / C), deps=[t_sum_evac])
        t_mub = p.dve.op(lambda v: v.tensor_copy(mu_b[:], rowa[:]), deps=[t1])
        t2 = p.dve.op(lambda v: v.tensor_scalar_mul(roww[:], sqr[:], 1.0 / C), deps=[t_sq_evac])
        t3 = p.dve.op(lambda v: v.tensor_tensor(sumr[:], rowa[:], rowa[:], ALU.mult), deps=[t1, t2])
        t4 = p.dve.op(lambda v: v.tensor_sub(roww[:], roww[:], sumr[:]), deps=[t3])
        t5 = p.act.op(lambda s: s.activation(rowa[:], roww[:], AF.Sqrt, bias=c_eps5[0:1, :]),
                      deps=[t4, t_mub, tg_e5])
        t6 = p.dve.op(lambda v: v.reciprocal(roww[:], rowa[:]), deps=[t5])
        t_rsb = p.dve.op(lambda v: v.tensor_copy(rs_b[:], roww[:]), deps=[t6])
        p.barrier()

        mub_s = sb("mub_sF", [128, TT], BF16, s2)
        rsb_s = sb("rsb_sF", [128, TT], BF16, s2)
        with nc.psum_tensor("psbF0", [128, TT], F32) as psb0, \
             nc.psum_tensor("psbF1", [128, TT], F32) as psb1:
            te1 = te2 = None
            tb_last = None
            for tt in range(NTT):
                tm1 = p.pe.op(lambda t, tt=tt: t.matmul(psb0[:], onesr[:],
                                                        mu_b[0:1, tt * TT:(tt + 1) * TT],
                                                        start=True, stop=True),
                              deps=[t_mub, t_or, te1])
                tm2 = p.pe.op(lambda t, tt=tt: t.matmul(psb1[:], onesr[:],
                                                        rs_b[0:1, tt * TT:(tt + 1) * TT],
                                                        start=True, stop=True),
                              deps=[t_rsb, te2])
                te1 = p.dve.op(lambda v: v.tensor_copy(mub_s[:], psb0[:]), deps=[tm1, tb_last])
                te2 = p.dve.op(lambda v: v.tensor_copy(rsb_s[:], psb1[:]), deps=[tm2])
                for c in range(16):
                    ta = p.dve.op(lambda v, c=c, tt=tt:
                                  v.tensor_sub(xn[:, c * TPC + tt * TT: c * TPC + (tt + 1) * TT],
                                               xfs(c, tt), mub_s[:]),
                                  deps=[te1])
                    tb_last = p.dve.op(lambda v, c=c, tt=tt: v.tensor_tensor(
                        xn[:, c * TPC + tt * TT: c * TPC + (tt + 1) * TT],
                        xn[:, c * TPC + tt * TT: c * TPC + (tt + 1) * TT], rsb_s[:], ALU.mult),
                        deps=[ta, te2])
        p.barrier()
        s2.close()
        t_xn = p.dve.tag()

        def xns(kc, tt, w=TT):
            return xn[:, kc * TPC + tt * w: kc * TPC + (tt + 1) * w]

        s4 = ctx.enter_context(ExitStack())
        w16 = sb("w16", [16, 2], F32, s4)
        m16 = sb("m16", [16, 1], F32, s4)
        s16 = sb("s16d", [16, 16], F32, s4)
        pat_l = sb("pat_l", [16, 128], F32, s4)
        pat_h = sb("pat_h", [16, 128], F32, s4)
        id16 = sb("id16", [16, 16], F32, s4)
        csq = sb("csq", [128, 16], F32, s4)
        mrep = sb("mrep", [128, 16], F32, s4)
        tw1 = p.dma_hw(p.sp, "wrec", w16[:, 0:1], wrec)
        tw2 = p.dma_hw(p.sp, "wstd", w16[:, 1:2], wstd)
        tp1 = p.dma_hw(p.sp, "patl", pat_l[:], patlow)
        tp2 = p.dma_hw(p.sp, "path", pat_h[:], pathigh)
        tp3 = p.dma_hw(p.sp, "id16", id16[:], ident16)
        tm_ = p.dve.op(lambda v: v.tensor_single_scalar(m16[:], w16[:, 0:1], 0.1, ALU.is_gt),
                       deps=[tw1])
        ts1 = p.dve.op(lambda v: v.tensor_scalar(w16[:, 0:1], w16[:, 0:1], 1e-8, None, ALU.max),
                       deps=[tm_])
        ts2 = p.dve.op(lambda v: v.tensor_scalar(w16[:, 1:2], w16[:, 1:2], 1e-8, None, ALU.max),
                       deps=[tw2, ts1])
        ts3 = p.act.op(lambda s: s.activation(w16[:, 0:2], w16[:, 0:2], AF.Sqrt), deps=[ts2])
        with nc.psum_tensor("pscs", [128, 16], F32) as pscs, \
             nc.psum_tensor("psmr", [128, 16], F32) as psmr:
            td1_ = p.dve.op(lambda v: v.tensor_scalar(s16[:], id16[:], w16[:, 0:1], None, ALU.mult),
                            deps=[ts3, tp3])
            tcs1 = p.pe.op(lambda t: t.matmul(pscs[:], pat_l[:], s16[:], start=True, stop=False),
                           deps=[td1_, tp1])
            td2_ = p.dve.op(lambda v: v.tensor_scalar(s16[:], id16[:], w16[:, 1:2], None, ALU.mult),
                            deps=[tcs1])
            tcs2 = p.pe.op(lambda t: t.matmul(pscs[:], pat_h[:], s16[:], start=False, stop=True),
                           deps=[td2_, tp2])
            tcse = p.dve.op(lambda v: v.tensor_copy(csq[:], pscs[:]), deps=[tcs2])
            td3_ = p.dve.op(lambda v: v.tensor_scalar(s16[:], id16[:], m16[:], None, ALU.mult),
                            deps=[tcse, tm_])
            tmr1 = p.pe.op(lambda t: t.matmul(psmr[:], pat_l[:], s16[:], start=True, stop=False),
                           deps=[td3_, tcse])
            tmr2 = p.pe.op(lambda t: t.matmul(psmr[:], pat_h[:], s16[:], start=False, stop=True),
                           deps=[tmr1])
            tmre = p.dve.op(lambda v: v.tensor_copy(mrep[:], psmr[:]), deps=[tmr2])
        p.barrier()

        wb_ = [sb(f"wbF{i}", [128, 16 * 128], BF16, s4) for i in range(2)]
        wlo = [sb(f"wlo{i}", [128, 16 * 8], BF16, s4) for i in range(2)]
        stg = [sb(f"stgF{i}", [128, TT], BF16, s4) for i in range(2)]
        with nc.psum_tensor("psqF0", [128, TT], F32) as psq0, \
             nc.psum_tensor("psqF1", [128, TT], F32) as psq1:
            psqs = [psq0, psq1]
            t_w_free = [None, None]
            t_lo_free = [None, None]
            t_dma = [None, None]
            t_ev = [None, None]
            i = 0
            for which, wmain, walt, outv in [("q", waqT, waklT, qfv), ("k", wakT, waqlT, kfv)]:
                for hh in range(16):
                    bo = hh % 2
                    tw = p.dma_sw(f"wF{bo}", wb_[bo][:], w_cols(wmain, hh, 16),
                                  deps=[t_w_free[bo]])
                    twl = p.dma_sw(f"wl{bo}", wlo[bo][:],
                                   walt.rearrange("(kc p) o -> p kc o", p=128)[:, :, hh * 8:(hh + 1) * 8],
                                   deps=[t_lo_free[bo]])
                    t_bl = None
                    for kc in range(16):
                        tdiff = p.dve.op(lambda v, kc=kc, bo=bo: v.tensor_sub(
                            wlo[bo][:, kc * 8:(kc + 1) * 8], wlo[bo][:, kc * 8:(kc + 1) * 8],
                            wb_[bo][:, kc * 128:kc * 128 + 8]), deps=[tw, twl])
                        t_bl = p.dve.op(lambda v, kc=kc, bo=bo, hh=hh: v.scalar_tensor_tensor(
                            wb_[bo][:, kc * 128:kc * 128 + 8], wlo[bo][:, kc * 8:(kc + 1) * 8],
                            mrep[:, hh:hh + 1], wb_[bo][:, kc * 128:kc * 128 + 8],
                            ALU.mult, ALU.add), deps=[tdiff, tmre])
                    t_lo_free[bo] = t_bl
                    t_mm = None
                    for tt in range(NTT):
                        ps = psqs[i % 2]
                        st_ = stg[i % 2]
                        for kc in range(16):
                            fn = p.pe.op if kc == 15 else p.pe.op_quiet
                            t_mm = fn(lambda t, kc=kc, tt=tt, ps=ps, bo=bo, s=(kc == 0), e=(kc == 15):
                                      t.matmul(ps[:], wb_[bo][:, kc * 128:(kc + 1) * 128],
                                               xns(kc, tt), start=s, stop=e),
                                      deps=([tw, t_bl, t_ev[i % 2]] if kc == 0 else ()))
                        t_e = p.dve.op(lambda v, ps=ps, st_=st_, hh=hh: v.tensor_scalar(
                            st_[:], ps[:], csq[:, hh:hh + 1], None, ALU.mult),
                            deps=[t_mm, t_dma[i % 2], tcse])
                        t_ev[i % 2] = t_e
                        t_dma[i % 2] = p.dma_hw(p.sp, f"sF{i % 2}",
                                                outv[hh][:, tt * TT:(tt + 1) * TT], st_[:],
                                                deps=[t_e])
                        i += 1
                    t_w_free[bo] = t_mm
            wv_sb = sb("wv_sb", [128, 16 * TT], BF16, s4)
            t_vw_free = None
            for vg in range(4):
                t_vw = p.dma_sw("wvF", wv_sb[:],
                                wavT.rearrange("(kc p) o -> p kc o", p=128)[:, :, vg * TT:(vg + 1) * TT],
                                deps=[t_vw_free])
                t_mm = None
                for tsb in range(16):
                    ps = psqs[tsb % 2]
                    st_ = stg[tsb % 2]
                    for kc in range(16):
                        fn = p.pe.op if kc == 15 else p.pe.op_quiet
                        t_mm = fn(lambda t, kc=kc, tsb=tsb, vg=vg, ps=ps, s=(kc == 0), e=(kc == 15):
                                  t.matmul(ps[:], xn[:, kc * TPC + tsb * 128:kc * TPC + (tsb + 1) * 128],
                                           wv_sb[:, kc * TT:(kc + 1) * TT], start=s, stop=e),
                                  deps=([t_vw, t_ev[tsb % 2]] if kc == 0 else ()))
                    t_e = p.act.op(lambda s, ps=ps, st_=st_: s.copy(st_[:], ps[:]),
                                   deps=[t_mm, t_dma[tsb % 2]])
                    t_ev[tsb % 2] = t_e
                    t_dma[tsb % 2] = p.dma_hw(p.sp, f"vF{tsb % 2}",
                                              v_o[tsb * 128:(tsb + 1) * 128, vg * TT:(vg + 1) * TT],
                                              st_[:], deps=[t_e])
                t_vw_free = t_mm
        p.finish()
        p.emit()
    return nc


def build_G1():
    T = 8192
    nc = bass.Bass("TRN2", target_bir_lowering=False)
    qh4 = nc.dram_tensor("qh4", [512, T], BF16, kind="ExternalInput").ap()
    kh4 = nc.dram_tensor("kh4", [512, T], BF16, kind="ExternalInput").ap()
    vh4 = nc.dram_tensor("vh4", [T, 512], BF16, kind="ExternalInput").ap()
    trineg = nc.dram_tensor("trineg", [128, 128], F32, kind="ExternalInput").ap()
    identf = nc.dram_tensor("identf", [128, 128], F32, kind="ExternalInput").ap()
    outF_o = nc.dram_tensor("outF_o", [512, T], BF16, kind="ExternalOutput").ap()
    SCALE = 1.0 / np.sqrt(128.0)

    with ExitStack() as ctx:
        def sb(nm, shp, dt, stack=None):
            return (stack or ctx).enter_context(nc.sbuf_tensor(nm, shp, dt))
        p = Prog(nc, ctx)
        idf = sb("idfG", [128, 128], F32)
        tri = sb("triG", [128, 128], F32)
        onesb = sb("onesbG", [128, 1], BF16)
        onesr = sb("onesrG", [1, 128], BF16)
        onesrf = sb("onesrfG", [1, 128], F32)
        t_ob = p.dve.op(lambda v: v.memset(onesb[:], 1.0))
        t_or = p.dve.op(lambda v: v.memset(onesr[:], 1.0))
        t_orf = p.dve.op(lambda v: v.memset(onesrf[:], 1.0))
        t_idf = p.dma_hw(p.sp, "idf", idf[:], identf)
        t_tri = p.dma_hw(p.sp, "tri", tri[:], trineg)

        Kh = sb("Kh", [128, T], BF16)
        Vh = sb("Vh", [128, T], BF16)
        Qall = sb("Qall", [128, T], BF16)
        sqh = sb("sqh", [128, T], BF16)
        Pb = [sb(f"Pb{i}", [128, TT], BF16) for i in range(2)]
        lrow = sb("lrowG", [1, TT], F32)
        lrowb = sb("lrowbG", [1, TT], BF16)
        rb = sb("rbG", [128, TT], BF16)
        mrow = sb("mrowG", [1, 4], F32)
        mcol = sb("mcolG", [128, 1], F32)
        negMh = sb("negMhG", [128, 1], F32)
        oste = sb("osteG", [128, TT], BF16)

        psS = [ctx.enter_context(nc.psum_tensor(f"psSG{i}", [128, TT], F32)) for i in range(2)]
        psO = ctx.enter_context(nc.psum_tensor("psOG", [128, TT], F32))
        psL = ctx.enter_context(nc.psum_tensor("psLG", [1, TT], F32))
        psM = ctx.enter_context(nc.psum_tensor("psMG", [128, 64], F32))
        psB = ctx.enter_context(nc.psum_tensor("psBG", [128, TT], F32))

        t_head_done = None
        t_l_ev = None
        t_o_ev = None
        t_rb_ev = None
        t_ofdma = None
        for i4 in range(4):
            tkl = p.dma_hw(p.sp, "khl", Kh[:], kh4[i4 * 128:(i4 + 1) * 128, :],
                           deps=[t_head_done])
            tvl = p.dma_hw(p.sp, "vhl", Vh[:],
                           vh4.rearrange("(kb p) d -> p kb d", p=128)[:, :, i4 * 128:(i4 + 1) * 128],
                           deps=[t_head_done])
            tq0 = p.dma_hw(p.sp, "qhl", Qall[:], qh4[i4 * 128:(i4 + 1) * 128, :],
                           deps=[t_head_done])
            tql = p.dve.op(lambda v: v.tensor_scalar_mul(Qall[:], Qall[:], SCALE),
                           deps=[tq0, t_head_done])
            tsq = p.act.op(lambda s: s.activation(sqh[:], Kh[:], AF.Square),
                           deps=[tkl, t_head_done])
            t_mx = None
            for kb in range(64):
                fn = p.pe.op if kb == 63 else p.pe.op_quiet
                t_mx = fn(lambda t, kb=kb: t.matmul(psM[:, kb:kb + 1],
                                                    sqh[:, kb * 128:(kb + 1) * 128], onesb[:],
                                                    start=True, stop=True),
                          deps=[tsq, t_ob, t_head_done] if kb == 0 else ())
            tmk1 = p.dve.op(lambda v: v.tensor_reduce(mcol[:], psM[:, 0:64], AX.X, ALU.max),
                            deps=[t_mx])
            ttk = p.pe.op(lambda t: t.transpose(psS[0][0:1, 0:128], mcol[:], idf[:]),
                          deps=[tmk1, t_idf])
            tmk2 = p.dve.op(lambda v: v.tensor_reduce(mrow[0:1, 1:2], psS[0][0:1, 0:128], AX.X,
                                                      ALU.max), deps=[ttk])
            tsq2 = p.act.op(lambda s: s.activation(sqh[:], Qall[:], AF.Square),
                            deps=[tql, t_mx])
            t_mq = None
            for kb in range(64):
                fn = p.pe.op if kb == 63 else p.pe.op_quiet
                t_mq = fn(lambda t, kb=kb: t.matmul(psM[:, kb:kb + 1],
                                                    sqh[:, kb * 128:(kb + 1) * 128], onesb[:],
                                                    start=True, stop=True),
                          deps=[tsq2, tmk1] if kb == 0 else ())
            tmq1 = p.dve.op(lambda v: v.tensor_reduce(mcol[:], psM[:, 0:64], AX.X, ALU.max),
                            deps=[t_mq, tmk2])
            ttq = p.pe.op(lambda t: t.transpose(psS[0][0:1, 0:128], mcol[:], idf[:]),
                          deps=[tmq1, tmk2])
            tmq2 = p.dve.op(lambda v: v.tensor_reduce(mrow[0:1, 0:1], psS[0][0:1, 0:128], AX.X,
                                                      ALU.max), deps=[ttq])
            tm3 = p.dve.op(lambda v: v.tensor_tensor(mrow[0:1, 2:3], mrow[0:1, 0:1],
                                                     mrow[0:1, 1:2], ALU.mult), deps=[tmq2])
            tm4 = p.act.op(lambda s: s.activation(mrow[0:1, 2:3], mrow[0:1, 2:3], AF.Sqrt),
                           deps=[tm3])
            tm5 = p.dve.op(lambda v: v.tensor_scalar_mul(mrow[0:1, 2:3], mrow[0:1, 2:3], -1.0),
                           deps=[tm4])
            tbc = p.pe.op(lambda t: t.matmul(psB[:, 0:1], onesrf[:], mrow[0:1, 2:3],
                                             start=True, stop=True), deps=[tm5, t_orf, t_rb_ev])
            t_nm = p.dve.op(lambda v: v.tensor_copy(negMh[:], psB[:, 0:1]), deps=[tbc])

            for qt in range(16):
                nkb = 4 * qt + 4
                t_exp = [None, None]
                t_av = [None, None, None]
                for kb in range(nkb):
                    sb_ = kb % 2
                    tmm = p.pe.op(lambda t, kb=kb, qt=qt, sb_=sb_:
                                  t.matmul(psS[sb_][:], Kh[:, kb * 128:(kb + 1) * 128],
                                           Qall[:, qt * TT:(qt + 1) * TT], start=True, stop=True),
                                  deps=[tql, t_nm, t_exp[sb_]])
                    dsub = kb - 4 * qt
                    if dsub < 0:
                        texp = p.act.op(lambda s, sb_=sb_: s.activation(
                            Pb[sb_][:], psS[sb_][:], AF.Exp, bias=negMh[:]),
                            deps=[tmm, t_av[sb_]])
                    else:
                        ops = []
                        if dsub > 0:
                            ops.append(p.dve.op(lambda v, sb_=sb_, dsub=dsub: v.memset(
                                Pb[sb_][:, 0:dsub * 128], 0.0), deps=[t_av[sb_]]))
                        td = p.dve.op(lambda v, sb_=sb_, dsub=dsub: v.tensor_tensor(
                            psS[sb_][:, dsub * 128:(dsub + 1) * 128],
                            psS[sb_][:, dsub * 128:(dsub + 1) * 128], tri[:], ALU.add),
                            deps=[tmm, t_tri, t_av[sb_]])
                        ops.append(td)
                        texp = p.act.op(lambda s, sb_=sb_, dsub=dsub: s.activation(
                            Pb[sb_][:, dsub * 128:], psS[sb_][:, dsub * 128:], AF.Exp,
                            bias=negMh[:]),
                            deps=ops)
                    tlm = p.pe.op(lambda t, kb=kb, s=(kb == 0), e=(kb == nkb - 1):
                                  t.matmul(psL[:], onesb[:], Pb[kb % 2][:], start=s, stop=e),
                                  deps=[texp] + ([t_l_ev] if kb == 0 else []))
                    tom = p.pe.op(lambda t, kb=kb, s=(kb == 0), e=(kb == nkb - 1):
                                  t.matmul(psO[:], Vh[:, kb * 128:(kb + 1) * 128], Pb[kb % 2][:],
                                           start=s, stop=e),
                                  deps=[tvl] + ([t_o_ev] if kb == 0 else []))
                    t_exp[sb_] = texp
                    t_av[sb_] = tom
                t_l1 = p.dve.op(lambda v: v.tensor_copy(lrow[:], psL[:]), deps=[tlm])
                t_l_ev = t_l1
                t_l2 = p.dve.op(lambda v: v.reciprocal(lrow[:], lrow[:]), deps=[t_l1])
                t_l3 = p.dve.op(lambda v: v.tensor_copy(lrowb[:], lrow[:]), deps=[t_l2])
                t_bc2 = p.pe.op(lambda t: t.matmul(psB[:], onesr[:], lrowb[:], start=True,
                                                   stop=True), deps=[t_l3, t_or, t_rb_ev])
                t_rb = p.act.op(lambda s: s.copy(rb[:], psB[:]), deps=[t_bc2])
                t_rb_ev = t_rb
                t_of = p.dve.op(lambda v: v.tensor_tensor(oste[:], psO[:], rb[:], ALU.mult),
                                deps=[t_rb, tom, t_ofdma])
                t_o_ev = t_of
                t_ofdma = p.dma_hw(p.sp, f"of{qt % 2}",
                                   outF_o[i4 * 128:(i4 + 1) * 128, qt * TT:(qt + 1) * TT],
                                   oste[:], deps=[t_of])
            t_head_done = t_of
        p.finish()
        p.emit()
    return nc


def build_G2():
    nc = bass.Bass("TRN2", target_bir_lowering=False)
    x3_in = nc.dram_tensor("x3_in", [C, TPC], F32, kind="ExternalInput").ap()
    outF_in = nc.dram_tensor("outF_in", [C, TPC], BF16, kind="ExternalInput").ap()
    wp_in = nc.dram_tensor("wp_in", [C, C], F32, kind="ExternalInput").ap()
    identf = nc.dram_tensor("identf", [128, 128], F32, kind="ExternalInput").ap()
    out_o = nc.dram_tensor("out_o", [TPC, C], F32, kind="ExternalOutput").ap()
    x3v = x3_in.rearrange("(c p) t -> c p t", p=128)
    ofv = outF_in.rearrange("(c p) t -> c p t", p=128)

    with ExitStack() as ctx:
        def sb(nm, shp, dt, stack=None):
            return (stack or ctx).enter_context(nc.sbuf_tensor(nm, shp, dt))
        p = Prog(nc, ctx)
        idf = sb("idfG2", [128, 128], F32)
        t_idf = p.dma_hw(p.sp, "idf", idf[:], identf)
        ofb = sb("ofbG2", [128, 16 * TT], BF16)
        wpb = [sb(f"wpbG2{i}", [128, 16 * 128], BF16) for i in range(2)]
        x3t = [sb(f"x3tG2{i}", [128, TT], F32) for i in range(2)]
        x4b = sb("x4bG2", [128, 16 * TT], F32)
        tmt = sb("tmtG2", [128, C], F32)

        def w_cols(wT, oc, width=128):
            return wT.rearrange("(kc p) o -> p kc o", p=128)[:, :, oc * width:(oc + 1) * width]

        with nc.psum_tensor("psp0G2", [128, TT], F32) as psp0, \
             nc.psum_tensor("psp1G2", [128, TT], F32) as psp1, \
             nc.psum_tensor("pst0G2", [128, 128], F32) as pst0, \
             nc.psum_tensor("pst1G2", [128, 128], F32) as pst1:
            psps = [psp0, psp1]
            psts = [pst0, pst1]
            t_wfree = [None, None]
            t_x3l = [None, None]
            t_x4w = [None, None]
            t_tp_ev = [None, None]
            t_tmdma = None
            t_of_used = None
            t_x4u = None
            for qt in range(NTT):
                t_ofl = []
                for c in range(16):
                    t_ofl.append(p.dma_hw(p.sp, f"ofl{c % 4}", ofb[:, c * TT:(c + 1) * TT],
                                          ofv[c][:, qt * TT:(qt + 1) * TT],
                                          deps=[t_of_used, t_ofl[c - 4] if c >= 4 else None]))
                for oc in range(16):
                    bo = oc % 2
                    tw = p.dma_sw(f"wpG{bo}", wpb[bo][:], w_cols(wp_in, oc), deps=[t_wfree[bo]])
                    ps = psps[bo]
                    t_mm = None
                    for kc in range(16):
                        fn = p.pe.op if kc == 15 else p.pe.op_quiet
                        t_mm = fn(lambda t, kc=kc, oc=oc, ps=ps, bo=bo, s=(kc == 0), e=(kc == 15):
                                  t.matmul(ps[:], wpb[bo][:, kc * 128:(kc + 1) * 128],
                                           ofb[:, kc * TT:(kc + 1) * TT], start=s, stop=e),
                                  deps=([tw, t_x4w[bo]] + t_ofl if kc == 0 else ()))
                    t_wfree[bo] = t_mm
                    t_of_used = t_mm
                    tx3 = p.dma_hw(p.sp, f"x3G{bo}", x3t[bo][:],
                                   x3v[oc][:, qt * TT:(qt + 1) * TT], deps=[t_x3l[bo]])
                    txa = p.dve.op(lambda v, oc=oc, bo=bo, ps=ps: v.tensor_tensor(
                        x4b[:, oc * TT:(oc + 1) * TT], x3t[bo][:], ps[:], ALU.add),
                        deps=[t_mm, tx3, t_x4u])
                    t_x3l[bo] = txa
                    t_x4w[bo] = txa
                t_x4done = txa
                for tb in range(4):
                    for oc in range(16):
                        bo2 = oc % 2
                        ttp = p.pe.op(lambda t, oc=oc, tb=tb, bo2=bo2:
                                      t.transpose(psts[bo2][:],
                                                  x4b[:, oc * TT + tb * 128: oc * TT + (tb + 1) * 128],
                                                  idf[:]),
                                      deps=[t_x4done, t_idf, t_tp_ev[bo2]])
                        t_tp_ev[bo2] = p.act.op(lambda s, oc=oc, bo2=bo2:
                                                s.copy(tmt[:, oc * 128:(oc + 1) * 128],
                                                       psts[bo2][:]),
                                                deps=[ttp, t_tmdma])
                    t_tmdma = p.dma_hw(p.sp, f"outG{tb % 2}",
                                       out_o[qt * TT + tb * 128: qt * TT + (tb + 1) * 128, :],
                                       tmt[:], deps=[t_tp_ev[0], t_tp_ev[1]])
                    t_x4u = t_tmdma
        p.finish()
        p.emit()
    return nc


def blocks_for_core(core):
    b, m = divmod(core, 4)
    return b, [4 * j + m for j in range(16)]


def shard_x(x):
    out = []
    for core in range(NCORE):
        b, blks = blocks_for_core(core)
        rows = np.concatenate([np.asarray(x[b, blk * 128:(blk + 1) * 128, :]) for blk in blks], axis=0)
        out.append(np.ascontiguousarray(rows, dtype=np.float32))
    return out


def make_A_inmaps(inputs):
    x = np.asarray(inputs["L0_attn_x"])
    w_qkv = np.asarray(inputs["L0_attn_w_qkv"])
    wqT = np.ascontiguousarray(w_qkv[0:C].T)
    wkT = np.ascontiguousarray(w_qkv[C:2 * C].T)
    wvT = np.ascontiguousarray(w_qkv[2 * C:].T)
    identf = np.eye(128, dtype=np.float32)
    identb = np.eye(128, dtype=np.float32).astype(BF)
    gup = np.asarray(inputs["L0_ffn_gate_up_packed"])
    gsc = np.asarray(inputs["L0_ffn_gate_up_scales"])
    dwp = np.asarray(inputs["L0_ffn_down_packed"])
    dsc = np.asarray(inputs["L0_ffn_down_scales"])
    xs = shard_x(x)
    maps = []
    for core in range(NCORE):
        r0, r1 = core * 1408, (core + 1) * 1408
        d0, d1 = core * 256, (core + 1) * 256
        maps.append({
            "x_in": xs[core],
            "wqT": wqT, "wkT": wkT, "wvT": wvT,
            "identf": identf, "identb": identb,
            "gup": np.ascontiguousarray(gup[r0:r1]),
            "gup_sc": np.ascontiguousarray(gsc[:, r0:r1].T),
            "dwp": np.ascontiguousarray(dwp[d0:d1]),
            "dwp_sc": np.ascontiguousarray(dsc[:, d0:d1].T),
        })
    return maps


def make_F_inmaps(inputs, c_outs, e_outs):
    Wa = np.asarray(inputs["L1_ffn_Wa"])
    W_out = np.asarray(inputs["L1_attn_W_out"])
    w_std = np.asarray(inputs["L1_ffn_w_std"])
    w_rec = np.asarray(inputs["L1_ffn_w_rec"])
    H, Ds, R = 16, 120, 8
    q_std = Wa[:, :H * Ds]
    k_std = Wa[:, H * Ds:2 * H * Ds]
    v_w = Wa[:, 2 * H * Ds:2 * H * Ds + 2048]
    q_low = Wa[:, 2 * H * Ds + 2048:2 * H * Ds + 2048 + 128]
    k_low = Wa[:, 2 * H * Ds + 2048 + 128:]
    waq = np.concatenate([np.concatenate([q_low[:, h * 8:(h + 1) * 8],
                                          q_std[:, h * Ds:(h + 1) * Ds]], 1) for h in range(H)], 1)
    wak = np.concatenate([np.concatenate([k_low[:, h * 8:(h + 1) * 8],
                                          k_std[:, h * Ds:(h + 1) * Ds]], 1) for h in range(H)], 1)
    patlow = np.broadcast_to((np.arange(128) < 8).astype(np.float32), (16, 128)).copy()
    pathigh = np.broadcast_to((np.arange(128) >= 8).astype(np.float32), (16, 128)).copy()
    maps = []
    og_fm = {}
    for bb in range(2):
        og_full = np.concatenate([np.asarray(e_outs[4 * bb + mm2]["og_o"]).view(BF)
                                  for mm2 in range(4)], axis=1)
        og_fm[bb] = np.ascontiguousarray(og_full.T)
    for core in range(NCORE):
        bb, blks = blocks_for_core(core)
        og_c = np.concatenate([og_fm[bb][:, blk * 128:(blk + 1) * 128] for blk in blks], axis=1)
        maps.append({
            "x2_in": np.ascontiguousarray(c_outs[core]["x2_o"]),
            "og_in": np.ascontiguousarray(og_c),
            "woT": np.ascontiguousarray(W_out.T),
            "waqT": np.ascontiguousarray(waq), "wakT": np.ascontiguousarray(wak),
            "waqlT": np.ascontiguousarray(q_low), "waklT": np.ascontiguousarray(k_low),
            "wavT": np.ascontiguousarray(v_w),
            "wrec": np.ascontiguousarray(w_rec[:, None]),
            "wstd": np.ascontiguousarray(w_std[:, None]),
            "patlow": patlow, "pathigh": pathigh,
            "ident16": np.eye(16, dtype=np.float32),
        })
    return maps


def make_G1_inmaps(inputs, f_outs):
    sidx = np.arange(128)
    trineg = np.where(sidx[:, None] <= sidx[None, :], 0.0, -1e30).astype(np.float32)
    full = {}
    for bb in range(2):
        kf = np.zeros((2048, 8192), dtype=BF)
        qf = np.zeros((2048, 8192), dtype=BF)
        vf = np.zeros((8192, 2048), dtype=BF)
        for mm2 in range(4):
            core = 4 * bb + mm2
            kc = np.asarray(f_outs[core]["kf_o"]).view(BF)
            qc = np.asarray(f_outs[core]["qf_o"]).view(BF)
            vc = np.asarray(f_outs[core]["v_o"]).view(BF)
            for j in range(16):
                blk = 4 * j + mm2
                kf[:, blk * 128:(blk + 1) * 128] = kc[:, j * 128:(j + 1) * 128]
                qf[:, blk * 128:(blk + 1) * 128] = qc[:, j * 128:(j + 1) * 128]
                vf[blk * 128:(blk + 1) * 128, :] = vc[j * 128:(j + 1) * 128, :]
        full[bb] = (qf, kf, vf)
    maps = []
    for core in range(NCORE):
        bb, mm2 = divmod(core, 4)
        qf, kf, vf = full[bb]
        maps.append({
            "qh4": np.ascontiguousarray(qf[mm2 * 512:(mm2 + 1) * 512]),
            "kh4": np.ascontiguousarray(kf[mm2 * 512:(mm2 + 1) * 512]),
            "vh4": np.ascontiguousarray(vf[:, mm2 * 512:(mm2 + 1) * 512]),
            "trineg": trineg,
            "identf": np.eye(128, dtype=np.float32),
        })
    return maps


def make_G2_inmaps(inputs, f_outs, g1_outs):
    Wp = np.asarray(inputs["L1_ffn_Wp"])
    maps = []
    ofm = {}
    for bb in range(2):
        ofm[bb] = np.concatenate([np.asarray(g1_outs[4 * bb + mm2]["outF_o"]).view(BF)
                                  for mm2 in range(4)], axis=0)
    for core in range(NCORE):
        bb, blks = blocks_for_core(core)
        of_c = np.concatenate([ofm[bb][:, blk * 128:(blk + 1) * 128] for blk in blks], axis=1)
        maps.append({
            "x3_in": np.ascontiguousarray(f_outs[core]["x3_o"]),
            "outF_in": np.ascontiguousarray(of_c),
            "wp_in": Wp,
            "identf": np.eye(128, dtype=np.float32),
        })
    return maps


def kernel(**inputs):
    a_maps = make_A_inmaps(inputs)
    ncA = build_A()
    a_outs = run_bass_kernel_spmd(ncA, a_maps, core_ids=list(range(NCORE))).results
    b_maps = make_B_inmaps(inputs, a_outs)
    ncB = build_B()
    b_outs = run_bass_kernel_spmd(ncB, b_maps, core_ids=list(range(NCORE))).results
    c_maps = make_C_inmaps(inputs, a_outs, b_outs)
    ncC = build_C()
    c_outs = run_bass_kernel_spmd(ncC, c_maps, core_ids=list(range(NCORE))).results
    d_maps = make_D_inmaps(inputs, c_outs)
    ncD = build_D()
    d_outs = run_bass_kernel_spmd(ncD, d_maps, core_ids=list(range(NCORE))).results
    e_maps = make_E_inmaps(inputs, d_outs)
    ncE = build_E()
    e_outs = run_bass_kernel_spmd(ncE, e_maps, core_ids=list(range(NCORE))).results
    f_maps = make_F_inmaps(inputs, c_outs, e_outs)
    ncF = build_F()
    f_outs = run_bass_kernel_spmd(ncF, f_maps, core_ids=list(range(NCORE))).results
    g1_maps = make_G1_inmaps(inputs, f_outs)
    ncG1 = build_G1()
    g1_outs = run_bass_kernel_spmd(ncG1, g1_maps, core_ids=list(range(NCORE))).results
    g2_maps = make_G2_inmaps(inputs, f_outs, g1_outs)
    ncG2 = build_G2()
    g2_outs = run_bass_kernel_spmd(ncG2, g2_maps, core_ids=list(range(NCORE))).results
    out = np.zeros((2, 8192, C), dtype=np.float32)
    for core in range(NCORE):
        bb, blks = blocks_for_core(core)
        shard = np.asarray(g2_outs[core]["out_o"])
        for j, blk in enumerate(blks):
            out[bb, blk * 128:(blk + 1) * 128, :] = shard[j * 128:(j + 1) * 128, :]
    return out
